# version 02
# baseline (speedup 1.0000x reference)
import numpy as np
import ml_dtypes
import math
from contextlib import ExitStack

import concourse.bass as bass
import concourse.mybir as mybir
from concourse.bass_utils import run_bass_kernel_spmd

F32 = mybir.dt.float32
BF16 = mybir.dt.bfloat16
I16 = mybir.dt.int16
AF = mybir.ActivationFunctionType
ALU = mybir.AluOpType
AX = mybir.AxisListType
BF = ml_dtypes.bfloat16

NCORES = 8
B, S, D, H, DH = 2, 4096, 2048, 16, 128
T = B * S
TPC = T // NCORES
NMLP = 16384
NPC = NMLP // NCORES
MH = 1408
EXP_M = 40.0
GELU_A = 1.5957691216057308
GELU_C = 0.044715

def alibi_slopes(n):
    def pow2(m):
        start = 2 ** (-(2 ** -(math.log2(m) - 3)))
        return [start * start ** i for i in range(m)]
    if math.log2(n).is_integer():
        return pow2(n)
    cc = 2 ** math.floor(math.log2(n))
    return pow2(cc) + alibi_slopes(2 * cc)[0::2][: n - cc]

SLOPES = np.array(alibi_slopes(H), dtype=np.float32)


class Prog:
    ENGS = ("sync", "gpsimd", "vector", "scalar", "tensor")

    def __init__(self, nc):
        self.nc = nc
        self.ops = {e: [] for e in self.ENGS}
        self.nld = {"sync": 0, "gpsimd": 0}
        self.c = 0
        self.nst = 0

    def load(self, out_ap, in_ap, queue="sync"):
        tok = self.nld[queue] + 1
        self.nld[queue] = tok
        cwait = self.c
        stwait = self.nst

        def emit(eng, sems):
            if cwait:
                eng.wait_ge(sems["sC"], cwait)
            if stwait:
                eng.wait_ge(sems["st"], 16 * stwait)
            if tok > 1:
                eng.wait_ge(sems["ld_" + queue], 16 * (tok - 1))
            eng.dma_start(out=out_ap, in_=in_ap).then_inc(sems["ld_" + queue], 16)
        self.ops[queue].append(emit)
        return (queue, tok)

    def comp(self, ename, fns, loads=()):
        if callable(fns):
            fns = [fns]
        cwait = self.c
        stwait = self.nst
        self.c += 1
        loads = list(loads)

        def emit(eng, sems):
            if cwait:
                eng.wait_ge(sems["sC"], cwait)
            if stwait:
                eng.wait_ge(sems["st"], 16 * stwait)
            for (q, tok) in loads:
                eng.wait_ge(sems["ld_" + q], 16 * tok)
            last = None
            for fn in fns:
                last = fn(eng)
            last.then_inc(sems["sC"], 1)
        self.ops[ename].append(emit)
        return self.c

    def store(self, out_ap, in_ap, queue="gpsimd"):
        cwait = self.c
        self.nst += 1
        tok = self.nst

        def emit(eng, sems):
            if cwait:
                eng.wait_ge(sems["sC"], cwait)
            if tok > 1:
                eng.wait_ge(sems["st"], 16 * (tok - 1))
            eng.dma_start(out=out_ap, in_=in_ap).then_inc(sems["st"], 16)
        self.ops[queue].append(emit)

    def build(self, block, sems):
        final_c, final_st = self.c, self.nst

        def closer(eng, sems2=sems):
            eng.wait_ge(sems2["sC"], final_c)
            eng.wait_ge(sems2["st"], 16 * final_st)
        self.ops["gpsimd"].append(lambda eng, sems2: closer(eng, sems2))
        for ename in self.ENGS:
            section = getattr(block, ename)
            oplist = self.ops[ename]

            def body(eng, oplist=oplist):
                for op in oplist:
                    op(eng, sems)
            section(body)


class Ctx:
    def __init__(self, nc, stack):
        self.nc = nc
        self.stack = stack
        self.sems = {}
        for name in ("ld_sync", "ld_gpsimd", "st", "sC"):
            self.sems[name] = stack.enter_context(nc.semaphore(name))

    def sb(self, name, shape, dtype):
        return self.stack.enter_context(self.nc.sbuf_tensor(name, shape, dtype))

    def ps(self, name, shape, dtype=F32):
        return self.stack.enter_context(self.nc.psum_tensor(name, shape, dtype))


def new_nc():
    return bass.Bass("TRN2", target_bir_lowering=False)


def launch(nc, in_maps, **kw):
    return run_bass_kernel_spmd(nc, in_maps, core_ids=list(range(NCORES)), **kw)


def emit_ln(p, x_tile, y_out, xc, sq, stats, eps_ap, loads=()):
    nm, var, sd, rstd = (stats[:, 0:1], stats[:, 1:2], stats[:, 2:3], stats[:, 3:4])
    p.comp("vector", lambda v: v.tensor_reduce(out=nm, in_=x_tile, axis=AX.X, op=ALU.add, negate=True), loads=loads)
    p.comp("vector", lambda v: v.tensor_scalar(out=nm, in0=nm, scalar1=1.0 / D, scalar2=None, op0=ALU.mult))
    p.comp("vector", lambda v: v.tensor_scalar(out=xc, in0=x_tile, scalar1=nm, scalar2=None, op0=ALU.add))
    p.comp("scalar", lambda s: s.activation(out=sq, in_=xc, func=AF.Square, accum_out=var))
    p.comp("scalar", lambda s: s.activation(out=sd, in_=var, func=AF.Sqrt, bias=eps_ap, scale=1.0 / D))
    p.comp("vector", lambda v: v.reciprocal(out=rstd, in_=sd))
    p.comp("vector", lambda v: v.tensor_scalar(out=y_out, in0=xc, scalar1=rstd, scalar2=None, op0=ALU.mult))


def emit_decode(p, t, pk_load, val_out):
    pk, T16, n16, a1, a2, z16, w16, alt = (t["pk16"], t["T16"], t["n16"],
                                           t["a1"], t["a2"], t["z16"], t["w16"], t["alt"])
    ev = lambda x: x.rearrange("p (f two) -> p f two", two=2)[:, :, 0:1]
    od = lambda x: x.rearrange("p (f two) -> p f two", two=2)[:, :, 1:2]
    p.comp("vector", lambda v: v.tensor_copy(out=T16[:], in_=pk[:]), loads=pk_load)
    p.comp("vector", lambda v: v.tensor_copy(out=od(T16[:]), in_=ev(pk[:])))
    p.comp("vector", lambda v: v.tensor_tensor(out=n16[:], in0=T16[:], in1=alt[:], op=ALU.logical_shift_right))
    p.comp("vector", lambda v: v.tensor_scalar(out=a1[:], in0=n16[:], scalar1=7, scalar2=6, op0=ALU.bitwise_and, op1=ALU.logical_shift_left))
    p.comp("vector", lambda v: v.tensor_scalar(out=a2[:], in0=n16[:], scalar1=8, scalar2=12, op0=ALU.bitwise_and, op1=ALU.logical_shift_left))
    p.comp("vector", lambda v: v.tensor_scalar(out=T16[:], in0=a1[:], scalar1=0x3F00, scalar2=None, op0=ALU.add))
    p.comp("vector", lambda v: v.tensor_tensor(out=T16[:], in0=T16[:], in1=a2[:], op=ALU.bitwise_or))
    p.comp("vector", lambda v: v.tensor_scalar(out=z16[:], in0=n16[:], scalar1=6, scalar2=None, op0=ALU.bitwise_and))
    p.comp("vector", lambda v: v.tensor_scalar(out=z16[:], in0=z16[:], scalar1=0, scalar2=None, op0=ALU.is_equal))
    p.comp("vector", lambda v: v.tensor_scalar(out=a1[:], in0=a2[:], scalar1=0x3F00, scalar2=None, op0=ALU.bitwise_or))
    p.comp("vector", lambda v: v.tensor_scalar(out=w16[:], in0=n16[:], scalar1=1, scalar2=7, op0=ALU.bitwise_and, op1=ALU.logical_shift_left))
    p.comp("vector", lambda v: v.tensor_tensor(out=a1[:], in0=a1[:], in1=w16[:], op=ALU.subtract))
    p.comp("vector", lambda v: v.tensor_tensor(out=z16[:], in0=z16[:], in1=a1[:], op=ALU.mult))
    p.comp("vector", lambda v: v.tensor_tensor(out=val_out, in0=T16[:].bitcast(BF16), in1=z16[:].bitcast(BF16), op=ALU.subtract))


def build_qkv(layer):
    nc = new_nc()
    rope = (layer == 1)
    d_x = nc.dram_tensor("x", [TPC, D], F32, kind="ExternalInput").ap()
    d_w = {n: nc.dram_tensor(n, [D, D], F32, kind="ExternalInput").ap() for n in ("wqT", "wkT", "wvT")}
    if rope:
        d_b = {n: nc.dram_tensor(n, [1, D], F32, kind="ExternalInput").ap() for n in ("bq", "bk", "bv")}
        d_cos = nc.dram_tensor("cos4", [TPC, 512], F32, kind="ExternalInput").ap()
        d_sin = nc.dram_tensor("sin4", [TPC, 512], F32, kind="ExternalInput").ap()
    d_out = {n: nc.dram_tensor("o_" + n, [TPC, D], F32, kind="ExternalOutput").ap() for n in ("q", "k", "v")}
    d_ident = nc.inline_tensor(np.eye(128, dtype=BF), "identbf")

    NT = TPC // 128
    with ExitStack() as stack:
        ctx = Ctx(nc, stack)
        p = Prog(nc)
        xsb = ctx.sb("xsb", [128, D], F32)
        xc = ctx.sb("xc", [128, D], F32)
        sq = ctx.sb("sq", [128, D], F32)
        stats = ctx.sb("stats", [128, 4], F32)
        epsc = ctx.sb("epsc", [128, 1], F32)
        ytile = ctx.sb("ytile", [128, D], BF16)
        yT = ctx.sb("yT", [128, 16 * TPC], BF16)
        ident = ctx.sb("ident", [128, 128], BF16)
        wtile = ctx.sb("wtile", [128, 16 * 512], BF16)
        onesr = ctx.sb("onesr", [1, 128], BF16)
        brow = ctx.sb("brow", [1, D], BF16)
        outf = ctx.sb("outf", [128, 512], F32)
        costile = ctx.sb("costile", [128, 512], F32)
        sintile = ctx.sb("sintile", [128, 512], F32)
        ra = ctx.sb("ra", [128, 512], F32)
        rb = ctx.sb("rb", [128, 512], F32)
        psA = ctx.ps("psA", [128, 512], F32)
        psT = ctx.ps("psT", [128, 128], BF16)
        block = stack.enter_context(nc.Block())

        li = p.load(ident[:], d_ident.ap())
        p.comp("vector", lambda v: v.memset(onesr[:], 1.0))
        p.comp("vector", lambda v: v.memset(epsc[:], 1e-5))

        for tt in range(NT):
            lx = p.load(xsb[:], d_x[tt * 128:(tt + 1) * 128, :])
            emit_ln(p, xsb[:], ytile[:], xc[:], sq[:], stats[:], epsc[:, 0:1], loads=[lx])
            for kt in range(16):
                p.comp("tensor", lambda t, kt=kt: t.transpose(
                    psT[:], ytile[:, kt * 128:(kt + 1) * 128], ident[:]), loads=[li])
                p.comp("vector", lambda v, kt=kt, tt=tt: v.tensor_copy(
                    out=yT[:, kt * TPC + tt * 128: kt * TPC + (tt + 1) * 128], in_=psT[:]))

        for wn, on in (("wqT", "q"), ("wkT", "k"), ("wvT", "v")):
            if rope:
                lb = p.load(brow[:], d_b[{"wqT": "bq", "wkT": "bk", "wvT": "bv"}[wn]], queue="gpsimd")
            for oc in range(4):
                lws = [p.load(wtile[:, kt * 512:(kt + 1) * 512],
                              d_w[wn][kt * 128:(kt + 1) * 128, oc * 512:(oc + 1) * 512],
                              queue="gpsimd") for kt in range(16)]
                for tt in range(NT):
                    fns = [lambda t, kt=kt, tt=tt: t.matmul(
                        psA[:], yT[:, kt * TPC + tt * 128: kt * TPC + (tt + 1) * 128],
                        wtile[:, kt * 512:(kt + 1) * 512],
                        start=(kt == 0), stop=(kt == 15 and not rope)) for kt in range(16)]
                    if rope:
                        fns.append(lambda t, oc=oc: t.matmul(
                            psA[:], onesr[:], brow[:, oc * 512:(oc + 1) * 512],
                            start=False, stop=True))
                        p.comp("tensor", fns, loads=lws + [lb])
                    else:
                        p.comp("tensor", fns, loads=lws)
                    if rope and wn in ("wqT", "wkT"):
                        lc = p.load(costile[:], d_cos[tt * 128:(tt + 1) * 128, :])
                        ls = p.load(sintile[:], d_sin[tt * 128:(tt + 1) * 128, :])
                        p.comp("vector", lambda v: v.tensor_tensor(out=ra[:], in0=psA[:], in1=costile[:], op=ALU.mult), loads=[lc])
                        p.comp("vector", lambda v: v.tensor_tensor(out=rb[:], in0=psA[:], in1=sintile[:], op=ALU.mult), loads=[ls])
                        ev = lambda x: x.rearrange("p (f two) -> p f two", two=2)[:, :, 0:1]
                        od = lambda x: x.rearrange("p (f two) -> p f two", two=2)[:, :, 1:2]
                        p.comp("vector", lambda v: v.tensor_tensor(out=ev(outf[:]), in0=ev(ra[:]), in1=od(rb[:]), op=ALU.subtract))
                        p.comp("vector", lambda v: v.tensor_tensor(out=od(outf[:]), in0=od(ra[:]), in1=ev(rb[:]), op=ALU.add))
                    else:
                        p.comp("vector", lambda v: v.tensor_copy(out=outf[:], in_=psA[:]))
                    p.store(d_out[on][tt * 128:(tt + 1) * 128, oc * 512:(oc + 1) * 512], outf[:])

        p.build(block, ctx.sems)
    return nc


def build_attn(layer):
    nc = new_nc()
    alibi = (layer == 0)
    d_qT = nc.dram_tensor("qTu", [4, 128, S], F32, kind="ExternalInput").ap()
    d_kT = nc.dram_tensor("kTu", [4, 128, S], F32, kind="ExternalInput").ap()
    d_v = nc.dram_tensor("vu", [4, S, 128], F32, kind="ExternalInput").ap()
    if alibi:
        d_acol = nc.dram_tensor("acolu", [4, 128, 32], F32, kind="ExternalInput").ap()
        d_shift = nc.dram_tensor("shiftu", [4, 8, 512], F32, kind="ExternalInput").ap()
        d_ub = nc.dram_tensor("ubu", [4, 8, 512], F32, kind="ExternalInput").ap()
        d_lb = nc.dram_tensor("lbu", [4, 8, 512], F32, kind="ExternalInput").ap()
    d_z = nc.dram_tensor("o_z", [4, S, 128], F32, kind="ExternalOutput").ap()

    tri_np = np.where(np.arange(128)[:, None] <= np.arange(128)[None, :], 0.0, -1e30).astype(np.float32)
    d_tri = nc.inline_tensor(tri_np, "trimask")

    NB = S // 128
    with ExitStack() as stack:
        ctx = Ctx(nc, stack)
        p = Prog(nc)
        qstage = ctx.sb("qstage", [128, 512], F32)
        qTb = ctx.sb("qTb", [128, S], BF16)
        kTb = ctx.sb("kTb", [128, S], BF16)
        vsb = ctx.sb("vsb", [128, NB * 132], BF16)
        acol = ctx.sb("acol", [128, 32], F32)
        ubt = ctx.sb("ubt", [128, 512], F32)
        lbt = ctx.sb("lbt", [128, 512], F32)
        shrow = ctx.sb("shrow", [1, 512], F32)
        onesc = ctx.sb("onesc", [1, 128], F32)
        tri = ctx.sb("tri", [128, 128], F32)
        Ssb = ctx.sb("Ssb", [128, 512], F32)
        PT = ctx.sb("PT", [128, 512], BF16)
        recip = ctx.sb("recip", [128, 1], F32)
        mcol = ctx.sb("mcol", [128, 1], F32)
        zsb = ctx.sb("zsb", [128, 128], F32)
        psS = ctx.ps("psS", [128, 512], F32)
        zps = [ctx.ps(f"zps{i}", [128, 132], F32) for i in range(4)]
        block = stack.enter_context(nc.Block())

        lt = p.load(tri[:], d_tri.ap())
        p.comp("vector", lambda v: v.memset(onesc[:], 1.0))
        p.comp("vector", lambda v: v.memset(mcol[:], -EXP_M))
        p.comp("vector", lambda v: v.memset(
            vsb[:].rearrange("p (b w) -> p b w", w=132)[:, :, 128:129], 1.0))

        for u in range(4):
            lk = p.load(kTb[:], d_kT[u], queue="gpsimd")
            lvs = []
            for g in range(4):
                lvs.append(p.load(
                    vsb[:].rearrange("p (b w) -> p b w", w=132)[:, g * 8:(g + 1) * 8, 0:128],
                    d_v[u][g * 1024:(g + 1) * 1024, :].rearrange("(b p) w -> p b w", p=128),
                    queue="gpsimd"))
            la = []
            if alibi:
                la = [p.load(acol[:], d_acol[u], queue="sync")]
            lqs = []
            for ch in range(8):
                lq = p.load(qstage[:], d_qT[u][:, ch * 512:(ch + 1) * 512])
                p.comp("scalar", lambda s, ch=ch: s.activation(
                    out=qTb[:, ch * 512:(ch + 1) * 512], in_=qstage[:],
                    func=AF.Identity, scale=1.0 / math.sqrt(DH)), loads=[lq])

            for r in range(8):
                lrun = []
                if alibi:
                    lrun.append(p.load(ubt[:], d_ub[u][r:r + 1, :].broadcast_to([128, 512])))
                    lrun.append(p.load(lbt[:], d_lb[u][r:r + 1, :].broadcast_to([128, 512])))
                    lrun.append(p.load(shrow[:], d_shift[u][r:r + 1, :]))
                for kb in range(4 * r + 4):
                    if alibi:
                        p.comp("tensor", [
                            lambda t, kb=kb, r=r: t.matmul(
                                psS[:], kTb[:, kb * 128:(kb + 1) * 128],
                                qTb[:, r * 512:(r + 1) * 512], start=True, stop=False),
                            lambda t: t.matmul(psS[:], onesc[:], shrow[:], start=False, stop=True),
                        ], loads=[lk] + lrun)
                        p.comp("vector", lambda v: v.tensor_tensor(out=Ssb[:], in0=psS[:], in1=ubt[:], op=ALU.min))
                        p.comp("vector", lambda v: v.tensor_tensor(out=Ssb[:], in0=Ssb[:], in1=lbt[:], op=ALU.max))
                        j = kb - 4 * r
                        if j >= 0:
                            if j > 0:
                                p.comp("vector", lambda v, j=j: v.memset(Ssb[:, 0:j * 128], -1e30))
                            p.comp("vector", lambda v, j=j: v.tensor_tensor(
                                out=Ssb[:, j * 128:(j + 1) * 128], in0=Ssb[:, j * 128:(j + 1) * 128],
                                in1=tri[:], op=ALU.add), loads=[lt])
                        p.comp("scalar", lambda s, kb=kb: s.activation(
                            out=PT[:], in_=Ssb[:], func=AF.Exp, bias=acol[:, kb:kb + 1]), loads=la)
                    else:
                        p.comp("tensor", lambda t, kb=kb, r=r: t.matmul(
                            psS[:], kTb[:, kb * 128:(kb + 1) * 128],
                            qTb[:, r * 512:(r + 1) * 512], start=True, stop=True), loads=[lk])
                        j = kb - 4 * r
                        if j >= 0:
                            p.comp("vector", lambda v, j=j: v.tensor_tensor(
                                out=psS[:, j * 128:(j + 1) * 128], in0=psS[:, j * 128:(j + 1) * 128],
                                in1=tri[:], op=ALU.add), loads=[lt])
                        p.comp("scalar", lambda s: s.activation(
                            out=PT[:], in_=psS[:], func=AF.Exp, bias=mcol[:, 0:1]))
                    fns = []
                    for i in range(4):
                        if kb <= 4 * r + i:
                            fns.append(lambda t, i=i, kb=kb, r=r: t.matmul(
                                zps[i][:, 0:129], PT[:, i * 128:(i + 1) * 128],
                                vsb[:].rearrange("p (b w) -> p b w", w=132)[:, kb, 0:129],
                                start=(kb == 0), stop=(kb == 4 * r + i)))
                    p.comp("tensor", fns, loads=lvs)
                for i in range(4):
                    p.comp("vector", lambda v, i=i: v.reciprocal(out=recip[:], in_=zps[i][:, 128:129]))
                    p.comp("vector", lambda v, i=i: v.tensor_scalar(
                        out=zsb[:], in0=zps[i][:, 0:128], scalar1=recip[:, 0:1], scalar2=None, op0=ALU.mult))
                    p.store(d_z[u][(4 * r + i) * 128:(4 * r + i + 1) * 128, :], zsb[:])

        p.build(block, ctx.sems)
    return nc


def build_oproj(layer):
    nc = new_nc()
    bias = (layer == 1)
    d_zT = nc.dram_tensor("zT", [D, TPC], F32, kind="ExternalInput").ap()
    d_wo = nc.dram_tensor("woT", [D, D], F32, kind="ExternalInput").ap()
    d_x = nc.dram_tensor("x", [TPC, D], F32, kind="ExternalInput").ap()
    if bias:
        d_bo = nc.dram_tensor("bo", [1, D], F32, kind="ExternalInput").ap()
    d_xn = nc.dram_tensor("o_xn", [TPC, D], F32, kind="ExternalOutput").ap()
    d_y = nc.dram_tensor("o_y", [TPC, D], BF16, kind="ExternalOutput").ap()
    if bias:
        d_y32 = nc.dram_tensor("o_y32", [TPC, D], F32, kind="ExternalOutput").ap()

    NT = TPC // 128
    with ExitStack() as stack:
        ctx = Ctx(nc, stack)
        p = Prog(nc)
        zTb = ctx.sb("zTb", [128, 16 * TPC], BF16)
        wsb = ctx.sb("wsb", [128, 16 * 2048], BF16)
        xsb = ctx.sb("xsb", [128, D], F32)
        xn = ctx.sb("xn", [128, D], F32)
        xc = ctx.sb("xc", [128, D], F32)
        sq = ctx.sb("sq", [128, D], F32)
        stats = ctx.sb("stats", [128, 4], F32)
        epsc = ctx.sb("epsc", [128, 1], F32)
        ysb = ctx.sb("ysb", [128, D], BF16)
        onesr = ctx.sb("onesr", [1, 128], BF16)
        brow = ctx.sb("brow", [1, D], BF16)
        psA = ctx.ps("psA", [128, 512], F32)
        block = stack.enter_context(nc.Block())

        p.comp("vector", lambda v: v.memset(onesr[:], 1.0))
        p.comp("vector", lambda v: v.memset(epsc[:], 1e-5))
        lb = []
        if bias:
            lb = [p.load(brow[:], d_bo, queue="gpsimd")]
        lz = [p.load(zTb[:, kt * TPC:(kt + 1) * TPC], d_zT[kt * 128:(kt + 1) * 128, :], queue="gpsimd")
              for kt in range(16)]
        lw = [p.load(wsb[:, kt * 2048:(kt + 1) * 2048], d_wo[kt * 128:(kt + 1) * 128, :], queue="gpsimd")
              for kt in range(16)]

        for tt in range(NT):
            lx = p.load(xsb[:], d_x[tt * 128:(tt + 1) * 128, :])
            for oc in range(4):
                fns = [lambda t, kt=kt, tt=tt, oc=oc: t.matmul(
                    psA[:], zTb[:, kt * TPC + tt * 128: kt * TPC + (tt + 1) * 128],
                    wsb[:, kt * 2048 + oc * 512: kt * 2048 + (oc + 1) * 512],
                    start=(kt == 0), stop=(kt == 15 and not bias)) for kt in range(16)]
                if bias:
                    fns.append(lambda t, oc=oc: t.matmul(
                        psA[:], onesr[:], brow[:, oc * 512:(oc + 1) * 512], start=False, stop=True))
                p.comp("tensor", fns, loads=lz + lw + lb)
                p.comp("vector", lambda v, oc=oc: v.tensor_tensor(
                    out=xn[:, oc * 512:(oc + 1) * 512], in0=psA[:],
                    in1=xsb[:, oc * 512:(oc + 1) * 512], op=ALU.add), loads=[lx])
            p.store(d_xn[tt * 128:(tt + 1) * 128, :], xn[:])
            emit_ln(p, xn[:], ysb[:], xc[:], sq[:], stats[:], epsc[:, 0:1])
            p.store(d_y[tt * 128:(tt + 1) * 128, :], ysb[:])
            if bias:
                p.comp("vector", lambda v: v.tensor_scalar(
                    out=xc[:], in0=xc[:], scalar1=stats[:, 3:4], scalar2=None, op0=ALU.mult))
                p.store(d_y32[tt * 128:(tt + 1) * 128, :], xc[:])

        p.build(block, ctx.sems)
    return nc


def build_ffn0():
    nc = new_nc()
    d_y = nc.dram_tensor("y1T", [D, T], BF16, kind="ExternalInput").ap()
    d_gpk = nc.dram_tensor("gpk", [NPC, 2048], I16, kind="ExternalInput").ap()
    d_upk = nc.dram_tensor("upk", [NPC, 2048], I16, kind="ExternalInput").ap()
    d_gsc = nc.dram_tensor("gsc", [NPC, 64], F32, kind="ExternalInput").ap()
    d_usc = nc.dram_tensor("usc", [NPC, 64], F32, kind="ExternalInput").ap()
    d_dpk = nc.dram_tensor("dpk", [D, NPC], I16, kind="ExternalInput").ap()
    d_dsc = nc.dram_tensor("dsc", [D, NPC // 32], F32, kind="ExternalInput").ap()
    d_part = nc.dram_tensor("o_part", [T, D], F32, kind="ExternalOutput").ap()
    d_inter = nc.dram_tensor("interT", [NPC, T], BF16, kind="Internal")
    d_ident = nc.inline_tensor(np.eye(128, dtype=BF), "identbf")
    alt_np = np.tile(np.array([0, 4], dtype=np.int16), 1024).reshape(1, 2048)
    d_alt = nc.inline_tensor(alt_np, "altshift")

    with ExitStack() as stack:
        ctx = Ctx(nc, stack)
        p = Prog(nc)
        dec = {nm: ctx.sb(nm, [128, 2048], I16) for nm in ("pk16", "T16", "n16", "a1", "a2", "z16", "w16", "alt")}
        val = ctx.sb("val", [128, 2048], BF16)
        scsb = ctx.sb("scsb", [128, 64], F32)
        scb = ctx.sb("scb", [128, 64], BF16)
        ident = ctx.sb("ident", [128, 128], BF16)
        wgT = ctx.sb("wgT", [128, 16 * 2048], BF16)
        wuT = ctx.sb("wuT", [128, 16 * 2048], BF16)
        ysb = ctx.sb("ysb", [128, 16 * 512], BF16)
        g32 = ctx.sb("g32", [128, 512], F32)
        t3 = ctx.sb("t3", [128, 512], F32)
        t4 = ctx.sb("t4", [128, 512], F32)
        sgm = ctx.sb("sgm", [128, 512], F32)
        w32 = ctx.sb("w32", [128, 512], F32)
        inter = ctx.sb("inter", [128, 512], BF16)
        osb = ctx.sb("osb", [128, 512], F32)
        psG = ctx.ps("psG", [128, 512], F32)
        psU = ctx.ps("psU", [128, 512], F32)
        psT = ctx.ps("psT", [128, 128], BF16)
        block = stack.enter_context(nc.Block())

        li = p.load(ident[:], d_ident.ap())
        lalt = p.load(dec["alt"][:], d_alt.ap().broadcast_to([128, 2048]))

        def decode_into(dst, d_pk, d_sc, ntile):
            lpk = p.load(dec["pk16"][:], d_pk[ntile * 128:(ntile + 1) * 128, :], queue="gpsimd")
            lsc = p.load(scsb[:], d_sc[ntile * 128:(ntile + 1) * 128, :], queue="gpsimd")
            emit_decode(p, dec, [lpk, lalt], val[:])
            p.comp("vector", lambda v: v.tensor_copy(out=scb[:], in_=scsb[:]), loads=[lsc])
            vv = val[:].rearrange("p (c q) -> p c q", q=32)
            sbb = scb[:].unsqueeze(2).broadcast_to([128, 64, 32])
            p.comp("vector", lambda v: v.tensor_tensor(out=vv, in0=vv, in1=sbb, op=ALU.mult))
            for ht in range(16):
                p.comp("tensor", lambda t, ht=ht: t.transpose(
                    psT[:], val[:, ht * 128:(ht + 1) * 128], ident[:]), loads=[li])
                p.comp("vector", lambda v, ht=ht, ntile=ntile: v.tensor_copy(
                    out=dst[:, ht * 2048 + ntile * 128: ht * 2048 + (ntile + 1) * 128], in_=psT[:]))

        for ntile in range(16):
            decode_into(wgT, d_gpk, d_gsc, ntile)
            decode_into(wuT, d_upk, d_usc, ntile)

        for tc in range(16):
            ly = [p.load(ysb[:, ht * 512:(ht + 1) * 512],
                         d_y[ht * 128:(ht + 1) * 128, tc * 512:(tc + 1) * 512]) for ht in range(16)]
            for ntile in range(16):
                p.comp("tensor", [lambda t, ht=ht, ntile=ntile: t.matmul(
                    psG[:], wgT[:, ht * 2048 + ntile * 128: ht * 2048 + (ntile + 1) * 128],
                    ysb[:, ht * 512:(ht + 1) * 512], start=(ht == 0), stop=(ht == 15)) for ht in range(16)], loads=ly)
                p.comp("tensor", [lambda t, ht=ht, ntile=ntile: t.matmul(
                    psU[:], wuT[:, ht * 2048 + ntile * 128: ht * 2048 + (ntile + 1) * 128],
                    ysb[:, ht * 512:(ht + 1) * 512], start=(ht == 0), stop=(ht == 15)) for ht in range(16)])
                p.comp("vector", lambda v: v.tensor_copy(out=g32[:], in_=psG[:]))
                p.comp("scalar", lambda s: s.activation(out=t3[:], in_=g32[:], func=AF.Square))
                p.comp("vector", lambda v: v.tensor_scalar(out=t3[:], in0=t3[:], scalar1=GELU_C, scalar2=1.0, op0=ALU.mult, op1=ALU.add))
                p.comp("vector", lambda v: v.tensor_tensor(out=t4[:], in0=g32[:], in1=t3[:], op=ALU.mult))
                p.comp("scalar", lambda s: s.activation(out=sgm[:], in_=t4[:], func=AF.Sigmoid, scale=GELU_A))
                p.comp("vector", lambda v: v.tensor_tensor(out=w32[:], in0=g32[:], in1=sgm[:], op=ALU.mult))
                p.comp("vector", lambda v: v.scalar_tensor_tensor(out=inter[:], in0=w32[:], scalar=1.0, in1=psU[:], op0=ALU.mult, op1=ALU.mult))
                p.store(d_inter.ap()[ntile * 128:(ntile + 1) * 128, tc * 512:(tc + 1) * 512], inter[:])

        for ot in range(16):
            lpk = p.load(dec["pk16"][:], d_dpk[ot * 128:(ot + 1) * 128, :], queue="gpsimd")
            lsc = p.load(scsb[:], d_dsc[ot * 128:(ot + 1) * 128, :], queue="gpsimd")
            emit_decode(p, dec, [lpk, lalt], val[:])
            p.comp("vector", lambda v: v.tensor_copy(out=scb[:], in_=scsb[:]), loads=[lsc])
            vv = val[:].rearrange("p (c q) -> p c q", q=32)
            sbb = scb[:].unsqueeze(2).broadcast_to([128, 64, 32])
            p.comp("vector", lambda v: v.tensor_tensor(out=vv, in0=vv, in1=sbb, op=ALU.mult))
            for nt in range(16):
                p.comp("tensor", lambda t, nt=nt: t.transpose(
                    psT[:], val[:, nt * 128:(nt + 1) * 128], ident[:]), loads=[li])
                p.comp("vector", lambda v, nt=nt, ot=ot: v.tensor_copy(
                    out=wgT[:, nt * 2048 + ot * 128: nt * 2048 + (ot + 1) * 128], in_=psT[:]))

        for tc in range(16):
            lis = [p.load(ysb[:, nt * 512:(nt + 1) * 512],
                          d_inter.ap()[nt * 128:(nt + 1) * 128, tc * 512:(tc + 1) * 512]) for nt in range(16)]
            for tsub in range(4):
                for oc in range(4):
                    p.comp("tensor", [lambda t, nt=nt, tsub=tsub, oc=oc: t.matmul(
                        psG[:], ysb[:, nt * 512 + tsub * 128: nt * 512 + (tsub + 1) * 128],
                        wgT[:, nt * 2048 + oc * 512: nt * 2048 + (oc + 1) * 512],
                        start=(nt == 0), stop=(nt == 15)) for nt in range(16)], loads=lis)
                    p.comp("vector", lambda v: v.tensor_copy(out=osb[:], in_=psG[:]))
                    p.store(d_part[tc * 512 + tsub * 128: tc * 512 + (tsub + 1) * 128,
                                   oc * 512:(oc + 1) * 512], osb[:])

        p.build(block, ctx.sems)
    return nc


def build_reduce(nparts, parts_bf16, with_shared):
    nc = new_nc()
    pdt = BF16 if parts_bf16 else F32
    d_parts = nc.dram_tensor("parts", [nparts, TPC, D], pdt, kind="ExternalInput").ap()
    d_x = nc.dram_tensor("x", [TPC, D], F32, kind="ExternalInput").ap()
    if with_shared:
        d_sh = nc.dram_tensor("shared", [TPC, D], F32, kind="ExternalInput").ap()
    d_xn = nc.dram_tensor("o_xn", [TPC, D], F32, kind="ExternalOutput").ap()

    NT = TPC // 128
    with ExitStack() as stack:
        ctx = Ctx(nc, stack)
        p = Prog(nc)
        acc = ctx.sb("acc", [128, D], F32)
        pt = ctx.sb("pt", [128, D], F32)
        block = stack.enter_context(nc.Block())
        for tt in range(NT):
            lx = p.load(acc[:], d_x[tt * 128:(tt + 1) * 128, :])
            srcs = [d_parts[i][tt * 128:(tt + 1) * 128, :] for i in range(nparts)]
            if with_shared:
                srcs.append(d_sh[tt * 128:(tt + 1) * 128, :])
            first = True
            for sap in srcs:
                lp = p.load(pt[:], sap, queue="gpsimd")
                p.comp("vector", lambda v: v.tensor_tensor(out=acc[:], in0=acc[:], in1=pt[:], op=ALU.add),
                       loads=[lp] + ([lx] if first else []))
                first = False
            p.store(d_xn[tt * 128:(tt + 1) * 128, :], acc[:])
        p.build(block, ctx.sems)
    return nc


def build_moe():
    nc = new_nc()
    d_y = nc.dram_tensor("y3T32", [D, T], F32, kind="ExternalInput").ap()
    d_ym = nc.dram_tensor("y3Tmy32", [D, TPC], F32, kind="ExternalInput").ap()
    d_gw = nc.dram_tensor("gwT", [D, 16], F32, kind="ExternalInput").ap()
    d_w1 = nc.dram_tensor("w1", [2, D, MH], F32, kind="ExternalInput").ap()
    d_w2 = nc.dram_tensor("w2", [2, MH, D], F32, kind="ExternalInput").ap()
    d_sw1 = nc.dram_tensor("sw1", [D, MH], F32, kind="ExternalInput").ap()
    d_sw2 = nc.dram_tensor("sw2", [MH, D], F32, kind="ExternalInput").ap()
    d_esel = nc.dram_tensor("esel", [2, 1, 16], F32, kind="ExternalInput").ap()
    d_part = nc.dram_tensor("o_part", [2, T, D], BF16, kind="ExternalOutput").ap()
    d_shared = nc.dram_tensor("o_shared", [TPC, D], F32, kind="ExternalOutput").ap()

    NH = MH // 128
    with ExitStack() as stack:
        ctx = Ctx(nc, stack)
        p = Prog(nc)
        ysb = ctx.sb("ysb", [128, 16 * 512], BF16)
        yf = ctx.sb("yf", [128, 128], F32)
        gw = ctx.sb("gw", [128, 16 * 16], F32)
        comb = ctx.sb("comb", [128, 64 * 2], F32)
        combt = ctx.sb("combt", [128, 16], F32)
        eselt = ctx.sb("eselt", [128, 2 * 16], F32)
        ssb = ctx.sb("ssb", [128, 16], F32)
        r4 = ctx.sb("r4", [128, 4], F32)
        r4b = ctx.sb("r4b", [128, 4], F32)
        r16 = ctx.sb("r16", [128, 16], F32)
        r1 = ctx.sb("r1", [128, 8], F32)
        w1sb = ctx.sb("w1sb", [128, 16 * MH], BF16)
        w2sb = ctx.sb("w2sb", [128, NH * D], BF16)
        asb = ctx.sb("asb", [128, NH * 512], BF16)
        accb = ctx.sb("accb", [128, 512], BF16)
        accf = ctx.sb("accf", [128, 512], F32)
        psR = ctx.ps("psR", [128, 16], F32)
        psA = ctx.ps("psA", [128, 512], F32)
        psO = ctx.ps("psO", [128, 512], F32)
        block = stack.enter_context(nc.Block())

        lsel = [p.load(eselt[:, e * 16:(e + 1) * 16], d_esel[e].broadcast_to([128, 16])) for e in range(2)]
        lgw = [p.load(gw[:, kt * 16:(kt + 1) * 16], d_gw[kt * 128:(kt + 1) * 128, :]) for kt in range(16)]
        for tt in range(64):
            fns = []
            lyt = []
            for kt in range(16):
                lyt.append(p.load(yf[:], d_y[kt * 128:(kt + 1) * 128, tt * 128:(tt + 1) * 128]))
                fns = [lambda t, kt=kt, tt=tt: t.matmul(
                    psR[:], yf[:], gw[:, kt * 16:(kt + 1) * 16],
                    start=(kt == 0), stop=(kt == 15))]
                p.comp("tensor", fns, loads=[lyt[-1]] + (lgw if kt == 0 else []))
            p.comp("scalar", lambda s: s.activation(out=ssb[:], in_=psR[:], func=AF.Sigmoid))
            sg = ssb[:].rearrange("p (g q) -> p g q", q=4)
            p.comp("vector", lambda v: v.tensor_reduce(out=r4[:], in_=sg, axis=AX.X, op=ALU.max))
            bc4 = r4[:].unsqueeze(2).broadcast_to([128, 4, 4])
            p.comp("vector", lambda v: v.tensor_tensor(out=r16[:].rearrange("p (g q) -> p g q", q=4), in0=sg, in1=bc4, op=ALU.is_equal))
            p.comp("vector", lambda v: v.scalar_tensor_tensor(out=r16[:], in0=r16[:], scalar=-1e30, in1=ssb[:], op0=ALU.mult, op1=ALU.add))
            p.comp("vector", lambda v: v.tensor_reduce(out=r4b[:], in_=r16[:].rearrange("p (g q) -> p g q", q=4), axis=AX.X, op=ALU.max))
            p.comp("vector", lambda v: v.tensor_tensor(out=r4[:], in0=r4[:], in1=r4b[:], op=ALU.add))
            p.comp("vector", lambda v: v.tensor_reduce(out=r1[:, 0:1], in_=r4[:], axis=AX.X, op=ALU.max))
            p.comp("vector", lambda v: v.tensor_scalar(out=r4b[:], in0=r4[:], scalar1=r1[:, 0:1], scalar2=None, op0=ALU.is_equal))
            p.comp("vector", lambda v: v.scalar_tensor_tensor(out=r4b[:], in0=r4b[:], scalar=-1e30, in1=r4[:], op0=ALU.mult, op1=ALU.add))
            p.comp("vector", lambda v: v.tensor_reduce(out=r1[:, 1:2], in_=r4b[:], axis=AX.X, op=ALU.max))
            p.comp("vector", lambda v: v.tensor_scalar(out=r4b[:], in0=r4[:], scalar1=r1[:, 1:2], scalar2=None, op0=ALU.is_ge))
            bc4b = r4b[:].unsqueeze(2).broadcast_to([128, 4, 4])
            p.comp("vector", lambda v: v.tensor_tensor(out=r16[:].rearrange("p (g q) -> p g q", q=4), in0=sg, in1=bc4b, op=ALU.mult))
            p.comp("vector", lambda v: v.tensor_reduce(out=r1[:, 2:3], in_=r16[:], axis=AX.X, op=ALU.max))
            p.comp("vector", lambda v: v.tensor_scalar(out=ssb[:], in0=r16[:], scalar1=r1[:, 2:3], scalar2=None, op0=ALU.is_equal))
            p.comp("vector", lambda v: v.scalar_tensor_tensor(out=ssb[:], in0=ssb[:], scalar=-1e30, in1=r16[:], op0=ALU.mult, op1=ALU.add))
            p.comp("vector", lambda v: v.tensor_reduce(out=r1[:, 3:4], in_=ssb[:], axis=AX.X, op=ALU.max))
            p.comp("vector", lambda v: v.tensor_tensor(out=r1[:, 4:5], in0=r1[:, 2:3], in1=r1[:, 3:4], op=ALU.add))
            p.comp("vector", lambda v: v.tensor_scalar(out=r1[:, 4:5], in0=r1[:, 4:5], scalar1=1e-20, scalar2=None, op0=ALU.add))
            p.comp("vector", lambda v: v.reciprocal(out=r1[:, 5:6], in_=r1[:, 4:5]))
            p.comp("vector", lambda v: v.tensor_scalar(out=ssb[:], in0=r16[:], scalar1=r1[:, 3:4], scalar2=None, op0=ALU.is_ge))
            p.comp("vector", lambda v: v.tensor_tensor(out=ssb[:], in0=ssb[:], in1=r16[:], op=ALU.mult))
            p.comp("vector", lambda v: v.tensor_scalar(out=ssb[:], in0=ssb[:], scalar1=r1[:, 5:6], scalar2=None, op0=ALU.mult))
            for e in range(2):
                p.comp("vector", lambda v, e=e: v.tensor_tensor(out=combt[:], in0=ssb[:], in1=eselt[:, e * 16:(e + 1) * 16], op=ALU.mult), loads=[lsel[e]])
                p.comp("vector", lambda v, e=e, tt=tt: v.tensor_reduce(out=comb[:, tt * 2 + e: tt * 2 + e + 1], in_=combt[:], axis=AX.X, op=ALU.add))

        def expert_pass(d_w1src, d_w2src, n_tc, d_ysrc, use_comb, e, d_out):
            lw1 = [p.load(w1sb[:, kt * MH:(kt + 1) * MH], d_w1src[kt * 128:(kt + 1) * 128, :], queue="gpsimd")
                   for kt in range(16)]
            lw2 = [p.load(w2sb[:, ht * D:(ht + 1) * D], d_w2src[ht * 128:(ht + 1) * 128, :], queue="gpsimd")
                   for ht in range(NH)]
            for tc in range(n_tc):
                ly = [p.load(ysb[:, kt * 512:(kt + 1) * 512],
                             d_ysrc[kt * 128:(kt + 1) * 128, tc * 512:(tc + 1) * 512], queue="gpsimd")
                      for kt in range(16)]
                for ht in range(NH):
                    p.comp("tensor", [lambda t, kt=kt, ht=ht: t.matmul(
                        psA[:], w1sb[:, kt * MH + ht * 128: kt * MH + (ht + 1) * 128],
                        ysb[:, kt * 512:(kt + 1) * 512], start=(kt == 0), stop=(kt == 15))
                        for kt in range(16)], loads=ly + lw1)
                    p.comp("scalar", lambda s, ht=ht: s.activation(
                        out=asb[:, ht * 512:(ht + 1) * 512], in_=psA[:], func=AF.Silu))
                for tsub in range(4):
                    for oc in range(4):
                        p.comp("tensor", [lambda t, ht=ht, tsub=tsub, oc=oc: t.matmul(
                            psO[:], asb[:, ht * 512 + tsub * 128: ht * 512 + (tsub + 1) * 128],
                            w2sb[:, ht * D + oc * 512: ht * D + (oc + 1) * 512],
                            start=(ht == 0), stop=(ht == NH - 1)) for ht in range(NH)], loads=lw2)
                        if use_comb:
                            cap = comb[:, (tc * 4 + tsub) * 2 + e: (tc * 4 + tsub) * 2 + e + 1]
                            p.comp("vector", lambda v, cap=cap: v.tensor_scalar(
                                out=accb[:], in0=psO[:], scalar1=cap, scalar2=None, op0=ALU.mult))
                            p.store(d_out[tc * 512 + tsub * 128: tc * 512 + (tsub + 1) * 128,
                                          oc * 512:(oc + 1) * 512], accb[:])
                        else:
                            p.comp("vector", lambda v: v.tensor_copy(out=accf[:], in_=psO[:]))
                            p.store(d_out[tc * 512 + tsub * 128: tc * 512 + (tsub + 1) * 128,
                                          oc * 512:(oc + 1) * 512], accf[:])

        for e in range(2):
            expert_pass(d_w1[e], d_w2[e], 16, d_y, True, e, d_part[e])
        expert_pass(d_sw1, d_sw2, TPC // 512, d_ym, False, 0, d_shared)

        p.build(block, ctx.sems)
    return nc


def _t(a):
    return np.ascontiguousarray(np.asarray(a).T)


def attn_inputs(q, k, v, alibi):
    maps = []
    pos = np.arange(S, dtype=np.float32)
    for c in range(NCORES):
        units = [(b, h) for b in range(B) for h in (2 * c, 2 * c + 1)]
        qTu = np.stack([_t(q.reshape(B, S, H, DH)[b, :, h, :]) for (b, h) in units])
        kTu = np.stack([_t(k.reshape(B, S, H, DH)[b, :, h, :]) for (b, h) in units])
        vu = np.stack([np.ascontiguousarray(v.reshape(B, S, H, DH)[b, :, h, :]) for (b, h) in units])
        m = dict(qTu=qTu, kTu=kTu, vu=vu)
        if alibi:
            acolu = np.stack([
                (SLOPES[h] * pos - EXP_M).reshape(32, 128).T.copy() for (b, h) in units])
            shiftu = np.stack([
                np.stack([-SLOPES[h] * (r * 512 + np.arange(512, dtype=np.float32)) for r in range(8)])
                for (b, h) in units])
            m.update(acolu=np.ascontiguousarray(acolu, dtype=np.float32),
                     shiftu=np.ascontiguousarray(shiftu, dtype=np.float32),
                     ubu=np.ascontiguousarray(1024.0 + shiftu, dtype=np.float32),
                     lbu=np.ascontiguousarray(-1024.0 + shiftu, dtype=np.float32))
        maps.append(m)
    return maps


def attn_gather(res):
    z = np.zeros((B, S, H, DH), dtype=np.float32)
    for c in range(NCORES):
        units = [(b, h) for b in range(B) for h in (2 * c, 2 * c + 1)]
        for ui, (b, h) in enumerate(units):
            z[b, :, h, :] = res.results[c]["o_z"][ui]
    return z.reshape(T, D)


def kernel_multi(**inputs):
    xf = np.ascontiguousarray(np.asarray(inputs["L0_attn_resid_pre"], dtype=np.float32).reshape(T, D))
    my = lambda a, c: np.ascontiguousarray(a[c * TPC:(c + 1) * TPC])

    w0 = {n: _t(inputs["L0_attn_" + k]) for n, k in (("wqT", "Wq"), ("wkT", "Wk"), ("wvT", "Wv"))}
    nc = build_qkv(0)
    res = launch(nc, [dict(x=my(xf, c), **w0) for c in range(NCORES)])
    q0 = np.concatenate([r["o_q"] for r in res.results])
    k0 = np.concatenate([r["o_k"] for r in res.results])
    v0 = np.concatenate([r["o_v"] for r in res.results])

    nc = build_attn(0)
    res = launch(nc, attn_inputs(q0, k0, v0, True))
    z0 = attn_gather(res)

    wo0T = _t(inputs["L0_attn_Wo"])
    z0T = _t(z0)
    nc = build_oproj(0)
    res = launch(nc, [dict(zT=np.ascontiguousarray(z0T[:, c * TPC:(c + 1) * TPC]),
                           woT=wo0T, x=my(xf, c)) for c in range(NCORES)])
    x1 = np.concatenate([r["o_xn"] for r in res.results])
    y1 = np.concatenate([r["o_y"] for r in res.results])

    y1T = _t(y1)
    gpk = np.asarray(inputs["L0_ffn_gate_W_packed"], dtype=np.int32)
    upk = np.asarray(inputs["L0_ffn_up_W_packed"], dtype=np.int32)
    dpk = np.asarray(inputs["L0_ffn_down_W_packed"], dtype=np.int32)
    gsc = np.asarray(inputs["L0_ffn_gate_scales"], dtype=np.float32)
    usc = np.asarray(inputs["L0_ffn_up_scales"], dtype=np.float32)
    dsc = np.asarray(inputs["L0_ffn_down_scales"], dtype=np.float32)
    nc = build_ffn0()
    maps = []
    for c in range(NCORES):
        n0 = c * NPC
        maps.append(dict(
            y1T=y1T,
            gpk=np.ascontiguousarray(gpk[n0:n0 + NPC]).view(np.int16),
            upk=np.ascontiguousarray(upk[n0:n0 + NPC]).view(np.int16),
            gsc=np.ascontiguousarray(gsc[n0:n0 + NPC]),
            usc=np.ascontiguousarray(usc[n0:n0 + NPC]),
            dpk=np.ascontiguousarray(dpk[:, n0 // 2:(n0 + NPC) // 2]).view(np.int16),
            dsc=np.ascontiguousarray(dsc[:, n0 // 32:(n0 + NPC) // 32]),
        ))
    res = launch(nc, maps)
    parts0 = [r["o_part"] for r in res.results]

    nc = build_reduce(NCORES, False, False)
    res = launch(nc, [dict(parts=np.stack([pp[c * TPC:(c + 1) * TPC] for pp in parts0]),
                           x=my(x1, c)) for c in range(NCORES)])
    x2 = np.concatenate([r["o_xn"] for r in res.results])

    w1a = {n: _t(inputs["L1_attn_" + k]) for n, k in (("wqT", "Wq"), ("wkT", "Wk"), ("wvT", "Wv"))}
    b1 = {n: np.asarray(inputs["L1_attn_" + k], dtype=np.float32).reshape(1, D)
          for n, k in (("bq", "bq"), ("bk", "bk"), ("bv", "bv"))}
    half = DH // 2
    inv = 1.0 / (10000.0 ** (np.arange(half, dtype=np.float32) / half))
    tpos = np.arange(S, dtype=np.float32)
    fr = tpos[:, None] * inv[None, :]
    cosS = np.repeat(np.cos(fr), 2, axis=1).astype(np.float32)
    sinS = np.repeat(np.sin(fr), 2, axis=1).astype(np.float32)
    nc = build_qkv(1)
    maps = []
    for c in range(NCORES):
        rows = np.arange(c * TPC, (c + 1) * TPC) % S
        maps.append(dict(x=my(x2, c), **w1a, **b1,
                         cos4=np.ascontiguousarray(np.tile(cosS[rows], (1, 4))),
                         sin4=np.ascontiguousarray(np.tile(sinS[rows], (1, 4)))))
    res = launch(nc, maps)
    q1 = np.concatenate([r["o_q"] for r in res.results])
    k1 = np.concatenate([r["o_k"] for r in res.results])
    v1 = np.concatenate([r["o_v"] for r in res.results])

    nc = build_attn(1)
    res = launch(nc, attn_inputs(q1, k1, v1, False))
    z1 = attn_gather(res)

    wo1T = _t(inputs["L1_attn_Wo"])
    bo1 = np.asarray(inputs["L1_attn_bo"], dtype=np.float32).reshape(1, D)
    z1T = _t(z1)
    nc = build_oproj(1)
    res = launch(nc, [dict(zT=np.ascontiguousarray(z1T[:, c * TPC:(c + 1) * TPC]),
                           woT=wo1T, x=my(x2, c), bo=bo1) for c in range(NCORES)])
    x3 = np.concatenate([r["o_xn"] for r in res.results])
    y3_32 = np.concatenate([r["o_y32"] for r in res.results])

    y3T32 = _t(y3_32)
    gwT = _t(inputs["L1_ffn_gate_w"])
    w1m = np.asarray(inputs["L1_ffn_w1"], dtype=np.float32)
    w2m = np.asarray(inputs["L1_ffn_w2"], dtype=np.float32)
    sw1 = np.ascontiguousarray(np.asarray(inputs["L1_ffn_sw1"], dtype=np.float32)[0])
    sw2 = np.ascontiguousarray(np.asarray(inputs["L1_ffn_sw2"], dtype=np.float32)[0])
    nc = build_moe()
    maps = []
    for c in range(NCORES):
        esel = np.zeros((2, 1, 16), dtype=np.float32)
        esel[0, 0, 2 * c] = 1.0
        esel[1, 0, 2 * c + 1] = 1.0
        maps.append(dict(
            y3T32=y3T32,
            y3Tmy32=np.ascontiguousarray(y3T32[:, c * TPC:(c + 1) * TPC]),
            esel=esel,
            gwT=gwT,
            w1=np.ascontiguousarray(w1m[2 * c:2 * c + 2]),
            w2=np.ascontiguousarray(w2m[2 * c:2 * c + 2]),
            sw1=sw1, sw2=sw2))
    res = launch(nc, maps)
    partsM = [r["o_part"][e] for r in res.results for e in range(2)]
    sharedM = [r["o_shared"] for r in res.results]

    nc = build_reduce(2 * NCORES, True, True)
    res = launch(nc, [dict(parts=np.stack([pp[c * TPC:(c + 1) * TPC] for pp in partsM]),
                           x=my(x3, c), shared=sharedM[c]) for c in range(NCORES)])
    out = np.concatenate([r["o_xn"] for r in res.results])
    return out.reshape(B, S, D).astype(np.float32)


def build_fused(upto=9):
    nc = new_nc()
    RG = [list(range(NCORES))]

    d_x0 = nc.dram_tensor("x0", [TPC, D], F32, kind="ExternalInput").ap()
    d_wq0 = nc.dram_tensor("wq0", [D, 256], F32, kind="ExternalInput").ap()
    d_wk0 = nc.dram_tensor("wk0", [D, 256], F32, kind="ExternalInput").ap()
    d_wv0 = nc.dram_tensor("wv0", [D, 256], F32, kind="ExternalInput").ap()
    d_wo0 = nc.dram_tensor("wo0", [256, D], F32, kind="ExternalInput").ap()
    d_acol = nc.dram_tensor("acolu", [4, 128, 32], F32, kind="ExternalInput").ap()
    d_shift = nc.dram_tensor("shiftu", [4, 8, 512], F32, kind="ExternalInput").ap()
    d_ub = nc.dram_tensor("ubu", [4, 8, 512], F32, kind="ExternalInput").ap()
    d_lb = nc.dram_tensor("lbu", [4, 8, 512], F32, kind="ExternalInput").ap()
    d_gpk = nc.dram_tensor("gpk", [NPC, 2048], I16, kind="ExternalInput").ap()
    d_upk = nc.dram_tensor("upk", [NPC, 2048], I16, kind="ExternalInput").ap()
    d_gsc = nc.dram_tensor("gsc", [NPC, 64], F32, kind="ExternalInput").ap()
    d_usc = nc.dram_tensor("usc", [NPC, 64], F32, kind="ExternalInput").ap()
    d_dpk = nc.dram_tensor("dpk", [D, NPC], I16, kind="ExternalInput").ap()
    d_dsc = nc.dram_tensor("dsc", [D, NPC // 32], F32, kind="ExternalInput").ap()
    d_wq1 = nc.dram_tensor("wq1", [D, 256], F32, kind="ExternalInput").ap()
    d_wk1 = nc.dram_tensor("wk1", [D, 256], F32, kind="ExternalInput").ap()
    d_wv1 = nc.dram_tensor("wv1", [D, 256], F32, kind="ExternalInput").ap()
    d_bq1 = nc.dram_tensor("bq1", [1, 256], F32, kind="ExternalInput").ap()
    d_bk1 = nc.dram_tensor("bk1", [1, 256], F32, kind="ExternalInput").ap()
    d_bv1 = nc.dram_tensor("bv1", [1, 256], F32, kind="ExternalInput").ap()
    d_wo1 = nc.dram_tensor("wo1", [256, D], F32, kind="ExternalInput").ap()
    d_bo1 = nc.dram_tensor("bo1", [1, D], F32, kind="ExternalInput").ap()
    d_cosS = nc.dram_tensor("cosS", [S, 128], F32, kind="ExternalInput").ap()
    d_sinS = nc.dram_tensor("sinS", [S, 128], F32, kind="ExternalInput").ap()
    d_gw = nc.dram_tensor("gwT", [D, 16], F32, kind="ExternalInput").ap()
    d_w1 = nc.dram_tensor("w1", [2, D, MH], F32, kind="ExternalInput").ap()
    d_w2 = nc.dram_tensor("w2", [2, MH, D], F32, kind="ExternalInput").ap()
    d_sw1 = nc.dram_tensor("sw1", [D, MH], F32, kind="ExternalInput").ap()
    d_sw2 = nc.dram_tensor("sw2", [MH, D], F32, kind="ExternalInput").ap()
    d_esel = nc.dram_tensor("esel", [2, 1, 16], F32, kind="ExternalInput").ap()
    d_out = nc.dram_tensor("o_x4", [TPC, D], F32, kind="ExternalOutput").ap()

    tri_np = np.where(np.arange(128)[:, None] <= np.arange(128)[None, :], 0.0, -1e30).astype(np.float32)
    d_tri = nc.inline_tensor(tri_np, "trimask")
    d_identb = nc.inline_tensor(np.eye(128, dtype=BF), "identb")
    d_identf = nc.inline_tensor(np.eye(128, dtype=np.float32), "identf")
    alt_np = np.tile(np.array([0, 4], dtype=np.int16), 1024).reshape(1, 2048)
    d_alt = nc.inline_tensor(alt_np, "altshift")

    it = lambda nm, sh, dt: nc.dram_tensor(nm, sh, dt, kind="Internal")
    yTl = it("yTl", [D, TPC], BF16)
    yTl32 = it("yTl32", [D, TPC], F32)
    G = [it(f"G{i}", [NCORES * D, TPC], BF16) for i in range(3)]
    G32 = it("G32", [NCORES * D, TPC], F32)
    qtu = it("qtu", [4, 128, S], BF16)
    ktu = it("ktu", [4, 128, S], BF16)
    vu = it("vu", [4, S, 128], BF16)
    zTl = [it(f"zTl{i}", [256, T], BF16) for i in range(2)]
    part = [it(f"part{i}", [T, D], F32) for i in range(4)]
    rs = [it(f"rs{i}", [TPC, D], F32) for i in range(4)]
    xsc = [it(f"xsc{i}", [TPC, D], F32) for i in range(3)]
    interT = it("interT", [NPC, T], BF16)
    e0p = it("e0p", [T, D], BF16)
    shl = it("shl", [TPC, D], F32)

    NT = TPC // 128

    _uid = [0]

    def _u():
        _uid[0] += 1
        return f"u{_uid[0]}_"

    with ExitStack() as topstack:
        ctx = Ctx(nc, topstack)
        p = Prog(nc)
        block = topstack.enter_context(nc.Block())

        def ph_ln_ag(x_src, ytl_ap, g_ap, f32mode):
            dt = F32 if f32mode else BF16
            with ExitStack() as st_:
                sb = lambda nm, sh, dtt: st_.enter_context(nc.sbuf_tensor(_u() + nm, sh, dtt))
                xsb = sb("ln_x", [128, D], F32)
                xc = sb("ln_xc", [128, D], F32)
                sq = sb("ln_sq", [128, D], F32)
                stats = sb("ln_st", [128, 4], F32)
                epsc = sb("ln_eps", [128, 1], F32)
                ytile = sb("ln_y", [128, D], dt)
                ident = sb("ln_id", [128, 128], dt)
                stage = sb("ln_stg", [128, 128], dt)
                psT_ = st_.enter_context(nc.psum_tensor(_u() + "ln_psT", [128, 128], dt))
                li = p.load(ident[:], (d_identf if f32mode else d_identb).ap())
                p.comp("vector", lambda v: v.memset(epsc[:], 1e-5))
                for tt in range(NT):
                    lx = p.load(xsb[:], x_src[tt * 128:(tt + 1) * 128, :])
                    emit_ln(p, xsb[:], ytile[:], xc[:], sq[:], stats[:], epsc[:, 0:1], loads=[lx])
                    for kt in range(16):
                        p.comp("tensor", lambda t, kt=kt: t.transpose(
                            psT_[:], ytile[:, kt * 128:(kt + 1) * 128], ident[:]), loads=[li])
                        p.comp("vector", lambda v: v.tensor_copy(out=stage[:], in_=psT_[:]))
                        p.store(ytl_ap[kt * 128:(kt + 1) * 128, tt * 128:(tt + 1) * 128], stage[:])
            p.comp("gpsimd", lambda g_: g_.collective_compute(
                "AllGather", ALU.bypass, replica_groups=RG,
                ins=[ytl_ap.opt()], outs=[g_ap.opt()]))

        def ph_qkv_units0(g_ap, wq_ap, wk_ap, wv_ap, scaleq):
            with ExitStack() as st_:
                sb = lambda nm, sh, dtt: st_.enter_context(nc.sbuf_tensor(_u() + nm, sh, dtt))
                wsb = sb("qk_w", [128, 6 * 16 * 128], BF16)
                rhs = sb("qk_rhs", [128, 512], BF16)
                stg = sb("qk_stg", [128, 512], BF16)
                vstg = sb("qk_vstg", [128, 512], BF16)
                identb_ = sb("qk_id", [128, 128], BF16)
                pss = [st_.enter_context(nc.psum_tensor(_u() + f"qk_ps{i}", [128, 512], F32)) for i in range(6)]
                psT_ = st_.enter_context(nc.psum_tensor(_u() + "qk_psT", [128, 128], BF16))
                li = p.load(identb_[:], d_identb.ap())
                lw = []
                for wi, wap in enumerate((wq_ap, wk_ap, wv_ap)):
                    for hl in range(2):
                        for kt in range(16):
                            lw.append(p.load(
                                wsb[:, ((wi * 2 + hl) * 16 + kt) * 128:((wi * 2 + hl) * 16 + kt + 1) * 128],
                                wap[kt * 128:(kt + 1) * 128, hl * 128:(hl + 1) * 128], queue="gpsimd"))
                for r in range(NCORES):
                    b = r // 4
                    for tc in range(2):
                        bcol = (r % 4) * TPC + tc * 512
                        lr = []
                        for kt in range(16):
                            lr.append(p.load(rhs[:], g_ap[r * D + kt * 128: r * D + kt * 128 + 128,
                                                          tc * 512:(tc + 1) * 512]))
                            p.comp("tensor", [lambda t, slot=slot, kt=kt: t.matmul(
                                pss[slot][:], wsb[:, (slot * 16 + kt) * 128:(slot * 16 + kt + 1) * 128],
                                rhs[:], start=(kt == 0), stop=(kt == 15)) for slot in range(6)],
                                loads=[lr[-1]] + (lw if kt == 0 else []))
                        for wi, dst in ((0, qtu), (1, ktu)):
                            for hl in range(2):
                                u = b * 2 + hl
                                slot = wi * 2 + hl
                                if wi == 0:
                                    p.comp("scalar", lambda s, slot=slot: s.activation(
                                        out=stg[:], in_=pss[slot][:], func=AF.Identity, scale=scaleq))
                                else:
                                    p.comp("vector", lambda v, slot=slot: v.tensor_copy(out=stg[:], in_=pss[slot][:]))
                                p.store(dst.ap()[u][:, bcol:bcol + 512], stg[:])
                        for hl in range(2):
                            u = b * 2 + hl
                            slot = 4 + hl
                            p.comp("vector", lambda v, slot=slot: v.tensor_copy(out=vstg[:], in_=pss[slot][:]))
                            for i in range(4):
                                p.comp("tensor", lambda t, i=i: t.transpose(
                                    psT_[:], vstg[:, i * 128:(i + 1) * 128], identb_[:]), loads=[li])
                                p.comp("vector", lambda v: v.tensor_copy(out=stg[:, 0:128], in_=psT_[:]))
                                p.store(vu.ap()[u][bcol + i * 128: bcol + (i + 1) * 128, :], stg[:, 0:128])

        def ph_qkv_units1(g_ap):
            with ExitStack() as st_:
                sb = lambda nm, sh, dtt: st_.enter_context(nc.sbuf_tensor(_u() + nm, sh, dtt))
                wsb = sb("q1_w", [128, 4 * 16 * 128], BF16)
                wvsb = sb("q1_wv", [128, 2 * 16 * 128], BF16)
                ltile = sb("q1_l", [128, 128], BF16)
                bbc = sb("q1_bbc", [128, 4 * 128], F32)
                bvcol = sb("q1_bvc", [128, 2], F32)
                costl = sb("q1_cos", [128, 128], F32)
                sintl = sb("q1_sin", [128, 128], F32)
                f32a = sb("q1_a", [128, 128], F32)
                ra = sb("q1_ra", [128, 128], F32)
                rb = sb("q1_rb", [128, 128], F32)
                rop = sb("q1_ro", [128, 128], F32)
                identf_ = sb("q1_idf", [128, 128], F32)
                identb_ = sb("q1_idb", [128, 128], BF16)
                stg = sb("q1_stg", [128, 512], BF16)
                vstg = sb("q1_vstg", [128, 512], BF16)
                rhs = sb("q1_rhs", [128, 512], BF16)
                psQK = st_.enter_context(nc.psum_tensor(_u() + "q1_psqk", [128, 512], F32))
                psTf = st_.enter_context(nc.psum_tensor(_u() + "q1_psTf", [128, 128], F32))
                psTb = st_.enter_context(nc.psum_tensor(_u() + "q1_psTb", [128, 128], BF16))
                psV = [st_.enter_context(nc.psum_tensor(_u() + f"q1_psv{i}", [128, 512], F32)) for i in range(2)]
                lif = p.load(identf_[:], d_identf.ap())
                lib = p.load(identb_[:], d_identb.ap())
                lw = []
                for wi, wap in enumerate((d_wq1, d_wk1)):
                    for hl in range(2):
                        for kt in range(16):
                            lw.append(p.load(
                                wsb[:, ((wi * 2 + hl) * 16 + kt) * 128:((wi * 2 + hl) * 16 + kt + 1) * 128],
                                wap[kt * 128:(kt + 1) * 128, hl * 128:(hl + 1) * 128], queue="gpsimd"))
                for hl in range(2):
                    for kt in range(16):
                        lw.append(p.load(
                            wvsb[:, (hl * 16 + kt) * 128:(hl * 16 + kt + 1) * 128],
                            d_wv1[kt * 128:(kt + 1) * 128, hl * 128:(hl + 1) * 128], queue="gpsimd"))
                lb = []
                for wi, bap in enumerate((d_bq1, d_bk1)):
                    for hl in range(2):
                        lb.append(p.load(bbc[:, (wi * 2 + hl) * 128:(wi * 2 + hl + 1) * 128],
                                         bap[:, hl * 128:(hl + 1) * 128].broadcast_to([128, 128])))
                lbv = [p.load(bvcol[:, hl:hl + 1],
                              d_bv1[:, hl * 128:(hl + 1) * 128].rearrange("o p -> p o"))
                       for hl in range(2)]
                for b in range(B):
                    for pt in range(32):
                        r = b * 4 + pt // 8
                        coff = (pt % 8) * 128
                        lts = []
                        for kt in range(16):
                            lts.append(p.load(ltile[:], g_ap[r * D + kt * 128: r * D + kt * 128 + 128,
                                                             coff:coff + 128]))
                            fns = [lambda t, slot=slot, kt=kt: t.matmul(
                                psQK[:, slot * 128:(slot + 1) * 128], ltile[:],
                                wsb[:, (slot * 16 + kt) * 128:(slot * 16 + kt + 1) * 128],
                                start=(kt == 0), stop=(kt == 15)) for slot in range(4)]
                            p.comp("tensor", fns, loads=[lts[-1]] + (lw + lb if (kt == 0 and pt == 0 and b == 0) else []))
                        lc = p.load(costl[:], d_cosS[pt * 128:(pt + 1) * 128, :])
                        ls = p.load(sintl[:], d_sinS[pt * 128:(pt + 1) * 128, :])
                        for wi in range(2):
                            for hl in range(2):
                                slot = wi * 2 + hl
                                u = b * 2 + hl
                                p.comp("vector", lambda v, slot=slot: v.tensor_tensor(
                                    out=f32a[:], in0=psQK[:, slot * 128:(slot + 1) * 128],
                                    in1=bbc[:, slot * 128:(slot + 1) * 128], op=ALU.add))
                                p.comp("vector", lambda v: v.tensor_tensor(out=ra[:], in0=f32a[:], in1=costl[:], op=ALU.mult), loads=[lc])
                                p.comp("vector", lambda v: v.tensor_tensor(out=rb[:], in0=f32a[:], in1=sintl[:], op=ALU.mult), loads=[ls])
                                ev = lambda x: x.rearrange("p (f two) -> p f two", two=2)[:, :, 0:1]
                                od = lambda x: x.rearrange("p (f two) -> p f two", two=2)[:, :, 1:2]
                                p.comp("vector", lambda v: v.tensor_tensor(out=ev(rop[:]), in0=ev(ra[:]), in1=od(rb[:]), op=ALU.subtract))
                                p.comp("vector", lambda v: v.tensor_tensor(out=od(rop[:]), in0=od(ra[:]), in1=ev(rb[:]), op=ALU.add))
                                p.comp("tensor", lambda t: t.transpose(psTf[:], rop[:], identf_[:]), loads=[lif])
                                if wi == 0:
                                    p.comp("scalar", lambda s: s.activation(
                                        out=stg[:, 0:128], in_=psTf[:], func=AF.Identity, scale=1.0 / math.sqrt(DH)))
                                else:
                                    p.comp("vector", lambda v: v.tensor_copy(out=stg[:, 0:128], in_=psTf[:]))
                                dst = qtu if wi == 0 else ktu
                                p.store(dst.ap()[u][:, b * 0 + pt * 128: pt * 128 + 128], stg[:, 0:128])
                for r in range(NCORES):
                    b = r // 4
                    for tc in range(2):
                        bcol = (r % 4) * TPC + tc * 512
                        lr = []
                        for kt in range(16):
                            lr.append(p.load(rhs[:], g_ap[r * D + kt * 128: r * D + kt * 128 + 128,
                                                          tc * 512:(tc + 1) * 512]))
                            for hl in range(2):
                                p.comp("tensor", lambda t, hl=hl, kt=kt: t.matmul(
                                    psV[hl][:], wvsb[:, (hl * 16 + kt) * 128:(hl * 16 + kt + 1) * 128],
                                    rhs[:], start=(kt == 0), stop=(kt == 15)), loads=[lr[-1]])
                        for hl in range(2):
                            u = b * 2 + hl
                            p.comp("vector", lambda v, hl=hl: v.tensor_scalar(
                                out=vstg[:], in0=psV[hl][:], scalar1=bvcol[:, hl:hl + 1],
                                scalar2=None, op0=ALU.add), loads=[lbv[hl]])
                            for i in range(4):
                                p.comp("tensor", lambda t, i=i: t.transpose(
                                    psTb[:], vstg[:, i * 128:(i + 1) * 128], identb_[:]), loads=[lib])
                                p.comp("vector", lambda v: v.tensor_copy(out=stg[:, 0:128], in_=psTb[:]))
                                p.store(vu.ap()[u][bcol + i * 128: bcol + (i + 1) * 128, :], stg[:, 0:128])

        def ph_attn(layer, ztl_ap):
            alibi = (layer == 0)
            NB = S // 128
            with ExitStack() as st_:
                sb = lambda nm, sh, dtt: st_.enter_context(nc.sbuf_tensor(_u() + nm, sh, dtt))
                qTb = sb("at_q", [128, S], BF16)
                kTb = sb("at_k", [128, S], BF16)
                vsb = sb("at_v", [128, NB * 132], BF16)
                acol = sb("at_ac", [128, 32], F32)
                ubt = sb("at_ub", [128, 512], F32)
                lbt = sb("at_lb", [128, 512], F32)
                shrow = sb("at_sh", [1, 512], F32)
                onesc = sb("at_on", [1, 128], F32)
                tri = sb("at_tri", [128, 128], F32)
                Ssb = sb("at_S", [128, 512], F32)
                PT = sb("at_PT", [128, 512], BF16)
                recip = sb("at_rc", [128, 1], F32)
                mcol = sb("at_mc", [128, 1], F32)
                zsb = sb("at_z", [128, 128], F32)
                zbt = sb("at_zb", [128, 128], BF16)
                identf_ = sb("at_idf", [128, 128], F32)
                psS = st_.enter_context(nc.psum_tensor(_u() + "at_psS", [128, 512], F32))
                psTf = st_.enter_context(nc.psum_tensor(_u() + "at_psTf", [128, 128], F32))
                zps = [st_.enter_context(nc.psum_tensor(_u() + f"at_zps{i}", [128, 132], F32)) for i in range(4)]
                lt = p.load(tri[:], d_tri.ap())
                lif = p.load(identf_[:], d_identf.ap())
                p.comp("vector", lambda v: v.memset(onesc[:], 1.0))
                p.comp("vector", lambda v: v.memset(mcol[:], -EXP_M))
                p.comp("vector", lambda v: v.memset(
                    vsb[:].rearrange("p (b w) -> p b w", w=132)[:, :, 128:129], 1.0))
                for u in range(4):
                    b, hl = u // 2, u % 2
                    lq = p.load(qTb[:], qtu.ap()[u], queue="sync")
                    lk = p.load(kTb[:], ktu.ap()[u], queue="sync")
                    lvs = []
                    for g_ in range(4):
                        lvs.append(p.load(
                            vsb[:].rearrange("p (b w) -> p b w", w=132)[:, g_ * 8:(g_ + 1) * 8, 0:128],
                            vu.ap()[u][g_ * 1024:(g_ + 1) * 1024, :].rearrange("(b p) w -> p b w", p=128),
                            queue="gpsimd"))
                    la = []
                    if alibi:
                        la = [p.load(acol[:], d_acol[u], queue="sync")]
                    for r in range(8):
                        lrun = []
                        if alibi:
                            lrun.append(p.load(ubt[:], d_ub[u][r:r + 1, :].broadcast_to([128, 512])))
                            lrun.append(p.load(lbt[:], d_lb[u][r:r + 1, :].broadcast_to([128, 512])))
                            lrun.append(p.load(shrow[:], d_shift[u][r:r + 1, :]))
                        for kb in range(4 * r + 4):
                            if alibi:
                                p.comp("tensor", [
                                    lambda t, kb=kb, r=r: t.matmul(
                                        psS[:], kTb[:, kb * 128:(kb + 1) * 128],
                                        qTb[:, r * 512:(r + 1) * 512], start=True, stop=False),
                                    lambda t: t.matmul(psS[:], onesc[:], shrow[:], start=False, stop=True),
                                ], loads=[lq, lk] + lrun)
                                p.comp("vector", lambda v: v.tensor_tensor(out=Ssb[:], in0=psS[:], in1=ubt[:], op=ALU.min))
                                p.comp("vector", lambda v: v.tensor_tensor(out=Ssb[:], in0=Ssb[:], in1=lbt[:], op=ALU.max))
                                j = kb - 4 * r
                                if j >= 0:
                                    if j > 0:
                                        p.comp("vector", lambda v, j=j: v.memset(Ssb[:, 0:j * 128], -1e30))
                                    p.comp("vector", lambda v, j=j: v.tensor_tensor(
                                        out=Ssb[:, j * 128:(j + 1) * 128], in0=Ssb[:, j * 128:(j + 1) * 128],
                                        in1=tri[:], op=ALU.add), loads=[lt])
                                p.comp("scalar", lambda s, kb=kb: s.activation(
                                    out=PT[:], in_=Ssb[:], func=AF.Exp, bias=acol[:, kb:kb + 1]), loads=la)
                            else:
                                p.comp("tensor", lambda t, kb=kb, r=r: t.matmul(
                                    psS[:], kTb[:, kb * 128:(kb + 1) * 128],
                                    qTb[:, r * 512:(r + 1) * 512], start=True, stop=True), loads=[lq, lk])
                                j = kb - 4 * r
                                if j >= 0:
                                    p.comp("vector", lambda v, j=j: v.tensor_tensor(
                                        out=psS[:, j * 128:(j + 1) * 128], in0=psS[:, j * 128:(j + 1) * 128],
                                        in1=tri[:], op=ALU.add), loads=[lt])
                                p.comp("scalar", lambda s: s.activation(
                                    out=PT[:], in_=psS[:], func=AF.Exp, bias=mcol[:, 0:1]))
                            fns = []
                            for i in range(4):
                                if kb <= 4 * r + i:
                                    fns.append(lambda t, i=i, kb=kb, r=r: t.matmul(
                                        zps[i][:, 0:129], PT[:, i * 128:(i + 1) * 128],
                                        vsb[:].rearrange("p (b w) -> p b w", w=132)[:, kb, 0:129],
                                        start=(kb == 0), stop=(kb == 4 * r + i)))
                            p.comp("tensor", fns, loads=lvs)
                        for i in range(4):
                            p.comp("vector", lambda v, i=i: v.reciprocal(out=recip[:], in_=zps[i][:, 128:129]))
                            p.comp("vector", lambda v, i=i: v.tensor_scalar(
                                out=zsb[:], in0=zps[i][:, 0:128], scalar1=recip[:, 0:1], scalar2=None, op0=ALU.mult))
                            p.comp("tensor", lambda t: t.transpose(psTf[:], zsb[:], identf_[:]), loads=[lif])
                            p.comp("vector", lambda v: v.tensor_copy(out=zbt[:], in_=psTf[:]))
                            p.store(ztl_ap[hl * 128:(hl + 1) * 128,
                                           b * S + (4 * r + i) * 128: b * S + (4 * r + i + 1) * 128], zbt[:])

        def ph_oproj_partial(ztl_ap, wo_ap, part_ap):
            with ExitStack() as st_:
                sb = lambda nm, sh, dtt: st_.enter_context(nc.sbuf_tensor(_u() + nm, sh, dtt))
                wsb = sb("op_w", [128, 2 * D], BF16)
                zt = sb("op_z", [128, 2 * 128], BF16)
                osb = sb("op_o", [128, 512], F32)
                psA = st_.enter_context(nc.psum_tensor(_u() + "op_ps", [128, 512], F32))
                lw = [p.load(wsb[:, kt * D:(kt + 1) * D], wo_ap[kt * 128:(kt + 1) * 128, :], queue="gpsimd")
                      for kt in range(2)]
                for tt in range(T // 128):
                    lz = [p.load(zt[:, kt * 128:(kt + 1) * 128],
                                 ztl_ap[kt * 128:(kt + 1) * 128, tt * 128:(tt + 1) * 128]) for kt in range(2)]
                    for oc in range(4):
                        p.comp("tensor", [lambda t, kt=kt, oc=oc: t.matmul(
                            psA[:], zt[:, kt * 128:(kt + 1) * 128],
                            wsb[:, kt * D + oc * 512: kt * D + (oc + 1) * 512],
                            start=(kt == 0), stop=(kt == 1)) for kt in range(2)], loads=lz + lw)
                        p.comp("vector", lambda v: v.tensor_copy(out=osb[:], in_=psA[:]))
                        p.store(part_ap[tt * 128:(tt + 1) * 128, oc * 512:(oc + 1) * 512], osb[:])

        def ph_rs(part_ap, rs_ap):
            p.comp("gpsimd", lambda g_: g_.collective_compute(
                "ReduceScatter", ALU.add, replica_groups=RG,
                ins=[part_ap.opt()], outs=[rs_ap.opt()]))

        def ph_resid(x_src, add_aps, out_ap, bo_bcast=False):
            with ExitStack() as st_:
                sb = lambda nm, sh, dtt: st_.enter_context(nc.sbuf_tensor(_u() + nm, sh, dtt))
                acc = sb("rd_acc", [128, D], F32)
                pt_ = sb("rd_pt", [128, D], F32)
                bob = sb("rd_bo", [128, D], F32)
                lbo = []
                if bo_bcast:
                    lbo = [p.load(bob[:], d_bo1.broadcast_to([128, D]), queue="sync")]
                for tt in range(NT):
                    lx = p.load(acc[:], x_src[tt * 128:(tt + 1) * 128, :])
                    first = [lx]
                    for (aap, isbf) in add_aps:
                        lp = p.load(pt_[:], aap[tt * 128:(tt + 1) * 128, :],
                                    queue="gpsimd" if isbf else "sync")
                        p.comp("vector", lambda v: v.tensor_tensor(out=acc[:], in0=acc[:], in1=pt_[:], op=ALU.add),
                               loads=[lp] + first)
                        first = []
                    if bo_bcast:
                        p.comp("vector", lambda v: v.tensor_tensor(out=acc[:], in0=acc[:], in1=bob[:], op=ALU.add),
                               loads=lbo)
                    p.store(out_ap[tt * 128:(tt + 1) * 128, :], acc[:])

        def ph_ffn0(g_ap):
            with ExitStack() as st_:
                sb = lambda nm, sh, dtt: st_.enter_context(nc.sbuf_tensor(_u() + nm, sh, dtt))
                dec = {nm: sb("f_" + nm, [128, 2048], I16) for nm in ("pk16", "T16", "n16", "a1", "a2", "z16", "w16", "alt")}
                val = sb("f_val", [128, 2048], BF16)
                scsb = sb("f_sc", [128, 64], F32)
                scb = sb("f_scb", [128, 64], BF16)
                ident = sb("f_id", [128, 128], BF16)
                wgT = sb("f_wg", [128, 16 * 2048], BF16)
                wuT = sb("f_wu", [128, 16 * 2048], BF16)
                ysb = sb("f_y", [128, 16 * 512], BF16)
                g32 = sb("f_g32", [128, 512], F32)
                t3 = sb("f_t3", [128, 512], F32)
                t4 = sb("f_t4", [128, 512], F32)
                sgm = sb("f_sgm", [128, 512], F32)
                w32 = sb("f_w32", [128, 512], F32)
                inter = sb("f_int", [128, 512], BF16)
                osb = sb("f_o", [128, 512], F32)
                psG = st_.enter_context(nc.psum_tensor(_u() + "f_psG", [128, 512], F32))
                psU = st_.enter_context(nc.psum_tensor(_u() + "f_psU", [128, 512], F32))
                psT_ = st_.enter_context(nc.psum_tensor(_u() + "f_psT", [128, 128], BF16))
                li = p.load(ident[:], d_identb.ap())
                lalt = p.load(dec["alt"][:], d_alt.ap().broadcast_to([128, 2048]))

                def decode_into(dst, d_pk, d_sc, ntile):
                    lpk = p.load(dec["pk16"][:], d_pk[ntile * 128:(ntile + 1) * 128, :], queue="gpsimd")
                    lsc = p.load(scsb[:], d_sc[ntile * 128:(ntile + 1) * 128, :], queue="gpsimd")
                    emit_decode(p, dec, [lpk, lalt], val[:])
                    p.comp("vector", lambda v: v.tensor_copy(out=scb[:], in_=scsb[:]), loads=[lsc])
                    vv = val[:].rearrange("p (c q) -> p c q", q=32)
                    sbb = scb[:].unsqueeze(2).broadcast_to([128, 64, 32])
                    p.comp("vector", lambda v: v.tensor_tensor(out=vv, in0=vv, in1=sbb, op=ALU.mult))
                    for ht in range(16):
                        p.comp("tensor", lambda t, ht=ht: t.transpose(
                            psT_[:], val[:, ht * 128:(ht + 1) * 128], ident[:]), loads=[li])
                        p.comp("vector", lambda v, ht=ht, ntile=ntile: v.tensor_copy(
                            out=dst[:, ht * 2048 + ntile * 128: ht * 2048 + (ntile + 1) * 128], in_=psT_[:]))

                for ntile in range(16):
                    decode_into(wgT, d_gpk, d_gsc, ntile)
                    decode_into(wuT, d_upk, d_usc, ntile)
                for tc in range(16):
                    r, sub = tc // 2, tc % 2
                    ly = [p.load(ysb[:, ht * 512:(ht + 1) * 512],
                                 g_ap[r * D + ht * 128: r * D + (ht + 1) * 128, sub * 512:(sub + 1) * 512])
                          for ht in range(16)]
                    for ntile in range(16):
                        p.comp("tensor", [lambda t, ht=ht, ntile=ntile: t.matmul(
                            psG[:], wgT[:, ht * 2048 + ntile * 128: ht * 2048 + (ntile + 1) * 128],
                            ysb[:, ht * 512:(ht + 1) * 512], start=(ht == 0), stop=(ht == 15)) for ht in range(16)], loads=ly)
                        p.comp("tensor", [lambda t, ht=ht, ntile=ntile: t.matmul(
                            psU[:], wuT[:, ht * 2048 + ntile * 128: ht * 2048 + (ntile + 1) * 128],
                            ysb[:, ht * 512:(ht + 1) * 512], start=(ht == 0), stop=(ht == 15)) for ht in range(16)])
                        p.comp("vector", lambda v: v.tensor_copy(out=g32[:], in_=psG[:]))
                        p.comp("scalar", lambda s: s.activation(out=t3[:], in_=g32[:], func=AF.Square))
                        p.comp("vector", lambda v: v.tensor_scalar(out=t3[:], in0=t3[:], scalar1=GELU_C, scalar2=1.0, op0=ALU.mult, op1=ALU.add))
                        p.comp("vector", lambda v: v.tensor_tensor(out=t4[:], in0=g32[:], in1=t3[:], op=ALU.mult))
                        p.comp("scalar", lambda s: s.activation(out=sgm[:], in_=t4[:], func=AF.Sigmoid, scale=GELU_A))
                        p.comp("vector", lambda v: v.tensor_tensor(out=w32[:], in0=g32[:], in1=sgm[:], op=ALU.mult))
                        p.comp("vector", lambda v: v.scalar_tensor_tensor(out=inter[:], in0=w32[:], scalar=1.0, in1=psU[:], op0=ALU.mult, op1=ALU.mult))
                        p.store(interT.ap()[ntile * 128:(ntile + 1) * 128, tc * 512:(tc + 1) * 512], inter[:])
                for ot in range(16):
                    lpk = p.load(dec["pk16"][:], d_dpk[ot * 128:(ot + 1) * 128, :], queue="gpsimd")
                    lsc = p.load(scsb[:], d_dsc[ot * 128:(ot + 1) * 128, :], queue="gpsimd")
                    emit_decode(p, dec, [lpk, lalt], val[:])
                    p.comp("vector", lambda v: v.tensor_copy(out=scb[:], in_=scsb[:]), loads=[lsc])
                    vv = val[:].rearrange("p (c q) -> p c q", q=32)
                    sbb = scb[:].unsqueeze(2).broadcast_to([128, 64, 32])
                    p.comp("vector", lambda v: v.tensor_tensor(out=vv, in0=vv, in1=sbb, op=ALU.mult))
                    for nt in range(16):
                        p.comp("tensor", lambda t, nt=nt: t.transpose(
                            psT_[:], val[:, nt * 128:(nt + 1) * 128], ident[:]), loads=[li])
                        p.comp("vector", lambda v, nt=nt, ot=ot: v.tensor_copy(
                            out=wgT[:, nt * 2048 + ot * 128: nt * 2048 + (ot + 1) * 128], in_=psT_[:]))
                for tc in range(16):
                    lis = [p.load(ysb[:, nt * 512:(nt + 1) * 512],
                                  interT.ap()[nt * 128:(nt + 1) * 128, tc * 512:(tc + 1) * 512]) for nt in range(16)]
                    for tsub in range(4):
                        for oc in range(4):
                            p.comp("tensor", [lambda t, nt=nt, tsub=tsub, oc=oc: t.matmul(
                                psG[:], ysb[:, nt * 512 + tsub * 128: nt * 512 + (tsub + 1) * 128],
                                wgT[:, nt * 2048 + oc * 512: nt * 2048 + (oc + 1) * 512],
                                start=(nt == 0), stop=(nt == 15)) for nt in range(16)], loads=lis)
                            p.comp("vector", lambda v: v.tensor_copy(out=osb[:], in_=psG[:]))
                            p.store(part[1].ap()[tc * 512 + tsub * 128: tc * 512 + (tsub + 1) * 128,
                                                 oc * 512:(oc + 1) * 512], osb[:])

        def ph_moe():
            NH = MH // 128
            with ExitStack() as st_:
                sb = lambda nm, sh, dtt: st_.enter_context(nc.sbuf_tensor(_u() + nm, sh, dtt))
                ysb = sb("m_y", [128, 16 * 512], BF16)
                yf = sb("m_yf", [128, 128], F32)
                gw = sb("m_gw", [128, 16 * 16], F32)
                comb = sb("m_cb", [128, 64 * 2], F32)
                combt = sb("m_cbt", [128, 16], F32)
                eselt = sb("m_es", [128, 2 * 16], F32)
                ssb = sb("m_s", [128, 16], F32)
                r4 = sb("m_r4", [128, 4], F32)
                r4b = sb("m_r4b", [128, 4], F32)
                r16 = sb("m_r16", [128, 16], F32)
                r1 = sb("m_r1", [128, 8], F32)
                w1sb = sb("m_w1", [128, 16 * MH], BF16)
                w2sb = sb("m_w2", [128, NH * D], BF16)
                asb = sb("m_a", [128, NH * 512], BF16)
                accb = sb("m_accb", [128, 512], BF16)
                accf = sb("m_accf", [128, 512], F32)
                e0t = sb("m_e0t", [128, 512], F32)
                psR = st_.enter_context(nc.psum_tensor(_u() + "m_psR", [128, 16], F32))
                psA = st_.enter_context(nc.psum_tensor(_u() + "m_psA", [128, 512], F32))
                psO = st_.enter_context(nc.psum_tensor(_u() + "m_psO", [128, 512], F32))
                lsel = [p.load(eselt[:, e * 16:(e + 1) * 16], d_esel[e].broadcast_to([128, 16])) for e in range(2)]
                lgw = [p.load(gw[:, kt * 16:(kt + 1) * 16], d_gw[kt * 128:(kt + 1) * 128, :]) for kt in range(16)]
                for tt in range(64):
                    r, coff = tt // 8, (tt % 8) * 128
                    for kt in range(16):
                        lyt = p.load(yf[:], G32.ap()[r * D + kt * 128: r * D + (kt + 1) * 128, coff:coff + 128])
                        p.comp("tensor", [lambda t, kt=kt: t.matmul(
                            psR[:], yf[:], gw[:, kt * 16:(kt + 1) * 16],
                            start=(kt == 0), stop=(kt == 15))], loads=[lyt] + (lgw if kt == 0 else []))
                    p.comp("scalar", lambda s: s.activation(out=ssb[:], in_=psR[:], func=AF.Sigmoid))
                    sg = ssb[:].rearrange("p (g q) -> p g q", q=4)
                    p.comp("vector", lambda v: v.tensor_reduce(out=r4[:], in_=sg, axis=AX.X, op=ALU.max))
                    bc4 = r4[:].unsqueeze(2).broadcast_to([128, 4, 4])
                    p.comp("vector", lambda v: v.tensor_tensor(out=r16[:].rearrange("p (g q) -> p g q", q=4), in0=sg, in1=bc4, op=ALU.is_equal))
                    p.comp("vector", lambda v: v.scalar_tensor_tensor(out=r16[:], in0=r16[:], scalar=-1e30, in1=ssb[:], op0=ALU.mult, op1=ALU.add))
                    p.comp("vector", lambda v: v.tensor_reduce(out=r4b[:], in_=r16[:].rearrange("p (g q) -> p g q", q=4), axis=AX.X, op=ALU.max))
                    p.comp("vector", lambda v: v.tensor_tensor(out=r4[:], in0=r4[:], in1=r4b[:], op=ALU.add))
                    p.comp("vector", lambda v: v.tensor_reduce(out=r1[:, 0:1], in_=r4[:], axis=AX.X, op=ALU.max))
                    p.comp("vector", lambda v: v.tensor_scalar(out=r4b[:], in0=r4[:], scalar1=r1[:, 0:1], scalar2=None, op0=ALU.is_equal))
                    p.comp("vector", lambda v: v.scalar_tensor_tensor(out=r4b[:], in0=r4b[:], scalar=-1e30, in1=r4[:], op0=ALU.mult, op1=ALU.add))
                    p.comp("vector", lambda v: v.tensor_reduce(out=r1[:, 1:2], in_=r4b[:], axis=AX.X, op=ALU.max))
                    p.comp("vector", lambda v: v.tensor_scalar(out=r4b[:], in0=r4[:], scalar1=r1[:, 1:2], scalar2=None, op0=ALU.is_ge))
                    bc4b = r4b[:].unsqueeze(2).broadcast_to([128, 4, 4])
                    p.comp("vector", lambda v: v.tensor_tensor(out=r16[:].rearrange("p (g q) -> p g q", q=4), in0=sg, in1=bc4b, op=ALU.mult))
                    p.comp("vector", lambda v: v.tensor_reduce(out=r1[:, 2:3], in_=r16[:], axis=AX.X, op=ALU.max))
                    p.comp("vector", lambda v: v.tensor_scalar(out=ssb[:], in0=r16[:], scalar1=r1[:, 2:3], scalar2=None, op0=ALU.is_equal))
                    p.comp("vector", lambda v: v.scalar_tensor_tensor(out=ssb[:], in0=ssb[:], scalar=-1e30, in1=r16[:], op0=ALU.mult, op1=ALU.add))
                    p.comp("vector", lambda v: v.tensor_reduce(out=r1[:, 3:4], in_=ssb[:], axis=AX.X, op=ALU.max))
                    p.comp("vector", lambda v: v.tensor_tensor(out=r1[:, 4:5], in0=r1[:, 2:3], in1=r1[:, 3:4], op=ALU.add))
                    p.comp("vector", lambda v: v.tensor_scalar(out=r1[:, 4:5], in0=r1[:, 4:5], scalar1=1e-20, scalar2=None, op0=ALU.add))
                    p.comp("vector", lambda v: v.reciprocal(out=r1[:, 5:6], in_=r1[:, 4:5]))
                    p.comp("vector", lambda v: v.tensor_scalar(out=ssb[:], in0=r16[:], scalar1=r1[:, 3:4], scalar2=None, op0=ALU.is_ge))
                    p.comp("vector", lambda v: v.tensor_tensor(out=ssb[:], in0=ssb[:], in1=r16[:], op=ALU.mult))
                    p.comp("vector", lambda v: v.tensor_scalar(out=ssb[:], in0=ssb[:], scalar1=r1[:, 5:6], scalar2=None, op0=ALU.mult))
                    for e in range(2):
                        p.comp("vector", lambda v, e=e: v.tensor_tensor(out=combt[:], in0=ssb[:], in1=eselt[:, e * 16:(e + 1) * 16], op=ALU.mult), loads=[lsel[e]])
                        p.comp("vector", lambda v, e=e, tt=tt: v.tensor_reduce(out=comb[:, tt * 2 + e: tt * 2 + e + 1], in_=combt[:], axis=AX.X, op=ALU.add))

                def expert_pass(d_w1src, d_w2src, n_tc, ysrc, e, mode):
                    lw1 = [p.load(w1sb[:, kt * MH:(kt + 1) * MH], d_w1src[kt * 128:(kt + 1) * 128, :], queue="gpsimd")
                           for kt in range(16)]
                    lw2 = [p.load(w2sb[:, ht * D:(ht + 1) * D], d_w2src[ht * 128:(ht + 1) * 128, :], queue="gpsimd")
                           for ht in range(NH)]
                    for tc in range(n_tc):
                        if ysrc == "G32":
                            r, sub = tc // 2, tc % 2
                            ly = [p.load(ysb[:, kt * 512:(kt + 1) * 512],
                                         G32.ap()[r * D + kt * 128: r * D + (kt + 1) * 128, sub * 512:(sub + 1) * 512],
                                         queue="gpsimd") for kt in range(16)]
                        else:
                            ly = [p.load(ysb[:, kt * 512:(kt + 1) * 512],
                                         yTl32.ap()[kt * 128:(kt + 1) * 128, tc * 512:(tc + 1) * 512],
                                         queue="gpsimd") for kt in range(16)]
                        for ht in range(NH):
                            p.comp("tensor", [lambda t, kt=kt, ht=ht: t.matmul(
                                psA[:], w1sb[:, kt * MH + ht * 128: kt * MH + (ht + 1) * 128],
                                ysb[:, kt * 512:(kt + 1) * 512], start=(kt == 0), stop=(kt == 15))
                                for kt in range(16)], loads=ly + lw1)
                            p.comp("scalar", lambda s, ht=ht: s.activation(
                                out=asb[:, ht * 512:(ht + 1) * 512], in_=psA[:], func=AF.Silu))
                        for tsub in range(4):
                            for oc in range(4):
                                p.comp("tensor", [lambda t, ht=ht, tsub=tsub, oc=oc: t.matmul(
                                    psO[:], asb[:, ht * 512 + tsub * 128: ht * 512 + (tsub + 1) * 128],
                                    w2sb[:, ht * D + oc * 512: ht * D + (oc + 1) * 512],
                                    start=(ht == 0), stop=(ht == NH - 1)) for ht in range(NH)], loads=lw2)
                                row0 = tc * 512 + tsub * 128
                                if mode == 0:
                                    cap = comb[:, (tc * 4 + tsub) * 2 + 0: (tc * 4 + tsub) * 2 + 1]
                                    p.comp("vector", lambda v, cap=cap: v.tensor_scalar(
                                        out=accb[:], in0=psO[:], scalar1=cap, scalar2=None, op0=ALU.mult))
                                    p.store(e0p.ap()[row0:row0 + 128, oc * 512:(oc + 1) * 512], accb[:])
                                elif mode == 1:
                                    cap = comb[:, (tc * 4 + tsub) * 2 + 1: (tc * 4 + tsub) * 2 + 2]
                                    le0 = p.load(e0t[:], e0p.ap()[row0:row0 + 128, oc * 512:(oc + 1) * 512],
                                                 queue="gpsimd")
                                    p.comp("vector", lambda v, cap=cap: v.scalar_tensor_tensor(
                                        out=accf[:], in0=psO[:], scalar=cap, in1=e0t[:],
                                        op0=ALU.mult, op1=ALU.add), loads=[le0])
                                    p.store(part[3].ap()[row0:row0 + 128, oc * 512:(oc + 1) * 512], accf[:])
                                else:
                                    p.comp("vector", lambda v: v.tensor_copy(out=accf[:], in_=psO[:]))
                                    p.store(shl.ap()[row0:row0 + 128, oc * 512:(oc + 1) * 512], accf[:])

                expert_pass(d_w1[0], d_w2[0], 16, "G32", 0, 0)
                expert_pass(d_w1[1], d_w2[1], 16, "G32", 1, 1)
                expert_pass(d_sw1, d_sw2, TPC // 512, "local", 0, 2)

        cur_x = d_x0
        if upto >= 0:
            ph_ln_ag(d_x0, yTl.ap(), G[0].ap(), False)
        if upto >= 1:
            ph_qkv_units0(G[0].ap(), d_wq0, d_wk0, d_wv0, 1.0 / math.sqrt(DH))
            ph_attn(0, zTl[0].ap())
        if upto >= 2:
            ph_oproj_partial(zTl[0].ap(), d_wo0, part[0].ap())
            ph_rs(part[0].ap(), rs[0].ap())
            ph_resid(d_x0, [(rs[0].ap(), False)], xsc[0].ap())
            cur_x = xsc[0].ap()
        if upto >= 3:
            ph_ln_ag(xsc[0].ap(), yTl.ap(), G[1].ap(), False)
        if upto >= 4:
            ph_ffn0(G[1].ap())
            ph_rs(part[1].ap(), rs[1].ap())
            ph_resid(xsc[0].ap(), [(rs[1].ap(), False)], xsc[1].ap())
            cur_x = xsc[1].ap()
        if upto >= 5:
            ph_ln_ag(xsc[1].ap(), yTl.ap(), G[2].ap(), False)
        if upto >= 6:
            ph_qkv_units1(G[2].ap())
            ph_attn(1, zTl[1].ap())
        if upto >= 7:
            ph_oproj_partial(zTl[1].ap(), d_wo1, part[2].ap())
            ph_rs(part[2].ap(), rs[2].ap())
            ph_resid(xsc[1].ap(), [(rs[2].ap(), False)], xsc[2].ap(), bo_bcast=True)
            cur_x = xsc[2].ap()
        if upto >= 8:
            ph_ln_ag(xsc[2].ap(), yTl32.ap(), G32.ap(), True)
        if upto >= 9:
            ph_moe()
            ph_rs(part[3].ap(), rs[3].ap())
            ph_resid(xsc[2].ap(), [(rs[3].ap(), False), (shl.ap(), False)], d_out)
        if upto < 9:
            with ExitStack() as st_:
                cpy = st_.enter_context(nc.sbuf_tensor(_u() + "dbgcpy", [128, D], F32))
                cpy2 = st_.enter_context(nc.sbuf_tensor(_u() + "dbgcpy2", [128, D], F32))
                for tt in range(NT):
                    lx = p.load(cpy[:], cur_x[tt * 128:(tt + 1) * 128, :])
                    p.comp("vector", lambda v: v.tensor_copy(out=cpy2[:], in_=cpy[:]), loads=[lx])
                    p.store(d_out[tt * 128:(tt + 1) * 128, :], cpy2[:])

        p.build(block, ctx.sems)
    return nc


def fused_in_maps(inputs):
    xf = np.ascontiguousarray(np.asarray(inputs["L0_attn_resid_pre"], dtype=np.float32).reshape(T, D))
    Wq0T, Wk0T, Wv0T = (_t(inputs["L0_attn_Wq"]), _t(inputs["L0_attn_Wk"]), _t(inputs["L0_attn_Wv"]))
    Wo0T = _t(inputs["L0_attn_Wo"])
    Wq1T, Wk1T, Wv1T = (_t(inputs["L1_attn_Wq"]), _t(inputs["L1_attn_Wk"]), _t(inputs["L1_attn_Wv"]))
    Wo1T = _t(inputs["L1_attn_Wo"])
    bq1 = np.asarray(inputs["L1_attn_bq"], dtype=np.float32)
    bk1 = np.asarray(inputs["L1_attn_bk"], dtype=np.float32)
    bv1 = np.asarray(inputs["L1_attn_bv"], dtype=np.float32)
    bo1 = np.asarray(inputs["L1_attn_bo"], dtype=np.float32).reshape(1, D)
    gpk = np.asarray(inputs["L0_ffn_gate_W_packed"], dtype=np.int32)
    upk = np.asarray(inputs["L0_ffn_up_W_packed"], dtype=np.int32)
    dpk = np.asarray(inputs["L0_ffn_down_W_packed"], dtype=np.int32)
    gsc = np.asarray(inputs["L0_ffn_gate_scales"], dtype=np.float32)
    usc = np.asarray(inputs["L0_ffn_up_scales"], dtype=np.float32)
    dsc = np.asarray(inputs["L0_ffn_down_scales"], dtype=np.float32)
    gwT = _t(inputs["L1_ffn_gate_w"])
    w1m = np.asarray(inputs["L1_ffn_w1"], dtype=np.float32)
    w2m = np.asarray(inputs["L1_ffn_w2"], dtype=np.float32)
    sw1 = np.ascontiguousarray(np.asarray(inputs["L1_ffn_sw1"], dtype=np.float32)[0])
    sw2 = np.ascontiguousarray(np.asarray(inputs["L1_ffn_sw2"], dtype=np.float32)[0])
    half = DH // 2
    inv = 1.0 / (10000.0 ** (np.arange(half, dtype=np.float32) / half))
    fr = np.arange(S, dtype=np.float32)[:, None] * inv[None, :]
    cosS = np.ascontiguousarray(np.repeat(np.cos(fr), 2, axis=1).astype(np.float32))
    sinS = np.ascontiguousarray(np.repeat(np.sin(fr), 2, axis=1).astype(np.float32))
    pos = np.arange(S, dtype=np.float32)
    maps = []
    for c in range(NCORES):
        n0 = c * NPC
        hc = slice(2 * c * 128, (2 * c + 2) * 128)
        units = [(b, h) for b in range(B) for h in (2 * c, 2 * c + 1)]
        acolu = np.stack([(SLOPES[h] * pos - EXP_M).reshape(32, 128).T.copy() for (b, h) in units])
        shiftu = np.stack([
            np.stack([-SLOPES[h] * (r * 512 + np.arange(512, dtype=np.float32)) for r in range(8)])
            for (b, h) in units])
        esel = np.zeros((2, 1, 16), dtype=np.float32)
        esel[0, 0, 2 * c] = 1.0
        esel[1, 0, 2 * c + 1] = 1.0
        maps.append(dict(
            x0=np.ascontiguousarray(xf[c * TPC:(c + 1) * TPC]),
            wq0=np.ascontiguousarray(Wq0T[:, hc]), wk0=np.ascontiguousarray(Wk0T[:, hc]),
            wv0=np.ascontiguousarray(Wv0T[:, hc]), wo0=np.ascontiguousarray(Wo0T[hc, :]),
            acolu=np.ascontiguousarray(acolu, dtype=np.float32),
            shiftu=np.ascontiguousarray(shiftu, dtype=np.float32),
            ubu=np.ascontiguousarray(1024.0 + shiftu, dtype=np.float32),
            lbu=np.ascontiguousarray(-1024.0 + shiftu, dtype=np.float32),
            gpk=np.ascontiguousarray(gpk[n0:n0 + NPC]).view(np.int16),
            upk=np.ascontiguousarray(upk[n0:n0 + NPC]).view(np.int16),
            gsc=np.ascontiguousarray(gsc[n0:n0 + NPC]),
            usc=np.ascontiguousarray(usc[n0:n0 + NPC]),
            dpk=np.ascontiguousarray(dpk[:, n0 // 2:(n0 + NPC) // 2]).view(np.int16),
            dsc=np.ascontiguousarray(dsc[:, n0 // 32:(n0 + NPC) // 32]),
            wq1=np.ascontiguousarray(Wq1T[:, hc]), wk1=np.ascontiguousarray(Wk1T[:, hc]),
            wv1=np.ascontiguousarray(Wv1T[:, hc]),
            bq1=np.ascontiguousarray(bq1[hc]).reshape(1, 256),
            bk1=np.ascontiguousarray(bk1[hc]).reshape(1, 256),
            bv1=np.ascontiguousarray(bv1[hc]).reshape(1, 256),
            wo1=np.ascontiguousarray(Wo1T[hc, :]), bo1=bo1,
            cosS=cosS, sinS=sinS,
            gwT=gwT,
            w1=np.ascontiguousarray(w1m[2 * c:2 * c + 2]),
            w2=np.ascontiguousarray(w2m[2 * c:2 * c + 2]),
            sw1=sw1, sw2=sw2, esel=esel,
        ))
    return maps


def kernel_fused(upto=9, **inputs):
    nc = build_fused(upto)
    res = launch(nc, fused_in_maps(inputs))
    out = np.concatenate([r["o_x4"] for r in res.results])
    return out.reshape(B, S, D).astype(np.float32)


def kernel(**inputs):
    return kernel_fused(9, **inputs)


# version 03
# speedup vs baseline: 1.5143x; 1.5143x over previous
import numpy as np
import ml_dtypes
import math
from contextlib import ExitStack

import concourse.bass as bass
import concourse.mybir as mybir
from concourse.bass_utils import run_bass_kernel_spmd

F32 = mybir.dt.float32
BF16 = mybir.dt.bfloat16
I16 = mybir.dt.int16
AF = mybir.ActivationFunctionType
ALU = mybir.AluOpType
AX = mybir.AxisListType
BF = ml_dtypes.bfloat16

NCORES = 8
B, S, D, H, DH = 2, 4096, 2048, 16, 128
T = B * S
TPC = T // NCORES
NMLP = 16384
NPC = NMLP // NCORES
MH = 1408
EXP_M = 40.0
GELU_A = 1.5957691216057308
GELU_C = 0.044715

def alibi_slopes(n):
    def pow2(m):
        start = 2 ** (-(2 ** -(math.log2(m) - 3)))
        return [start * start ** i for i in range(m)]
    if math.log2(n).is_integer():
        return pow2(n)
    cc = 2 ** math.floor(math.log2(n))
    return pow2(cc) + alibi_slopes(2 * cc)[0::2][: n - cc]

SLOPES = np.array(alibi_slopes(H), dtype=np.float32)


class Prog:
    ENGS = ("sync", "gpsimd", "vector", "scalar", "tensor")

    def __init__(self, nc):
        self.nc = nc
        self.ops = {e: [] for e in self.ENGS}
        self.nld = {"sync": 0, "gpsimd": 0}
        self.c = 0
        self.nst = 0

    _OWN_INC_OK = ("vector", "scalar", "tensor")
    ELIDE = True
    OWNINC = False

    def _wait(self, eng, ename, sems, sem_key, val, st):
        if not Prog.ELIDE:
            if val > 0:
                eng.wait_ge(sems[sem_key], val)
            return
        if val > st[ename].get(sem_key, 0):
            eng.wait_ge(sems[sem_key], val)
            st[ename][sem_key] = val

    def load(self, out_ap, in_ap, queue="sync"):
        tok = self.nld[queue] + 1
        self.nld[queue] = tok
        cwait = self.c
        stwait = self.nst

        def emit(eng, sems, st):
            self._wait(eng, queue, sems, "sC", cwait, st)
            self._wait(eng, queue, sems, "st", 16 * stwait, st)
            self._wait(eng, queue, sems, "ld_" + queue, 16 * (tok - 1), st)
            eng.dma_start(out=out_ap, in_=in_ap).then_inc(sems["ld_" + queue], 16)
        self.ops[queue].append(emit)
        return (queue, tok)

    def comp(self, ename, fns, loads=()):
        if callable(fns):
            fns = [fns]
        cwait = self.c
        stwait = self.nst
        self.c += 1
        cval = self.c
        loads = list(loads)

        def emit(eng, sems, st):
            self._wait(eng, ename, sems, "sC", cwait, st)
            self._wait(eng, ename, sems, "st", 16 * stwait, st)
            for (q, tok) in loads:
                self._wait(eng, ename, sems, "ld_" + q, 16 * tok, st)
            last = None
            for fn in fns:
                last = fn(eng)
            last.then_inc(sems["sC"], 1)
            if Prog.ELIDE and Prog.OWNINC and ename in self._OWN_INC_OK:
                st[ename]["sC"] = max(st[ename].get("sC", 0), cval)
        self.ops[ename].append(emit)
        return cval

    def store(self, out_ap, in_ap, queue="gpsimd"):
        cwait = self.c
        self.nst += 1
        tok = self.nst

        def emit(eng, sems, st):
            self._wait(eng, queue, sems, "sC", cwait, st)
            self._wait(eng, queue, sems, "st", 16 * (tok - 1), st)
            eng.dma_start(out=out_ap, in_=in_ap).then_inc(sems["st"], 16)
        self.ops[queue].append(emit)

    def build(self, block, sems):
        final_c, final_st = self.c, self.nst
        st = {e: {} for e in self.ENGS}

        def closer(eng, sems2, st2):
            eng.wait_ge(sems2["sC"], final_c)
            eng.wait_ge(sems2["st"], 16 * final_st)
        self.ops["gpsimd"].append(closer)
        for ename in self.ENGS:
            section = getattr(block, ename)
            oplist = self.ops[ename]

            def body(eng, oplist=oplist):
                for op in oplist:
                    op(eng, sems, st)
            section(body)


class Ctx:
    def __init__(self, nc, stack):
        self.nc = nc
        self.stack = stack
        self.sems = {}
        for name in ("ld_sync", "ld_gpsimd", "st", "sC"):
            self.sems[name] = stack.enter_context(nc.semaphore(name))

    def sb(self, name, shape, dtype):
        return self.stack.enter_context(self.nc.sbuf_tensor(name, shape, dtype))

    def ps(self, name, shape, dtype=F32):
        return self.stack.enter_context(self.nc.psum_tensor(name, shape, dtype))


def new_nc():
    return bass.Bass("TRN2", target_bir_lowering=False)


def launch(nc, in_maps, **kw):
    return run_bass_kernel_spmd(nc, in_maps, core_ids=list(range(NCORES)), **kw)


def emit_ln(p, x_tile, y_out, xc, sq, stats, eps_ap, loads=()):
    nm, var, sd, rstd = (stats[:, 0:1], stats[:, 1:2], stats[:, 2:3], stats[:, 3:4])
    p.comp("vector", lambda v: v.tensor_reduce(out=nm, in_=x_tile, axis=AX.X, op=ALU.add, negate=True), loads=loads)
    p.comp("vector", lambda v: v.tensor_scalar(out=nm, in0=nm, scalar1=1.0 / D, scalar2=None, op0=ALU.mult))
    p.comp("vector", lambda v: v.tensor_scalar(out=xc, in0=x_tile, scalar1=nm, scalar2=None, op0=ALU.add))
    p.comp("scalar", lambda s: s.activation(out=sq, in_=xc, func=AF.Square, accum_out=var))
    p.comp("scalar", lambda s: s.activation(out=sd, in_=var, func=AF.Sqrt, bias=eps_ap, scale=1.0 / D))
    p.comp("vector", lambda v: v.reciprocal(out=rstd, in_=sd))
    p.comp("vector", lambda v: v.tensor_scalar(out=y_out, in0=xc, scalar1=rstd, scalar2=None, op0=ALU.mult))


def emit_decode(p, t, pk_load, val_out):
    pk, T16, n16, a1, a2, z16, w16, alt = (t["pk16"], t["T16"], t["n16"],
                                           t["a1"], t["a2"], t["z16"], t["w16"], t["alt"])
    ev = lambda x: x.rearrange("p (f two) -> p f two", two=2)[:, :, 0:1]
    od = lambda x: x.rearrange("p (f two) -> p f two", two=2)[:, :, 1:2]
    p.comp("vector", lambda v: v.tensor_copy(out=T16[:], in_=pk[:]), loads=pk_load)
    p.comp("vector", lambda v: v.tensor_copy(out=od(T16[:]), in_=ev(pk[:])))
    p.comp("vector", lambda v: v.tensor_tensor(out=n16[:], in0=T16[:], in1=alt[:], op=ALU.logical_shift_right))
    p.comp("vector", lambda v: v.tensor_scalar(out=a1[:], in0=n16[:], scalar1=7, scalar2=6, op0=ALU.bitwise_and, op1=ALU.logical_shift_left))
    p.comp("vector", lambda v: v.tensor_scalar(out=a2[:], in0=n16[:], scalar1=8, scalar2=12, op0=ALU.bitwise_and, op1=ALU.logical_shift_left))
    p.comp("vector", lambda v: v.tensor_scalar(out=T16[:], in0=a1[:], scalar1=0x3F00, scalar2=None, op0=ALU.add))
    p.comp("vector", lambda v: v.tensor_tensor(out=T16[:], in0=T16[:], in1=a2[:], op=ALU.bitwise_or))
    p.comp("vector", lambda v: v.tensor_scalar(out=z16[:], in0=n16[:], scalar1=6, scalar2=None, op0=ALU.bitwise_and))
    p.comp("vector", lambda v: v.tensor_scalar(out=z16[:], in0=z16[:], scalar1=0, scalar2=None, op0=ALU.is_equal))
    p.comp("vector", lambda v: v.tensor_scalar(out=a1[:], in0=a2[:], scalar1=0x3F00, scalar2=None, op0=ALU.bitwise_or))
    p.comp("vector", lambda v: v.tensor_scalar(out=w16[:], in0=n16[:], scalar1=1, scalar2=7, op0=ALU.bitwise_and, op1=ALU.logical_shift_left))
    p.comp("vector", lambda v: v.tensor_tensor(out=a1[:], in0=a1[:], in1=w16[:], op=ALU.subtract))
    p.comp("vector", lambda v: v.tensor_tensor(out=z16[:], in0=z16[:], in1=a1[:], op=ALU.mult))
    p.comp("vector", lambda v: v.tensor_tensor(out=val_out, in0=T16[:].bitcast(BF16), in1=z16[:].bitcast(BF16), op=ALU.subtract))


def build_qkv(layer):
    nc = new_nc()
    rope = (layer == 1)
    d_x = nc.dram_tensor("x", [TPC, D], F32, kind="ExternalInput").ap()
    d_w = {n: nc.dram_tensor(n, [D, D], F32, kind="ExternalInput").ap() for n in ("wqT", "wkT", "wvT")}
    if rope:
        d_b = {n: nc.dram_tensor(n, [1, D], F32, kind="ExternalInput").ap() for n in ("bq", "bk", "bv")}
        d_cos = nc.dram_tensor("cos4", [TPC, 512], F32, kind="ExternalInput").ap()
        d_sin = nc.dram_tensor("sin4", [TPC, 512], F32, kind="ExternalInput").ap()
    d_out = {n: nc.dram_tensor("o_" + n, [TPC, D], F32, kind="ExternalOutput").ap() for n in ("q", "k", "v")}
    d_ident = nc.inline_tensor(np.eye(128, dtype=BF), "identbf")

    NT = TPC // 128
    with ExitStack() as stack:
        ctx = Ctx(nc, stack)
        p = Prog(nc)
        xsb = ctx.sb("xsb", [128, D], F32)
        xc = ctx.sb("xc", [128, D], F32)
        sq = ctx.sb("sq", [128, D], F32)
        stats = ctx.sb("stats", [128, 4], F32)
        epsc = ctx.sb("epsc", [128, 1], F32)
        ytile = ctx.sb("ytile", [128, D], BF16)
        yT = ctx.sb("yT", [128, 16 * TPC], BF16)
        ident = ctx.sb("ident", [128, 128], BF16)
        wtile = ctx.sb("wtile", [128, 16 * 512], BF16)
        onesr = ctx.sb("onesr", [1, 128], BF16)
        brow = ctx.sb("brow", [1, D], BF16)
        outf = ctx.sb("outf", [128, 512], F32)
        costile = ctx.sb("costile", [128, 512], F32)
        sintile = ctx.sb("sintile", [128, 512], F32)
        ra = ctx.sb("ra", [128, 512], F32)
        rb = ctx.sb("rb", [128, 512], F32)
        psA = ctx.ps("psA", [128, 512], F32)
        psT = ctx.ps("psT", [128, 128], BF16)
        block = stack.enter_context(nc.Block())

        li = p.load(ident[:], d_ident.ap())
        p.comp("vector", lambda v: v.memset(onesr[:], 1.0))
        p.comp("vector", lambda v: v.memset(epsc[:], 1e-5))

        for tt in range(NT):
            lx = p.load(xsb[:], d_x[tt * 128:(tt + 1) * 128, :])
            emit_ln(p, xsb[:], ytile[:], xc[:], sq[:], stats[:], epsc[:, 0:1], loads=[lx])
            for kt in range(16):
                p.comp("tensor", lambda t, kt=kt: t.transpose(
                    psT[:], ytile[:, kt * 128:(kt + 1) * 128], ident[:]), loads=[li])
                p.comp("vector", lambda v, kt=kt, tt=tt: v.tensor_copy(
                    out=yT[:, kt * TPC + tt * 128: kt * TPC + (tt + 1) * 128], in_=psT[:]))

        for wn, on in (("wqT", "q"), ("wkT", "k"), ("wvT", "v")):
            if rope:
                lb = p.load(brow[:], d_b[{"wqT": "bq", "wkT": "bk", "wvT": "bv"}[wn]], queue="gpsimd")
            for oc in range(4):
                lws = [p.load(wtile[:, kt * 512:(kt + 1) * 512],
                              d_w[wn][kt * 128:(kt + 1) * 128, oc * 512:(oc + 1) * 512],
                              queue="gpsimd") for kt in range(16)]
                for tt in range(NT):
                    fns = [lambda t, kt=kt, tt=tt: t.matmul(
                        psA[:], yT[:, kt * TPC + tt * 128: kt * TPC + (tt + 1) * 128],
                        wtile[:, kt * 512:(kt + 1) * 512],
                        start=(kt == 0), stop=(kt == 15 and not rope)) for kt in range(16)]
                    if rope:
                        fns.append(lambda t, oc=oc: t.matmul(
                            psA[:], onesr[:], brow[:, oc * 512:(oc + 1) * 512],
                            start=False, stop=True))
                        p.comp("tensor", fns, loads=lws + [lb])
                    else:
                        p.comp("tensor", fns, loads=lws)
                    if rope and wn in ("wqT", "wkT"):
                        lc = p.load(costile[:], d_cos[tt * 128:(tt + 1) * 128, :])
                        ls = p.load(sintile[:], d_sin[tt * 128:(tt + 1) * 128, :])
                        p.comp("vector", lambda v: v.tensor_tensor(out=ra[:], in0=psA[:], in1=costile[:], op=ALU.mult), loads=[lc])
                        p.comp("vector", lambda v: v.tensor_tensor(out=rb[:], in0=psA[:], in1=sintile[:], op=ALU.mult), loads=[ls])
                        ev = lambda x: x.rearrange("p (f two) -> p f two", two=2)[:, :, 0:1]
                        od = lambda x: x.rearrange("p (f two) -> p f two", two=2)[:, :, 1:2]
                        p.comp("vector", lambda v: v.tensor_tensor(out=ev(outf[:]), in0=ev(ra[:]), in1=od(rb[:]), op=ALU.subtract))
                        p.comp("vector", lambda v: v.tensor_tensor(out=od(outf[:]), in0=od(ra[:]), in1=ev(rb[:]), op=ALU.add))
                    else:
                        p.comp("vector", lambda v: v.tensor_copy(out=outf[:], in_=psA[:]))
                    p.store(d_out[on][tt * 128:(tt + 1) * 128, oc * 512:(oc + 1) * 512], outf[:])

        p.build(block, ctx.sems)
    return nc


def build_attn(layer):
    nc = new_nc()
    alibi = (layer == 0)
    d_qT = nc.dram_tensor("qTu", [4, 128, S], F32, kind="ExternalInput").ap()
    d_kT = nc.dram_tensor("kTu", [4, 128, S], F32, kind="ExternalInput").ap()
    d_v = nc.dram_tensor("vu", [4, S, 128], F32, kind="ExternalInput").ap()
    if alibi:
        d_acol = nc.dram_tensor("acolu", [4, 128, 32], F32, kind="ExternalInput").ap()
        d_shift = nc.dram_tensor("shiftu", [4, 8, 512], F32, kind="ExternalInput").ap()
        d_ub = nc.dram_tensor("ubu", [4, 8, 512], F32, kind="ExternalInput").ap()
        d_lb = nc.dram_tensor("lbu", [4, 8, 512], F32, kind="ExternalInput").ap()
    d_z = nc.dram_tensor("o_z", [4, S, 128], F32, kind="ExternalOutput").ap()

    tri_np = np.where(np.arange(128)[:, None] <= np.arange(128)[None, :], 0.0, -1e30).astype(np.float32)
    d_tri = nc.inline_tensor(tri_np, "trimask")

    NB = S // 128
    with ExitStack() as stack:
        ctx = Ctx(nc, stack)
        p = Prog(nc)
        qstage = ctx.sb("qstage", [128, 512], F32)
        qTb = ctx.sb("qTb", [128, S], BF16)
        kTb = ctx.sb("kTb", [128, S], BF16)
        vsb = ctx.sb("vsb", [128, NB * 132], BF16)
        acol = ctx.sb("acol", [128, 32], F32)
        ubt = ctx.sb("ubt", [128, 512], F32)
        lbt = ctx.sb("lbt", [128, 512], F32)
        shrow = ctx.sb("shrow", [1, 512], F32)
        onesc = ctx.sb("onesc", [1, 128], F32)
        tri = ctx.sb("tri", [128, 128], F32)
        Ssb = ctx.sb("Ssb", [128, 512], F32)
        PT = ctx.sb("PT", [128, 512], BF16)
        recip = ctx.sb("recip", [128, 1], F32)
        mcol = ctx.sb("mcol", [128, 1], F32)
        zsb = ctx.sb("zsb", [128, 128], F32)
        psS = ctx.ps("psS", [128, 512], F32)
        zps = [ctx.ps(f"zps{i}", [128, 132], F32) for i in range(4)]
        block = stack.enter_context(nc.Block())

        lt = p.load(tri[:], d_tri.ap())
        p.comp("vector", lambda v: v.memset(onesc[:], 1.0))
        p.comp("vector", lambda v: v.memset(mcol[:], -EXP_M))
        p.comp("vector", lambda v: v.memset(
            vsb[:].rearrange("p (b w) -> p b w", w=132)[:, :, 128:129], 1.0))

        for u in range(4):
            lk = p.load(kTb[:], d_kT[u], queue="gpsimd")
            lvs = []
            for g in range(4):
                lvs.append(p.load(
                    vsb[:].rearrange("p (b w) -> p b w", w=132)[:, g * 8:(g + 1) * 8, 0:128],
                    d_v[u][g * 1024:(g + 1) * 1024, :].rearrange("(b p) w -> p b w", p=128),
                    queue="gpsimd"))
            la = []
            if alibi:
                la = [p.load(acol[:], d_acol[u], queue="sync")]
            lqs = []
            for ch in range(8):
                lq = p.load(qstage[:], d_qT[u][:, ch * 512:(ch + 1) * 512])
                p.comp("scalar", lambda s, ch=ch: s.activation(
                    out=qTb[:, ch * 512:(ch + 1) * 512], in_=qstage[:],
                    func=AF.Identity, scale=1.0 / math.sqrt(DH)), loads=[lq])

            for r in range(8):
                lrun = []
                if alibi:
                    lrun.append(p.load(ubt[:], d_ub[u][r:r + 1, :].broadcast_to([128, 512])))
                    lrun.append(p.load(lbt[:], d_lb[u][r:r + 1, :].broadcast_to([128, 512])))
                    lrun.append(p.load(shrow[:], d_shift[u][r:r + 1, :]))
                for kb in range(4 * r + 4):
                    if alibi:
                        p.comp("tensor", [
                            lambda t, kb=kb, r=r: t.matmul(
                                psS[:], kTb[:, kb * 128:(kb + 1) * 128],
                                qTb[:, r * 512:(r + 1) * 512], start=True, stop=False),
                            lambda t: t.matmul(psS[:], onesc[:], shrow[:], start=False, stop=True),
                        ], loads=[lk] + lrun)
                        p.comp("vector", lambda v: v.tensor_tensor(out=Ssb[:], in0=psS[:], in1=ubt[:], op=ALU.min))
                        p.comp("vector", lambda v: v.tensor_tensor(out=Ssb[:], in0=Ssb[:], in1=lbt[:], op=ALU.max))
                        j = kb - 4 * r
                        if j >= 0:
                            if j > 0:
                                p.comp("vector", lambda v, j=j: v.memset(Ssb[:, 0:j * 128], -1e30))
                            p.comp("vector", lambda v, j=j: v.tensor_tensor(
                                out=Ssb[:, j * 128:(j + 1) * 128], in0=Ssb[:, j * 128:(j + 1) * 128],
                                in1=tri[:], op=ALU.add), loads=[lt])
                        p.comp("scalar", lambda s, kb=kb: s.activation(
                            out=PT[:], in_=Ssb[:], func=AF.Exp, bias=acol[:, kb:kb + 1]), loads=la)
                    else:
                        p.comp("tensor", lambda t, kb=kb, r=r: t.matmul(
                            psS[:], kTb[:, kb * 128:(kb + 1) * 128],
                            qTb[:, r * 512:(r + 1) * 512], start=True, stop=True), loads=[lk])
                        j = kb - 4 * r
                        if j >= 0:
                            p.comp("vector", lambda v, j=j: v.tensor_tensor(
                                out=psS[:, j * 128:(j + 1) * 128], in0=psS[:, j * 128:(j + 1) * 128],
                                in1=tri[:], op=ALU.add), loads=[lt])
                        p.comp("scalar", lambda s: s.activation(
                            out=PT[:], in_=psS[:], func=AF.Exp, bias=mcol[:, 0:1]))
                    fns = []
                    for i in range(4):
                        if kb <= 4 * r + i:
                            fns.append(lambda t, i=i, kb=kb, r=r: t.matmul(
                                zps[i][:, 0:129], PT[:, i * 128:(i + 1) * 128],
                                vsb[:].rearrange("p (b w) -> p b w", w=132)[:, kb, 0:129],
                                start=(kb == 0), stop=(kb == 4 * r + i)))
                    p.comp("tensor", fns, loads=lvs)
                for i in range(4):
                    p.comp("vector", lambda v, i=i: v.reciprocal(out=recip[:], in_=zps[i][:, 128:129]))
                    p.comp("vector", lambda v, i=i: v.tensor_scalar(
                        out=zsb[:], in0=zps[i][:, 0:128], scalar1=recip[:, 0:1], scalar2=None, op0=ALU.mult))
                    p.store(d_z[u][(4 * r + i) * 128:(4 * r + i + 1) * 128, :], zsb[:])

        p.build(block, ctx.sems)
    return nc


def build_oproj(layer):
    nc = new_nc()
    bias = (layer == 1)
    d_zT = nc.dram_tensor("zT", [D, TPC], F32, kind="ExternalInput").ap()
    d_wo = nc.dram_tensor("woT", [D, D], F32, kind="ExternalInput").ap()
    d_x = nc.dram_tensor("x", [TPC, D], F32, kind="ExternalInput").ap()
    if bias:
        d_bo = nc.dram_tensor("bo", [1, D], F32, kind="ExternalInput").ap()
    d_xn = nc.dram_tensor("o_xn", [TPC, D], F32, kind="ExternalOutput").ap()
    d_y = nc.dram_tensor("o_y", [TPC, D], BF16, kind="ExternalOutput").ap()
    if bias:
        d_y32 = nc.dram_tensor("o_y32", [TPC, D], F32, kind="ExternalOutput").ap()

    NT = TPC // 128
    with ExitStack() as stack:
        ctx = Ctx(nc, stack)
        p = Prog(nc)
        zTb = ctx.sb("zTb", [128, 16 * TPC], BF16)
        wsb = ctx.sb("wsb", [128, 16 * 2048], BF16)
        xsb = ctx.sb("xsb", [128, D], F32)
        xn = ctx.sb("xn", [128, D], F32)
        xc = ctx.sb("xc", [128, D], F32)
        sq = ctx.sb("sq", [128, D], F32)
        stats = ctx.sb("stats", [128, 4], F32)
        epsc = ctx.sb("epsc", [128, 1], F32)
        ysb = ctx.sb("ysb", [128, D], BF16)
        onesr = ctx.sb("onesr", [1, 128], BF16)
        brow = ctx.sb("brow", [1, D], BF16)
        psA = ctx.ps("psA", [128, 512], F32)
        block = stack.enter_context(nc.Block())

        p.comp("vector", lambda v: v.memset(onesr[:], 1.0))
        p.comp("vector", lambda v: v.memset(epsc[:], 1e-5))
        lb = []
        if bias:
            lb = [p.load(brow[:], d_bo, queue="gpsimd")]
        lz = [p.load(zTb[:, kt * TPC:(kt + 1) * TPC], d_zT[kt * 128:(kt + 1) * 128, :], queue="gpsimd")
              for kt in range(16)]
        lw = [p.load(wsb[:, kt * 2048:(kt + 1) * 2048], d_wo[kt * 128:(kt + 1) * 128, :], queue="gpsimd")
              for kt in range(16)]

        for tt in range(NT):
            lx = p.load(xsb[:], d_x[tt * 128:(tt + 1) * 128, :])
            for oc in range(4):
                fns = [lambda t, kt=kt, tt=tt, oc=oc: t.matmul(
                    psA[:], zTb[:, kt * TPC + tt * 128: kt * TPC + (tt + 1) * 128],
                    wsb[:, kt * 2048 + oc * 512: kt * 2048 + (oc + 1) * 512],
                    start=(kt == 0), stop=(kt == 15 and not bias)) for kt in range(16)]
                if bias:
                    fns.append(lambda t, oc=oc: t.matmul(
                        psA[:], onesr[:], brow[:, oc * 512:(oc + 1) * 512], start=False, stop=True))
                p.comp("tensor", fns, loads=lz + lw + lb)
                p.comp("vector", lambda v, oc=oc: v.tensor_tensor(
                    out=xn[:, oc * 512:(oc + 1) * 512], in0=psA[:],
                    in1=xsb[:, oc * 512:(oc + 1) * 512], op=ALU.add), loads=[lx])
            p.store(d_xn[tt * 128:(tt + 1) * 128, :], xn[:])
            emit_ln(p, xn[:], ysb[:], xc[:], sq[:], stats[:], epsc[:, 0:1])
            p.store(d_y[tt * 128:(tt + 1) * 128, :], ysb[:])
            if bias:
                p.comp("vector", lambda v: v.tensor_scalar(
                    out=xc[:], in0=xc[:], scalar1=stats[:, 3:4], scalar2=None, op0=ALU.mult))
                p.store(d_y32[tt * 128:(tt + 1) * 128, :], xc[:])

        p.build(block, ctx.sems)
    return nc


def build_ffn0():
    nc = new_nc()
    d_y = nc.dram_tensor("y1T", [D, T], BF16, kind="ExternalInput").ap()
    d_gpk = nc.dram_tensor("gpk", [NPC, 2048], I16, kind="ExternalInput").ap()
    d_upk = nc.dram_tensor("upk", [NPC, 2048], I16, kind="ExternalInput").ap()
    d_gsc = nc.dram_tensor("gsc", [NPC, 64], F32, kind="ExternalInput").ap()
    d_usc = nc.dram_tensor("usc", [NPC, 64], F32, kind="ExternalInput").ap()
    d_dpk = nc.dram_tensor("dpk", [D, NPC], I16, kind="ExternalInput").ap()
    d_dsc = nc.dram_tensor("dsc", [D, NPC // 32], F32, kind="ExternalInput").ap()
    d_part = nc.dram_tensor("o_part", [T, D], F32, kind="ExternalOutput").ap()
    d_inter = nc.dram_tensor("interT", [NPC, T], BF16, kind="Internal")
    d_ident = nc.inline_tensor(np.eye(128, dtype=BF), "identbf")
    alt_np = np.tile(np.array([0, 4], dtype=np.int16), 1024).reshape(1, 2048)
    d_alt = nc.inline_tensor(alt_np, "altshift")

    with ExitStack() as stack:
        ctx = Ctx(nc, stack)
        p = Prog(nc)
        dec = {nm: ctx.sb(nm, [128, 2048], I16) for nm in ("pk16", "T16", "n16", "a1", "a2", "z16", "w16", "alt")}
        val = ctx.sb("val", [128, 2048], BF16)
        scsb = ctx.sb("scsb", [128, 64], F32)
        scb = ctx.sb("scb", [128, 64], BF16)
        ident = ctx.sb("ident", [128, 128], BF16)
        wgT = ctx.sb("wgT", [128, 16 * 2048], BF16)
        wuT = ctx.sb("wuT", [128, 16 * 2048], BF16)
        ysb = ctx.sb("ysb", [128, 16 * 512], BF16)
        g32 = ctx.sb("g32", [128, 512], F32)
        t3 = ctx.sb("t3", [128, 512], F32)
        t4 = ctx.sb("t4", [128, 512], F32)
        sgm = ctx.sb("sgm", [128, 512], F32)
        w32 = ctx.sb("w32", [128, 512], F32)
        inter = ctx.sb("inter", [128, 512], BF16)
        osb = ctx.sb("osb", [128, 512], F32)
        psG = ctx.ps("psG", [128, 512], F32)
        psU = ctx.ps("psU", [128, 512], F32)
        psT = ctx.ps("psT", [128, 128], BF16)
        block = stack.enter_context(nc.Block())

        li = p.load(ident[:], d_ident.ap())
        lalt = p.load(dec["alt"][:], d_alt.ap().broadcast_to([128, 2048]))

        def decode_into(dst, d_pk, d_sc, ntile):
            lpk = p.load(dec["pk16"][:], d_pk[ntile * 128:(ntile + 1) * 128, :], queue="gpsimd")
            lsc = p.load(scsb[:], d_sc[ntile * 128:(ntile + 1) * 128, :], queue="gpsimd")
            emit_decode(p, dec, [lpk, lalt], val[:])
            p.comp("vector", lambda v: v.tensor_copy(out=scb[:], in_=scsb[:]), loads=[lsc])
            vv = val[:].rearrange("p (c q) -> p c q", q=32)
            sbb = scb[:].unsqueeze(2).broadcast_to([128, 64, 32])
            p.comp("vector", lambda v: v.tensor_tensor(out=vv, in0=vv, in1=sbb, op=ALU.mult))
            for ht in range(16):
                p.comp("tensor", lambda t, ht=ht: t.transpose(
                    psT[:], val[:, ht * 128:(ht + 1) * 128], ident[:]), loads=[li])
                p.comp("vector", lambda v, ht=ht, ntile=ntile: v.tensor_copy(
                    out=dst[:, ht * 2048 + ntile * 128: ht * 2048 + (ntile + 1) * 128], in_=psT[:]))

        for ntile in range(16):
            decode_into(wgT, d_gpk, d_gsc, ntile)
            decode_into(wuT, d_upk, d_usc, ntile)

        for tc in range(16):
            ly = [p.load(ysb[:, ht * 512:(ht + 1) * 512],
                         d_y[ht * 128:(ht + 1) * 128, tc * 512:(tc + 1) * 512]) for ht in range(16)]
            for ntile in range(16):
                p.comp("tensor", [lambda t, ht=ht, ntile=ntile: t.matmul(
                    psG[:], wgT[:, ht * 2048 + ntile * 128: ht * 2048 + (ntile + 1) * 128],
                    ysb[:, ht * 512:(ht + 1) * 512], start=(ht == 0), stop=(ht == 15)) for ht in range(16)], loads=ly)
                p.comp("tensor", [lambda t, ht=ht, ntile=ntile: t.matmul(
                    psU[:], wuT[:, ht * 2048 + ntile * 128: ht * 2048 + (ntile + 1) * 128],
                    ysb[:, ht * 512:(ht + 1) * 512], start=(ht == 0), stop=(ht == 15)) for ht in range(16)])
                p.comp("vector", lambda v: v.tensor_copy(out=g32[:], in_=psG[:]))
                p.comp("scalar", lambda s: s.activation(out=t3[:], in_=g32[:], func=AF.Square))
                p.comp("vector", lambda v: v.tensor_scalar(out=t3[:], in0=t3[:], scalar1=GELU_C, scalar2=1.0, op0=ALU.mult, op1=ALU.add))
                p.comp("vector", lambda v: v.tensor_tensor(out=t4[:], in0=g32[:], in1=t3[:], op=ALU.mult))
                p.comp("scalar", lambda s: s.activation(out=sgm[:], in_=t4[:], func=AF.Sigmoid, scale=GELU_A))
                p.comp("vector", lambda v: v.tensor_tensor(out=w32[:], in0=g32[:], in1=sgm[:], op=ALU.mult))
                p.comp("vector", lambda v: v.scalar_tensor_tensor(out=inter[:], in0=w32[:], scalar=1.0, in1=psU[:], op0=ALU.mult, op1=ALU.mult))
                p.store(d_inter.ap()[ntile * 128:(ntile + 1) * 128, tc * 512:(tc + 1) * 512], inter[:])

        for ot in range(16):
            lpk = p.load(dec["pk16"][:], d_dpk[ot * 128:(ot + 1) * 128, :], queue="gpsimd")
            lsc = p.load(scsb[:], d_dsc[ot * 128:(ot + 1) * 128, :], queue="gpsimd")
            emit_decode(p, dec, [lpk, lalt], val[:])
            p.comp("vector", lambda v: v.tensor_copy(out=scb[:], in_=scsb[:]), loads=[lsc])
            vv = val[:].rearrange("p (c q) -> p c q", q=32)
            sbb = scb[:].unsqueeze(2).broadcast_to([128, 64, 32])
            p.comp("vector", lambda v: v.tensor_tensor(out=vv, in0=vv, in1=sbb, op=ALU.mult))
            for nt in range(16):
                p.comp("tensor", lambda t, nt=nt: t.transpose(
                    psT[:], val[:, nt * 128:(nt + 1) * 128], ident[:]), loads=[li])
                p.comp("vector", lambda v, nt=nt, ot=ot: v.tensor_copy(
                    out=wgT[:, nt * 2048 + ot * 128: nt * 2048 + (ot + 1) * 128], in_=psT[:]))

        for tc in range(16):
            lis = [p.load(ysb[:, nt * 512:(nt + 1) * 512],
                          d_inter.ap()[nt * 128:(nt + 1) * 128, tc * 512:(tc + 1) * 512]) for nt in range(16)]
            for tsub in range(4):
                for oc in range(4):
                    p.comp("tensor", [lambda t, nt=nt, tsub=tsub, oc=oc: t.matmul(
                        psG[:], ysb[:, nt * 512 + tsub * 128: nt * 512 + (tsub + 1) * 128],
                        wgT[:, nt * 2048 + oc * 512: nt * 2048 + (oc + 1) * 512],
                        start=(nt == 0), stop=(nt == 15)) for nt in range(16)], loads=lis)
                    p.comp("vector", lambda v: v.tensor_copy(out=osb[:], in_=psG[:]))
                    p.store(d_part[tc * 512 + tsub * 128: tc * 512 + (tsub + 1) * 128,
                                   oc * 512:(oc + 1) * 512], osb[:])

        p.build(block, ctx.sems)
    return nc


def build_reduce(nparts, parts_bf16, with_shared):
    nc = new_nc()
    pdt = BF16 if parts_bf16 else F32
    d_parts = nc.dram_tensor("parts", [nparts, TPC, D], pdt, kind="ExternalInput").ap()
    d_x = nc.dram_tensor("x", [TPC, D], F32, kind="ExternalInput").ap()
    if with_shared:
        d_sh = nc.dram_tensor("shared", [TPC, D], F32, kind="ExternalInput").ap()
    d_xn = nc.dram_tensor("o_xn", [TPC, D], F32, kind="ExternalOutput").ap()

    NT = TPC // 128
    with ExitStack() as stack:
        ctx = Ctx(nc, stack)
        p = Prog(nc)
        acc = ctx.sb("acc", [128, D], F32)
        pt = ctx.sb("pt", [128, D], F32)
        block = stack.enter_context(nc.Block())
        for tt in range(NT):
            lx = p.load(acc[:], d_x[tt * 128:(tt + 1) * 128, :])
            srcs = [d_parts[i][tt * 128:(tt + 1) * 128, :] for i in range(nparts)]
            if with_shared:
                srcs.append(d_sh[tt * 128:(tt + 1) * 128, :])
            first = True
            for sap in srcs:
                lp = p.load(pt[:], sap, queue="gpsimd")
                p.comp("vector", lambda v: v.tensor_tensor(out=acc[:], in0=acc[:], in1=pt[:], op=ALU.add),
                       loads=[lp] + ([lx] if first else []))
                first = False
            p.store(d_xn[tt * 128:(tt + 1) * 128, :], acc[:])
        p.build(block, ctx.sems)
    return nc


def build_moe():
    nc = new_nc()
    d_y = nc.dram_tensor("y3T32", [D, T], F32, kind="ExternalInput").ap()
    d_ym = nc.dram_tensor("y3Tmy32", [D, TPC], F32, kind="ExternalInput").ap()
    d_gw = nc.dram_tensor("gwT", [D, 16], F32, kind="ExternalInput").ap()
    d_w1 = nc.dram_tensor("w1", [2, D, MH], F32, kind="ExternalInput").ap()
    d_w2 = nc.dram_tensor("w2", [2, MH, D], F32, kind="ExternalInput").ap()
    d_sw1 = nc.dram_tensor("sw1", [D, MH], F32, kind="ExternalInput").ap()
    d_sw2 = nc.dram_tensor("sw2", [MH, D], F32, kind="ExternalInput").ap()
    d_esel = nc.dram_tensor("esel", [2, 1, 16], F32, kind="ExternalInput").ap()
    d_part = nc.dram_tensor("o_part", [2, T, D], BF16, kind="ExternalOutput").ap()
    d_shared = nc.dram_tensor("o_shared", [TPC, D], F32, kind="ExternalOutput").ap()

    NH = MH // 128
    with ExitStack() as stack:
        ctx = Ctx(nc, stack)
        p = Prog(nc)
        ysb = ctx.sb("ysb", [128, 16 * 512], BF16)
        yf = ctx.sb("yf", [128, 128], F32)
        gw = ctx.sb("gw", [128, 16 * 16], F32)
        comb = ctx.sb("comb", [128, 64 * 2], F32)
        combt = ctx.sb("combt", [128, 16], F32)
        eselt = ctx.sb("eselt", [128, 2 * 16], F32)
        ssb = ctx.sb("ssb", [128, 16], F32)
        r4 = ctx.sb("r4", [128, 4], F32)
        r4b = ctx.sb("r4b", [128, 4], F32)
        r16 = ctx.sb("r16", [128, 16], F32)
        r1 = ctx.sb("r1", [128, 8], F32)
        w1sb = ctx.sb("w1sb", [128, 16 * MH], BF16)
        w2sb = ctx.sb("w2sb", [128, NH * D], BF16)
        asb = ctx.sb("asb", [128, NH * 512], BF16)
        accb = ctx.sb("accb", [128, 512], BF16)
        accf = ctx.sb("accf", [128, 512], F32)
        psR = ctx.ps("psR", [128, 16], F32)
        psA = ctx.ps("psA", [128, 512], F32)
        psO = ctx.ps("psO", [128, 512], F32)
        block = stack.enter_context(nc.Block())

        lsel = [p.load(eselt[:, e * 16:(e + 1) * 16], d_esel[e].broadcast_to([128, 16])) for e in range(2)]
        lgw = [p.load(gw[:, kt * 16:(kt + 1) * 16], d_gw[kt * 128:(kt + 1) * 128, :]) for kt in range(16)]
        for tt in range(64):
            fns = []
            lyt = []
            for kt in range(16):
                lyt.append(p.load(yf[:], d_y[kt * 128:(kt + 1) * 128, tt * 128:(tt + 1) * 128]))
                fns = [lambda t, kt=kt, tt=tt: t.matmul(
                    psR[:], yf[:], gw[:, kt * 16:(kt + 1) * 16],
                    start=(kt == 0), stop=(kt == 15))]
                p.comp("tensor", fns, loads=[lyt[-1]] + (lgw if kt == 0 else []))
            p.comp("scalar", lambda s: s.activation(out=ssb[:], in_=psR[:], func=AF.Sigmoid))
            sg = ssb[:].rearrange("p (g q) -> p g q", q=4)
            p.comp("vector", lambda v: v.tensor_reduce(out=r4[:], in_=sg, axis=AX.X, op=ALU.max))
            bc4 = r4[:].unsqueeze(2).broadcast_to([128, 4, 4])
            p.comp("vector", lambda v: v.tensor_tensor(out=r16[:].rearrange("p (g q) -> p g q", q=4), in0=sg, in1=bc4, op=ALU.is_equal))
            p.comp("vector", lambda v: v.scalar_tensor_tensor(out=r16[:], in0=r16[:], scalar=-1e30, in1=ssb[:], op0=ALU.mult, op1=ALU.add))
            p.comp("vector", lambda v: v.tensor_reduce(out=r4b[:], in_=r16[:].rearrange("p (g q) -> p g q", q=4), axis=AX.X, op=ALU.max))
            p.comp("vector", lambda v: v.tensor_tensor(out=r4[:], in0=r4[:], in1=r4b[:], op=ALU.add))
            p.comp("vector", lambda v: v.tensor_reduce(out=r1[:, 0:1], in_=r4[:], axis=AX.X, op=ALU.max))
            p.comp("vector", lambda v: v.tensor_scalar(out=r4b[:], in0=r4[:], scalar1=r1[:, 0:1], scalar2=None, op0=ALU.is_equal))
            p.comp("vector", lambda v: v.scalar_tensor_tensor(out=r4b[:], in0=r4b[:], scalar=-1e30, in1=r4[:], op0=ALU.mult, op1=ALU.add))
            p.comp("vector", lambda v: v.tensor_reduce(out=r1[:, 1:2], in_=r4b[:], axis=AX.X, op=ALU.max))
            p.comp("vector", lambda v: v.tensor_scalar(out=r4b[:], in0=r4[:], scalar1=r1[:, 1:2], scalar2=None, op0=ALU.is_ge))
            bc4b = r4b[:].unsqueeze(2).broadcast_to([128, 4, 4])
            p.comp("vector", lambda v: v.tensor_tensor(out=r16[:].rearrange("p (g q) -> p g q", q=4), in0=sg, in1=bc4b, op=ALU.mult))
            p.comp("vector", lambda v: v.tensor_reduce(out=r1[:, 2:3], in_=r16[:], axis=AX.X, op=ALU.max))
            p.comp("vector", lambda v: v.tensor_scalar(out=ssb[:], in0=r16[:], scalar1=r1[:, 2:3], scalar2=None, op0=ALU.is_equal))
            p.comp("vector", lambda v: v.scalar_tensor_tensor(out=ssb[:], in0=ssb[:], scalar=-1e30, in1=r16[:], op0=ALU.mult, op1=ALU.add))
            p.comp("vector", lambda v: v.tensor_reduce(out=r1[:, 3:4], in_=ssb[:], axis=AX.X, op=ALU.max))
            p.comp("vector", lambda v: v.tensor_tensor(out=r1[:, 4:5], in0=r1[:, 2:3], in1=r1[:, 3:4], op=ALU.add))
            p.comp("vector", lambda v: v.tensor_scalar(out=r1[:, 4:5], in0=r1[:, 4:5], scalar1=1e-20, scalar2=None, op0=ALU.add))
            p.comp("vector", lambda v: v.reciprocal(out=r1[:, 5:6], in_=r1[:, 4:5]))
            p.comp("vector", lambda v: v.tensor_scalar(out=ssb[:], in0=r16[:], scalar1=r1[:, 3:4], scalar2=None, op0=ALU.is_ge))
            p.comp("vector", lambda v: v.tensor_tensor(out=ssb[:], in0=ssb[:], in1=r16[:], op=ALU.mult))
            p.comp("vector", lambda v: v.tensor_scalar(out=ssb[:], in0=ssb[:], scalar1=r1[:, 5:6], scalar2=None, op0=ALU.mult))
            for e in range(2):
                p.comp("vector", lambda v, e=e: v.tensor_tensor(out=combt[:], in0=ssb[:], in1=eselt[:, e * 16:(e + 1) * 16], op=ALU.mult), loads=[lsel[e]])
                p.comp("vector", lambda v, e=e, tt=tt: v.tensor_reduce(out=comb[:, tt * 2 + e: tt * 2 + e + 1], in_=combt[:], axis=AX.X, op=ALU.add))

        def expert_pass(d_w1src, d_w2src, n_tc, d_ysrc, use_comb, e, d_out):
            lw1 = [p.load(w1sb[:, kt * MH:(kt + 1) * MH], d_w1src[kt * 128:(kt + 1) * 128, :], queue="gpsimd")
                   for kt in range(16)]
            lw2 = [p.load(w2sb[:, ht * D:(ht + 1) * D], d_w2src[ht * 128:(ht + 1) * 128, :], queue="gpsimd")
                   for ht in range(NH)]
            for tc in range(n_tc):
                ly = [p.load(ysb[:, kt * 512:(kt + 1) * 512],
                             d_ysrc[kt * 128:(kt + 1) * 128, tc * 512:(tc + 1) * 512], queue="gpsimd")
                      for kt in range(16)]
                for ht in range(NH):
                    p.comp("tensor", [lambda t, kt=kt, ht=ht: t.matmul(
                        psA[:], w1sb[:, kt * MH + ht * 128: kt * MH + (ht + 1) * 128],
                        ysb[:, kt * 512:(kt + 1) * 512], start=(kt == 0), stop=(kt == 15))
                        for kt in range(16)], loads=ly + lw1)
                    p.comp("scalar", lambda s, ht=ht: s.activation(
                        out=asb[:, ht * 512:(ht + 1) * 512], in_=psA[:], func=AF.Silu))
                for tsub in range(4):
                    for oc in range(4):
                        p.comp("tensor", [lambda t, ht=ht, tsub=tsub, oc=oc: t.matmul(
                            psO[:], asb[:, ht * 512 + tsub * 128: ht * 512 + (tsub + 1) * 128],
                            w2sb[:, ht * D + oc * 512: ht * D + (oc + 1) * 512],
                            start=(ht == 0), stop=(ht == NH - 1)) for ht in range(NH)], loads=lw2)
                        if use_comb:
                            cap = comb[:, (tc * 4 + tsub) * 2 + e: (tc * 4 + tsub) * 2 + e + 1]
                            p.comp("vector", lambda v, cap=cap: v.tensor_scalar(
                                out=accb[:], in0=psO[:], scalar1=cap, scalar2=None, op0=ALU.mult))
                            p.store(d_out[tc * 512 + tsub * 128: tc * 512 + (tsub + 1) * 128,
                                          oc * 512:(oc + 1) * 512], accb[:])
                        else:
                            p.comp("vector", lambda v: v.tensor_copy(out=accf[:], in_=psO[:]))
                            p.store(d_out[tc * 512 + tsub * 128: tc * 512 + (tsub + 1) * 128,
                                          oc * 512:(oc + 1) * 512], accf[:])

        for e in range(2):
            expert_pass(d_w1[e], d_w2[e], 16, d_y, True, e, d_part[e])
        expert_pass(d_sw1, d_sw2, TPC // 512, d_ym, False, 0, d_shared)

        p.build(block, ctx.sems)
    return nc


def _t(a):
    return np.ascontiguousarray(np.asarray(a).T)


def attn_inputs(q, k, v, alibi):
    maps = []
    pos = np.arange(S, dtype=np.float32)
    for c in range(NCORES):
        units = [(b, h) for b in range(B) for h in (2 * c, 2 * c + 1)]
        qTu = np.stack([_t(q.reshape(B, S, H, DH)[b, :, h, :]) for (b, h) in units])
        kTu = np.stack([_t(k.reshape(B, S, H, DH)[b, :, h, :]) for (b, h) in units])
        vu = np.stack([np.ascontiguousarray(v.reshape(B, S, H, DH)[b, :, h, :]) for (b, h) in units])
        m = dict(qTu=qTu, kTu=kTu, vu=vu)
        if alibi:
            acolu = np.stack([
                (SLOPES[h] * pos - EXP_M).reshape(32, 128).T.copy() for (b, h) in units])
            shiftu = np.stack([
                np.stack([-SLOPES[h] * (r * 512 + np.arange(512, dtype=np.float32)) for r in range(8)])
                for (b, h) in units])
            m.update(acolu=np.ascontiguousarray(acolu, dtype=np.float32),
                     shiftu=np.ascontiguousarray(shiftu, dtype=np.float32),
                     ubu=np.ascontiguousarray(1024.0 + shiftu, dtype=np.float32),
                     lbu=np.ascontiguousarray(-1024.0 + shiftu, dtype=np.float32))
        maps.append(m)
    return maps


def attn_gather(res):
    z = np.zeros((B, S, H, DH), dtype=np.float32)
    for c in range(NCORES):
        units = [(b, h) for b in range(B) for h in (2 * c, 2 * c + 1)]
        for ui, (b, h) in enumerate(units):
            z[b, :, h, :] = res.results[c]["o_z"][ui]
    return z.reshape(T, D)


def kernel_multi(**inputs):
    xf = np.ascontiguousarray(np.asarray(inputs["L0_attn_resid_pre"], dtype=np.float32).reshape(T, D))
    my = lambda a, c: np.ascontiguousarray(a[c * TPC:(c + 1) * TPC])

    w0 = {n: _t(inputs["L0_attn_" + k]) for n, k in (("wqT", "Wq"), ("wkT", "Wk"), ("wvT", "Wv"))}
    nc = build_qkv(0)
    res = launch(nc, [dict(x=my(xf, c), **w0) for c in range(NCORES)])
    q0 = np.concatenate([r["o_q"] for r in res.results])
    k0 = np.concatenate([r["o_k"] for r in res.results])
    v0 = np.concatenate([r["o_v"] for r in res.results])

    nc = build_attn(0)
    res = launch(nc, attn_inputs(q0, k0, v0, True))
    z0 = attn_gather(res)

    wo0T = _t(inputs["L0_attn_Wo"])
    z0T = _t(z0)
    nc = build_oproj(0)
    res = launch(nc, [dict(zT=np.ascontiguousarray(z0T[:, c * TPC:(c + 1) * TPC]),
                           woT=wo0T, x=my(xf, c)) for c in range(NCORES)])
    x1 = np.concatenate([r["o_xn"] for r in res.results])
    y1 = np.concatenate([r["o_y"] for r in res.results])

    y1T = _t(y1)
    gpk = np.asarray(inputs["L0_ffn_gate_W_packed"], dtype=np.int32)
    upk = np.asarray(inputs["L0_ffn_up_W_packed"], dtype=np.int32)
    dpk = np.asarray(inputs["L0_ffn_down_W_packed"], dtype=np.int32)
    gsc = np.asarray(inputs["L0_ffn_gate_scales"], dtype=np.float32)
    usc = np.asarray(inputs["L0_ffn_up_scales"], dtype=np.float32)
    dsc = np.asarray(inputs["L0_ffn_down_scales"], dtype=np.float32)
    nc = build_ffn0()
    maps = []
    for c in range(NCORES):
        n0 = c * NPC
        maps.append(dict(
            y1T=y1T,
            gpk=np.ascontiguousarray(gpk[n0:n0 + NPC]).view(np.int16),
            upk=np.ascontiguousarray(upk[n0:n0 + NPC]).view(np.int16),
            gsc=np.ascontiguousarray(gsc[n0:n0 + NPC]),
            usc=np.ascontiguousarray(usc[n0:n0 + NPC]),
            dpk=np.ascontiguousarray(dpk[:, n0 // 2:(n0 + NPC) // 2]).view(np.int16),
            dsc=np.ascontiguousarray(dsc[:, n0 // 32:(n0 + NPC) // 32]),
        ))
    res = launch(nc, maps)
    parts0 = [r["o_part"] for r in res.results]

    nc = build_reduce(NCORES, False, False)
    res = launch(nc, [dict(parts=np.stack([pp[c * TPC:(c + 1) * TPC] for pp in parts0]),
                           x=my(x1, c)) for c in range(NCORES)])
    x2 = np.concatenate([r["o_xn"] for r in res.results])

    w1a = {n: _t(inputs["L1_attn_" + k]) for n, k in (("wqT", "Wq"), ("wkT", "Wk"), ("wvT", "Wv"))}
    b1 = {n: np.asarray(inputs["L1_attn_" + k], dtype=np.float32).reshape(1, D)
          for n, k in (("bq", "bq"), ("bk", "bk"), ("bv", "bv"))}
    half = DH // 2
    inv = 1.0 / (10000.0 ** (np.arange(half, dtype=np.float32) / half))
    tpos = np.arange(S, dtype=np.float32)
    fr = tpos[:, None] * inv[None, :]
    cosS = np.repeat(np.cos(fr), 2, axis=1).astype(np.float32)
    sinS = np.repeat(np.sin(fr), 2, axis=1).astype(np.float32)
    nc = build_qkv(1)
    maps = []
    for c in range(NCORES):
        rows = np.arange(c * TPC, (c + 1) * TPC) % S
        maps.append(dict(x=my(x2, c), **w1a, **b1,
                         cos4=np.ascontiguousarray(np.tile(cosS[rows], (1, 4))),
                         sin4=np.ascontiguousarray(np.tile(sinS[rows], (1, 4)))))
    res = launch(nc, maps)
    q1 = np.concatenate([r["o_q"] for r in res.results])
    k1 = np.concatenate([r["o_k"] for r in res.results])
    v1 = np.concatenate([r["o_v"] for r in res.results])

    nc = build_attn(1)
    res = launch(nc, attn_inputs(q1, k1, v1, False))
    z1 = attn_gather(res)

    wo1T = _t(inputs["L1_attn_Wo"])
    bo1 = np.asarray(inputs["L1_attn_bo"], dtype=np.float32).reshape(1, D)
    z1T = _t(z1)
    nc = build_oproj(1)
    res = launch(nc, [dict(zT=np.ascontiguousarray(z1T[:, c * TPC:(c + 1) * TPC]),
                           woT=wo1T, x=my(x2, c), bo=bo1) for c in range(NCORES)])
    x3 = np.concatenate([r["o_xn"] for r in res.results])
    y3_32 = np.concatenate([r["o_y32"] for r in res.results])

    y3T32 = _t(y3_32)
    gwT = _t(inputs["L1_ffn_gate_w"])
    w1m = np.asarray(inputs["L1_ffn_w1"], dtype=np.float32)
    w2m = np.asarray(inputs["L1_ffn_w2"], dtype=np.float32)
    sw1 = np.ascontiguousarray(np.asarray(inputs["L1_ffn_sw1"], dtype=np.float32)[0])
    sw2 = np.ascontiguousarray(np.asarray(inputs["L1_ffn_sw2"], dtype=np.float32)[0])
    nc = build_moe()
    maps = []
    for c in range(NCORES):
        esel = np.zeros((2, 1, 16), dtype=np.float32)
        esel[0, 0, 2 * c] = 1.0
        esel[1, 0, 2 * c + 1] = 1.0
        maps.append(dict(
            y3T32=y3T32,
            y3Tmy32=np.ascontiguousarray(y3T32[:, c * TPC:(c + 1) * TPC]),
            esel=esel,
            gwT=gwT,
            w1=np.ascontiguousarray(w1m[2 * c:2 * c + 2]),
            w2=np.ascontiguousarray(w2m[2 * c:2 * c + 2]),
            sw1=sw1, sw2=sw2))
    res = launch(nc, maps)
    partsM = [r["o_part"][e] for r in res.results for e in range(2)]
    sharedM = [r["o_shared"] for r in res.results]

    nc = build_reduce(2 * NCORES, True, True)
    res = launch(nc, [dict(parts=np.stack([pp[c * TPC:(c + 1) * TPC] for pp in partsM]),
                           x=my(x3, c), shared=sharedM[c]) for c in range(NCORES)])
    out = np.concatenate([r["o_xn"] for r in res.results])
    return out.reshape(B, S, D).astype(np.float32)


def _g3(src2d, p=128):
    return src2d.rearrange("(kt p) c -> p kt c", p=p)


def _s3(dst2d, k):
    return dst2d.rearrange("p (kt c) -> p kt c", kt=k)


def build_fused(upto=9):
    nc = new_nc()
    RG = [list(range(NCORES))]

    d_x0 = nc.dram_tensor("x0", [TPC, D], F32, kind="ExternalInput").ap()
    d_wq0 = nc.dram_tensor("wq0", [D, 256], F32, kind="ExternalInput").ap()
    d_wk0 = nc.dram_tensor("wk0", [D, 256], F32, kind="ExternalInput").ap()
    d_wv0 = nc.dram_tensor("wv0", [D, 256], F32, kind="ExternalInput").ap()
    d_wo0 = nc.dram_tensor("wo0", [256, D], F32, kind="ExternalInput").ap()
    d_acol = nc.dram_tensor("acolu", [4, 128, 32], F32, kind="ExternalInput").ap()
    d_shift = nc.dram_tensor("shiftu", [4, 8, 512], F32, kind="ExternalInput").ap()
    d_ub = nc.dram_tensor("ubu", [4, 8, 512], F32, kind="ExternalInput").ap()
    d_lb = nc.dram_tensor("lbu", [4, 8, 512], F32, kind="ExternalInput").ap()
    d_gpk = nc.dram_tensor("gpk", [NPC, 2048], I16, kind="ExternalInput").ap()
    d_upk = nc.dram_tensor("upk", [NPC, 2048], I16, kind="ExternalInput").ap()
    d_gsc = nc.dram_tensor("gsc", [NPC, 64], F32, kind="ExternalInput").ap()
    d_usc = nc.dram_tensor("usc", [NPC, 64], F32, kind="ExternalInput").ap()
    d_dpk = nc.dram_tensor("dpk", [D, NPC], I16, kind="ExternalInput").ap()
    d_dsc = nc.dram_tensor("dsc", [D, NPC // 32], F32, kind="ExternalInput").ap()
    d_wq1 = nc.dram_tensor("wq1", [D, 256], F32, kind="ExternalInput").ap()
    d_wk1 = nc.dram_tensor("wk1", [D, 256], F32, kind="ExternalInput").ap()
    d_wv1 = nc.dram_tensor("wv1", [D, 256], F32, kind="ExternalInput").ap()
    d_bq1 = nc.dram_tensor("bq1", [1, 256], F32, kind="ExternalInput").ap()
    d_bk1 = nc.dram_tensor("bk1", [1, 256], F32, kind="ExternalInput").ap()
    d_bv1 = nc.dram_tensor("bv1", [1, 256], F32, kind="ExternalInput").ap()
    d_wo1 = nc.dram_tensor("wo1", [256, D], F32, kind="ExternalInput").ap()
    d_bo1 = nc.dram_tensor("bo1", [1, D], F32, kind="ExternalInput").ap()
    d_cosS = nc.dram_tensor("cosS", [S, 128], F32, kind="ExternalInput").ap()
    d_sinS = nc.dram_tensor("sinS", [S, 128], F32, kind="ExternalInput").ap()
    d_gw = nc.dram_tensor("gwT", [D, 16], F32, kind="ExternalInput").ap()
    d_w1 = nc.dram_tensor("w1", [2, D, MH], F32, kind="ExternalInput").ap()
    d_w2 = nc.dram_tensor("w2", [2, MH, D], F32, kind="ExternalInput").ap()
    d_sw1 = nc.dram_tensor("sw1", [D, MH], F32, kind="ExternalInput").ap()
    d_sw2 = nc.dram_tensor("sw2", [MH, D], F32, kind="ExternalInput").ap()
    d_esel = nc.dram_tensor("esel", [2, 1, 16], F32, kind="ExternalInput").ap()
    d_out = nc.dram_tensor("o_x4", [TPC, D], F32, kind="ExternalOutput").ap()

    tri_np = np.where(np.arange(128)[:, None] <= np.arange(128)[None, :], 0.0, -1e30).astype(np.float32)
    d_tri = nc.inline_tensor(tri_np, "trimask")
    d_identb = nc.inline_tensor(np.eye(128, dtype=BF), "identb")
    d_identf = nc.inline_tensor(np.eye(128, dtype=np.float32), "identf")
    alt_np = np.tile(np.array([0, 4], dtype=np.int16), 1024).reshape(1, 2048)
    d_alt = nc.inline_tensor(alt_np, "altshift")

    it = lambda nm, sh, dt: nc.dram_tensor(nm, sh, dt, kind="Internal")
    yTl = it("yTl", [D, TPC], BF16)
    yTl32 = it("yTl32", [D, TPC], F32)
    G = [it(f"G{i}", [NCORES * D, TPC], BF16) for i in range(3)]
    G32 = it("G32", [NCORES * D, TPC], F32)
    qtu = it("qtu", [4, 128, S], BF16)
    ktu = it("ktu", [4, 128, S], BF16)
    vu = it("vu", [4, S, 128], BF16)
    zTl = [it(f"zTl{i}", [256, T], BF16) for i in range(2)]
    part = [it(f"part{i}", [T, D], F32) for i in range(4)]
    rs = [it(f"rs{i}", [TPC, D], F32) for i in range(4)]
    xsc = [it(f"xsc{i}", [TPC, D], F32) for i in range(3)]
    interT = it("interT", [NPC, T], BF16)
    e0p = it("e0p", [T, D], BF16)
    shl = it("shl", [TPC, D], F32)

    NT = TPC // 128

    _uid = [0]

    def _u():
        _uid[0] += 1
        return f"u{_uid[0]}_"

    with ExitStack() as topstack:
        ctx = Ctx(nc, topstack)
        p = Prog(nc)
        block = topstack.enter_context(nc.Block())

        def ph_ln_ag(x_src, ytl_ap, g_ap, f32mode):
            dt = F32 if f32mode else BF16
            with ExitStack() as st_:
                sb = lambda nm, sh, dtt: st_.enter_context(nc.sbuf_tensor(_u() + nm, sh, dtt))
                xsb = sb("ln_x", [128, D], F32)
                xc = sb("ln_xc", [128, D], F32)
                sq = sb("ln_sq", [128, D], F32)
                stats = sb("ln_st", [128, 4], F32)
                epsc = sb("ln_eps", [128, 1], F32)
                ytile = sb("ln_y", [128, D], dt)
                ident = sb("ln_id", [128, 128], dt)
                stage = sb("ln_stg", [128, 16 * 128], dt)
                psT_ = st_.enter_context(nc.psum_tensor(_u() + "ln_psT", [128, 128], dt))
                li = p.load(ident[:], (d_identf if f32mode else d_identb).ap())
                p.comp("vector", lambda v: v.memset(epsc[:], 1e-5))
                for tt in range(NT):
                    lx = p.load(xsb[:], x_src[tt * 128:(tt + 1) * 128, :])
                    emit_ln(p, xsb[:], ytile[:], xc[:], sq[:], stats[:], epsc[:, 0:1], loads=[lx])
                    for kt in range(16):
                        p.comp("tensor", lambda t, kt=kt: t.transpose(
                            psT_[:], ytile[:, kt * 128:(kt + 1) * 128], ident[:]), loads=[li])
                        p.comp("vector", lambda v, kt=kt: v.tensor_copy(
                            out=stage[:, kt * 128:(kt + 1) * 128], in_=psT_[:]))
                    p.store(_g3(ytl_ap[:, tt * 128:(tt + 1) * 128]), _s3(stage[:], 16))
            p.comp("gpsimd", lambda g_: g_.collective_compute(
                "AllGather", ALU.bypass, replica_groups=RG,
                ins=[ytl_ap.opt()], outs=[g_ap.opt()]))

        def ph_qkv_units0(g_ap, wq_ap, wk_ap, wv_ap, scaleq):
            with ExitStack() as st_:
                sb = lambda nm, sh, dtt: st_.enter_context(nc.sbuf_tensor(_u() + nm, sh, dtt))
                wsb = sb("qk_w", [128, 6 * 16 * 128], BF16)
                rhs = sb("qk_rhs", [128, 16 * 512], BF16)
                stg = sb("qk_stg", [128, 512], BF16)
                vstg = sb("qk_vstg", [128, 512], BF16)
                identb_ = sb("qk_id", [128, 128], BF16)
                pss = [st_.enter_context(nc.psum_tensor(_u() + f"qk_ps{i}", [128, 512], F32)) for i in range(6)]
                psT_ = st_.enter_context(nc.psum_tensor(_u() + "qk_psT", [128, 128], BF16))
                li = p.load(identb_[:], d_identb.ap())
                lw = []
                for wi, wap in enumerate((wq_ap, wk_ap, wv_ap)):
                    for hl in range(2):
                        lw.append(p.load(
                            _s3(wsb[:, (wi * 2 + hl) * 16 * 128:((wi * 2 + hl) * 16 + 16) * 128], 16),
                            _g3(wap[:, hl * 128:(hl + 1) * 128]),
                            queue="gpsimd"))
                for r in range(NCORES):
                    b = r // 4
                    for tc in range(2):
                        bcol = (r % 4) * TPC + tc * 512
                        lr = p.load(_s3(rhs[:], 16), _g3(g_ap[r * D:(r + 1) * D, tc * 512:(tc + 1) * 512]))
                        p.comp("tensor", [lambda t, slot=slot, kt=kt: t.matmul(
                            pss[slot][:], wsb[:, (slot * 16 + kt) * 128:(slot * 16 + kt + 1) * 128],
                            rhs[:, kt * 512:(kt + 1) * 512], start=(kt == 0), stop=(kt == 15))
                            for kt in range(16) for slot in range(6)],
                            loads=[lr] + lw)
                        for wi, dst in ((0, qtu), (1, ktu)):
                            for hl in range(2):
                                u = b * 2 + hl
                                slot = wi * 2 + hl
                                if wi == 0:
                                    p.comp("scalar", lambda s, slot=slot: s.activation(
                                        out=stg[:], in_=pss[slot][:], func=AF.Identity, scale=scaleq))
                                else:
                                    p.comp("vector", lambda v, slot=slot: v.tensor_copy(out=stg[:], in_=pss[slot][:]))
                                p.store(dst.ap()[u][:, bcol:bcol + 512], stg[:])
                        for hl in range(2):
                            u = b * 2 + hl
                            slot = 4 + hl
                            p.comp("vector", lambda v, slot=slot: v.tensor_copy(out=vstg[:], in_=pss[slot][:]))
                            for i in range(4):
                                p.comp("tensor", lambda t, i=i: t.transpose(
                                    psT_[:], vstg[:, i * 128:(i + 1) * 128], identb_[:]), loads=[li])
                                p.comp("vector", lambda v: v.tensor_copy(out=stg[:, 0:128], in_=psT_[:]))
                                p.store(vu.ap()[u][bcol + i * 128: bcol + (i + 1) * 128, :], stg[:, 0:128])

        def ph_qkv_units1(g_ap):
            with ExitStack() as st_:
                sb = lambda nm, sh, dtt: st_.enter_context(nc.sbuf_tensor(_u() + nm, sh, dtt))
                wsb = sb("q1_w", [128, 4 * 16 * 128], BF16)
                wvsb = sb("q1_wv", [128, 2 * 16 * 128], BF16)
                ltile = sb("q1_l", [128, 16 * 128], BF16)
                bbc = sb("q1_bbc", [128, 4 * 128], F32)
                bvcol = sb("q1_bvc", [128, 2], F32)
                costl = sb("q1_cos", [128, 128], F32)
                sintl = sb("q1_sin", [128, 128], F32)
                f32a = sb("q1_a", [128, 128], F32)
                ra = sb("q1_ra", [128, 128], F32)
                rb = sb("q1_rb", [128, 128], F32)
                rop = sb("q1_ro", [128, 128], F32)
                identf_ = sb("q1_idf", [128, 128], F32)
                identb_ = sb("q1_idb", [128, 128], BF16)
                stg = sb("q1_stg", [128, 512], BF16)
                vstg = sb("q1_vstg", [128, 512], BF16)
                rhs = sb("q1_rhs", [128, 16 * 512], BF16)
                psQK = st_.enter_context(nc.psum_tensor(_u() + "q1_psqk", [128, 512], F32))
                psTf = st_.enter_context(nc.psum_tensor(_u() + "q1_psTf", [128, 128], F32))
                psTb = st_.enter_context(nc.psum_tensor(_u() + "q1_psTb", [128, 128], BF16))
                psV = [st_.enter_context(nc.psum_tensor(_u() + f"q1_psv{i}", [128, 512], F32)) for i in range(2)]
                lif = p.load(identf_[:], d_identf.ap())
                lib = p.load(identb_[:], d_identb.ap())
                lw = []
                for wi, wap in enumerate((d_wq1, d_wk1)):
                    for hl in range(2):
                        lw.append(p.load(
                            _s3(wsb[:, (wi * 2 + hl) * 16 * 128:((wi * 2 + hl) * 16 + 16) * 128], 16),
                            _g3(wap[:, hl * 128:(hl + 1) * 128]),
                            queue="gpsimd"))
                for hl in range(2):
                    lw.append(p.load(
                        _s3(wvsb[:, hl * 16 * 128:(hl * 16 + 16) * 128], 16),
                        _g3(d_wv1[:, hl * 128:(hl + 1) * 128]),
                        queue="gpsimd"))
                lb = []
                for wi, bap in enumerate((d_bq1, d_bk1)):
                    for hl in range(2):
                        lb.append(p.load(bbc[:, (wi * 2 + hl) * 128:(wi * 2 + hl + 1) * 128],
                                         bap[:, hl * 128:(hl + 1) * 128].broadcast_to([128, 128])))
                lbv = [p.load(bvcol[:, hl:hl + 1],
                              d_bv1[:, hl * 128:(hl + 1) * 128].rearrange("o p -> p o"))
                       for hl in range(2)]
                for b in range(B):
                    for pt in range(32):
                        r = b * 4 + pt // 8
                        coff = (pt % 8) * 128
                        lts = p.load(_s3(ltile[:], 16), _g3(g_ap[r * D:(r + 1) * D, coff:coff + 128]))
                        p.comp("tensor", [lambda t, slot=slot, kt=kt: t.matmul(
                            psQK[:, slot * 128:(slot + 1) * 128], ltile[:, kt * 128:(kt + 1) * 128],
                            wsb[:, (slot * 16 + kt) * 128:(slot * 16 + kt + 1) * 128],
                            start=(kt == 0), stop=(kt == 15)) for kt in range(16) for slot in range(4)],
                            loads=[lts] + (lw + lb if (pt == 0 and b == 0) else []))
                        lc = p.load(costl[:], d_cosS[pt * 128:(pt + 1) * 128, :])
                        ls = p.load(sintl[:], d_sinS[pt * 128:(pt + 1) * 128, :])
                        for wi in range(2):
                            for hl in range(2):
                                slot = wi * 2 + hl
                                u = b * 2 + hl
                                p.comp("vector", lambda v, slot=slot: v.tensor_tensor(
                                    out=f32a[:], in0=psQK[:, slot * 128:(slot + 1) * 128],
                                    in1=bbc[:, slot * 128:(slot + 1) * 128], op=ALU.add))
                                p.comp("vector", lambda v: v.tensor_tensor(out=ra[:], in0=f32a[:], in1=costl[:], op=ALU.mult), loads=[lc])
                                p.comp("vector", lambda v: v.tensor_tensor(out=rb[:], in0=f32a[:], in1=sintl[:], op=ALU.mult), loads=[ls])
                                ev = lambda x: x.rearrange("p (f two) -> p f two", two=2)[:, :, 0:1]
                                od = lambda x: x.rearrange("p (f two) -> p f two", two=2)[:, :, 1:2]
                                p.comp("vector", lambda v: v.tensor_tensor(out=ev(rop[:]), in0=ev(ra[:]), in1=od(rb[:]), op=ALU.subtract))
                                p.comp("vector", lambda v: v.tensor_tensor(out=od(rop[:]), in0=od(ra[:]), in1=ev(rb[:]), op=ALU.add))
                                p.comp("tensor", lambda t: t.transpose(psTf[:], rop[:], identf_[:]), loads=[lif])
                                if wi == 0:
                                    p.comp("scalar", lambda s: s.activation(
                                        out=stg[:, 0:128], in_=psTf[:], func=AF.Identity, scale=1.0 / math.sqrt(DH)))
                                else:
                                    p.comp("vector", lambda v: v.tensor_copy(out=stg[:, 0:128], in_=psTf[:]))
                                dst = qtu if wi == 0 else ktu
                                p.store(dst.ap()[u][:, b * 0 + pt * 128: pt * 128 + 128], stg[:, 0:128])
                for r in range(NCORES):
                    b = r // 4
                    for tc in range(2):
                        bcol = (r % 4) * TPC + tc * 512
                        lr = p.load(_s3(rhs[:], 16), _g3(g_ap[r * D:(r + 1) * D, tc * 512:(tc + 1) * 512]))
                        p.comp("tensor", [lambda t, hl=hl, kt=kt: t.matmul(
                            psV[hl][:], wvsb[:, (hl * 16 + kt) * 128:(hl * 16 + kt + 1) * 128],
                            rhs[:, kt * 512:(kt + 1) * 512], start=(kt == 0), stop=(kt == 15))
                            for kt in range(16) for hl in range(2)], loads=[lr])
                        for hl in range(2):
                            u = b * 2 + hl
                            p.comp("vector", lambda v, hl=hl: v.tensor_scalar(
                                out=vstg[:], in0=psV[hl][:], scalar1=bvcol[:, hl:hl + 1],
                                scalar2=None, op0=ALU.add), loads=[lbv[hl]])
                            for i in range(4):
                                p.comp("tensor", lambda t, i=i: t.transpose(
                                    psTb[:], vstg[:, i * 128:(i + 1) * 128], identb_[:]), loads=[lib])
                                p.comp("vector", lambda v: v.tensor_copy(out=stg[:, 0:128], in_=psTb[:]))
                                p.store(vu.ap()[u][bcol + i * 128: bcol + (i + 1) * 128, :], stg[:, 0:128])

        def ph_attn(layer, ztl_ap):
            alibi = (layer == 0)
            NB = S // 128
            with ExitStack() as st_:
                sb = lambda nm, sh, dtt: st_.enter_context(nc.sbuf_tensor(_u() + nm, sh, dtt))
                qTb = sb("at_q", [128, S], BF16)
                kTb = sb("at_k", [128, S], BF16)
                vsb = sb("at_v", [128, NB * 132], BF16)
                acol = sb("at_ac", [128, 32], F32)
                sht = sb("at_sht", [128, 512], F32)
                tri = sb("at_tri", [128, 128], F32)
                Ssb = sb("at_S", [128, 512], F32)
                PT = sb("at_PT", [128, 512], BF16)
                recip = sb("at_rc", [128, 1], F32)
                mcol = sb("at_mc", [128, 1], F32)
                zsb = sb("at_z", [128, 128], F32)
                zbt = sb("at_zb", [128, 128], BF16)
                identf_ = sb("at_idf", [128, 128], F32)
                psS = st_.enter_context(nc.psum_tensor(_u() + "at_psS", [128, 512], F32))
                psTf = st_.enter_context(nc.psum_tensor(_u() + "at_psTf", [128, 128], F32))
                zps = [st_.enter_context(nc.psum_tensor(_u() + f"at_zps{i}", [128, 132], F32)) for i in range(4)]
                lt = p.load(tri[:], d_tri.ap())
                lif = p.load(identf_[:], d_identf.ap())
                p.comp("vector", lambda v: v.memset(mcol[:], -EXP_M))
                p.comp("vector", lambda v: v.memset(
                    vsb[:].rearrange("p (b w) -> p b w", w=132)[:, :, 128:129], 1.0))
                for u in range(4):
                    b, hl = u // 2, u % 2
                    lq = p.load(qTb[:], qtu.ap()[u], queue="sync")
                    lk = p.load(kTb[:], ktu.ap()[u], queue="sync")
                    lvs = []
                    for g_ in range(4):
                        lvs.append(p.load(
                            vsb[:].rearrange("p (b w) -> p b w", w=132)[:, g_ * 8:(g_ + 1) * 8, 0:128],
                            vu.ap()[u][g_ * 1024:(g_ + 1) * 1024, :].rearrange("(b p) w -> p b w", p=128),
                            queue="gpsimd"))
                    la = []
                    if alibi:
                        la = [p.load(acol[:], d_acol[u], queue="sync")]
                    for r in range(8):
                        lrun = []
                        if alibi:
                            lrun.append(p.load(sht[:], d_shift[u][r:r + 1, :].broadcast_to([128, 512])))
                        for kb in range(4 * r + 4):
                            if alibi:
                                p.comp("tensor", lambda t, kb=kb, r=r: t.matmul(
                                    psS[:], kTb[:, kb * 128:(kb + 1) * 128],
                                    qTb[:, r * 512:(r + 1) * 512], start=True, stop=True),
                                    loads=[lq, lk] + lrun)
                                p.comp("vector", lambda v: v.tensor_scalar(
                                    out=Ssb[:], in0=psS[:], scalar1=1024.0, scalar2=-1024.0,
                                    op0=ALU.min, op1=ALU.max))
                                p.comp("vector", lambda v: v.tensor_tensor(out=Ssb[:], in0=Ssb[:], in1=sht[:], op=ALU.add))
                                j = kb - 4 * r
                                if j >= 0:
                                    if j > 0:
                                        p.comp("vector", lambda v, j=j: v.memset(Ssb[:, 0:j * 128], -1e30))
                                    p.comp("vector", lambda v, j=j: v.tensor_tensor(
                                        out=Ssb[:, j * 128:(j + 1) * 128], in0=Ssb[:, j * 128:(j + 1) * 128],
                                        in1=tri[:], op=ALU.add), loads=[lt])
                                p.comp("scalar", lambda s, kb=kb: s.activation(
                                    out=PT[:], in_=Ssb[:], func=AF.Exp, bias=acol[:, kb:kb + 1]), loads=la)
                            else:
                                p.comp("tensor", lambda t, kb=kb, r=r: t.matmul(
                                    psS[:], kTb[:, kb * 128:(kb + 1) * 128],
                                    qTb[:, r * 512:(r + 1) * 512], start=True, stop=True), loads=[lq, lk])
                                j = kb - 4 * r
                                if j >= 0:
                                    p.comp("vector", lambda v, j=j: v.tensor_tensor(
                                        out=psS[:, j * 128:(j + 1) * 128], in0=psS[:, j * 128:(j + 1) * 128],
                                        in1=tri[:], op=ALU.add), loads=[lt])
                                p.comp("scalar", lambda s: s.activation(
                                    out=PT[:], in_=psS[:], func=AF.Exp, bias=mcol[:, 0:1]))
                            fns = []
                            for i in range(4):
                                if kb <= 4 * r + i:
                                    fns.append(lambda t, i=i, kb=kb, r=r: t.matmul(
                                        zps[i][:, 0:129], PT[:, i * 128:(i + 1) * 128],
                                        vsb[:].rearrange("p (b w) -> p b w", w=132)[:, kb, 0:129],
                                        start=(kb == 0), stop=(kb == 4 * r + i)))
                            p.comp("tensor", fns, loads=lvs)
                        for i in range(4):
                            p.comp("vector", lambda v, i=i: v.reciprocal(out=recip[:], in_=zps[i][:, 128:129]))
                            p.comp("vector", lambda v, i=i: v.tensor_scalar(
                                out=zsb[:], in0=zps[i][:, 0:128], scalar1=recip[:, 0:1], scalar2=None, op0=ALU.mult))
                            p.comp("tensor", lambda t: t.transpose(psTf[:], zsb[:], identf_[:]), loads=[lif])
                            p.comp("vector", lambda v: v.tensor_copy(out=zbt[:], in_=psTf[:]))
                            p.store(ztl_ap[hl * 128:(hl + 1) * 128,
                                           b * S + (4 * r + i) * 128: b * S + (4 * r + i + 1) * 128], zbt[:])

        def ph_oproj_partial(ztl_ap, wo_ap, part_ap):
            with ExitStack() as st_:
                sb = lambda nm, sh, dtt: st_.enter_context(nc.sbuf_tensor(_u() + nm, sh, dtt))
                wsb = sb("op_w", [128, 2 * D], BF16)
                zt = sb("op_z", [128, 2 * 128], BF16)
                osb = sb("op_o", [128, 512], F32)
                psA = st_.enter_context(nc.psum_tensor(_u() + "op_ps", [128, 512], F32))
                lw = [p.load(_s3(wsb[:], 2), _g3(wo_ap), queue="gpsimd")]
                for tt in range(T // 128):
                    lz = [p.load(_s3(zt[:], 2), _g3(ztl_ap[:, tt * 128:(tt + 1) * 128]))]
                    for oc in range(4):
                        p.comp("tensor", [lambda t, kt=kt, oc=oc: t.matmul(
                            psA[:], zt[:, kt * 128:(kt + 1) * 128],
                            wsb[:, kt * D + oc * 512: kt * D + (oc + 1) * 512],
                            start=(kt == 0), stop=(kt == 1)) for kt in range(2)], loads=lz + lw)
                        p.comp("vector", lambda v: v.tensor_copy(out=osb[:], in_=psA[:]))
                        p.store(part_ap[tt * 128:(tt + 1) * 128, oc * 512:(oc + 1) * 512], osb[:])

        def ph_rs(part_ap, rs_ap):
            p.comp("gpsimd", lambda g_: g_.collective_compute(
                "ReduceScatter", ALU.add, replica_groups=RG,
                ins=[part_ap.opt()], outs=[rs_ap.opt()]))

        def ph_resid(x_src, add_aps, out_ap, bo_bcast=False):
            with ExitStack() as st_:
                sb = lambda nm, sh, dtt: st_.enter_context(nc.sbuf_tensor(_u() + nm, sh, dtt))
                acc = sb("rd_acc", [128, D], F32)
                pt_ = sb("rd_pt", [128, D], F32)
                bob = sb("rd_bo", [128, D], F32)
                lbo = []
                if bo_bcast:
                    lbo = [p.load(bob[:], d_bo1.broadcast_to([128, D]), queue="sync")]
                for tt in range(NT):
                    lx = p.load(acc[:], x_src[tt * 128:(tt + 1) * 128, :])
                    first = [lx]
                    for (aap, isbf) in add_aps:
                        lp = p.load(pt_[:], aap[tt * 128:(tt + 1) * 128, :],
                                    queue="gpsimd" if isbf else "sync")
                        p.comp("vector", lambda v: v.tensor_tensor(out=acc[:], in0=acc[:], in1=pt_[:], op=ALU.add),
                               loads=[lp] + first)
                        first = []
                    if bo_bcast:
                        p.comp("vector", lambda v: v.tensor_tensor(out=acc[:], in0=acc[:], in1=bob[:], op=ALU.add),
                               loads=lbo)
                    p.store(out_ap[tt * 128:(tt + 1) * 128, :], acc[:])

        def ph_ffn0(g_ap):
            with ExitStack() as st_:
                sb = lambda nm, sh, dtt: st_.enter_context(nc.sbuf_tensor(_u() + nm, sh, dtt))
                dec = {nm: sb("f_" + nm, [128, 2048], I16) for nm in ("pk16", "T16", "n16", "a1", "a2", "z16", "w16", "alt")}
                val = sb("f_val", [128, 2048], BF16)
                scsb = sb("f_sc", [128, 64], F32)
                scb = sb("f_scb", [128, 64], BF16)
                ident = sb("f_id", [128, 128], BF16)
                wgT = sb("f_wg", [128, 16 * 2048], BF16)
                wuT = sb("f_wu", [128, 16 * 2048], BF16)
                ysb = sb("f_y", [128, 16 * 512], BF16)
                g32 = sb("f_g32", [128, 512], F32)
                t3 = sb("f_t3", [128, 512], F32)
                t4 = sb("f_t4", [128, 512], F32)
                sgm = sb("f_sgm", [128, 512], F32)
                w32 = sb("f_w32", [128, 512], F32)
                inter = sb("f_int", [128, 512], BF16)
                osb = sb("f_o", [128, 512], F32)
                psG = st_.enter_context(nc.psum_tensor(_u() + "f_psG", [128, 512], F32))
                psU = st_.enter_context(nc.psum_tensor(_u() + "f_psU", [128, 512], F32))
                psT_ = st_.enter_context(nc.psum_tensor(_u() + "f_psT", [128, 128], BF16))
                li = p.load(ident[:], d_identb.ap())
                lalt = p.load(dec["alt"][:], d_alt.ap().broadcast_to([128, 2048]))

                def decode_into(dst, d_pk, d_sc, ntile):
                    lpk = p.load(dec["pk16"][:], d_pk[ntile * 128:(ntile + 1) * 128, :], queue="gpsimd")
                    lsc = p.load(scsb[:], d_sc[ntile * 128:(ntile + 1) * 128, :], queue="gpsimd")
                    emit_decode(p, dec, [lpk, lalt], val[:])
                    p.comp("vector", lambda v: v.tensor_copy(out=scb[:], in_=scsb[:]), loads=[lsc])
                    vv = val[:].rearrange("p (c q) -> p c q", q=32)
                    sbb = scb[:].unsqueeze(2).broadcast_to([128, 64, 32])
                    p.comp("vector", lambda v: v.tensor_tensor(out=vv, in0=vv, in1=sbb, op=ALU.mult))
                    for ht in range(16):
                        p.comp("tensor", lambda t, ht=ht: t.transpose(
                            psT_[:], val[:, ht * 128:(ht + 1) * 128], ident[:]), loads=[li])
                        p.comp("vector", lambda v, ht=ht, ntile=ntile: v.tensor_copy(
                            out=dst[:, ht * 2048 + ntile * 128: ht * 2048 + (ntile + 1) * 128], in_=psT_[:]))

                for ntile in range(16):
                    decode_into(wgT, d_gpk, d_gsc, ntile)
                    decode_into(wuT, d_upk, d_usc, ntile)
                for tc in range(16):
                    r, sub = tc // 2, tc % 2
                    ly = [p.load(_s3(ysb[:], 16), _g3(g_ap[r * D:(r + 1) * D, sub * 512:(sub + 1) * 512]))]
                    for ntile in range(16):
                        p.comp("tensor", [lambda t, ht=ht, ntile=ntile: t.matmul(
                            psG[:], wgT[:, ht * 2048 + ntile * 128: ht * 2048 + (ntile + 1) * 128],
                            ysb[:, ht * 512:(ht + 1) * 512], start=(ht == 0), stop=(ht == 15)) for ht in range(16)], loads=ly)
                        p.comp("tensor", [lambda t, ht=ht, ntile=ntile: t.matmul(
                            psU[:], wuT[:, ht * 2048 + ntile * 128: ht * 2048 + (ntile + 1) * 128],
                            ysb[:, ht * 512:(ht + 1) * 512], start=(ht == 0), stop=(ht == 15)) for ht in range(16)])
                        p.comp("vector", lambda v: v.tensor_copy(out=g32[:], in_=psG[:]))
                        p.comp("scalar", lambda s: s.activation(out=t3[:], in_=g32[:], func=AF.Square))
                        p.comp("vector", lambda v: v.tensor_scalar(out=t3[:], in0=t3[:], scalar1=GELU_C, scalar2=1.0, op0=ALU.mult, op1=ALU.add))
                        p.comp("vector", lambda v: v.tensor_tensor(out=t4[:], in0=g32[:], in1=t3[:], op=ALU.mult))
                        p.comp("scalar", lambda s: s.activation(out=sgm[:], in_=t4[:], func=AF.Sigmoid, scale=GELU_A))
                        p.comp("vector", lambda v: v.tensor_tensor(out=w32[:], in0=g32[:], in1=sgm[:], op=ALU.mult))
                        p.comp("vector", lambda v: v.scalar_tensor_tensor(out=inter[:], in0=w32[:], scalar=1.0, in1=psU[:], op0=ALU.mult, op1=ALU.mult))
                        p.store(interT.ap()[ntile * 128:(ntile + 1) * 128, tc * 512:(tc + 1) * 512], inter[:])
                for ot in range(16):
                    lpk = p.load(dec["pk16"][:], d_dpk[ot * 128:(ot + 1) * 128, :], queue="gpsimd")
                    lsc = p.load(scsb[:], d_dsc[ot * 128:(ot + 1) * 128, :], queue="gpsimd")
                    emit_decode(p, dec, [lpk, lalt], val[:])
                    p.comp("vector", lambda v: v.tensor_copy(out=scb[:], in_=scsb[:]), loads=[lsc])
                    vv = val[:].rearrange("p (c q) -> p c q", q=32)
                    sbb = scb[:].unsqueeze(2).broadcast_to([128, 64, 32])
                    p.comp("vector", lambda v: v.tensor_tensor(out=vv, in0=vv, in1=sbb, op=ALU.mult))
                    for nt in range(16):
                        p.comp("tensor", lambda t, nt=nt: t.transpose(
                            psT_[:], val[:, nt * 128:(nt + 1) * 128], ident[:]), loads=[li])
                        p.comp("vector", lambda v, nt=nt, ot=ot: v.tensor_copy(
                            out=wgT[:, nt * 2048 + ot * 128: nt * 2048 + (ot + 1) * 128], in_=psT_[:]))
                for tc in range(16):
                    lis = [p.load(_s3(ysb[:], 16), _g3(interT.ap()[:, tc * 512:(tc + 1) * 512]))]
                    for tsub in range(4):
                        for oc in range(4):
                            p.comp("tensor", [lambda t, nt=nt, tsub=tsub, oc=oc: t.matmul(
                                psG[:], ysb[:, nt * 512 + tsub * 128: nt * 512 + (tsub + 1) * 128],
                                wgT[:, nt * 2048 + oc * 512: nt * 2048 + (oc + 1) * 512],
                                start=(nt == 0), stop=(nt == 15)) for nt in range(16)], loads=lis)
                            p.comp("vector", lambda v: v.tensor_copy(out=osb[:], in_=psG[:]))
                            p.store(part[1].ap()[tc * 512 + tsub * 128: tc * 512 + (tsub + 1) * 128,
                                                 oc * 512:(oc + 1) * 512], osb[:])

        def ph_moe():
            NH = MH // 128
            with ExitStack() as st_:
                sb = lambda nm, sh, dtt: st_.enter_context(nc.sbuf_tensor(_u() + nm, sh, dtt))
                ysb = sb("m_y", [128, 16 * 512], BF16)
                yf = sb("m_yf", [128, 16 * 128], F32)
                gw = sb("m_gw", [128, 16 * 16], F32)
                comb = sb("m_cb", [128, 64 * 2], F32)
                combt = sb("m_cbt", [128, 16], F32)
                eselt = sb("m_es", [128, 2 * 16], F32)
                ssb = sb("m_s", [128, 16], F32)
                r4 = sb("m_r4", [128, 4], F32)
                r4b = sb("m_r4b", [128, 4], F32)
                r16 = sb("m_r16", [128, 16], F32)
                r1 = sb("m_r1", [128, 8], F32)
                w1sb = sb("m_w1", [128, 16 * MH], BF16)
                w2sb = sb("m_w2", [128, NH * D], BF16)
                asb = sb("m_a", [128, NH * 512], BF16)
                accb = sb("m_accb", [128, 512], BF16)
                accf = sb("m_accf", [128, 512], F32)
                e0t = sb("m_e0t", [128, 512], F32)
                psR = st_.enter_context(nc.psum_tensor(_u() + "m_psR", [128, 16], F32))
                psA = st_.enter_context(nc.psum_tensor(_u() + "m_psA", [128, 512], F32))
                psO = st_.enter_context(nc.psum_tensor(_u() + "m_psO", [128, 512], F32))
                lsel = [p.load(eselt[:, e * 16:(e + 1) * 16], d_esel[e].broadcast_to([128, 16])) for e in range(2)]
                lgw = [p.load(gw[:, kt * 16:(kt + 1) * 16], d_gw[kt * 128:(kt + 1) * 128, :]) for kt in range(16)]
                for tt in range(64):
                    r, coff = tt // 8, (tt % 8) * 128
                    lyt = p.load(_s3(yf[:], 16), _g3(G32.ap()[r * D:(r + 1) * D, coff:coff + 128]))
                    p.comp("tensor", [lambda t, kt=kt: t.matmul(
                        psR[:], yf[:, kt * 128:(kt + 1) * 128], gw[:, kt * 16:(kt + 1) * 16],
                        start=(kt == 0), stop=(kt == 15)) for kt in range(16)],
                        loads=[lyt] + (lgw if tt == 0 else []))
                    p.comp("scalar", lambda s: s.activation(out=ssb[:], in_=psR[:], func=AF.Sigmoid))
                    sg = ssb[:].rearrange("p (g q) -> p g q", q=4)
                    p.comp("vector", lambda v: v.tensor_reduce(out=r4[:], in_=sg, axis=AX.X, op=ALU.max))
                    bc4 = r4[:].unsqueeze(2).broadcast_to([128, 4, 4])
                    p.comp("vector", lambda v: v.tensor_tensor(out=r16[:].rearrange("p (g q) -> p g q", q=4), in0=sg, in1=bc4, op=ALU.is_equal))
                    p.comp("vector", lambda v: v.scalar_tensor_tensor(out=r16[:], in0=r16[:], scalar=-1e30, in1=ssb[:], op0=ALU.mult, op1=ALU.add))
                    p.comp("vector", lambda v: v.tensor_reduce(out=r4b[:], in_=r16[:].rearrange("p (g q) -> p g q", q=4), axis=AX.X, op=ALU.max))
                    p.comp("vector", lambda v: v.tensor_tensor(out=r4[:], in0=r4[:], in1=r4b[:], op=ALU.add))
                    p.comp("vector", lambda v: v.tensor_reduce(out=r1[:, 0:1], in_=r4[:], axis=AX.X, op=ALU.max))
                    p.comp("vector", lambda v: v.tensor_scalar(out=r4b[:], in0=r4[:], scalar1=r1[:, 0:1], scalar2=None, op0=ALU.is_equal))
                    p.comp("vector", lambda v: v.scalar_tensor_tensor(out=r4b[:], in0=r4b[:], scalar=-1e30, in1=r4[:], op0=ALU.mult, op1=ALU.add))
                    p.comp("vector", lambda v: v.tensor_reduce(out=r1[:, 1:2], in_=r4b[:], axis=AX.X, op=ALU.max))
                    p.comp("vector", lambda v: v.tensor_scalar(out=r4b[:], in0=r4[:], scalar1=r1[:, 1:2], scalar2=None, op0=ALU.is_ge))
                    bc4b = r4b[:].unsqueeze(2).broadcast_to([128, 4, 4])
                    p.comp("vector", lambda v: v.tensor_tensor(out=r16[:].rearrange("p (g q) -> p g q", q=4), in0=sg, in1=bc4b, op=ALU.mult))
                    p.comp("vector", lambda v: v.tensor_reduce(out=r1[:, 2:3], in_=r16[:], axis=AX.X, op=ALU.max))
                    p.comp("vector", lambda v: v.tensor_scalar(out=ssb[:], in0=r16[:], scalar1=r1[:, 2:3], scalar2=None, op0=ALU.is_equal))
                    p.comp("vector", lambda v: v.scalar_tensor_tensor(out=ssb[:], in0=ssb[:], scalar=-1e30, in1=r16[:], op0=ALU.mult, op1=ALU.add))
                    p.comp("vector", lambda v: v.tensor_reduce(out=r1[:, 3:4], in_=ssb[:], axis=AX.X, op=ALU.max))
                    p.comp("vector", lambda v: v.tensor_tensor(out=r1[:, 4:5], in0=r1[:, 2:3], in1=r1[:, 3:4], op=ALU.add))
                    p.comp("vector", lambda v: v.tensor_scalar(out=r1[:, 4:5], in0=r1[:, 4:5], scalar1=1e-20, scalar2=None, op0=ALU.add))
                    p.comp("vector", lambda v: v.reciprocal(out=r1[:, 5:6], in_=r1[:, 4:5]))
                    p.comp("vector", lambda v: v.tensor_scalar(out=ssb[:], in0=r16[:], scalar1=r1[:, 3:4], scalar2=None, op0=ALU.is_ge))
                    p.comp("vector", lambda v: v.tensor_tensor(out=ssb[:], in0=ssb[:], in1=r16[:], op=ALU.mult))
                    p.comp("vector", lambda v: v.tensor_scalar(out=ssb[:], in0=ssb[:], scalar1=r1[:, 5:6], scalar2=None, op0=ALU.mult))
                    for e in range(2):
                        p.comp("vector", lambda v, e=e: v.tensor_tensor(out=combt[:], in0=ssb[:], in1=eselt[:, e * 16:(e + 1) * 16], op=ALU.mult), loads=[lsel[e]])
                        p.comp("vector", lambda v, e=e, tt=tt: v.tensor_reduce(out=comb[:, tt * 2 + e: tt * 2 + e + 1], in_=combt[:], axis=AX.X, op=ALU.add))

                def expert_pass(d_w1src, d_w2src, n_tc, ysrc, e, mode):
                    lw1 = [p.load(_s3(w1sb[:], 16), _g3(d_w1src), queue="gpsimd")]
                    lw2 = [p.load(_s3(w2sb[:], NH), _g3(d_w2src), queue="gpsimd")]
                    for tc in range(n_tc):
                        if ysrc == "G32":
                            r, sub = tc // 2, tc % 2
                            ly = [p.load(_s3(ysb[:], 16), _g3(G32.ap()[r * D:(r + 1) * D, sub * 512:(sub + 1) * 512]), queue="gpsimd")]
                        else:
                            ly = [p.load(_s3(ysb[:], 16), _g3(yTl32.ap()[:, tc * 512:(tc + 1) * 512]), queue="gpsimd")]
                        for ht in range(NH):
                            p.comp("tensor", [lambda t, kt=kt, ht=ht: t.matmul(
                                psA[:], w1sb[:, kt * MH + ht * 128: kt * MH + (ht + 1) * 128],
                                ysb[:, kt * 512:(kt + 1) * 512], start=(kt == 0), stop=(kt == 15))
                                for kt in range(16)], loads=ly + lw1)
                            p.comp("scalar", lambda s, ht=ht: s.activation(
                                out=asb[:, ht * 512:(ht + 1) * 512], in_=psA[:], func=AF.Silu))
                        for tsub in range(4):
                            for oc in range(4):
                                p.comp("tensor", [lambda t, ht=ht, tsub=tsub, oc=oc: t.matmul(
                                    psO[:], asb[:, ht * 512 + tsub * 128: ht * 512 + (tsub + 1) * 128],
                                    w2sb[:, ht * D + oc * 512: ht * D + (oc + 1) * 512],
                                    start=(ht == 0), stop=(ht == NH - 1)) for ht in range(NH)], loads=lw2)
                                row0 = tc * 512 + tsub * 128
                                if mode == 0:
                                    cap = comb[:, (tc * 4 + tsub) * 2 + 0: (tc * 4 + tsub) * 2 + 1]
                                    p.comp("vector", lambda v, cap=cap: v.tensor_scalar(
                                        out=accb[:], in0=psO[:], scalar1=cap, scalar2=None, op0=ALU.mult))
                                    p.store(e0p.ap()[row0:row0 + 128, oc * 512:(oc + 1) * 512], accb[:])
                                elif mode == 1:
                                    cap = comb[:, (tc * 4 + tsub) * 2 + 1: (tc * 4 + tsub) * 2 + 2]
                                    le0 = p.load(e0t[:], e0p.ap()[row0:row0 + 128, oc * 512:(oc + 1) * 512],
                                                 queue="gpsimd")
                                    p.comp("vector", lambda v, cap=cap: v.scalar_tensor_tensor(
                                        out=accf[:], in0=psO[:], scalar=cap, in1=e0t[:],
                                        op0=ALU.mult, op1=ALU.add), loads=[le0])
                                    p.store(part[3].ap()[row0:row0 + 128, oc * 512:(oc + 1) * 512], accf[:])
                                else:
                                    p.comp("vector", lambda v: v.tensor_copy(out=accf[:], in_=psO[:]))
                                    p.store(shl.ap()[row0:row0 + 128, oc * 512:(oc + 1) * 512], accf[:])

                expert_pass(d_w1[0], d_w2[0], 16, "G32", 0, 0)
                expert_pass(d_w1[1], d_w2[1], 16, "G32", 1, 1)
                expert_pass(d_sw1, d_sw2, TPC // 512, "local", 0, 2)

        cur_x = d_x0
        if upto >= 0:
            ph_ln_ag(d_x0, yTl.ap(), G[0].ap(), False)
        if upto >= 1:
            ph_qkv_units0(G[0].ap(), d_wq0, d_wk0, d_wv0, 1.0 / math.sqrt(DH))
            ph_attn(0, zTl[0].ap())
        if upto >= 2:
            ph_oproj_partial(zTl[0].ap(), d_wo0, part[0].ap())
            ph_rs(part[0].ap(), rs[0].ap())
            ph_resid(d_x0, [(rs[0].ap(), False)], xsc[0].ap())
            cur_x = xsc[0].ap()
        if upto >= 3:
            ph_ln_ag(xsc[0].ap(), yTl.ap(), G[1].ap(), False)
        if upto >= 4:
            ph_ffn0(G[1].ap())
            ph_rs(part[1].ap(), rs[1].ap())
            ph_resid(xsc[0].ap(), [(rs[1].ap(), False)], xsc[1].ap())
            cur_x = xsc[1].ap()
        if upto >= 5:
            ph_ln_ag(xsc[1].ap(), yTl.ap(), G[2].ap(), False)
        if upto >= 6:
            ph_qkv_units1(G[2].ap())
            ph_attn(1, zTl[1].ap())
        if upto >= 7:
            ph_oproj_partial(zTl[1].ap(), d_wo1, part[2].ap())
            ph_rs(part[2].ap(), rs[2].ap())
            ph_resid(xsc[1].ap(), [(rs[2].ap(), False)], xsc[2].ap(), bo_bcast=True)
            cur_x = xsc[2].ap()
        if upto >= 8:
            ph_ln_ag(xsc[2].ap(), yTl32.ap(), G32.ap(), True)
        if upto >= 9:
            ph_moe()
            ph_rs(part[3].ap(), rs[3].ap())
            ph_resid(xsc[2].ap(), [(rs[3].ap(), False), (shl.ap(), False)], d_out)
        if upto < 9:
            with ExitStack() as st_:
                cpy = st_.enter_context(nc.sbuf_tensor(_u() + "dbgcpy", [128, D], F32))
                cpy2 = st_.enter_context(nc.sbuf_tensor(_u() + "dbgcpy2", [128, D], F32))
                for tt in range(NT):
                    lx = p.load(cpy[:], cur_x[tt * 128:(tt + 1) * 128, :])
                    p.comp("vector", lambda v: v.tensor_copy(out=cpy2[:], in_=cpy[:]), loads=[lx])
                    p.store(d_out[tt * 128:(tt + 1) * 128, :], cpy2[:])

        p.build(block, ctx.sems)
    return nc


def fused_in_maps(inputs):
    xf = np.ascontiguousarray(np.asarray(inputs["L0_attn_resid_pre"], dtype=np.float32).reshape(T, D))
    Wq0T, Wk0T, Wv0T = (_t(inputs["L0_attn_Wq"]), _t(inputs["L0_attn_Wk"]), _t(inputs["L0_attn_Wv"]))
    Wo0T = _t(inputs["L0_attn_Wo"])
    Wq1T, Wk1T, Wv1T = (_t(inputs["L1_attn_Wq"]), _t(inputs["L1_attn_Wk"]), _t(inputs["L1_attn_Wv"]))
    Wo1T = _t(inputs["L1_attn_Wo"])
    bq1 = np.asarray(inputs["L1_attn_bq"], dtype=np.float32)
    bk1 = np.asarray(inputs["L1_attn_bk"], dtype=np.float32)
    bv1 = np.asarray(inputs["L1_attn_bv"], dtype=np.float32)
    bo1 = np.asarray(inputs["L1_attn_bo"], dtype=np.float32).reshape(1, D)
    gpk = np.asarray(inputs["L0_ffn_gate_W_packed"], dtype=np.int32)
    upk = np.asarray(inputs["L0_ffn_up_W_packed"], dtype=np.int32)
    dpk = np.asarray(inputs["L0_ffn_down_W_packed"], dtype=np.int32)
    gsc = np.asarray(inputs["L0_ffn_gate_scales"], dtype=np.float32)
    usc = np.asarray(inputs["L0_ffn_up_scales"], dtype=np.float32)
    dsc = np.asarray(inputs["L0_ffn_down_scales"], dtype=np.float32)
    gwT = _t(inputs["L1_ffn_gate_w"])
    w1m = np.asarray(inputs["L1_ffn_w1"], dtype=np.float32)
    w2m = np.asarray(inputs["L1_ffn_w2"], dtype=np.float32)
    sw1 = np.ascontiguousarray(np.asarray(inputs["L1_ffn_sw1"], dtype=np.float32)[0])
    sw2 = np.ascontiguousarray(np.asarray(inputs["L1_ffn_sw2"], dtype=np.float32)[0])
    half = DH // 2
    inv = 1.0 / (10000.0 ** (np.arange(half, dtype=np.float32) / half))
    fr = np.arange(S, dtype=np.float32)[:, None] * inv[None, :]
    cosS = np.ascontiguousarray(np.repeat(np.cos(fr), 2, axis=1).astype(np.float32))
    sinS = np.ascontiguousarray(np.repeat(np.sin(fr), 2, axis=1).astype(np.float32))
    pos = np.arange(S, dtype=np.float32)
    maps = []
    for c in range(NCORES):
        n0 = c * NPC
        hc = slice(2 * c * 128, (2 * c + 2) * 128)
        units = [(b, h) for b in range(B) for h in (2 * c, 2 * c + 1)]
        acolu = np.stack([(SLOPES[h] * pos - EXP_M).reshape(32, 128).T.copy() for (b, h) in units])
        shiftu = np.stack([
            np.stack([-SLOPES[h] * (r * 512 + np.arange(512, dtype=np.float32)) for r in range(8)])
            for (b, h) in units])
        esel = np.zeros((2, 1, 16), dtype=np.float32)
        esel[0, 0, 2 * c] = 1.0
        esel[1, 0, 2 * c + 1] = 1.0
        maps.append(dict(
            x0=np.ascontiguousarray(xf[c * TPC:(c + 1) * TPC]),
            wq0=np.ascontiguousarray(Wq0T[:, hc]), wk0=np.ascontiguousarray(Wk0T[:, hc]),
            wv0=np.ascontiguousarray(Wv0T[:, hc]), wo0=np.ascontiguousarray(Wo0T[hc, :]),
            acolu=np.ascontiguousarray(acolu, dtype=np.float32),
            shiftu=np.ascontiguousarray(shiftu, dtype=np.float32),
            ubu=np.ascontiguousarray(1024.0 + shiftu, dtype=np.float32),
            lbu=np.ascontiguousarray(-1024.0 + shiftu, dtype=np.float32),
            gpk=np.ascontiguousarray(gpk[n0:n0 + NPC]).view(np.int16),
            upk=np.ascontiguousarray(upk[n0:n0 + NPC]).view(np.int16),
            gsc=np.ascontiguousarray(gsc[n0:n0 + NPC]),
            usc=np.ascontiguousarray(usc[n0:n0 + NPC]),
            dpk=np.ascontiguousarray(dpk[:, n0 // 2:(n0 + NPC) // 2]).view(np.int16),
            dsc=np.ascontiguousarray(dsc[:, n0 // 32:(n0 + NPC) // 32]),
            wq1=np.ascontiguousarray(Wq1T[:, hc]), wk1=np.ascontiguousarray(Wk1T[:, hc]),
            wv1=np.ascontiguousarray(Wv1T[:, hc]),
            bq1=np.ascontiguousarray(bq1[hc]).reshape(1, 256),
            bk1=np.ascontiguousarray(bk1[hc]).reshape(1, 256),
            bv1=np.ascontiguousarray(bv1[hc]).reshape(1, 256),
            wo1=np.ascontiguousarray(Wo1T[hc, :]), bo1=bo1,
            cosS=cosS, sinS=sinS,
            gwT=gwT,
            w1=np.ascontiguousarray(w1m[2 * c:2 * c + 2]),
            w2=np.ascontiguousarray(w2m[2 * c:2 * c + 2]),
            sw1=sw1, sw2=sw2, esel=esel,
        ))
    return maps


def kernel_fused(upto=9, **inputs):
    nc = build_fused(upto)
    res = launch(nc, fused_in_maps(inputs))
    out = np.concatenate([r["o_x4"] for r in res.results])
    return out.reshape(B, S, D).astype(np.float32)


def kernel(**inputs):
    return kernel_fused(9, **inputs)


# version 04
# speedup vs baseline: 1.6190x; 1.0691x over previous
import numpy as np
import ml_dtypes
import math
from contextlib import ExitStack

import concourse.bass as bass
import concourse.mybir as mybir
from concourse.bass_utils import run_bass_kernel_spmd

F32 = mybir.dt.float32
BF16 = mybir.dt.bfloat16
I16 = mybir.dt.int16
AF = mybir.ActivationFunctionType
ALU = mybir.AluOpType
AX = mybir.AxisListType
BF = ml_dtypes.bfloat16

NCORES = 8
B, S, D, H, DH = 2, 4096, 2048, 16, 128
T = B * S
TPC = T // NCORES
NMLP = 16384
NPC = NMLP // NCORES
MH = 1408
EXP_M = 40.0
GELU_A = 1.5957691216057308
GELU_C = 0.044715

def alibi_slopes(n):
    def pow2(m):
        start = 2 ** (-(2 ** -(math.log2(m) - 3)))
        return [start * start ** i for i in range(m)]
    if math.log2(n).is_integer():
        return pow2(n)
    cc = 2 ** math.floor(math.log2(n))
    return pow2(cc) + alibi_slopes(2 * cc)[0::2][: n - cc]

SLOPES = np.array(alibi_slopes(H), dtype=np.float32)


class Prog:
    ENGS = ("sync", "gpsimd", "vector", "scalar", "tensor")

    def __init__(self, nc):
        self.nc = nc
        self.ops = {e: [] for e in self.ENGS}
        self.nld = {"sync": 0, "gpsimd": 0}
        self.c = 0
        self.nst = 0

    _OWN_INC_OK = ("vector", "scalar", "tensor")
    ELIDE = True
    OWNINC = False

    def _wait(self, eng, ename, sems, sem_key, val, st):
        if not Prog.ELIDE:
            if val > 0:
                eng.wait_ge(sems[sem_key], val)
            return
        if val > st[ename].get(sem_key, 0):
            eng.wait_ge(sems[sem_key], val)
            st[ename][sem_key] = val

    def load(self, out_ap, in_ap, queue="sync"):
        tok = self.nld[queue] + 1
        self.nld[queue] = tok
        cwait = self.c
        stwait = self.nst

        def emit(eng, sems, st):
            self._wait(eng, queue, sems, "sC", cwait, st)
            self._wait(eng, queue, sems, "st", 16 * stwait, st)
            self._wait(eng, queue, sems, "ld_" + queue, 16 * (tok - 1), st)
            eng.dma_start(out=out_ap, in_=in_ap).then_inc(sems["ld_" + queue], 16)
        self.ops[queue].append(emit)
        return (queue, tok)

    def comp(self, ename, fns, loads=()):
        if callable(fns):
            fns = [fns]
        cwait = self.c
        stwait = self.nst
        self.c += 1
        cval = self.c
        loads = list(loads)

        def emit(eng, sems, st):
            self._wait(eng, ename, sems, "sC", cwait, st)
            self._wait(eng, ename, sems, "st", 16 * stwait, st)
            for (q, tok) in loads:
                self._wait(eng, ename, sems, "ld_" + q, 16 * tok, st)
            last = None
            for fn in fns:
                last = fn(eng)
            last.then_inc(sems["sC"], 1)
            if Prog.ELIDE and Prog.OWNINC and ename in self._OWN_INC_OK:
                st[ename]["sC"] = max(st[ename].get("sC", 0), cval)
        self.ops[ename].append(emit)
        return cval

    def store(self, out_ap, in_ap, queue="gpsimd"):
        cwait = self.c
        self.nst += 1
        tok = self.nst

        def emit(eng, sems, st):
            self._wait(eng, queue, sems, "sC", cwait, st)
            self._wait(eng, queue, sems, "st", 16 * (tok - 1), st)
            eng.dma_start(out=out_ap, in_=in_ap).then_inc(sems["st"], 16)
        self.ops[queue].append(emit)

    def build(self, block, sems):
        final_c, final_st = self.c, self.nst
        st = {e: {} for e in self.ENGS}

        def closer(eng, sems2, st2):
            eng.wait_ge(sems2["sC"], final_c)
            eng.wait_ge(sems2["st"], 16 * final_st)
        self.ops["gpsimd"].append(closer)
        for ename in self.ENGS:
            section = getattr(block, ename)
            oplist = self.ops[ename]

            def body(eng, oplist=oplist):
                for op in oplist:
                    op(eng, sems, st)
            section(body)


class Ctx:
    def __init__(self, nc, stack):
        self.nc = nc
        self.stack = stack
        self.sems = {}
        for name in ("ld_sync", "ld_gpsimd", "st", "sC"):
            self.sems[name] = stack.enter_context(nc.semaphore(name))

    def sb(self, name, shape, dtype):
        return self.stack.enter_context(self.nc.sbuf_tensor(name, shape, dtype))

    def ps(self, name, shape, dtype=F32):
        return self.stack.enter_context(self.nc.psum_tensor(name, shape, dtype))


def new_nc():
    return bass.Bass("TRN2", target_bir_lowering=False)


def launch(nc, in_maps, **kw):
    return run_bass_kernel_spmd(nc, in_maps, core_ids=list(range(NCORES)), **kw)


def emit_ln(p, x_tile, y_out, xc, sq, stats, eps_ap, loads=()):
    nm, var, sd, rstd = (stats[:, 0:1], stats[:, 1:2], stats[:, 2:3], stats[:, 3:4])
    p.comp("vector", lambda v: v.tensor_reduce(out=nm, in_=x_tile, axis=AX.X, op=ALU.add, negate=True), loads=loads)
    p.comp("vector", lambda v: v.tensor_scalar(out=nm, in0=nm, scalar1=1.0 / D, scalar2=None, op0=ALU.mult))
    p.comp("vector", lambda v: v.tensor_scalar(out=xc, in0=x_tile, scalar1=nm, scalar2=None, op0=ALU.add))
    p.comp("scalar", lambda s: s.activation(out=sq, in_=xc, func=AF.Square, accum_out=var))
    p.comp("scalar", lambda s: s.activation(out=sd, in_=var, func=AF.Sqrt, bias=eps_ap, scale=1.0 / D))
    p.comp("vector", lambda v: v.reciprocal(out=rstd, in_=sd))
    p.comp("vector", lambda v: v.tensor_scalar(out=y_out, in0=xc, scalar1=rstd, scalar2=None, op0=ALU.mult))


def emit_decode(p, t, pk_load, val_out):
    pk, T16, n16, a1, a2, z16, w16, alt = (t["pk16"], t["T16"], t["n16"],
                                           t["a1"], t["a2"], t["z16"], t["w16"], t["alt"])
    ev = lambda x: x.rearrange("p (f two) -> p f two", two=2)[:, :, 0:1]
    od = lambda x: x.rearrange("p (f two) -> p f two", two=2)[:, :, 1:2]
    p.comp("vector", lambda v: v.tensor_copy(out=T16[:], in_=pk[:]), loads=pk_load)
    p.comp("vector", lambda v: v.tensor_copy(out=od(T16[:]), in_=ev(pk[:])))
    p.comp("vector", lambda v: v.tensor_tensor(out=n16[:], in0=T16[:], in1=alt[:], op=ALU.logical_shift_right))
    p.comp("vector", lambda v: v.tensor_scalar(out=a1[:], in0=n16[:], scalar1=7, scalar2=6, op0=ALU.bitwise_and, op1=ALU.logical_shift_left))
    p.comp("vector", lambda v: v.tensor_scalar(out=a2[:], in0=n16[:], scalar1=8, scalar2=12, op0=ALU.bitwise_and, op1=ALU.logical_shift_left))
    p.comp("vector", lambda v: v.tensor_scalar(out=T16[:], in0=a1[:], scalar1=0x3F00, scalar2=None, op0=ALU.add))
    p.comp("vector", lambda v: v.tensor_tensor(out=T16[:], in0=T16[:], in1=a2[:], op=ALU.bitwise_or))
    p.comp("vector", lambda v: v.tensor_scalar(out=z16[:], in0=n16[:], scalar1=6, scalar2=None, op0=ALU.bitwise_and))
    p.comp("vector", lambda v: v.tensor_scalar(out=z16[:], in0=z16[:], scalar1=0, scalar2=None, op0=ALU.is_equal))
    p.comp("vector", lambda v: v.tensor_scalar(out=a1[:], in0=a2[:], scalar1=0x3F00, scalar2=None, op0=ALU.bitwise_or))
    p.comp("vector", lambda v: v.tensor_scalar(out=w16[:], in0=n16[:], scalar1=1, scalar2=7, op0=ALU.bitwise_and, op1=ALU.logical_shift_left))
    p.comp("vector", lambda v: v.tensor_tensor(out=a1[:], in0=a1[:], in1=w16[:], op=ALU.subtract))
    p.comp("vector", lambda v: v.tensor_tensor(out=z16[:], in0=z16[:], in1=a1[:], op=ALU.mult))
    p.comp("vector", lambda v: v.tensor_tensor(out=val_out, in0=T16[:].bitcast(BF16), in1=z16[:].bitcast(BF16), op=ALU.subtract))


def build_qkv(layer):
    nc = new_nc()
    rope = (layer == 1)
    d_x = nc.dram_tensor("x", [TPC, D], F32, kind="ExternalInput").ap()
    d_w = {n: nc.dram_tensor(n, [D, D], F32, kind="ExternalInput").ap() for n in ("wqT", "wkT", "wvT")}
    if rope:
        d_b = {n: nc.dram_tensor(n, [1, D], F32, kind="ExternalInput").ap() for n in ("bq", "bk", "bv")}
        d_cos = nc.dram_tensor("cos4", [TPC, 512], F32, kind="ExternalInput").ap()
        d_sin = nc.dram_tensor("sin4", [TPC, 512], F32, kind="ExternalInput").ap()
    d_out = {n: nc.dram_tensor("o_" + n, [TPC, D], F32, kind="ExternalOutput").ap() for n in ("q", "k", "v")}
    d_ident = nc.inline_tensor(np.eye(128, dtype=BF), "identbf")

    NT = TPC // 128
    with ExitStack() as stack:
        ctx = Ctx(nc, stack)
        p = Prog(nc)
        xsb = ctx.sb("xsb", [128, D], F32)
        xc = ctx.sb("xc", [128, D], F32)
        sq = ctx.sb("sq", [128, D], F32)
        stats = ctx.sb("stats", [128, 4], F32)
        epsc = ctx.sb("epsc", [128, 1], F32)
        ytile = ctx.sb("ytile", [128, D], BF16)
        yT = ctx.sb("yT", [128, 16 * TPC], BF16)
        ident = ctx.sb("ident", [128, 128], BF16)
        wtile = ctx.sb("wtile", [128, 16 * 512], BF16)
        onesr = ctx.sb("onesr", [1, 128], BF16)
        brow = ctx.sb("brow", [1, D], BF16)
        outf = ctx.sb("outf", [128, 512], F32)
        costile = ctx.sb("costile", [128, 512], F32)
        sintile = ctx.sb("sintile", [128, 512], F32)
        ra = ctx.sb("ra", [128, 512], F32)
        rb = ctx.sb("rb", [128, 512], F32)
        psA = ctx.ps("psA", [128, 512], F32)
        psT = ctx.ps("psT", [128, 128], BF16)
        block = stack.enter_context(nc.Block())

        li = p.load(ident[:], d_ident.ap())
        p.comp("vector", lambda v: v.memset(onesr[:], 1.0))
        p.comp("vector", lambda v: v.memset(epsc[:], 1e-5))

        for tt in range(NT):
            lx = p.load(xsb[:], d_x[tt * 128:(tt + 1) * 128, :])
            emit_ln(p, xsb[:], ytile[:], xc[:], sq[:], stats[:], epsc[:, 0:1], loads=[lx])
            for kt in range(16):
                p.comp("tensor", lambda t, kt=kt: t.transpose(
                    psT[:], ytile[:, kt * 128:(kt + 1) * 128], ident[:]), loads=[li])
                p.comp("vector", lambda v, kt=kt, tt=tt: v.tensor_copy(
                    out=yT[:, kt * TPC + tt * 128: kt * TPC + (tt + 1) * 128], in_=psT[:]))

        for wn, on in (("wqT", "q"), ("wkT", "k"), ("wvT", "v")):
            if rope:
                lb = p.load(brow[:], d_b[{"wqT": "bq", "wkT": "bk", "wvT": "bv"}[wn]], queue="gpsimd")
            for oc in range(4):
                lws = [p.load(wtile[:, kt * 512:(kt + 1) * 512],
                              d_w[wn][kt * 128:(kt + 1) * 128, oc * 512:(oc + 1) * 512],
                              queue="gpsimd") for kt in range(16)]
                for tt in range(NT):
                    fns = [lambda t, kt=kt, tt=tt: t.matmul(
                        psA[:], yT[:, kt * TPC + tt * 128: kt * TPC + (tt + 1) * 128],
                        wtile[:, kt * 512:(kt + 1) * 512],
                        start=(kt == 0), stop=(kt == 15 and not rope)) for kt in range(16)]
                    if rope:
                        fns.append(lambda t, oc=oc: t.matmul(
                            psA[:], onesr[:], brow[:, oc * 512:(oc + 1) * 512],
                            start=False, stop=True))
                        p.comp("tensor", fns, loads=lws + [lb])
                    else:
                        p.comp("tensor", fns, loads=lws)
                    if rope and wn in ("wqT", "wkT"):
                        lc = p.load(costile[:], d_cos[tt * 128:(tt + 1) * 128, :])
                        ls = p.load(sintile[:], d_sin[tt * 128:(tt + 1) * 128, :])
                        p.comp("vector", lambda v: v.tensor_tensor(out=ra[:], in0=psA[:], in1=costile[:], op=ALU.mult), loads=[lc])
                        p.comp("vector", lambda v: v.tensor_tensor(out=rb[:], in0=psA[:], in1=sintile[:], op=ALU.mult), loads=[ls])
                        ev = lambda x: x.rearrange("p (f two) -> p f two", two=2)[:, :, 0:1]
                        od = lambda x: x.rearrange("p (f two) -> p f two", two=2)[:, :, 1:2]
                        p.comp("vector", lambda v: v.tensor_tensor(out=ev(outf[:]), in0=ev(ra[:]), in1=od(rb[:]), op=ALU.subtract))
                        p.comp("vector", lambda v: v.tensor_tensor(out=od(outf[:]), in0=od(ra[:]), in1=ev(rb[:]), op=ALU.add))
                    else:
                        p.comp("vector", lambda v: v.tensor_copy(out=outf[:], in_=psA[:]))
                    p.store(d_out[on][tt * 128:(tt + 1) * 128, oc * 512:(oc + 1) * 512], outf[:])

        p.build(block, ctx.sems)
    return nc


def build_attn(layer):
    nc = new_nc()
    alibi = (layer == 0)
    d_qT = nc.dram_tensor("qTu", [4, 128, S], F32, kind="ExternalInput").ap()
    d_kT = nc.dram_tensor("kTu", [4, 128, S], F32, kind="ExternalInput").ap()
    d_v = nc.dram_tensor("vu", [4, S, 128], F32, kind="ExternalInput").ap()
    if alibi:
        d_acol = nc.dram_tensor("acolu", [4, 128, 32], F32, kind="ExternalInput").ap()
        d_shift = nc.dram_tensor("shiftu", [4, 8, 512], F32, kind="ExternalInput").ap()
        d_ub = nc.dram_tensor("ubu", [4, 8, 512], F32, kind="ExternalInput").ap()
        d_lb = nc.dram_tensor("lbu", [4, 8, 512], F32, kind="ExternalInput").ap()
    d_z = nc.dram_tensor("o_z", [4, S, 128], F32, kind="ExternalOutput").ap()

    tri_np = np.where(np.arange(128)[:, None] <= np.arange(128)[None, :], 0.0, -1e30).astype(np.float32)
    d_tri = nc.inline_tensor(tri_np, "trimask")

    NB = S // 128
    with ExitStack() as stack:
        ctx = Ctx(nc, stack)
        p = Prog(nc)
        qstage = ctx.sb("qstage", [128, 512], F32)
        qTb = ctx.sb("qTb", [128, S], BF16)
        kTb = ctx.sb("kTb", [128, S], BF16)
        vsb = ctx.sb("vsb", [128, NB * 132], BF16)
        acol = ctx.sb("acol", [128, 32], F32)
        ubt = ctx.sb("ubt", [128, 512], F32)
        lbt = ctx.sb("lbt", [128, 512], F32)
        shrow = ctx.sb("shrow", [1, 512], F32)
        onesc = ctx.sb("onesc", [1, 128], F32)
        tri = ctx.sb("tri", [128, 128], F32)
        Ssb = ctx.sb("Ssb", [128, 512], F32)
        PT = ctx.sb("PT", [128, 512], BF16)
        recip = ctx.sb("recip", [128, 1], F32)
        mcol = ctx.sb("mcol", [128, 1], F32)
        zsb = ctx.sb("zsb", [128, 128], F32)
        psS = ctx.ps("psS", [128, 512], F32)
        zps = [ctx.ps(f"zps{i}", [128, 132], F32) for i in range(4)]
        block = stack.enter_context(nc.Block())

        lt = p.load(tri[:], d_tri.ap())
        p.comp("vector", lambda v: v.memset(onesc[:], 1.0))
        p.comp("vector", lambda v: v.memset(mcol[:], -EXP_M))
        p.comp("vector", lambda v: v.memset(
            vsb[:].rearrange("p (b w) -> p b w", w=132)[:, :, 128:129], 1.0))

        for u in range(4):
            lk = p.load(kTb[:], d_kT[u], queue="gpsimd")
            lvs = []
            for g in range(4):
                lvs.append(p.load(
                    vsb[:].rearrange("p (b w) -> p b w", w=132)[:, g * 8:(g + 1) * 8, 0:128],
                    d_v[u][g * 1024:(g + 1) * 1024, :].rearrange("(b p) w -> p b w", p=128),
                    queue="gpsimd"))
            la = []
            if alibi:
                la = [p.load(acol[:], d_acol[u], queue="sync")]
            lqs = []
            for ch in range(8):
                lq = p.load(qstage[:], d_qT[u][:, ch * 512:(ch + 1) * 512])
                p.comp("scalar", lambda s, ch=ch: s.activation(
                    out=qTb[:, ch * 512:(ch + 1) * 512], in_=qstage[:],
                    func=AF.Identity, scale=1.0 / math.sqrt(DH)), loads=[lq])

            for r in range(8):
                lrun = []
                if alibi:
                    lrun.append(p.load(ubt[:], d_ub[u][r:r + 1, :].broadcast_to([128, 512])))
                    lrun.append(p.load(lbt[:], d_lb[u][r:r + 1, :].broadcast_to([128, 512])))
                    lrun.append(p.load(shrow[:], d_shift[u][r:r + 1, :]))
                for kb in range(4 * r + 4):
                    if alibi:
                        p.comp("tensor", [
                            lambda t, kb=kb, r=r: t.matmul(
                                psS[:], kTb[:, kb * 128:(kb + 1) * 128],
                                qTb[:, r * 512:(r + 1) * 512], start=True, stop=False),
                            lambda t: t.matmul(psS[:], onesc[:], shrow[:], start=False, stop=True),
                        ], loads=[lk] + lrun)
                        p.comp("vector", lambda v: v.tensor_tensor(out=Ssb[:], in0=psS[:], in1=ubt[:], op=ALU.min))
                        p.comp("vector", lambda v: v.tensor_tensor(out=Ssb[:], in0=Ssb[:], in1=lbt[:], op=ALU.max))
                        j = kb - 4 * r
                        if j >= 0:
                            if j > 0:
                                p.comp("vector", lambda v, j=j: v.memset(Ssb[:, 0:j * 128], -1e30))
                            p.comp("vector", lambda v, j=j: v.tensor_tensor(
                                out=Ssb[:, j * 128:(j + 1) * 128], in0=Ssb[:, j * 128:(j + 1) * 128],
                                in1=tri[:], op=ALU.add), loads=[lt])
                        p.comp("scalar", lambda s, kb=kb: s.activation(
                            out=PT[:], in_=Ssb[:], func=AF.Exp, bias=acol[:, kb:kb + 1]), loads=la)
                    else:
                        p.comp("tensor", lambda t, kb=kb, r=r: t.matmul(
                            psS[:], kTb[:, kb * 128:(kb + 1) * 128],
                            qTb[:, r * 512:(r + 1) * 512], start=True, stop=True), loads=[lk])
                        j = kb - 4 * r
                        if j >= 0:
                            p.comp("vector", lambda v, j=j: v.tensor_tensor(
                                out=psS[:, j * 128:(j + 1) * 128], in0=psS[:, j * 128:(j + 1) * 128],
                                in1=tri[:], op=ALU.add), loads=[lt])
                        p.comp("scalar", lambda s: s.activation(
                            out=PT[:], in_=psS[:], func=AF.Exp, bias=mcol[:, 0:1]))
                    fns = []
                    for i in range(4):
                        if kb <= 4 * r + i:
                            fns.append(lambda t, i=i, kb=kb, r=r: t.matmul(
                                zps[i][:, 0:129], PT[:, i * 128:(i + 1) * 128],
                                vsb[:].rearrange("p (b w) -> p b w", w=132)[:, kb, 0:129],
                                start=(kb == 0), stop=(kb == 4 * r + i)))
                    p.comp("tensor", fns, loads=lvs)
                for i in range(4):
                    p.comp("vector", lambda v, i=i: v.reciprocal(out=recip[:], in_=zps[i][:, 128:129]))
                    p.comp("vector", lambda v, i=i: v.tensor_scalar(
                        out=zsb[:], in0=zps[i][:, 0:128], scalar1=recip[:, 0:1], scalar2=None, op0=ALU.mult))
                    p.store(d_z[u][(4 * r + i) * 128:(4 * r + i + 1) * 128, :], zsb[:])

        p.build(block, ctx.sems)
    return nc


def build_oproj(layer):
    nc = new_nc()
    bias = (layer == 1)
    d_zT = nc.dram_tensor("zT", [D, TPC], F32, kind="ExternalInput").ap()
    d_wo = nc.dram_tensor("woT", [D, D], F32, kind="ExternalInput").ap()
    d_x = nc.dram_tensor("x", [TPC, D], F32, kind="ExternalInput").ap()
    if bias:
        d_bo = nc.dram_tensor("bo", [1, D], F32, kind="ExternalInput").ap()
    d_xn = nc.dram_tensor("o_xn", [TPC, D], F32, kind="ExternalOutput").ap()
    d_y = nc.dram_tensor("o_y", [TPC, D], BF16, kind="ExternalOutput").ap()
    if bias:
        d_y32 = nc.dram_tensor("o_y32", [TPC, D], F32, kind="ExternalOutput").ap()

    NT = TPC // 128
    with ExitStack() as stack:
        ctx = Ctx(nc, stack)
        p = Prog(nc)
        zTb = ctx.sb("zTb", [128, 16 * TPC], BF16)
        wsb = ctx.sb("wsb", [128, 16 * 2048], BF16)
        xsb = ctx.sb("xsb", [128, D], F32)
        xn = ctx.sb("xn", [128, D], F32)
        xc = ctx.sb("xc", [128, D], F32)
        sq = ctx.sb("sq", [128, D], F32)
        stats = ctx.sb("stats", [128, 4], F32)
        epsc = ctx.sb("epsc", [128, 1], F32)
        ysb = ctx.sb("ysb", [128, D], BF16)
        onesr = ctx.sb("onesr", [1, 128], BF16)
        brow = ctx.sb("brow", [1, D], BF16)
        psA = ctx.ps("psA", [128, 512], F32)
        block = stack.enter_context(nc.Block())

        p.comp("vector", lambda v: v.memset(onesr[:], 1.0))
        p.comp("vector", lambda v: v.memset(epsc[:], 1e-5))
        lb = []
        if bias:
            lb = [p.load(brow[:], d_bo, queue="gpsimd")]
        lz = [p.load(zTb[:, kt * TPC:(kt + 1) * TPC], d_zT[kt * 128:(kt + 1) * 128, :], queue="gpsimd")
              for kt in range(16)]
        lw = [p.load(wsb[:, kt * 2048:(kt + 1) * 2048], d_wo[kt * 128:(kt + 1) * 128, :], queue="gpsimd")
              for kt in range(16)]

        for tt in range(NT):
            lx = p.load(xsb[:], d_x[tt * 128:(tt + 1) * 128, :])
            for oc in range(4):
                fns = [lambda t, kt=kt, tt=tt, oc=oc: t.matmul(
                    psA[:], zTb[:, kt * TPC + tt * 128: kt * TPC + (tt + 1) * 128],
                    wsb[:, kt * 2048 + oc * 512: kt * 2048 + (oc + 1) * 512],
                    start=(kt == 0), stop=(kt == 15 and not bias)) for kt in range(16)]
                if bias:
                    fns.append(lambda t, oc=oc: t.matmul(
                        psA[:], onesr[:], brow[:, oc * 512:(oc + 1) * 512], start=False, stop=True))
                p.comp("tensor", fns, loads=lz + lw + lb)
                p.comp("vector", lambda v, oc=oc: v.tensor_tensor(
                    out=xn[:, oc * 512:(oc + 1) * 512], in0=psA[:],
                    in1=xsb[:, oc * 512:(oc + 1) * 512], op=ALU.add), loads=[lx])
            p.store(d_xn[tt * 128:(tt + 1) * 128, :], xn[:])
            emit_ln(p, xn[:], ysb[:], xc[:], sq[:], stats[:], epsc[:, 0:1])
            p.store(d_y[tt * 128:(tt + 1) * 128, :], ysb[:])
            if bias:
                p.comp("vector", lambda v: v.tensor_scalar(
                    out=xc[:], in0=xc[:], scalar1=stats[:, 3:4], scalar2=None, op0=ALU.mult))
                p.store(d_y32[tt * 128:(tt + 1) * 128, :], xc[:])

        p.build(block, ctx.sems)
    return nc


def build_ffn0():
    nc = new_nc()
    d_y = nc.dram_tensor("y1T", [D, T], BF16, kind="ExternalInput").ap()
    d_gpk = nc.dram_tensor("gpk", [NPC, 2048], I16, kind="ExternalInput").ap()
    d_upk = nc.dram_tensor("upk", [NPC, 2048], I16, kind="ExternalInput").ap()
    d_gsc = nc.dram_tensor("gsc", [NPC, 64], F32, kind="ExternalInput").ap()
    d_usc = nc.dram_tensor("usc", [NPC, 64], F32, kind="ExternalInput").ap()
    d_dpk = nc.dram_tensor("dpk", [D, NPC], I16, kind="ExternalInput").ap()
    d_dsc = nc.dram_tensor("dsc", [D, NPC // 32], F32, kind="ExternalInput").ap()
    d_part = nc.dram_tensor("o_part", [T, D], F32, kind="ExternalOutput").ap()
    d_inter = nc.dram_tensor("interT", [NPC, T], BF16, kind="Internal")
    d_ident = nc.inline_tensor(np.eye(128, dtype=BF), "identbf")
    alt_np = np.tile(np.array([0, 4], dtype=np.int16), 1024).reshape(1, 2048)
    d_alt = nc.inline_tensor(alt_np, "altshift")

    with ExitStack() as stack:
        ctx = Ctx(nc, stack)
        p = Prog(nc)
        dec = {nm: ctx.sb(nm, [128, 2048], I16) for nm in ("pk16", "T16", "n16", "a1", "a2", "z16", "w16", "alt")}
        val = ctx.sb("val", [128, 2048], BF16)
        scsb = ctx.sb("scsb", [128, 64], F32)
        scb = ctx.sb("scb", [128, 64], BF16)
        ident = ctx.sb("ident", [128, 128], BF16)
        wgT = ctx.sb("wgT", [128, 16 * 2048], BF16)
        wuT = ctx.sb("wuT", [128, 16 * 2048], BF16)
        ysb = ctx.sb("ysb", [128, 16 * 512], BF16)
        g32 = ctx.sb("g32", [128, 512], F32)
        t3 = ctx.sb("t3", [128, 512], F32)
        t4 = ctx.sb("t4", [128, 512], F32)
        sgm = ctx.sb("sgm", [128, 512], F32)
        w32 = ctx.sb("w32", [128, 512], F32)
        inter = ctx.sb("inter", [128, 512], BF16)
        osb = ctx.sb("osb", [128, 512], F32)
        psG = ctx.ps("psG", [128, 512], F32)
        psU = ctx.ps("psU", [128, 512], F32)
        psT = ctx.ps("psT", [128, 128], BF16)
        block = stack.enter_context(nc.Block())

        li = p.load(ident[:], d_ident.ap())
        lalt = p.load(dec["alt"][:], d_alt.ap().broadcast_to([128, 2048]))

        def decode_into(dst, d_pk, d_sc, ntile):
            lpk = p.load(dec["pk16"][:], d_pk[ntile * 128:(ntile + 1) * 128, :], queue="gpsimd")
            lsc = p.load(scsb[:], d_sc[ntile * 128:(ntile + 1) * 128, :], queue="gpsimd")
            emit_decode(p, dec, [lpk, lalt], val[:])
            p.comp("vector", lambda v: v.tensor_copy(out=scb[:], in_=scsb[:]), loads=[lsc])
            vv = val[:].rearrange("p (c q) -> p c q", q=32)
            sbb = scb[:].unsqueeze(2).broadcast_to([128, 64, 32])
            p.comp("vector", lambda v: v.tensor_tensor(out=vv, in0=vv, in1=sbb, op=ALU.mult))
            for ht in range(16):
                p.comp("tensor", lambda t, ht=ht: t.transpose(
                    psT[:], val[:, ht * 128:(ht + 1) * 128], ident[:]), loads=[li])
                p.comp("vector", lambda v, ht=ht, ntile=ntile: v.tensor_copy(
                    out=dst[:, ht * 2048 + ntile * 128: ht * 2048 + (ntile + 1) * 128], in_=psT[:]))

        for ntile in range(16):
            decode_into(wgT, d_gpk, d_gsc, ntile)
            decode_into(wuT, d_upk, d_usc, ntile)

        for tc in range(16):
            ly = [p.load(ysb[:, ht * 512:(ht + 1) * 512],
                         d_y[ht * 128:(ht + 1) * 128, tc * 512:(tc + 1) * 512]) for ht in range(16)]
            for ntile in range(16):
                p.comp("tensor", [lambda t, ht=ht, ntile=ntile: t.matmul(
                    psG[:], wgT[:, ht * 2048 + ntile * 128: ht * 2048 + (ntile + 1) * 128],
                    ysb[:, ht * 512:(ht + 1) * 512], start=(ht == 0), stop=(ht == 15)) for ht in range(16)], loads=ly)
                p.comp("tensor", [lambda t, ht=ht, ntile=ntile: t.matmul(
                    psU[:], wuT[:, ht * 2048 + ntile * 128: ht * 2048 + (ntile + 1) * 128],
                    ysb[:, ht * 512:(ht + 1) * 512], start=(ht == 0), stop=(ht == 15)) for ht in range(16)])
                p.comp("vector", lambda v: v.tensor_copy(out=g32[:], in_=psG[:]))
                p.comp("scalar", lambda s: s.activation(out=t3[:], in_=g32[:], func=AF.Square))
                p.comp("vector", lambda v: v.tensor_scalar(out=t3[:], in0=t3[:], scalar1=GELU_C, scalar2=1.0, op0=ALU.mult, op1=ALU.add))
                p.comp("vector", lambda v: v.tensor_tensor(out=t4[:], in0=g32[:], in1=t3[:], op=ALU.mult))
                p.comp("scalar", lambda s: s.activation(out=sgm[:], in_=t4[:], func=AF.Sigmoid, scale=GELU_A))
                p.comp("vector", lambda v: v.tensor_tensor(out=w32[:], in0=g32[:], in1=sgm[:], op=ALU.mult))
                p.comp("vector", lambda v: v.scalar_tensor_tensor(out=inter[:], in0=w32[:], scalar=1.0, in1=psU[:], op0=ALU.mult, op1=ALU.mult))
                p.store(d_inter.ap()[ntile * 128:(ntile + 1) * 128, tc * 512:(tc + 1) * 512], inter[:])

        for ot in range(16):
            lpk = p.load(dec["pk16"][:], d_dpk[ot * 128:(ot + 1) * 128, :], queue="gpsimd")
            lsc = p.load(scsb[:], d_dsc[ot * 128:(ot + 1) * 128, :], queue="gpsimd")
            emit_decode(p, dec, [lpk, lalt], val[:])
            p.comp("vector", lambda v: v.tensor_copy(out=scb[:], in_=scsb[:]), loads=[lsc])
            vv = val[:].rearrange("p (c q) -> p c q", q=32)
            sbb = scb[:].unsqueeze(2).broadcast_to([128, 64, 32])
            p.comp("vector", lambda v: v.tensor_tensor(out=vv, in0=vv, in1=sbb, op=ALU.mult))
            for nt in range(16):
                p.comp("tensor", lambda t, nt=nt: t.transpose(
                    psT[:], val[:, nt * 128:(nt + 1) * 128], ident[:]), loads=[li])
                p.comp("vector", lambda v, nt=nt, ot=ot: v.tensor_copy(
                    out=wgT[:, nt * 2048 + ot * 128: nt * 2048 + (ot + 1) * 128], in_=psT[:]))

        for tc in range(16):
            lis = [p.load(ysb[:, nt * 512:(nt + 1) * 512],
                          d_inter.ap()[nt * 128:(nt + 1) * 128, tc * 512:(tc + 1) * 512]) for nt in range(16)]
            for tsub in range(4):
                for oc in range(4):
                    p.comp("tensor", [lambda t, nt=nt, tsub=tsub, oc=oc: t.matmul(
                        psG[:], ysb[:, nt * 512 + tsub * 128: nt * 512 + (tsub + 1) * 128],
                        wgT[:, nt * 2048 + oc * 512: nt * 2048 + (oc + 1) * 512],
                        start=(nt == 0), stop=(nt == 15)) for nt in range(16)], loads=lis)
                    p.comp("vector", lambda v: v.tensor_copy(out=osb[:], in_=psG[:]))
                    p.store(d_part[tc * 512 + tsub * 128: tc * 512 + (tsub + 1) * 128,
                                   oc * 512:(oc + 1) * 512], osb[:])

        p.build(block, ctx.sems)
    return nc


def build_reduce(nparts, parts_bf16, with_shared):
    nc = new_nc()
    pdt = BF16 if parts_bf16 else F32
    d_parts = nc.dram_tensor("parts", [nparts, TPC, D], pdt, kind="ExternalInput").ap()
    d_x = nc.dram_tensor("x", [TPC, D], F32, kind="ExternalInput").ap()
    if with_shared:
        d_sh = nc.dram_tensor("shared", [TPC, D], F32, kind="ExternalInput").ap()
    d_xn = nc.dram_tensor("o_xn", [TPC, D], F32, kind="ExternalOutput").ap()

    NT = TPC // 128
    with ExitStack() as stack:
        ctx = Ctx(nc, stack)
        p = Prog(nc)
        acc = ctx.sb("acc", [128, D], F32)
        pt = ctx.sb("pt", [128, D], F32)
        block = stack.enter_context(nc.Block())
        for tt in range(NT):
            lx = p.load(acc[:], d_x[tt * 128:(tt + 1) * 128, :])
            srcs = [d_parts[i][tt * 128:(tt + 1) * 128, :] for i in range(nparts)]
            if with_shared:
                srcs.append(d_sh[tt * 128:(tt + 1) * 128, :])
            first = True
            for sap in srcs:
                lp = p.load(pt[:], sap, queue="gpsimd")
                p.comp("vector", lambda v: v.tensor_tensor(out=acc[:], in0=acc[:], in1=pt[:], op=ALU.add),
                       loads=[lp] + ([lx] if first else []))
                first = False
            p.store(d_xn[tt * 128:(tt + 1) * 128, :], acc[:])
        p.build(block, ctx.sems)
    return nc


def build_moe():
    nc = new_nc()
    d_y = nc.dram_tensor("y3T32", [D, T], F32, kind="ExternalInput").ap()
    d_ym = nc.dram_tensor("y3Tmy32", [D, TPC], F32, kind="ExternalInput").ap()
    d_gw = nc.dram_tensor("gwT", [D, 16], F32, kind="ExternalInput").ap()
    d_w1 = nc.dram_tensor("w1", [2, D, MH], F32, kind="ExternalInput").ap()
    d_w2 = nc.dram_tensor("w2", [2, MH, D], F32, kind="ExternalInput").ap()
    d_sw1 = nc.dram_tensor("sw1", [D, MH], F32, kind="ExternalInput").ap()
    d_sw2 = nc.dram_tensor("sw2", [MH, D], F32, kind="ExternalInput").ap()
    d_esel = nc.dram_tensor("esel", [2, 1, 16], F32, kind="ExternalInput").ap()
    d_part = nc.dram_tensor("o_part", [2, T, D], BF16, kind="ExternalOutput").ap()
    d_shared = nc.dram_tensor("o_shared", [TPC, D], F32, kind="ExternalOutput").ap()

    NH = MH // 128
    with ExitStack() as stack:
        ctx = Ctx(nc, stack)
        p = Prog(nc)
        ysb = ctx.sb("ysb", [128, 16 * 512], BF16)
        yf = ctx.sb("yf", [128, 128], F32)
        gw = ctx.sb("gw", [128, 16 * 16], F32)
        comb = ctx.sb("comb", [128, 64 * 2], F32)
        combt = ctx.sb("combt", [128, 16], F32)
        eselt = ctx.sb("eselt", [128, 2 * 16], F32)
        ssb = ctx.sb("ssb", [128, 16], F32)
        r4 = ctx.sb("r4", [128, 4], F32)
        r4b = ctx.sb("r4b", [128, 4], F32)
        r16 = ctx.sb("r16", [128, 16], F32)
        r1 = ctx.sb("r1", [128, 8], F32)
        w1sb = ctx.sb("w1sb", [128, 16 * MH], BF16)
        w2sb = ctx.sb("w2sb", [128, NH * D], BF16)
        asb = ctx.sb("asb", [128, NH * 512], BF16)
        accb = ctx.sb("accb", [128, 512], BF16)
        accf = ctx.sb("accf", [128, 512], F32)
        psR = ctx.ps("psR", [128, 16], F32)
        psA = ctx.ps("psA", [128, 512], F32)
        psO = ctx.ps("psO", [128, 512], F32)
        block = stack.enter_context(nc.Block())

        lsel = [p.load(eselt[:, e * 16:(e + 1) * 16], d_esel[e].broadcast_to([128, 16])) for e in range(2)]
        lgw = [p.load(gw[:, kt * 16:(kt + 1) * 16], d_gw[kt * 128:(kt + 1) * 128, :]) for kt in range(16)]
        for tt in range(64):
            fns = []
            lyt = []
            for kt in range(16):
                lyt.append(p.load(yf[:], d_y[kt * 128:(kt + 1) * 128, tt * 128:(tt + 1) * 128]))
                fns = [lambda t, kt=kt, tt=tt: t.matmul(
                    psR[:], yf[:], gw[:, kt * 16:(kt + 1) * 16],
                    start=(kt == 0), stop=(kt == 15))]
                p.comp("tensor", fns, loads=[lyt[-1]] + (lgw if kt == 0 else []))
            p.comp("scalar", lambda s: s.activation(out=ssb[:], in_=psR[:], func=AF.Sigmoid))
            sg = ssb[:].rearrange("p (g q) -> p g q", q=4)
            p.comp("vector", lambda v: v.tensor_reduce(out=r4[:], in_=sg, axis=AX.X, op=ALU.max))
            bc4 = r4[:].unsqueeze(2).broadcast_to([128, 4, 4])
            p.comp("vector", lambda v: v.tensor_tensor(out=r16[:].rearrange("p (g q) -> p g q", q=4), in0=sg, in1=bc4, op=ALU.is_equal))
            p.comp("vector", lambda v: v.scalar_tensor_tensor(out=r16[:], in0=r16[:], scalar=-1e30, in1=ssb[:], op0=ALU.mult, op1=ALU.add))
            p.comp("vector", lambda v: v.tensor_reduce(out=r4b[:], in_=r16[:].rearrange("p (g q) -> p g q", q=4), axis=AX.X, op=ALU.max))
            p.comp("vector", lambda v: v.tensor_tensor(out=r4[:], in0=r4[:], in1=r4b[:], op=ALU.add))
            p.comp("vector", lambda v: v.tensor_reduce(out=r1[:, 0:1], in_=r4[:], axis=AX.X, op=ALU.max))
            p.comp("vector", lambda v: v.tensor_scalar(out=r4b[:], in0=r4[:], scalar1=r1[:, 0:1], scalar2=None, op0=ALU.is_equal))
            p.comp("vector", lambda v: v.scalar_tensor_tensor(out=r4b[:], in0=r4b[:], scalar=-1e30, in1=r4[:], op0=ALU.mult, op1=ALU.add))
            p.comp("vector", lambda v: v.tensor_reduce(out=r1[:, 1:2], in_=r4b[:], axis=AX.X, op=ALU.max))
            p.comp("vector", lambda v: v.tensor_scalar(out=r4b[:], in0=r4[:], scalar1=r1[:, 1:2], scalar2=None, op0=ALU.is_ge))
            bc4b = r4b[:].unsqueeze(2).broadcast_to([128, 4, 4])
            p.comp("vector", lambda v: v.tensor_tensor(out=r16[:].rearrange("p (g q) -> p g q", q=4), in0=sg, in1=bc4b, op=ALU.mult))
            p.comp("vector", lambda v: v.tensor_reduce(out=r1[:, 2:3], in_=r16[:], axis=AX.X, op=ALU.max))
            p.comp("vector", lambda v: v.tensor_scalar(out=ssb[:], in0=r16[:], scalar1=r1[:, 2:3], scalar2=None, op0=ALU.is_equal))
            p.comp("vector", lambda v: v.scalar_tensor_tensor(out=ssb[:], in0=ssb[:], scalar=-1e30, in1=r16[:], op0=ALU.mult, op1=ALU.add))
            p.comp("vector", lambda v: v.tensor_reduce(out=r1[:, 3:4], in_=ssb[:], axis=AX.X, op=ALU.max))
            p.comp("vector", lambda v: v.tensor_tensor(out=r1[:, 4:5], in0=r1[:, 2:3], in1=r1[:, 3:4], op=ALU.add))
            p.comp("vector", lambda v: v.tensor_scalar(out=r1[:, 4:5], in0=r1[:, 4:5], scalar1=1e-20, scalar2=None, op0=ALU.add))
            p.comp("vector", lambda v: v.reciprocal(out=r1[:, 5:6], in_=r1[:, 4:5]))
            p.comp("vector", lambda v: v.tensor_scalar(out=ssb[:], in0=r16[:], scalar1=r1[:, 3:4], scalar2=None, op0=ALU.is_ge))
            p.comp("vector", lambda v: v.tensor_tensor(out=ssb[:], in0=ssb[:], in1=r16[:], op=ALU.mult))
            p.comp("vector", lambda v: v.tensor_scalar(out=ssb[:], in0=ssb[:], scalar1=r1[:, 5:6], scalar2=None, op0=ALU.mult))
            for e in range(2):
                p.comp("vector", lambda v, e=e: v.tensor_tensor(out=combt[:], in0=ssb[:], in1=eselt[:, e * 16:(e + 1) * 16], op=ALU.mult), loads=[lsel[e]])
                p.comp("vector", lambda v, e=e, tt=tt: v.tensor_reduce(out=comb[:, tt * 2 + e: tt * 2 + e + 1], in_=combt[:], axis=AX.X, op=ALU.add))

        def expert_pass(d_w1src, d_w2src, n_tc, d_ysrc, use_comb, e, d_out):
            lw1 = [p.load(w1sb[:, kt * MH:(kt + 1) * MH], d_w1src[kt * 128:(kt + 1) * 128, :], queue="gpsimd")
                   for kt in range(16)]
            lw2 = [p.load(w2sb[:, ht * D:(ht + 1) * D], d_w2src[ht * 128:(ht + 1) * 128, :], queue="gpsimd")
                   for ht in range(NH)]
            for tc in range(n_tc):
                ly = [p.load(ysb[:, kt * 512:(kt + 1) * 512],
                             d_ysrc[kt * 128:(kt + 1) * 128, tc * 512:(tc + 1) * 512], queue="gpsimd")
                      for kt in range(16)]
                for ht in range(NH):
                    p.comp("tensor", [lambda t, kt=kt, ht=ht: t.matmul(
                        psA[:], w1sb[:, kt * MH + ht * 128: kt * MH + (ht + 1) * 128],
                        ysb[:, kt * 512:(kt + 1) * 512], start=(kt == 0), stop=(kt == 15))
                        for kt in range(16)], loads=ly + lw1)
                    p.comp("scalar", lambda s, ht=ht: s.activation(
                        out=asb[:, ht * 512:(ht + 1) * 512], in_=psA[:], func=AF.Silu))
                for tsub in range(4):
                    for oc in range(4):
                        p.comp("tensor", [lambda t, ht=ht, tsub=tsub, oc=oc: t.matmul(
                            psO[:], asb[:, ht * 512 + tsub * 128: ht * 512 + (tsub + 1) * 128],
                            w2sb[:, ht * D + oc * 512: ht * D + (oc + 1) * 512],
                            start=(ht == 0), stop=(ht == NH - 1)) for ht in range(NH)], loads=lw2)
                        if use_comb:
                            cap = comb[:, (tc * 4 + tsub) * 2 + e: (tc * 4 + tsub) * 2 + e + 1]
                            p.comp("vector", lambda v, cap=cap: v.tensor_scalar(
                                out=accb[:], in0=psO[:], scalar1=cap, scalar2=None, op0=ALU.mult))
                            p.store(d_out[tc * 512 + tsub * 128: tc * 512 + (tsub + 1) * 128,
                                          oc * 512:(oc + 1) * 512], accb[:])
                        else:
                            p.comp("vector", lambda v: v.tensor_copy(out=accf[:], in_=psO[:]))
                            p.store(d_out[tc * 512 + tsub * 128: tc * 512 + (tsub + 1) * 128,
                                          oc * 512:(oc + 1) * 512], accf[:])

        for e in range(2):
            expert_pass(d_w1[e], d_w2[e], 16, d_y, True, e, d_part[e])
        expert_pass(d_sw1, d_sw2, TPC // 512, d_ym, False, 0, d_shared)

        p.build(block, ctx.sems)
    return nc


def _t(a):
    return np.ascontiguousarray(np.asarray(a).T)


def attn_inputs(q, k, v, alibi):
    maps = []
    pos = np.arange(S, dtype=np.float32)
    for c in range(NCORES):
        units = [(b, h) for b in range(B) for h in (2 * c, 2 * c + 1)]
        qTu = np.stack([_t(q.reshape(B, S, H, DH)[b, :, h, :]) for (b, h) in units])
        kTu = np.stack([_t(k.reshape(B, S, H, DH)[b, :, h, :]) for (b, h) in units])
        vu = np.stack([np.ascontiguousarray(v.reshape(B, S, H, DH)[b, :, h, :]) for (b, h) in units])
        m = dict(qTu=qTu, kTu=kTu, vu=vu)
        if alibi:
            acolu = np.stack([
                (SLOPES[h] * pos - EXP_M).reshape(32, 128).T.copy() for (b, h) in units])
            shiftu = np.stack([
                np.stack([-SLOPES[h] * (r * 512 + np.arange(512, dtype=np.float32)) for r in range(8)])
                for (b, h) in units])
            m.update(acolu=np.ascontiguousarray(acolu, dtype=np.float32),
                     shiftu=np.ascontiguousarray(shiftu, dtype=np.float32),
                     ubu=np.ascontiguousarray(1024.0 + shiftu, dtype=np.float32),
                     lbu=np.ascontiguousarray(-1024.0 + shiftu, dtype=np.float32))
        maps.append(m)
    return maps


def attn_gather(res):
    z = np.zeros((B, S, H, DH), dtype=np.float32)
    for c in range(NCORES):
        units = [(b, h) for b in range(B) for h in (2 * c, 2 * c + 1)]
        for ui, (b, h) in enumerate(units):
            z[b, :, h, :] = res.results[c]["o_z"][ui]
    return z.reshape(T, D)


def kernel_multi(**inputs):
    xf = np.ascontiguousarray(np.asarray(inputs["L0_attn_resid_pre"], dtype=np.float32).reshape(T, D))
    my = lambda a, c: np.ascontiguousarray(a[c * TPC:(c + 1) * TPC])

    w0 = {n: _t(inputs["L0_attn_" + k]) for n, k in (("wqT", "Wq"), ("wkT", "Wk"), ("wvT", "Wv"))}
    nc = build_qkv(0)
    res = launch(nc, [dict(x=my(xf, c), **w0) for c in range(NCORES)])
    q0 = np.concatenate([r["o_q"] for r in res.results])
    k0 = np.concatenate([r["o_k"] for r in res.results])
    v0 = np.concatenate([r["o_v"] for r in res.results])

    nc = build_attn(0)
    res = launch(nc, attn_inputs(q0, k0, v0, True))
    z0 = attn_gather(res)

    wo0T = _t(inputs["L0_attn_Wo"])
    z0T = _t(z0)
    nc = build_oproj(0)
    res = launch(nc, [dict(zT=np.ascontiguousarray(z0T[:, c * TPC:(c + 1) * TPC]),
                           woT=wo0T, x=my(xf, c)) for c in range(NCORES)])
    x1 = np.concatenate([r["o_xn"] for r in res.results])
    y1 = np.concatenate([r["o_y"] for r in res.results])

    y1T = _t(y1)
    gpk = np.asarray(inputs["L0_ffn_gate_W_packed"], dtype=np.int32)
    upk = np.asarray(inputs["L0_ffn_up_W_packed"], dtype=np.int32)
    dpk = np.asarray(inputs["L0_ffn_down_W_packed"], dtype=np.int32)
    gsc = np.asarray(inputs["L0_ffn_gate_scales"], dtype=np.float32)
    usc = np.asarray(inputs["L0_ffn_up_scales"], dtype=np.float32)
    dsc = np.asarray(inputs["L0_ffn_down_scales"], dtype=np.float32)
    nc = build_ffn0()
    maps = []
    for c in range(NCORES):
        n0 = c * NPC
        maps.append(dict(
            y1T=y1T,
            gpk=np.ascontiguousarray(gpk[n0:n0 + NPC]).view(np.int16),
            upk=np.ascontiguousarray(upk[n0:n0 + NPC]).view(np.int16),
            gsc=np.ascontiguousarray(gsc[n0:n0 + NPC]),
            usc=np.ascontiguousarray(usc[n0:n0 + NPC]),
            dpk=np.ascontiguousarray(dpk[:, n0 // 2:(n0 + NPC) // 2]).view(np.int16),
            dsc=np.ascontiguousarray(dsc[:, n0 // 32:(n0 + NPC) // 32]),
        ))
    res = launch(nc, maps)
    parts0 = [r["o_part"] for r in res.results]

    nc = build_reduce(NCORES, False, False)
    res = launch(nc, [dict(parts=np.stack([pp[c * TPC:(c + 1) * TPC] for pp in parts0]),
                           x=my(x1, c)) for c in range(NCORES)])
    x2 = np.concatenate([r["o_xn"] for r in res.results])

    w1a = {n: _t(inputs["L1_attn_" + k]) for n, k in (("wqT", "Wq"), ("wkT", "Wk"), ("wvT", "Wv"))}
    b1 = {n: np.asarray(inputs["L1_attn_" + k], dtype=np.float32).reshape(1, D)
          for n, k in (("bq", "bq"), ("bk", "bk"), ("bv", "bv"))}
    half = DH // 2
    inv = 1.0 / (10000.0 ** (np.arange(half, dtype=np.float32) / half))
    tpos = np.arange(S, dtype=np.float32)
    fr = tpos[:, None] * inv[None, :]
    cosS = np.repeat(np.cos(fr), 2, axis=1).astype(np.float32)
    sinS = np.repeat(np.sin(fr), 2, axis=1).astype(np.float32)
    nc = build_qkv(1)
    maps = []
    for c in range(NCORES):
        rows = np.arange(c * TPC, (c + 1) * TPC) % S
        maps.append(dict(x=my(x2, c), **w1a, **b1,
                         cos4=np.ascontiguousarray(np.tile(cosS[rows], (1, 4))),
                         sin4=np.ascontiguousarray(np.tile(sinS[rows], (1, 4)))))
    res = launch(nc, maps)
    q1 = np.concatenate([r["o_q"] for r in res.results])
    k1 = np.concatenate([r["o_k"] for r in res.results])
    v1 = np.concatenate([r["o_v"] for r in res.results])

    nc = build_attn(1)
    res = launch(nc, attn_inputs(q1, k1, v1, False))
    z1 = attn_gather(res)

    wo1T = _t(inputs["L1_attn_Wo"])
    bo1 = np.asarray(inputs["L1_attn_bo"], dtype=np.float32).reshape(1, D)
    z1T = _t(z1)
    nc = build_oproj(1)
    res = launch(nc, [dict(zT=np.ascontiguousarray(z1T[:, c * TPC:(c + 1) * TPC]),
                           woT=wo1T, x=my(x2, c), bo=bo1) for c in range(NCORES)])
    x3 = np.concatenate([r["o_xn"] for r in res.results])
    y3_32 = np.concatenate([r["o_y32"] for r in res.results])

    y3T32 = _t(y3_32)
    gwT = _t(inputs["L1_ffn_gate_w"])
    w1m = np.asarray(inputs["L1_ffn_w1"], dtype=np.float32)
    w2m = np.asarray(inputs["L1_ffn_w2"], dtype=np.float32)
    sw1 = np.ascontiguousarray(np.asarray(inputs["L1_ffn_sw1"], dtype=np.float32)[0])
    sw2 = np.ascontiguousarray(np.asarray(inputs["L1_ffn_sw2"], dtype=np.float32)[0])
    nc = build_moe()
    maps = []
    for c in range(NCORES):
        esel = np.zeros((2, 1, 16), dtype=np.float32)
        esel[0, 0, 2 * c] = 1.0
        esel[1, 0, 2 * c + 1] = 1.0
        maps.append(dict(
            y3T32=y3T32,
            y3Tmy32=np.ascontiguousarray(y3T32[:, c * TPC:(c + 1) * TPC]),
            esel=esel,
            gwT=gwT,
            w1=np.ascontiguousarray(w1m[2 * c:2 * c + 2]),
            w2=np.ascontiguousarray(w2m[2 * c:2 * c + 2]),
            sw1=sw1, sw2=sw2))
    res = launch(nc, maps)
    partsM = [r["o_part"][e] for r in res.results for e in range(2)]
    sharedM = [r["o_shared"] for r in res.results]

    nc = build_reduce(2 * NCORES, True, True)
    res = launch(nc, [dict(parts=np.stack([pp[c * TPC:(c + 1) * TPC] for pp in partsM]),
                           x=my(x3, c), shared=sharedM[c]) for c in range(NCORES)])
    out = np.concatenate([r["o_xn"] for r in res.results])
    return out.reshape(B, S, D).astype(np.float32)


def _g3(src2d, p=128):
    return src2d.rearrange("(kt p) c -> p kt c", p=p)


def _s3(dst2d, k):
    return dst2d.rearrange("p (kt c) -> p kt c", kt=k)


def build_fused(upto=9):
    nc = new_nc()
    RG = [list(range(NCORES))]

    d_x0 = nc.dram_tensor("x0", [TPC, D], F32, kind="ExternalInput").ap()
    d_wq0 = nc.dram_tensor("wq0", [D, 256], F32, kind="ExternalInput").ap()
    d_wk0 = nc.dram_tensor("wk0", [D, 256], F32, kind="ExternalInput").ap()
    d_wv0 = nc.dram_tensor("wv0", [D, 256], F32, kind="ExternalInput").ap()
    d_wo0 = nc.dram_tensor("wo0", [256, D], F32, kind="ExternalInput").ap()
    d_acol = nc.dram_tensor("acolu", [4, 128, 32], F32, kind="ExternalInput").ap()
    d_shift = nc.dram_tensor("shiftu", [4, 8, 512], F32, kind="ExternalInput").ap()
    d_ub = nc.dram_tensor("ubu", [4, 8, 512], F32, kind="ExternalInput").ap()
    d_lb = nc.dram_tensor("lbu", [4, 8, 512], F32, kind="ExternalInput").ap()
    d_gpk = nc.dram_tensor("gpk", [NPC, 2048], I16, kind="ExternalInput").ap()
    d_upk = nc.dram_tensor("upk", [NPC, 2048], I16, kind="ExternalInput").ap()
    d_gsc = nc.dram_tensor("gsc", [NPC, 64], F32, kind="ExternalInput").ap()
    d_usc = nc.dram_tensor("usc", [NPC, 64], F32, kind="ExternalInput").ap()
    d_dpk = nc.dram_tensor("dpk", [D, NPC], I16, kind="ExternalInput").ap()
    d_dsc = nc.dram_tensor("dsc", [D, NPC // 32], F32, kind="ExternalInput").ap()
    d_wq1 = nc.dram_tensor("wq1", [D, 256], F32, kind="ExternalInput").ap()
    d_wk1 = nc.dram_tensor("wk1", [D, 256], F32, kind="ExternalInput").ap()
    d_wv1 = nc.dram_tensor("wv1", [D, 256], F32, kind="ExternalInput").ap()
    d_bq1 = nc.dram_tensor("bq1", [1, 256], F32, kind="ExternalInput").ap()
    d_bk1 = nc.dram_tensor("bk1", [1, 256], F32, kind="ExternalInput").ap()
    d_bv1 = nc.dram_tensor("bv1", [1, 256], F32, kind="ExternalInput").ap()
    d_wo1 = nc.dram_tensor("wo1", [256, D], F32, kind="ExternalInput").ap()
    d_bo1 = nc.dram_tensor("bo1", [1, D], F32, kind="ExternalInput").ap()
    d_cosS = nc.dram_tensor("cosS", [S, 128], F32, kind="ExternalInput").ap()
    d_sinS = nc.dram_tensor("sinS", [S, 128], F32, kind="ExternalInput").ap()
    d_gw = nc.dram_tensor("gwT", [D, 16], F32, kind="ExternalInput").ap()
    d_w1 = nc.dram_tensor("w1", [2, D, MH], F32, kind="ExternalInput").ap()
    d_w2 = nc.dram_tensor("w2", [2, MH, D], F32, kind="ExternalInput").ap()
    d_sw1 = nc.dram_tensor("sw1", [D, MH], F32, kind="ExternalInput").ap()
    d_sw2 = nc.dram_tensor("sw2", [MH, D], F32, kind="ExternalInput").ap()
    d_esel = nc.dram_tensor("esel", [2, 1, 16], F32, kind="ExternalInput").ap()
    d_out = nc.dram_tensor("o_x4", [TPC, D], F32, kind="ExternalOutput").ap()

    tri_np = np.where(np.arange(128)[:, None] <= np.arange(128)[None, :], 0.0, -1e30).astype(np.float32)
    d_tri = nc.inline_tensor(tri_np, "trimask")
    d_identb = nc.inline_tensor(np.eye(128, dtype=BF), "identb")
    d_identf = nc.inline_tensor(np.eye(128, dtype=np.float32), "identf")
    alt_np = np.tile(np.array([0, 4], dtype=np.int16), 1024).reshape(1, 2048)
    d_alt = nc.inline_tensor(alt_np, "altshift")

    it = lambda nm, sh, dt: nc.dram_tensor(nm, sh, dt, kind="Internal")
    yTl = it("yTl", [D, TPC], BF16)
    yTl32 = it("yTl32", [D, TPC], F32)
    G = [it(f"G{i}", [NCORES * D, TPC], BF16) for i in range(3)]
    G32 = it("G32", [NCORES * D, TPC], F32)
    qtu = it("qtu", [4, 128, S], BF16)
    ktu = it("ktu", [4, 128, S], BF16)
    vu = it("vu", [4, S, 128], BF16)
    zTl = [it(f"zTl{i}", [256, T], BF16) for i in range(2)]
    part = [it(f"part{i}", [T, D], BF16) for i in range(4)]
    rs = [it(f"rs{i}", [TPC, D], BF16) for i in range(4)]
    xsc = [it(f"xsc{i}", [TPC, D], F32) for i in range(3)]
    interT = it("interT", [NPC, T], BF16)
    e0p = it("e0p", [T, D], BF16)
    shl = it("shl", [TPC, D], F32)

    NT = TPC // 128

    _uid = [0]

    def _u():
        _uid[0] += 1
        return f"u{_uid[0]}_"

    with ExitStack() as topstack:
        ctx = Ctx(nc, topstack)
        p = Prog(nc)
        block = topstack.enter_context(nc.Block())

        def ph_ln_ag(x_src, ytl_ap, g_ap, f32mode):
            dt = F32 if f32mode else BF16
            with ExitStack() as st_:
                sb = lambda nm, sh, dtt: st_.enter_context(nc.sbuf_tensor(_u() + nm, sh, dtt))
                xsb = sb("ln_x", [128, D], F32)
                xc = sb("ln_xc", [128, D], F32)
                sq = sb("ln_sq", [128, D], F32)
                stats = sb("ln_st", [128, 4], F32)
                epsc = sb("ln_eps", [128, 1], F32)
                ytile = sb("ln_y", [128, D], dt)
                ident = sb("ln_id", [128, 128], dt)
                stage = sb("ln_stg", [128, 16 * 128], dt)
                psT_ = st_.enter_context(nc.psum_tensor(_u() + "ln_psT", [128, 128], dt))
                li = p.load(ident[:], (d_identf if f32mode else d_identb).ap())
                p.comp("vector", lambda v: v.memset(epsc[:], 1e-5))
                for tt in range(NT):
                    lx = p.load(xsb[:], x_src[tt * 128:(tt + 1) * 128, :])
                    emit_ln(p, xsb[:], ytile[:], xc[:], sq[:], stats[:], epsc[:, 0:1], loads=[lx])
                    for kt in range(16):
                        p.comp("tensor", lambda t, kt=kt: t.transpose(
                            psT_[:], ytile[:, kt * 128:(kt + 1) * 128], ident[:]), loads=[li])
                        p.comp("vector", lambda v, kt=kt: v.tensor_copy(
                            out=stage[:, kt * 128:(kt + 1) * 128], in_=psT_[:]))
                    p.store(_g3(ytl_ap[:, tt * 128:(tt + 1) * 128]), _s3(stage[:], 16))
            p.comp("gpsimd", lambda g_: g_.collective_compute(
                "AllGather", ALU.bypass, replica_groups=RG,
                ins=[ytl_ap.opt()], outs=[g_ap.opt()]))

        def ph_qkv_units0(g_ap, wq_ap, wk_ap, wv_ap, scaleq):
            with ExitStack() as st_:
                sb = lambda nm, sh, dtt: st_.enter_context(nc.sbuf_tensor(_u() + nm, sh, dtt))
                wsb = sb("qk_w", [128, 6 * 16 * 128], BF16)
                rhs = sb("qk_rhs", [128, 16 * 512], BF16)
                stg = sb("qk_stg", [128, 512], BF16)
                vstg = sb("qk_vstg", [128, 512], BF16)
                identb_ = sb("qk_id", [128, 128], BF16)
                pss = [st_.enter_context(nc.psum_tensor(_u() + f"qk_ps{i}", [128, 512], F32)) for i in range(6)]
                psT_ = st_.enter_context(nc.psum_tensor(_u() + "qk_psT", [128, 128], BF16))
                li = p.load(identb_[:], d_identb.ap())
                lw = []
                for wi, wap in enumerate((wq_ap, wk_ap, wv_ap)):
                    for hl in range(2):
                        lw.append(p.load(
                            _s3(wsb[:, (wi * 2 + hl) * 16 * 128:((wi * 2 + hl) * 16 + 16) * 128], 16),
                            _g3(wap[:, hl * 128:(hl + 1) * 128]),
                            queue="gpsimd"))
                for r in range(NCORES):
                    b = r // 4
                    for tc in range(2):
                        bcol = (r % 4) * TPC + tc * 512
                        lr = p.load(_s3(rhs[:], 16), _g3(g_ap[r * D:(r + 1) * D, tc * 512:(tc + 1) * 512]))
                        p.comp("tensor", [lambda t, slot=slot, kt=kt: t.matmul(
                            pss[slot][:], wsb[:, (slot * 16 + kt) * 128:(slot * 16 + kt + 1) * 128],
                            rhs[:, kt * 512:(kt + 1) * 512], start=(kt == 0), stop=(kt == 15))
                            for kt in range(16) for slot in range(6)],
                            loads=[lr] + lw)
                        for wi, dst in ((0, qtu), (1, ktu)):
                            for hl in range(2):
                                u = b * 2 + hl
                                slot = wi * 2 + hl
                                if wi == 0:
                                    p.comp("scalar", lambda s, slot=slot: s.activation(
                                        out=stg[:], in_=pss[slot][:], func=AF.Identity, scale=scaleq))
                                else:
                                    p.comp("vector", lambda v, slot=slot: v.tensor_copy(out=stg[:], in_=pss[slot][:]))
                                p.store(dst.ap()[u][:, bcol:bcol + 512], stg[:])
                        for hl in range(2):
                            u = b * 2 + hl
                            slot = 4 + hl
                            p.comp("vector", lambda v, slot=slot: v.tensor_copy(out=vstg[:], in_=pss[slot][:]))
                            for i in range(4):
                                p.comp("tensor", lambda t, i=i: t.transpose(
                                    psT_[:], vstg[:, i * 128:(i + 1) * 128], identb_[:]), loads=[li])
                                p.comp("vector", lambda v: v.tensor_copy(out=stg[:, 0:128], in_=psT_[:]))
                                p.store(vu.ap()[u][bcol + i * 128: bcol + (i + 1) * 128, :], stg[:, 0:128])

        def ph_qkv_units1(g_ap):
            with ExitStack() as st_:
                sb = lambda nm, sh, dtt: st_.enter_context(nc.sbuf_tensor(_u() + nm, sh, dtt))
                wsb = sb("q1_w", [128, 4 * 16 * 128], BF16)
                wvsb = sb("q1_wv", [128, 2 * 16 * 128], BF16)
                ltile = sb("q1_l", [128, 16 * 128], BF16)
                bbc = sb("q1_bbc", [128, 4 * 128], F32)
                bvcol = sb("q1_bvc", [128, 2], F32)
                costl = sb("q1_cos", [128, 128], F32)
                sintl = sb("q1_sin", [128, 128], F32)
                f32a = sb("q1_a", [128, 128], F32)
                ra = sb("q1_ra", [128, 128], F32)
                rb = sb("q1_rb", [128, 128], F32)
                rop = sb("q1_ro", [128, 128], F32)
                identf_ = sb("q1_idf", [128, 128], F32)
                identb_ = sb("q1_idb", [128, 128], BF16)
                stg = sb("q1_stg", [128, 512], BF16)
                vstg = sb("q1_vstg", [128, 512], BF16)
                rhs = sb("q1_rhs", [128, 16 * 512], BF16)
                psQK = st_.enter_context(nc.psum_tensor(_u() + "q1_psqk", [128, 512], F32))
                psTf = st_.enter_context(nc.psum_tensor(_u() + "q1_psTf", [128, 128], F32))
                psTb = st_.enter_context(nc.psum_tensor(_u() + "q1_psTb", [128, 128], BF16))
                psV = [st_.enter_context(nc.psum_tensor(_u() + f"q1_psv{i}", [128, 512], F32)) for i in range(2)]
                lif = p.load(identf_[:], d_identf.ap())
                lib = p.load(identb_[:], d_identb.ap())
                lw = []
                for wi, wap in enumerate((d_wq1, d_wk1)):
                    for hl in range(2):
                        lw.append(p.load(
                            _s3(wsb[:, (wi * 2 + hl) * 16 * 128:((wi * 2 + hl) * 16 + 16) * 128], 16),
                            _g3(wap[:, hl * 128:(hl + 1) * 128]),
                            queue="gpsimd"))
                for hl in range(2):
                    lw.append(p.load(
                        _s3(wvsb[:, hl * 16 * 128:(hl * 16 + 16) * 128], 16),
                        _g3(d_wv1[:, hl * 128:(hl + 1) * 128]),
                        queue="gpsimd"))
                lb = []
                for wi, bap in enumerate((d_bq1, d_bk1)):
                    for hl in range(2):
                        lb.append(p.load(bbc[:, (wi * 2 + hl) * 128:(wi * 2 + hl + 1) * 128],
                                         bap[:, hl * 128:(hl + 1) * 128].broadcast_to([128, 128])))
                lbv = [p.load(bvcol[:, hl:hl + 1],
                              d_bv1[:, hl * 128:(hl + 1) * 128].rearrange("o p -> p o"))
                       for hl in range(2)]
                for b in range(B):
                    for pt in range(32):
                        r = b * 4 + pt // 8
                        coff = (pt % 8) * 128
                        lts = p.load(_s3(ltile[:], 16), _g3(g_ap[r * D:(r + 1) * D, coff:coff + 128]))
                        p.comp("tensor", [lambda t, slot=slot, kt=kt: t.matmul(
                            psQK[:, slot * 128:(slot + 1) * 128], ltile[:, kt * 128:(kt + 1) * 128],
                            wsb[:, (slot * 16 + kt) * 128:(slot * 16 + kt + 1) * 128],
                            start=(kt == 0), stop=(kt == 15)) for kt in range(16) for slot in range(4)],
                            loads=[lts] + (lw + lb if (pt == 0 and b == 0) else []))
                        lc = p.load(costl[:], d_cosS[pt * 128:(pt + 1) * 128, :])
                        ls = p.load(sintl[:], d_sinS[pt * 128:(pt + 1) * 128, :])
                        for wi in range(2):
                            for hl in range(2):
                                slot = wi * 2 + hl
                                u = b * 2 + hl
                                p.comp("vector", lambda v, slot=slot: v.tensor_tensor(
                                    out=f32a[:], in0=psQK[:, slot * 128:(slot + 1) * 128],
                                    in1=bbc[:, slot * 128:(slot + 1) * 128], op=ALU.add))
                                p.comp("vector", lambda v: v.tensor_tensor(out=ra[:], in0=f32a[:], in1=costl[:], op=ALU.mult), loads=[lc])
                                p.comp("vector", lambda v: v.tensor_tensor(out=rb[:], in0=f32a[:], in1=sintl[:], op=ALU.mult), loads=[ls])
                                ev = lambda x: x.rearrange("p (f two) -> p f two", two=2)[:, :, 0:1]
                                od = lambda x: x.rearrange("p (f two) -> p f two", two=2)[:, :, 1:2]
                                p.comp("vector", lambda v: v.tensor_tensor(out=ev(rop[:]), in0=ev(ra[:]), in1=od(rb[:]), op=ALU.subtract))
                                p.comp("vector", lambda v: v.tensor_tensor(out=od(rop[:]), in0=od(ra[:]), in1=ev(rb[:]), op=ALU.add))
                                p.comp("tensor", lambda t: t.transpose(psTf[:], rop[:], identf_[:]), loads=[lif])
                                if wi == 0:
                                    p.comp("scalar", lambda s: s.activation(
                                        out=stg[:, 0:128], in_=psTf[:], func=AF.Identity, scale=1.0 / math.sqrt(DH)))
                                else:
                                    p.comp("vector", lambda v: v.tensor_copy(out=stg[:, 0:128], in_=psTf[:]))
                                dst = qtu if wi == 0 else ktu
                                p.store(dst.ap()[u][:, b * 0 + pt * 128: pt * 128 + 128], stg[:, 0:128])
                for r in range(NCORES):
                    b = r // 4
                    for tc in range(2):
                        bcol = (r % 4) * TPC + tc * 512
                        lr = p.load(_s3(rhs[:], 16), _g3(g_ap[r * D:(r + 1) * D, tc * 512:(tc + 1) * 512]))
                        p.comp("tensor", [lambda t, hl=hl, kt=kt: t.matmul(
                            psV[hl][:], wvsb[:, (hl * 16 + kt) * 128:(hl * 16 + kt + 1) * 128],
                            rhs[:, kt * 512:(kt + 1) * 512], start=(kt == 0), stop=(kt == 15))
                            for kt in range(16) for hl in range(2)], loads=[lr])
                        for hl in range(2):
                            u = b * 2 + hl
                            p.comp("vector", lambda v, hl=hl: v.tensor_scalar(
                                out=vstg[:], in0=psV[hl][:], scalar1=bvcol[:, hl:hl + 1],
                                scalar2=None, op0=ALU.add), loads=[lbv[hl]])
                            for i in range(4):
                                p.comp("tensor", lambda t, i=i: t.transpose(
                                    psTb[:], vstg[:, i * 128:(i + 1) * 128], identb_[:]), loads=[lib])
                                p.comp("vector", lambda v: v.tensor_copy(out=stg[:, 0:128], in_=psTb[:]))
                                p.store(vu.ap()[u][bcol + i * 128: bcol + (i + 1) * 128, :], stg[:, 0:128])

        def ph_attn(layer, ztl_ap):
            alibi = (layer == 0)
            NB = S // 128
            with ExitStack() as st_:
                sb = lambda nm, sh, dtt: st_.enter_context(nc.sbuf_tensor(_u() + nm, sh, dtt))
                qTb = sb("at_q", [128, S], BF16)
                kTb = sb("at_k", [128, S], BF16)
                vsb = sb("at_v", [128, NB * 132], BF16)
                acol = sb("at_ac", [128, 32], F32)
                sht = sb("at_sht", [128, 512], F32)
                tri = sb("at_tri", [128, 128], F32)
                Ssb = sb("at_S", [128, 512], F32)
                PT = sb("at_PT", [128, 512], BF16)
                recip = sb("at_rc", [128, 1], F32)
                mcol = sb("at_mc", [128, 1], F32)
                zsb = sb("at_z", [128, 128], F32)
                zbt = sb("at_zb", [128, 512], BF16)
                identf_ = sb("at_idf", [128, 128], F32)
                psS = st_.enter_context(nc.psum_tensor(_u() + "at_psS", [128, 512], F32))
                psTf = st_.enter_context(nc.psum_tensor(_u() + "at_psTf", [128, 128], F32))
                zps = [st_.enter_context(nc.psum_tensor(_u() + f"at_zps{i}", [128, 132], F32)) for i in range(4)]
                lt = p.load(tri[:], d_tri.ap())
                lif = p.load(identf_[:], d_identf.ap())
                p.comp("vector", lambda v: v.memset(mcol[:], -EXP_M))
                p.comp("vector", lambda v: v.memset(
                    vsb[:].rearrange("p (b w) -> p b w", w=132)[:, :, 128:129], 1.0))
                for u in range(4):
                    b, hl = u // 2, u % 2
                    lq = p.load(qTb[:], qtu.ap()[u], queue="sync")
                    lk = p.load(kTb[:], ktu.ap()[u], queue="sync")
                    lvs = []
                    for g_ in range(4):
                        lvs.append(p.load(
                            vsb[:].rearrange("p (b w) -> p b w", w=132)[:, g_ * 8:(g_ + 1) * 8, 0:128],
                            vu.ap()[u][g_ * 1024:(g_ + 1) * 1024, :].rearrange("(b p) w -> p b w", p=128),
                            queue="gpsimd"))
                    la = []
                    if alibi:
                        la = [p.load(acol[:], d_acol[u], queue="sync")]
                    for r in range(8):
                        lrun = []
                        if alibi:
                            lrun.append(p.load(sht[:], d_shift[u][r:r + 1, :].broadcast_to([128, 512])))
                        for kb in range(4 * r + 4):
                            if alibi:
                                p.comp("tensor", lambda t, kb=kb, r=r: t.matmul(
                                    psS[:], kTb[:, kb * 128:(kb + 1) * 128],
                                    qTb[:, r * 512:(r + 1) * 512], start=True, stop=True),
                                    loads=[lq, lk] + lrun)
                                p.comp("vector", lambda v: v.scalar_tensor_tensor(
                                    out=Ssb[:], in0=psS[:], scalar=1024.0, in1=sht[:],
                                    op0=ALU.min, op1=ALU.add))
                                j = kb - 4 * r
                                if j >= 0:
                                    if j > 0:
                                        p.comp("vector", lambda v, j=j: v.memset(Ssb[:, 0:j * 128], -1e30))
                                    p.comp("vector", lambda v, j=j: v.tensor_tensor(
                                        out=Ssb[:, j * 128:(j + 1) * 128], in0=Ssb[:, j * 128:(j + 1) * 128],
                                        in1=tri[:], op=ALU.add), loads=[lt])
                                p.comp("scalar", lambda s, kb=kb: s.activation(
                                    out=PT[:], in_=Ssb[:], func=AF.Exp, bias=acol[:, kb:kb + 1]), loads=la)
                            else:
                                p.comp("tensor", lambda t, kb=kb, r=r: t.matmul(
                                    psS[:], kTb[:, kb * 128:(kb + 1) * 128],
                                    qTb[:, r * 512:(r + 1) * 512], start=True, stop=True), loads=[lq, lk])
                                j = kb - 4 * r
                                if j >= 0:
                                    p.comp("vector", lambda v, j=j: v.tensor_tensor(
                                        out=psS[:, j * 128:(j + 1) * 128], in0=psS[:, j * 128:(j + 1) * 128],
                                        in1=tri[:], op=ALU.add), loads=[lt])
                                p.comp("scalar", lambda s: s.activation(
                                    out=PT[:], in_=psS[:], func=AF.Exp, bias=mcol[:, 0:1]))
                            fns = []
                            for i in range(4):
                                if kb <= 4 * r + i:
                                    fns.append(lambda t, i=i, kb=kb, r=r: t.matmul(
                                        zps[i][:, 0:129], PT[:, i * 128:(i + 1) * 128],
                                        vsb[:].rearrange("p (b w) -> p b w", w=132)[:, kb, 0:129],
                                        start=(kb == 0), stop=(kb == 4 * r + i)))
                            p.comp("tensor", fns, loads=lvs)
                        for i in range(4):
                            p.comp("vector", lambda v, i=i: v.reciprocal(out=recip[:], in_=zps[i][:, 128:129]))
                            p.comp("vector", lambda v, i=i: v.tensor_scalar(
                                out=zsb[:], in0=zps[i][:, 0:128], scalar1=recip[:, 0:1], scalar2=None, op0=ALU.mult))
                            p.comp("tensor", lambda t: t.transpose(psTf[:], zsb[:], identf_[:]), loads=[lif])
                            p.comp("vector", lambda v, i=i: v.tensor_copy(
                                out=zbt[:, i * 128:(i + 1) * 128], in_=psTf[:]))
                        p.store(ztl_ap[hl * 128:(hl + 1) * 128,
                                       b * S + 4 * r * 128: b * S + (4 * r + 4) * 128], zbt[:])

        def ph_oproj_partial(ztl_ap, wo_ap, part_ap):
            with ExitStack() as st_:
                sb = lambda nm, sh, dtt: st_.enter_context(nc.sbuf_tensor(_u() + nm, sh, dtt))
                wsb = sb("op_w", [128, 2 * D], BF16)
                zt = sb("op_z", [128, 2 * 128], BF16)
                osb = sb("op_o", [128, 2048], BF16)
                psA = st_.enter_context(nc.psum_tensor(_u() + "op_ps", [128, 512], F32))
                lw = [p.load(_s3(wsb[:], 2), _g3(wo_ap), queue="gpsimd")]
                for tt in range(T // 128):
                    lz = [p.load(_s3(zt[:], 2), _g3(ztl_ap[:, tt * 128:(tt + 1) * 128]))]
                    for oc in range(4):
                        p.comp("tensor", [lambda t, kt=kt, oc=oc: t.matmul(
                            psA[:], zt[:, kt * 128:(kt + 1) * 128],
                            wsb[:, kt * D + oc * 512: kt * D + (oc + 1) * 512],
                            start=(kt == 0), stop=(kt == 1)) for kt in range(2)], loads=lz + lw)
                        p.comp("vector", lambda v, oc=oc: v.tensor_copy(
                            out=osb[:, oc * 512:(oc + 1) * 512], in_=psA[:]))
                    p.store(part_ap[tt * 128:(tt + 1) * 128, :], osb[:])

        def ph_rs(part_ap, rs_ap):
            p.comp("gpsimd", lambda g_: g_.collective_compute(
                "ReduceScatter", ALU.add, replica_groups=RG,
                ins=[part_ap.opt()], outs=[rs_ap.opt()]))

        def ph_resid(x_src, add_aps, out_ap, bo_bcast=False):
            with ExitStack() as st_:
                sb = lambda nm, sh, dtt: st_.enter_context(nc.sbuf_tensor(_u() + nm, sh, dtt))
                acc = sb("rd_acc", [128, D], F32)
                pt_ = sb("rd_pt", [128, D], F32)
                bob = sb("rd_bo", [128, D], F32)
                lbo = []
                if bo_bcast:
                    lbo = [p.load(bob[:], d_bo1.broadcast_to([128, D]), queue="sync")]
                for tt in range(NT):
                    lx = p.load(acc[:], x_src[tt * 128:(tt + 1) * 128, :])
                    first = [lx]
                    for (aap, isbf) in add_aps:
                        lp = p.load(pt_[:], aap[tt * 128:(tt + 1) * 128, :], queue="gpsimd")
                        p.comp("vector", lambda v: v.tensor_tensor(out=acc[:], in0=acc[:], in1=pt_[:], op=ALU.add),
                               loads=[lp] + first)
                        first = []
                    if bo_bcast:
                        p.comp("vector", lambda v: v.tensor_tensor(out=acc[:], in0=acc[:], in1=bob[:], op=ALU.add),
                               loads=lbo)
                    p.store(out_ap[tt * 128:(tt + 1) * 128, :], acc[:])

        def ph_ffn0(g_ap):
            with ExitStack() as st_:
                sb = lambda nm, sh, dtt: st_.enter_context(nc.sbuf_tensor(_u() + nm, sh, dtt))
                dec = {nm: sb("f_" + nm, [128, 2048], I16) for nm in ("pk16", "T16", "n16", "a1", "a2", "z16", "w16", "alt")}
                val = sb("f_val", [128, 2048], BF16)
                scsb = sb("f_sc", [128, 64], F32)
                scb = sb("f_scb", [128, 64], BF16)
                ident = sb("f_id", [128, 128], BF16)
                wgT = sb("f_wg", [128, 16 * 2048], BF16)
                wuT = sb("f_wu", [128, 16 * 2048], BF16)
                ysb = sb("f_y", [128, 16 * 512], BF16)
                g32 = sb("f_g32", [128, 512], F32)
                t3 = sb("f_t3", [128, 512], F32)
                t4 = sb("f_t4", [128, 512], F32)
                sgm = sb("f_sgm", [128, 512], F32)
                w32 = sb("f_w32", [128, 512], F32)
                inter = sb("f_int", [128, 512], BF16)
                osb = sb("f_o", [128, 2048], BF16)
                psG = st_.enter_context(nc.psum_tensor(_u() + "f_psG", [128, 512], F32))
                psU = st_.enter_context(nc.psum_tensor(_u() + "f_psU", [128, 512], F32))
                psT_ = st_.enter_context(nc.psum_tensor(_u() + "f_psT", [128, 128], BF16))
                li = p.load(ident[:], d_identb.ap())
                lalt = p.load(dec["alt"][:], d_alt.ap().broadcast_to([128, 2048]))

                def decode_into(dst, d_pk, d_sc, ntile):
                    lpk = p.load(dec["pk16"][:], d_pk[ntile * 128:(ntile + 1) * 128, :], queue="gpsimd")
                    lsc = p.load(scsb[:], d_sc[ntile * 128:(ntile + 1) * 128, :], queue="gpsimd")
                    emit_decode(p, dec, [lpk, lalt], val[:])
                    p.comp("vector", lambda v: v.tensor_copy(out=scb[:], in_=scsb[:]), loads=[lsc])
                    vv = val[:].rearrange("p (c q) -> p c q", q=32)
                    sbb = scb[:].unsqueeze(2).broadcast_to([128, 64, 32])
                    p.comp("vector", lambda v: v.tensor_tensor(out=vv, in0=vv, in1=sbb, op=ALU.mult))
                    for ht in range(16):
                        p.comp("tensor", lambda t, ht=ht: t.transpose(
                            psT_[:], val[:, ht * 128:(ht + 1) * 128], ident[:]), loads=[li])
                        p.comp("vector", lambda v, ht=ht, ntile=ntile: v.tensor_copy(
                            out=dst[:, ht * 2048 + ntile * 128: ht * 2048 + (ntile + 1) * 128], in_=psT_[:]))

                for ntile in range(16):
                    decode_into(wgT, d_gpk, d_gsc, ntile)
                    decode_into(wuT, d_upk, d_usc, ntile)
                for tc in range(16):
                    r, sub = tc // 2, tc % 2
                    ly = [p.load(_s3(ysb[:], 16), _g3(g_ap[r * D:(r + 1) * D, sub * 512:(sub + 1) * 512]))]
                    for ntile in range(16):
                        p.comp("tensor", [lambda t, ht=ht, ntile=ntile: t.matmul(
                            psG[:], wgT[:, ht * 2048 + ntile * 128: ht * 2048 + (ntile + 1) * 128],
                            ysb[:, ht * 512:(ht + 1) * 512], start=(ht == 0), stop=(ht == 15)) for ht in range(16)], loads=ly)
                        p.comp("tensor", [lambda t, ht=ht, ntile=ntile: t.matmul(
                            psU[:], wuT[:, ht * 2048 + ntile * 128: ht * 2048 + (ntile + 1) * 128],
                            ysb[:, ht * 512:(ht + 1) * 512], start=(ht == 0), stop=(ht == 15)) for ht in range(16)])
                        p.comp("vector", lambda v: v.tensor_copy(out=g32[:], in_=psG[:]))
                        p.comp("scalar", lambda s: s.activation(out=t3[:], in_=g32[:], func=AF.Square))
                        p.comp("vector", lambda v: v.tensor_scalar(out=t3[:], in0=t3[:], scalar1=GELU_C, scalar2=1.0, op0=ALU.mult, op1=ALU.add))
                        p.comp("vector", lambda v: v.tensor_tensor(out=t4[:], in0=g32[:], in1=t3[:], op=ALU.mult))
                        p.comp("scalar", lambda s: s.activation(out=sgm[:], in_=t4[:], func=AF.Sigmoid, scale=GELU_A))
                        p.comp("vector", lambda v: v.tensor_tensor(out=w32[:], in0=g32[:], in1=sgm[:], op=ALU.mult))
                        p.comp("vector", lambda v: v.scalar_tensor_tensor(out=inter[:], in0=w32[:], scalar=1.0, in1=psU[:], op0=ALU.mult, op1=ALU.mult))
                        p.store(interT.ap()[ntile * 128:(ntile + 1) * 128, tc * 512:(tc + 1) * 512], inter[:])
                for ot in range(16):
                    lpk = p.load(dec["pk16"][:], d_dpk[ot * 128:(ot + 1) * 128, :], queue="gpsimd")
                    lsc = p.load(scsb[:], d_dsc[ot * 128:(ot + 1) * 128, :], queue="gpsimd")
                    emit_decode(p, dec, [lpk, lalt], val[:])
                    p.comp("vector", lambda v: v.tensor_copy(out=scb[:], in_=scsb[:]), loads=[lsc])
                    vv = val[:].rearrange("p (c q) -> p c q", q=32)
                    sbb = scb[:].unsqueeze(2).broadcast_to([128, 64, 32])
                    p.comp("vector", lambda v: v.tensor_tensor(out=vv, in0=vv, in1=sbb, op=ALU.mult))
                    for nt in range(16):
                        p.comp("tensor", lambda t, nt=nt: t.transpose(
                            psT_[:], val[:, nt * 128:(nt + 1) * 128], ident[:]), loads=[li])
                        p.comp("vector", lambda v, nt=nt, ot=ot: v.tensor_copy(
                            out=wgT[:, nt * 2048 + ot * 128: nt * 2048 + (ot + 1) * 128], in_=psT_[:]))
                for tc in range(16):
                    lis = [p.load(_s3(ysb[:], 16), _g3(interT.ap()[:, tc * 512:(tc + 1) * 512]))]
                    for tsub in range(4):
                        for oc in range(4):
                            p.comp("tensor", [lambda t, nt=nt, tsub=tsub, oc=oc: t.matmul(
                                psG[:], ysb[:, nt * 512 + tsub * 128: nt * 512 + (tsub + 1) * 128],
                                wgT[:, nt * 2048 + oc * 512: nt * 2048 + (oc + 1) * 512],
                                start=(nt == 0), stop=(nt == 15)) for nt in range(16)], loads=lis)
                            p.comp("vector", lambda v, oc=oc: v.tensor_copy(
                                out=osb[:, oc * 512:(oc + 1) * 512], in_=psG[:]))
                        p.store(part[1].ap()[tc * 512 + tsub * 128: tc * 512 + (tsub + 1) * 128, :], osb[:])

        def ph_moe():
            NH = MH // 128
            with ExitStack() as st_:
                sb = lambda nm, sh, dtt: st_.enter_context(nc.sbuf_tensor(_u() + nm, sh, dtt))
                ysb = sb("m_y", [128, 16 * 512], BF16)
                yf = sb("m_yf", [128, 16 * 128], F32)
                gw = sb("m_gw", [128, 16 * 16], F32)
                comb = sb("m_cb", [128, 64 * 2], F32)
                combt = sb("m_cbt", [128, 16], F32)
                eselt = sb("m_es", [128, 2 * 16], F32)
                ssb = sb("m_s", [128, 16], F32)
                r4 = sb("m_r4", [128, 4], F32)
                r4b = sb("m_r4b", [128, 4], F32)
                r16 = sb("m_r16", [128, 16], F32)
                r1 = sb("m_r1", [128, 8], F32)
                w1sb = sb("m_w1", [128, 16 * MH], BF16)
                w2sb = sb("m_w2", [128, NH * D], BF16)
                asb = sb("m_a", [128, NH * 512], BF16)
                accb = sb("m_accb", [128, 2048], BF16)
                accf = sb("m_accf", [128, 2048], BF16)
                accs = sb("m_accs", [128, 2048], F32)
                e0t = sb("m_e0t", [128, 512], F32)
                psR = st_.enter_context(nc.psum_tensor(_u() + "m_psR", [128, 16], F32))
                psA = st_.enter_context(nc.psum_tensor(_u() + "m_psA", [128, 512], F32))
                psO = st_.enter_context(nc.psum_tensor(_u() + "m_psO", [128, 512], F32))
                lsel = [p.load(eselt[:, e * 16:(e + 1) * 16], d_esel[e].broadcast_to([128, 16])) for e in range(2)]
                lgw = [p.load(gw[:, kt * 16:(kt + 1) * 16], d_gw[kt * 128:(kt + 1) * 128, :]) for kt in range(16)]
                for tt in range(64):
                    r, coff = tt // 8, (tt % 8) * 128
                    lyt = p.load(_s3(yf[:], 16), _g3(G32.ap()[r * D:(r + 1) * D, coff:coff + 128]))
                    p.comp("tensor", [lambda t, kt=kt: t.matmul(
                        psR[:], yf[:, kt * 128:(kt + 1) * 128], gw[:, kt * 16:(kt + 1) * 16],
                        start=(kt == 0), stop=(kt == 15)) for kt in range(16)],
                        loads=[lyt] + (lgw if tt == 0 else []))
                    p.comp("scalar", lambda s: s.activation(out=ssb[:], in_=psR[:], func=AF.Sigmoid))
                    sg = ssb[:].rearrange("p (g q) -> p g q", q=4)
                    p.comp("vector", lambda v: v.tensor_reduce(out=r4[:], in_=sg, axis=AX.X, op=ALU.max))
                    bc4 = r4[:].unsqueeze(2).broadcast_to([128, 4, 4])
                    p.comp("vector", lambda v: v.tensor_tensor(out=r16[:].rearrange("p (g q) -> p g q", q=4), in0=sg, in1=bc4, op=ALU.is_equal))
                    p.comp("vector", lambda v: v.scalar_tensor_tensor(out=r16[:], in0=r16[:], scalar=-1e30, in1=ssb[:], op0=ALU.mult, op1=ALU.add))
                    p.comp("vector", lambda v: v.tensor_reduce(out=r4b[:], in_=r16[:].rearrange("p (g q) -> p g q", q=4), axis=AX.X, op=ALU.max))
                    p.comp("vector", lambda v: v.tensor_tensor(out=r4[:], in0=r4[:], in1=r4b[:], op=ALU.add))
                    p.comp("vector", lambda v: v.tensor_reduce(out=r1[:, 0:1], in_=r4[:], axis=AX.X, op=ALU.max))
                    p.comp("vector", lambda v: v.tensor_scalar(out=r4b[:], in0=r4[:], scalar1=r1[:, 0:1], scalar2=None, op0=ALU.is_equal))
                    p.comp("vector", lambda v: v.scalar_tensor_tensor(out=r4b[:], in0=r4b[:], scalar=-1e30, in1=r4[:], op0=ALU.mult, op1=ALU.add))
                    p.comp("vector", lambda v: v.tensor_reduce(out=r1[:, 1:2], in_=r4b[:], axis=AX.X, op=ALU.max))
                    p.comp("vector", lambda v: v.tensor_scalar(out=r4b[:], in0=r4[:], scalar1=r1[:, 1:2], scalar2=None, op0=ALU.is_ge))
                    bc4b = r4b[:].unsqueeze(2).broadcast_to([128, 4, 4])
                    p.comp("vector", lambda v: v.tensor_tensor(out=r16[:].rearrange("p (g q) -> p g q", q=4), in0=sg, in1=bc4b, op=ALU.mult))
                    p.comp("vector", lambda v: v.tensor_reduce(out=r1[:, 2:3], in_=r16[:], axis=AX.X, op=ALU.max))
                    p.comp("vector", lambda v: v.tensor_scalar(out=ssb[:], in0=r16[:], scalar1=r1[:, 2:3], scalar2=None, op0=ALU.is_equal))
                    p.comp("vector", lambda v: v.scalar_tensor_tensor(out=ssb[:], in0=ssb[:], scalar=-1e30, in1=r16[:], op0=ALU.mult, op1=ALU.add))
                    p.comp("vector", lambda v: v.tensor_reduce(out=r1[:, 3:4], in_=ssb[:], axis=AX.X, op=ALU.max))
                    p.comp("vector", lambda v: v.tensor_tensor(out=r1[:, 4:5], in0=r1[:, 2:3], in1=r1[:, 3:4], op=ALU.add))
                    p.comp("vector", lambda v: v.tensor_scalar(out=r1[:, 4:5], in0=r1[:, 4:5], scalar1=1e-20, scalar2=None, op0=ALU.add))
                    p.comp("vector", lambda v: v.reciprocal(out=r1[:, 5:6], in_=r1[:, 4:5]))
                    p.comp("vector", lambda v: v.tensor_scalar(out=ssb[:], in0=r16[:], scalar1=r1[:, 3:4], scalar2=None, op0=ALU.is_ge))
                    p.comp("vector", lambda v: v.tensor_tensor(out=ssb[:], in0=ssb[:], in1=r16[:], op=ALU.mult))
                    p.comp("vector", lambda v: v.tensor_scalar(out=ssb[:], in0=ssb[:], scalar1=r1[:, 5:6], scalar2=None, op0=ALU.mult))
                    for e in range(2):
                        p.comp("vector", lambda v, e=e: v.tensor_tensor(out=combt[:], in0=ssb[:], in1=eselt[:, e * 16:(e + 1) * 16], op=ALU.mult), loads=[lsel[e]])
                        p.comp("vector", lambda v, e=e, tt=tt: v.tensor_reduce(out=comb[:, tt * 2 + e: tt * 2 + e + 1], in_=combt[:], axis=AX.X, op=ALU.add))

                def expert_pass(d_w1src, d_w2src, n_tc, ysrc, e, mode):
                    lw1 = [p.load(_s3(w1sb[:], 16), _g3(d_w1src), queue="gpsimd")]
                    lw2 = [p.load(_s3(w2sb[:], NH), _g3(d_w2src), queue="gpsimd")]
                    for tc in range(n_tc):
                        if ysrc == "G32":
                            r, sub = tc // 2, tc % 2
                            ly = [p.load(_s3(ysb[:], 16), _g3(G32.ap()[r * D:(r + 1) * D, sub * 512:(sub + 1) * 512]), queue="gpsimd")]
                        else:
                            ly = [p.load(_s3(ysb[:], 16), _g3(yTl32.ap()[:, tc * 512:(tc + 1) * 512]), queue="gpsimd")]
                        for ht in range(NH):
                            p.comp("tensor", [lambda t, kt=kt, ht=ht: t.matmul(
                                psA[:], w1sb[:, kt * MH + ht * 128: kt * MH + (ht + 1) * 128],
                                ysb[:, kt * 512:(kt + 1) * 512], start=(kt == 0), stop=(kt == 15))
                                for kt in range(16)], loads=ly + lw1)
                            p.comp("scalar", lambda s, ht=ht: s.activation(
                                out=asb[:, ht * 512:(ht + 1) * 512], in_=psA[:], func=AF.Silu))
                        for tsub in range(4):
                            for oc in range(4):
                                p.comp("tensor", [lambda t, ht=ht, tsub=tsub, oc=oc: t.matmul(
                                    psO[:], asb[:, ht * 512 + tsub * 128: ht * 512 + (tsub + 1) * 128],
                                    w2sb[:, ht * D + oc * 512: ht * D + (oc + 1) * 512],
                                    start=(ht == 0), stop=(ht == NH - 1)) for ht in range(NH)], loads=lw2)
                                row0 = tc * 512 + tsub * 128
                                if mode == 0:
                                    cap = comb[:, (tc * 4 + tsub) * 2 + 0: (tc * 4 + tsub) * 2 + 1]
                                    p.comp("vector", lambda v, cap=cap, oc=oc: v.tensor_scalar(
                                        out=accb[:, oc * 512:(oc + 1) * 512], in0=psO[:],
                                        scalar1=cap, scalar2=None, op0=ALU.mult))
                                elif mode == 1:
                                    cap = comb[:, (tc * 4 + tsub) * 2 + 1: (tc * 4 + tsub) * 2 + 2]
                                    le0 = p.load(e0t[:], e0p.ap()[row0:row0 + 128, oc * 512:(oc + 1) * 512],
                                                 queue="gpsimd")
                                    p.comp("vector", lambda v, cap=cap, oc=oc: v.scalar_tensor_tensor(
                                        out=accf[:, oc * 512:(oc + 1) * 512], in0=psO[:], scalar=cap,
                                        in1=e0t[:], op0=ALU.mult, op1=ALU.add), loads=[le0])
                                else:
                                    p.comp("vector", lambda v, oc=oc: v.tensor_copy(
                                        out=accs[:, oc * 512:(oc + 1) * 512], in_=psO[:]))
                            if mode == 0:
                                p.store(e0p.ap()[row0:row0 + 128, :], accb[:])
                            elif mode == 1:
                                p.store(part[3].ap()[row0:row0 + 128, :], accf[:])
                            else:
                                p.store(shl.ap()[row0:row0 + 128, :], accs[:])

                expert_pass(d_w1[0], d_w2[0], 16, "G32", 0, 0)
                expert_pass(d_w1[1], d_w2[1], 16, "G32", 1, 1)
                expert_pass(d_sw1, d_sw2, TPC // 512, "local", 0, 2)

        cur_x = d_x0
        if upto >= 0:
            ph_ln_ag(d_x0, yTl.ap(), G[0].ap(), False)
        if upto >= 1:
            ph_qkv_units0(G[0].ap(), d_wq0, d_wk0, d_wv0, 1.0 / math.sqrt(DH))
            ph_attn(0, zTl[0].ap())
        if upto >= 2:
            ph_oproj_partial(zTl[0].ap(), d_wo0, part[0].ap())
            ph_rs(part[0].ap(), rs[0].ap())
            ph_resid(d_x0, [(rs[0].ap(), True)], xsc[0].ap())
            cur_x = xsc[0].ap()
        if upto >= 3:
            ph_ln_ag(xsc[0].ap(), yTl.ap(), G[1].ap(), False)
        if upto >= 4:
            ph_ffn0(G[1].ap())
            ph_rs(part[1].ap(), rs[1].ap())
            ph_resid(xsc[0].ap(), [(rs[1].ap(), True)], xsc[1].ap())
            cur_x = xsc[1].ap()
        if upto >= 5:
            ph_ln_ag(xsc[1].ap(), yTl.ap(), G[2].ap(), False)
        if upto >= 6:
            ph_qkv_units1(G[2].ap())
            ph_attn(1, zTl[1].ap())
        if upto >= 7:
            ph_oproj_partial(zTl[1].ap(), d_wo1, part[2].ap())
            ph_rs(part[2].ap(), rs[2].ap())
            ph_resid(xsc[1].ap(), [(rs[2].ap(), True)], xsc[2].ap(), bo_bcast=True)
            cur_x = xsc[2].ap()
        if upto >= 8:
            ph_ln_ag(xsc[2].ap(), yTl32.ap(), G32.ap(), True)
        if upto >= 9:
            ph_moe()
            ph_rs(part[3].ap(), rs[3].ap())
            ph_resid(xsc[2].ap(), [(rs[3].ap(), True), (shl.ap(), False)], d_out)
        if upto < 9:
            with ExitStack() as st_:
                cpy = st_.enter_context(nc.sbuf_tensor(_u() + "dbgcpy", [128, D], F32))
                cpy2 = st_.enter_context(nc.sbuf_tensor(_u() + "dbgcpy2", [128, D], F32))
                for tt in range(NT):
                    lx = p.load(cpy[:], cur_x[tt * 128:(tt + 1) * 128, :])
                    p.comp("vector", lambda v: v.tensor_copy(out=cpy2[:], in_=cpy[:]), loads=[lx])
                    p.store(d_out[tt * 128:(tt + 1) * 128, :], cpy2[:])

        p.build(block, ctx.sems)
    return nc


def fused_in_maps(inputs):
    xf = np.ascontiguousarray(np.asarray(inputs["L0_attn_resid_pre"], dtype=np.float32).reshape(T, D))
    Wq0T, Wk0T, Wv0T = (_t(inputs["L0_attn_Wq"]), _t(inputs["L0_attn_Wk"]), _t(inputs["L0_attn_Wv"]))
    Wo0T = _t(inputs["L0_attn_Wo"])
    Wq1T, Wk1T, Wv1T = (_t(inputs["L1_attn_Wq"]), _t(inputs["L1_attn_Wk"]), _t(inputs["L1_attn_Wv"]))
    Wo1T = _t(inputs["L1_attn_Wo"])
    bq1 = np.asarray(inputs["L1_attn_bq"], dtype=np.float32)
    bk1 = np.asarray(inputs["L1_attn_bk"], dtype=np.float32)
    bv1 = np.asarray(inputs["L1_attn_bv"], dtype=np.float32)
    bo1 = np.asarray(inputs["L1_attn_bo"], dtype=np.float32).reshape(1, D)
    gpk = np.asarray(inputs["L0_ffn_gate_W_packed"], dtype=np.int32)
    upk = np.asarray(inputs["L0_ffn_up_W_packed"], dtype=np.int32)
    dpk = np.asarray(inputs["L0_ffn_down_W_packed"], dtype=np.int32)
    gsc = np.asarray(inputs["L0_ffn_gate_scales"], dtype=np.float32)
    usc = np.asarray(inputs["L0_ffn_up_scales"], dtype=np.float32)
    dsc = np.asarray(inputs["L0_ffn_down_scales"], dtype=np.float32)
    gwT = _t(inputs["L1_ffn_gate_w"])
    w1m = np.asarray(inputs["L1_ffn_w1"], dtype=np.float32)
    w2m = np.asarray(inputs["L1_ffn_w2"], dtype=np.float32)
    sw1 = np.ascontiguousarray(np.asarray(inputs["L1_ffn_sw1"], dtype=np.float32)[0])
    sw2 = np.ascontiguousarray(np.asarray(inputs["L1_ffn_sw2"], dtype=np.float32)[0])
    half = DH // 2
    inv = 1.0 / (10000.0 ** (np.arange(half, dtype=np.float32) / half))
    fr = np.arange(S, dtype=np.float32)[:, None] * inv[None, :]
    cosS = np.ascontiguousarray(np.repeat(np.cos(fr), 2, axis=1).astype(np.float32))
    sinS = np.ascontiguousarray(np.repeat(np.sin(fr), 2, axis=1).astype(np.float32))
    pos = np.arange(S, dtype=np.float32)
    maps = []
    for c in range(NCORES):
        n0 = c * NPC
        hc = slice(2 * c * 128, (2 * c + 2) * 128)
        units = [(b, h) for b in range(B) for h in (2 * c, 2 * c + 1)]
        acolu = np.stack([(SLOPES[h] * pos - EXP_M).reshape(32, 128).T.copy() for (b, h) in units])
        shiftu = np.stack([
            np.stack([-SLOPES[h] * (r * 512 + np.arange(512, dtype=np.float32)) for r in range(8)])
            for (b, h) in units])
        esel = np.zeros((2, 1, 16), dtype=np.float32)
        esel[0, 0, 2 * c] = 1.0
        esel[1, 0, 2 * c + 1] = 1.0
        maps.append(dict(
            x0=np.ascontiguousarray(xf[c * TPC:(c + 1) * TPC]),
            wq0=np.ascontiguousarray(Wq0T[:, hc]), wk0=np.ascontiguousarray(Wk0T[:, hc]),
            wv0=np.ascontiguousarray(Wv0T[:, hc]), wo0=np.ascontiguousarray(Wo0T[hc, :]),
            acolu=np.ascontiguousarray(acolu, dtype=np.float32),
            shiftu=np.ascontiguousarray(shiftu, dtype=np.float32),
            ubu=np.ascontiguousarray(1024.0 + shiftu, dtype=np.float32),
            lbu=np.ascontiguousarray(-1024.0 + shiftu, dtype=np.float32),
            gpk=np.ascontiguousarray(gpk[n0:n0 + NPC]).view(np.int16),
            upk=np.ascontiguousarray(upk[n0:n0 + NPC]).view(np.int16),
            gsc=np.ascontiguousarray(gsc[n0:n0 + NPC]),
            usc=np.ascontiguousarray(usc[n0:n0 + NPC]),
            dpk=np.ascontiguousarray(dpk[:, n0 // 2:(n0 + NPC) // 2]).view(np.int16),
            dsc=np.ascontiguousarray(dsc[:, n0 // 32:(n0 + NPC) // 32]),
            wq1=np.ascontiguousarray(Wq1T[:, hc]), wk1=np.ascontiguousarray(Wk1T[:, hc]),
            wv1=np.ascontiguousarray(Wv1T[:, hc]),
            bq1=np.ascontiguousarray(bq1[hc]).reshape(1, 256),
            bk1=np.ascontiguousarray(bk1[hc]).reshape(1, 256),
            bv1=np.ascontiguousarray(bv1[hc]).reshape(1, 256),
            wo1=np.ascontiguousarray(Wo1T[hc, :]), bo1=bo1,
            cosS=cosS, sinS=sinS,
            gwT=gwT,
            w1=np.ascontiguousarray(w1m[2 * c:2 * c + 2]),
            w2=np.ascontiguousarray(w2m[2 * c:2 * c + 2]),
            sw1=sw1, sw2=sw2, esel=esel,
        ))
    return maps


def kernel_fused(upto=9, **inputs):
    nc = build_fused(upto)
    res = launch(nc, fused_in_maps(inputs))
    out = np.concatenate([r["o_x4"] for r in res.results])
    return out.reshape(B, S, D).astype(np.float32)


def kernel(**inputs):
    return kernel_fused(9, **inputs)


# version 05
# speedup vs baseline: 2.2851x; 1.4114x over previous
import numpy as np
import ml_dtypes
import math
from contextlib import ExitStack

import concourse.bass as bass
import concourse.mybir as mybir
from concourse.bass_utils import run_bass_kernel_spmd

F32 = mybir.dt.float32
BF16 = mybir.dt.bfloat16
I16 = mybir.dt.int16
AF = mybir.ActivationFunctionType
ALU = mybir.AluOpType
AX = mybir.AxisListType
BF = ml_dtypes.bfloat16

NCORES = 8
B, S, D, H, DH = 2, 4096, 2048, 16, 128
T = B * S
TPC = T // NCORES
NMLP = 16384
NPC = NMLP // NCORES
MH = 1408
EXP_M = 40.0
GELU_A = 1.5957691216057308
GELU_C = 0.044715

def alibi_slopes(n):
    def pow2(m):
        start = 2 ** (-(2 ** -(math.log2(m) - 3)))
        return [start * start ** i for i in range(m)]
    if math.log2(n).is_integer():
        return pow2(n)
    cc = 2 ** math.floor(math.log2(n))
    return pow2(cc) + alibi_slopes(2 * cc)[0::2][: n - cc]

SLOPES = np.array(alibi_slopes(H), dtype=np.float32)


class Prog:
    ENGS = ("sync", "gpsimd", "vector", "scalar", "tensor")
    ENG2SEM = {"vector": "eV", "scalar": "eS", "tensor": "eT", "gpsimd": "eG"}
    SEMS = ("eV", "eS", "eT", "eG", "cc", "ld_sync", "ld_gpsimd", "st")

    def __init__(self, nc):
        self.nc = nc
        self.ops = {e: [] for e in self.ENGS}
        self.nld = {"sync": 0, "gpsimd": 0}
        self.ec = {"eV": 0, "eS": 0, "eT": 0, "eG": 0, "cc": 0}
        self.nst = 0
        self.last = None

    def _emit_waits(self, eng, ename, sems, st, deps):
        for (semn, val) in deps:
            if val > st[ename].get(semn, 0):
                eng.wait_ge(sems[semn], val)
                st[ename][semn] = val

    def _norm_deps(self, deps, loads):
        if deps is None:
            mydeps = [self.last] if self.last is not None else []
        else:
            mydeps = [d for d in deps if d is not None]
        mydeps = [(("ld_" + s, 16 * v) if s in ("sync", "gpsimd") else (s, v))
                  for (s, v) in mydeps]
        for (q, tok) in loads:
            mydeps.append(("ld_" + q, 16 * tok))
        return mydeps

    def load(self, out_ap, in_ap, queue="sync", deps=None):
        tok = self.nld[queue] + 1
        self.nld[queue] = tok
        mydeps = self._norm_deps(deps, ())
        if tok > 1:
            mydeps.append(("ld_" + queue, 16 * (tok - 1)))

        def emit(eng, sems, st):
            self._emit_waits(eng, queue, sems, st, mydeps)
            eng.dma_start(out=out_ap, in_=in_ap).then_inc(sems["ld_" + queue], 16)
        self.ops[queue].append(emit)
        self.last = ("ld_" + queue, 16 * tok)
        return (queue, tok)

    def comp(self, ename, fns, loads=(), deps=None, collective=False):
        if callable(fns):
            fns = [fns]
        lane = "cc" if collective else self.ENG2SEM[ename]
        mydeps = self._norm_deps(deps, loads)
        self.ec[lane] += 1
        cval = self.ec[lane]

        def emit(eng, sems, st):
            self._emit_waits(eng, ename, sems, st, mydeps)
            last = None
            for fn in fns:
                last = fn(eng)
            last.then_inc(sems[lane], 1)
        self.ops[ename].append(emit)
        self.last = (lane, cval)
        return self.last

    def store(self, out_ap, in_ap, queue="gpsimd", deps=None):
        self.nst += 1
        tok = self.nst
        mydeps = self._norm_deps(deps, ())
        if tok > 1:
            mydeps.append(("st", 16 * (tok - 1)))

        def emit(eng, sems, st):
            self._emit_waits(eng, queue, sems, st, mydeps)
            eng.dma_start(out=out_ap, in_=in_ap).then_inc(sems["st"], 16)
        self.ops[queue].append(emit)
        self.last = ("st", 16 * tok)
        return ("st", 16 * tok)

    def build(self, block, sems):
        finals = [(k, v) for k, v in self.ec.items() if v] +                  [("st", 16 * self.nst)] +                  [("ld_" + q, 16 * n) for q, n in self.nld.items() if n]

        def closer(eng, sems2, st2):
            for semn, val in finals:
                if val > st2["gpsimd"].get(semn, 0):
                    eng.wait_ge(sems2[semn], val)
                    st2["gpsimd"][semn] = val
        self.ops["gpsimd"].append(closer)
        st = {e: {} for e in self.ENGS}
        for ename in self.ENGS:
            section = getattr(block, ename)
            oplist = self.ops[ename]

            def body(eng, oplist=oplist):
                for op in oplist:
                    op(eng, sems, st)
            section(body)


class Ctx:
    def __init__(self, nc, stack):
        self.nc = nc
        self.stack = stack
        self.sems = {}
        for name in Prog.SEMS:
            self.sems[name] = stack.enter_context(nc.semaphore(name))

    def sb(self, name, shape, dtype):
        return self.stack.enter_context(self.nc.sbuf_tensor(name, shape, dtype))

    def ps(self, name, shape, dtype=F32):
        return self.stack.enter_context(self.nc.psum_tensor(name, shape, dtype))


def new_nc():
    return bass.Bass("TRN2", target_bir_lowering=False)


def launch(nc, in_maps, **kw):
    return run_bass_kernel_spmd(nc, in_maps, core_ids=list(range(NCORES)), **kw)


def emit_ln(p, x_tile, y_out, xc, sq, stats, eps_ap, loads=()):
    nm, var, sd, rstd = (stats[:, 0:1], stats[:, 1:2], stats[:, 2:3], stats[:, 3:4])
    p.comp("vector", lambda v: v.tensor_reduce(out=nm, in_=x_tile, axis=AX.X, op=ALU.add, negate=True), loads=loads)
    p.comp("vector", lambda v: v.tensor_scalar(out=nm, in0=nm, scalar1=1.0 / D, scalar2=None, op0=ALU.mult))
    p.comp("vector", lambda v: v.tensor_scalar(out=xc, in0=x_tile, scalar1=nm, scalar2=None, op0=ALU.add))
    p.comp("scalar", lambda s: s.activation(out=sq, in_=xc, func=AF.Square, accum_out=var))
    p.comp("scalar", lambda s: s.activation(out=sd, in_=var, func=AF.Sqrt, bias=eps_ap, scale=1.0 / D))
    p.comp("vector", lambda v: v.reciprocal(out=rstd, in_=sd))
    p.comp("vector", lambda v: v.tensor_scalar(out=y_out, in0=xc, scalar1=rstd, scalar2=None, op0=ALU.mult))


def emit_decode(p, t, pk_load, val_out):
    pk, T16, n16, a1, a2, z16, w16, alt = (t["pk16"], t["T16"], t["n16"],
                                           t["a1"], t["a2"], t["z16"], t["w16"], t["alt"])
    ev = lambda x: x.rearrange("p (f two) -> p f two", two=2)[:, :, 0:1]
    od = lambda x: x.rearrange("p (f two) -> p f two", two=2)[:, :, 1:2]
    p.comp("vector", lambda v: v.tensor_copy(out=T16[:], in_=pk[:]), loads=pk_load)
    p.comp("vector", lambda v: v.tensor_copy(out=od(T16[:]), in_=ev(pk[:])))
    p.comp("vector", lambda v: v.tensor_tensor(out=n16[:], in0=T16[:], in1=alt[:], op=ALU.logical_shift_right))
    p.comp("vector", lambda v: v.tensor_scalar(out=a1[:], in0=n16[:], scalar1=7, scalar2=6, op0=ALU.bitwise_and, op1=ALU.logical_shift_left))
    p.comp("vector", lambda v: v.tensor_scalar(out=a2[:], in0=n16[:], scalar1=8, scalar2=12, op0=ALU.bitwise_and, op1=ALU.logical_shift_left))
    p.comp("vector", lambda v: v.tensor_scalar(out=T16[:], in0=a1[:], scalar1=0x3F00, scalar2=None, op0=ALU.add))
    p.comp("vector", lambda v: v.tensor_tensor(out=T16[:], in0=T16[:], in1=a2[:], op=ALU.bitwise_or))
    p.comp("vector", lambda v: v.tensor_scalar(out=z16[:], in0=n16[:], scalar1=6, scalar2=None, op0=ALU.bitwise_and))
    p.comp("vector", lambda v: v.tensor_scalar(out=z16[:], in0=z16[:], scalar1=0, scalar2=None, op0=ALU.is_equal))
    p.comp("vector", lambda v: v.tensor_scalar(out=a1[:], in0=a2[:], scalar1=0x3F00, scalar2=None, op0=ALU.bitwise_or))
    p.comp("vector", lambda v: v.tensor_scalar(out=w16[:], in0=n16[:], scalar1=1, scalar2=7, op0=ALU.bitwise_and, op1=ALU.logical_shift_left))
    p.comp("vector", lambda v: v.tensor_tensor(out=a1[:], in0=a1[:], in1=w16[:], op=ALU.subtract))
    p.comp("vector", lambda v: v.tensor_tensor(out=z16[:], in0=z16[:], in1=a1[:], op=ALU.mult))
    p.comp("vector", lambda v: v.tensor_tensor(out=val_out, in0=T16[:].bitcast(BF16), in1=z16[:].bitcast(BF16), op=ALU.subtract))


def build_qkv(layer):
    nc = new_nc()
    rope = (layer == 1)
    d_x = nc.dram_tensor("x", [TPC, D], F32, kind="ExternalInput").ap()
    d_w = {n: nc.dram_tensor(n, [D, D], F32, kind="ExternalInput").ap() for n in ("wqT", "wkT", "wvT")}
    if rope:
        d_b = {n: nc.dram_tensor(n, [1, D], F32, kind="ExternalInput").ap() for n in ("bq", "bk", "bv")}
        d_cos = nc.dram_tensor("cos4", [TPC, 512], F32, kind="ExternalInput").ap()
        d_sin = nc.dram_tensor("sin4", [TPC, 512], F32, kind="ExternalInput").ap()
    d_out = {n: nc.dram_tensor("o_" + n, [TPC, D], F32, kind="ExternalOutput").ap() for n in ("q", "k", "v")}
    d_ident = nc.inline_tensor(np.eye(128, dtype=BF), "identbf")

    NT = TPC // 128
    with ExitStack() as stack:
        ctx = Ctx(nc, stack)
        p = Prog(nc)
        xsb = ctx.sb("xsb", [128, D], F32)
        xc = ctx.sb("xc", [128, D], F32)
        sq = ctx.sb("sq", [128, D], F32)
        stats = ctx.sb("stats", [128, 4], F32)
        epsc = ctx.sb("epsc", [128, 1], F32)
        ytile = ctx.sb("ytile", [128, D], BF16)
        yT = ctx.sb("yT", [128, 16 * TPC], BF16)
        ident = ctx.sb("ident", [128, 128], BF16)
        wtile = ctx.sb("wtile", [128, 16 * 512], BF16)
        onesr = ctx.sb("onesr", [1, 128], BF16)
        brow = ctx.sb("brow", [1, D], BF16)
        outf = ctx.sb("outf", [128, 512], F32)
        costile = ctx.sb("costile", [128, 512], F32)
        sintile = ctx.sb("sintile", [128, 512], F32)
        ra = ctx.sb("ra", [128, 512], F32)
        rb = ctx.sb("rb", [128, 512], F32)
        psA = ctx.ps("psA", [128, 512], F32)
        psT = ctx.ps("psT", [128, 128], BF16)
        block = stack.enter_context(nc.Block())

        li = p.load(ident[:], d_ident.ap())
        p.comp("vector", lambda v: v.memset(onesr[:], 1.0))
        p.comp("vector", lambda v: v.memset(epsc[:], 1e-5))

        for tt in range(NT):
            lx = p.load(xsb[:], d_x[tt * 128:(tt + 1) * 128, :])
            emit_ln(p, xsb[:], ytile[:], xc[:], sq[:], stats[:], epsc[:, 0:1], loads=[lx])
            for kt in range(16):
                p.comp("tensor", lambda t, kt=kt: t.transpose(
                    psT[:], ytile[:, kt * 128:(kt + 1) * 128], ident[:]), loads=[li])
                p.comp("vector", lambda v, kt=kt, tt=tt: v.tensor_copy(
                    out=yT[:, kt * TPC + tt * 128: kt * TPC + (tt + 1) * 128], in_=psT[:]))

        for wn, on in (("wqT", "q"), ("wkT", "k"), ("wvT", "v")):
            if rope:
                lb = p.load(brow[:], d_b[{"wqT": "bq", "wkT": "bk", "wvT": "bv"}[wn]], queue="gpsimd")
            for oc in range(4):
                lws = [p.load(wtile[:, kt * 512:(kt + 1) * 512],
                              d_w[wn][kt * 128:(kt + 1) * 128, oc * 512:(oc + 1) * 512],
                              queue="gpsimd") for kt in range(16)]
                for tt in range(NT):
                    fns = [lambda t, kt=kt, tt=tt: t.matmul(
                        psA[:], yT[:, kt * TPC + tt * 128: kt * TPC + (tt + 1) * 128],
                        wtile[:, kt * 512:(kt + 1) * 512],
                        start=(kt == 0), stop=(kt == 15 and not rope)) for kt in range(16)]
                    if rope:
                        fns.append(lambda t, oc=oc: t.matmul(
                            psA[:], onesr[:], brow[:, oc * 512:(oc + 1) * 512],
                            start=False, stop=True))
                        p.comp("tensor", fns, loads=lws + [lb])
                    else:
                        p.comp("tensor", fns, loads=lws)
                    if rope and wn in ("wqT", "wkT"):
                        lc = p.load(costile[:], d_cos[tt * 128:(tt + 1) * 128, :])
                        ls = p.load(sintile[:], d_sin[tt * 128:(tt + 1) * 128, :])
                        p.comp("vector", lambda v: v.tensor_tensor(out=ra[:], in0=psA[:], in1=costile[:], op=ALU.mult), loads=[lc])
                        p.comp("vector", lambda v: v.tensor_tensor(out=rb[:], in0=psA[:], in1=sintile[:], op=ALU.mult), loads=[ls])
                        ev = lambda x: x.rearrange("p (f two) -> p f two", two=2)[:, :, 0:1]
                        od = lambda x: x.rearrange("p (f two) -> p f two", two=2)[:, :, 1:2]
                        p.comp("vector", lambda v: v.tensor_tensor(out=ev(outf[:]), in0=ev(ra[:]), in1=od(rb[:]), op=ALU.subtract))
                        p.comp("vector", lambda v: v.tensor_tensor(out=od(outf[:]), in0=od(ra[:]), in1=ev(rb[:]), op=ALU.add))
                    else:
                        p.comp("vector", lambda v: v.tensor_copy(out=outf[:], in_=psA[:]))
                    p.store(d_out[on][tt * 128:(tt + 1) * 128, oc * 512:(oc + 1) * 512], outf[:])

        p.build(block, ctx.sems)
    return nc


def build_attn(layer):
    nc = new_nc()
    alibi = (layer == 0)
    d_qT = nc.dram_tensor("qTu", [4, 128, S], F32, kind="ExternalInput").ap()
    d_kT = nc.dram_tensor("kTu", [4, 128, S], F32, kind="ExternalInput").ap()
    d_v = nc.dram_tensor("vu", [4, S, 128], F32, kind="ExternalInput").ap()
    if alibi:
        d_acol = nc.dram_tensor("acolu", [4, 128, 32], F32, kind="ExternalInput").ap()
        d_shift = nc.dram_tensor("shiftu", [4, 8, 512], F32, kind="ExternalInput").ap()
        d_ub = nc.dram_tensor("ubu", [4, 8, 512], F32, kind="ExternalInput").ap()
        d_lb = nc.dram_tensor("lbu", [4, 8, 512], F32, kind="ExternalInput").ap()
    d_z = nc.dram_tensor("o_z", [4, S, 128], F32, kind="ExternalOutput").ap()

    tri_np = np.where(np.arange(128)[:, None] <= np.arange(128)[None, :], 0.0, -1e30).astype(np.float32)
    d_tri = nc.inline_tensor(tri_np, "trimask")

    NB = S // 128
    with ExitStack() as stack:
        ctx = Ctx(nc, stack)
        p = Prog(nc)
        qstage = ctx.sb("qstage", [128, 512], F32)
        qTb = ctx.sb("qTb", [128, S], BF16)
        kTb = ctx.sb("kTb", [128, S], BF16)
        vsb = ctx.sb("vsb", [128, NB * 132], BF16)
        acol = ctx.sb("acol", [128, 32], F32)
        ubt = ctx.sb("ubt", [128, 512], F32)
        lbt = ctx.sb("lbt", [128, 512], F32)
        shrow = ctx.sb("shrow", [1, 512], F32)
        onesc = ctx.sb("onesc", [1, 128], F32)
        tri = ctx.sb("tri", [128, 128], F32)
        Ssb = ctx.sb("Ssb", [128, 512], F32)
        PT = ctx.sb("PT", [128, 512], BF16)
        recip = ctx.sb("recip", [128, 1], F32)
        mcol = ctx.sb("mcol", [128, 1], F32)
        zsb = ctx.sb("zsb", [128, 128], F32)
        psS = ctx.ps("psS", [128, 512], F32)
        zps = [ctx.ps(f"zps{i}", [128, 132], F32) for i in range(4)]
        block = stack.enter_context(nc.Block())

        lt = p.load(tri[:], d_tri.ap())
        p.comp("vector", lambda v: v.memset(onesc[:], 1.0))
        p.comp("vector", lambda v: v.memset(mcol[:], -EXP_M))
        p.comp("vector", lambda v: v.memset(
            vsb[:].rearrange("p (b w) -> p b w", w=132)[:, :, 128:129], 1.0))

        for u in range(4):
            lk = p.load(kTb[:], d_kT[u], queue="gpsimd")
            lvs = []
            for g in range(4):
                lvs.append(p.load(
                    vsb[:].rearrange("p (b w) -> p b w", w=132)[:, g * 8:(g + 1) * 8, 0:128],
                    d_v[u][g * 1024:(g + 1) * 1024, :].rearrange("(b p) w -> p b w", p=128),
                    queue="gpsimd"))
            la = []
            if alibi:
                la = [p.load(acol[:], d_acol[u], queue="sync")]
            lqs = []
            for ch in range(8):
                lq = p.load(qstage[:], d_qT[u][:, ch * 512:(ch + 1) * 512])
                p.comp("scalar", lambda s, ch=ch: s.activation(
                    out=qTb[:, ch * 512:(ch + 1) * 512], in_=qstage[:],
                    func=AF.Identity, scale=1.0 / math.sqrt(DH)), loads=[lq])

            for r in range(8):
                lrun = []
                if alibi:
                    lrun.append(p.load(ubt[:], d_ub[u][r:r + 1, :].broadcast_to([128, 512])))
                    lrun.append(p.load(lbt[:], d_lb[u][r:r + 1, :].broadcast_to([128, 512])))
                    lrun.append(p.load(shrow[:], d_shift[u][r:r + 1, :]))
                for kb in range(4 * r + 4):
                    if alibi:
                        p.comp("tensor", [
                            lambda t, kb=kb, r=r: t.matmul(
                                psS[:], kTb[:, kb * 128:(kb + 1) * 128],
                                qTb[:, r * 512:(r + 1) * 512], start=True, stop=False),
                            lambda t: t.matmul(psS[:], onesc[:], shrow[:], start=False, stop=True),
                        ], loads=[lk] + lrun)
                        p.comp("vector", lambda v: v.tensor_tensor(out=Ssb[:], in0=psS[:], in1=ubt[:], op=ALU.min))
                        p.comp("vector", lambda v: v.tensor_tensor(out=Ssb[:], in0=Ssb[:], in1=lbt[:], op=ALU.max))
                        j = kb - 4 * r
                        if j >= 0:
                            if j > 0:
                                p.comp("vector", lambda v, j=j: v.memset(Ssb[:, 0:j * 128], -1e30))
                            p.comp("vector", lambda v, j=j: v.tensor_tensor(
                                out=Ssb[:, j * 128:(j + 1) * 128], in0=Ssb[:, j * 128:(j + 1) * 128],
                                in1=tri[:], op=ALU.add), loads=[lt])
                        p.comp("scalar", lambda s, kb=kb: s.activation(
                            out=PT[:], in_=Ssb[:], func=AF.Exp, bias=acol[:, kb:kb + 1]), loads=la)
                    else:
                        p.comp("tensor", lambda t, kb=kb, r=r: t.matmul(
                            psS[:], kTb[:, kb * 128:(kb + 1) * 128],
                            qTb[:, r * 512:(r + 1) * 512], start=True, stop=True), loads=[lk])
                        j = kb - 4 * r
                        if j >= 0:
                            p.comp("vector", lambda v, j=j: v.tensor_tensor(
                                out=psS[:, j * 128:(j + 1) * 128], in0=psS[:, j * 128:(j + 1) * 128],
                                in1=tri[:], op=ALU.add), loads=[lt])
                        p.comp("scalar", lambda s: s.activation(
                            out=PT[:], in_=psS[:], func=AF.Exp, bias=mcol[:, 0:1]))
                    fns = []
                    for i in range(4):
                        if kb <= 4 * r + i:
                            fns.append(lambda t, i=i, kb=kb, r=r: t.matmul(
                                zps[i][:, 0:129], PT[:, i * 128:(i + 1) * 128],
                                vsb[:].rearrange("p (b w) -> p b w", w=132)[:, kb, 0:129],
                                start=(kb == 0), stop=(kb == 4 * r + i)))
                    p.comp("tensor", fns, loads=lvs)
                for i in range(4):
                    p.comp("vector", lambda v, i=i: v.reciprocal(out=recip[:], in_=zps[i][:, 128:129]))
                    p.comp("vector", lambda v, i=i: v.tensor_scalar(
                        out=zsb[:], in0=zps[i][:, 0:128], scalar1=recip[:, 0:1], scalar2=None, op0=ALU.mult))
                    p.store(d_z[u][(4 * r + i) * 128:(4 * r + i + 1) * 128, :], zsb[:])

        p.build(block, ctx.sems)
    return nc


def build_oproj(layer):
    nc = new_nc()
    bias = (layer == 1)
    d_zT = nc.dram_tensor("zT", [D, TPC], F32, kind="ExternalInput").ap()
    d_wo = nc.dram_tensor("woT", [D, D], F32, kind="ExternalInput").ap()
    d_x = nc.dram_tensor("x", [TPC, D], F32, kind="ExternalInput").ap()
    if bias:
        d_bo = nc.dram_tensor("bo", [1, D], F32, kind="ExternalInput").ap()
    d_xn = nc.dram_tensor("o_xn", [TPC, D], F32, kind="ExternalOutput").ap()
    d_y = nc.dram_tensor("o_y", [TPC, D], BF16, kind="ExternalOutput").ap()
    if bias:
        d_y32 = nc.dram_tensor("o_y32", [TPC, D], F32, kind="ExternalOutput").ap()

    NT = TPC // 128
    with ExitStack() as stack:
        ctx = Ctx(nc, stack)
        p = Prog(nc)
        zTb = ctx.sb("zTb", [128, 16 * TPC], BF16)
        wsb = ctx.sb("wsb", [128, 16 * 2048], BF16)
        xsb = ctx.sb("xsb", [128, D], F32)
        xn = ctx.sb("xn", [128, D], F32)
        xc = ctx.sb("xc", [128, D], F32)
        sq = ctx.sb("sq", [128, D], F32)
        stats = ctx.sb("stats", [128, 4], F32)
        epsc = ctx.sb("epsc", [128, 1], F32)
        ysb = ctx.sb("ysb", [128, D], BF16)
        onesr = ctx.sb("onesr", [1, 128], BF16)
        brow = ctx.sb("brow", [1, D], BF16)
        psA = ctx.ps("psA", [128, 512], F32)
        block = stack.enter_context(nc.Block())

        p.comp("vector", lambda v: v.memset(onesr[:], 1.0))
        p.comp("vector", lambda v: v.memset(epsc[:], 1e-5))
        lb = []
        if bias:
            lb = [p.load(brow[:], d_bo, queue="gpsimd")]
        lz = [p.load(zTb[:, kt * TPC:(kt + 1) * TPC], d_zT[kt * 128:(kt + 1) * 128, :], queue="gpsimd")
              for kt in range(16)]
        lw = [p.load(wsb[:, kt * 2048:(kt + 1) * 2048], d_wo[kt * 128:(kt + 1) * 128, :], queue="gpsimd")
              for kt in range(16)]

        for tt in range(NT):
            lx = p.load(xsb[:], d_x[tt * 128:(tt + 1) * 128, :])
            for oc in range(4):
                fns = [lambda t, kt=kt, tt=tt, oc=oc: t.matmul(
                    psA[:], zTb[:, kt * TPC + tt * 128: kt * TPC + (tt + 1) * 128],
                    wsb[:, kt * 2048 + oc * 512: kt * 2048 + (oc + 1) * 512],
                    start=(kt == 0), stop=(kt == 15 and not bias)) for kt in range(16)]
                if bias:
                    fns.append(lambda t, oc=oc: t.matmul(
                        psA[:], onesr[:], brow[:, oc * 512:(oc + 1) * 512], start=False, stop=True))
                p.comp("tensor", fns, loads=lz + lw + lb)
                p.comp("vector", lambda v, oc=oc: v.tensor_tensor(
                    out=xn[:, oc * 512:(oc + 1) * 512], in0=psA[:],
                    in1=xsb[:, oc * 512:(oc + 1) * 512], op=ALU.add), loads=[lx])
            p.store(d_xn[tt * 128:(tt + 1) * 128, :], xn[:])
            emit_ln(p, xn[:], ysb[:], xc[:], sq[:], stats[:], epsc[:, 0:1])
            p.store(d_y[tt * 128:(tt + 1) * 128, :], ysb[:])
            if bias:
                p.comp("vector", lambda v: v.tensor_scalar(
                    out=xc[:], in0=xc[:], scalar1=stats[:, 3:4], scalar2=None, op0=ALU.mult))
                p.store(d_y32[tt * 128:(tt + 1) * 128, :], xc[:])

        p.build(block, ctx.sems)
    return nc


def build_ffn0():
    nc = new_nc()
    d_y = nc.dram_tensor("y1T", [D, T], BF16, kind="ExternalInput").ap()
    d_gpk = nc.dram_tensor("gpk", [NPC, 2048], I16, kind="ExternalInput").ap()
    d_upk = nc.dram_tensor("upk", [NPC, 2048], I16, kind="ExternalInput").ap()
    d_gsc = nc.dram_tensor("gsc", [NPC, 64], F32, kind="ExternalInput").ap()
    d_usc = nc.dram_tensor("usc", [NPC, 64], F32, kind="ExternalInput").ap()
    d_dpk = nc.dram_tensor("dpk", [D, NPC], I16, kind="ExternalInput").ap()
    d_dsc = nc.dram_tensor("dsc", [D, NPC // 32], F32, kind="ExternalInput").ap()
    d_part = nc.dram_tensor("o_part", [T, D], F32, kind="ExternalOutput").ap()
    d_inter = nc.dram_tensor("interT", [NPC, T], BF16, kind="Internal")
    d_ident = nc.inline_tensor(np.eye(128, dtype=BF), "identbf")
    alt_np = np.tile(np.array([0, 4], dtype=np.int16), 1024).reshape(1, 2048)
    d_alt = nc.inline_tensor(alt_np, "altshift")

    with ExitStack() as stack:
        ctx = Ctx(nc, stack)
        p = Prog(nc)
        dec = {nm: ctx.sb(nm, [128, 2048], I16) for nm in ("pk16", "T16", "n16", "a1", "a2", "z16", "w16", "alt")}
        val = ctx.sb("val", [128, 2048], BF16)
        scsb = ctx.sb("scsb", [128, 64], F32)
        scb = ctx.sb("scb", [128, 64], BF16)
        ident = ctx.sb("ident", [128, 128], BF16)
        wgT = ctx.sb("wgT", [128, 16 * 2048], BF16)
        wuT = ctx.sb("wuT", [128, 16 * 2048], BF16)
        ysb = ctx.sb("ysb", [128, 16 * 512], BF16)
        g32 = ctx.sb("g32", [128, 512], F32)
        t3 = ctx.sb("t3", [128, 512], F32)
        t4 = ctx.sb("t4", [128, 512], F32)
        sgm = ctx.sb("sgm", [128, 512], F32)
        w32 = ctx.sb("w32", [128, 512], F32)
        inter = ctx.sb("inter", [128, 512], BF16)
        osb = ctx.sb("osb", [128, 512], F32)
        psG = ctx.ps("psG", [128, 512], F32)
        psU = ctx.ps("psU", [128, 512], F32)
        psT = ctx.ps("psT", [128, 128], BF16)
        block = stack.enter_context(nc.Block())

        li = p.load(ident[:], d_ident.ap())
        lalt = p.load(dec["alt"][:], d_alt.ap().broadcast_to([128, 2048]))

        def decode_into(dst, d_pk, d_sc, ntile):
            lpk = p.load(dec["pk16"][:], d_pk[ntile * 128:(ntile + 1) * 128, :], queue="gpsimd")
            lsc = p.load(scsb[:], d_sc[ntile * 128:(ntile + 1) * 128, :], queue="gpsimd")
            emit_decode(p, dec, [lpk, lalt], val[:])
            p.comp("vector", lambda v: v.tensor_copy(out=scb[:], in_=scsb[:]), loads=[lsc])
            vv = val[:].rearrange("p (c q) -> p c q", q=32)
            sbb = scb[:].unsqueeze(2).broadcast_to([128, 64, 32])
            p.comp("vector", lambda v: v.tensor_tensor(out=vv, in0=vv, in1=sbb, op=ALU.mult))
            for ht in range(16):
                p.comp("tensor", lambda t, ht=ht: t.transpose(
                    psT[:], val[:, ht * 128:(ht + 1) * 128], ident[:]), loads=[li])
                p.comp("vector", lambda v, ht=ht, ntile=ntile: v.tensor_copy(
                    out=dst[:, ht * 2048 + ntile * 128: ht * 2048 + (ntile + 1) * 128], in_=psT[:]))

        for ntile in range(16):
            decode_into(wgT, d_gpk, d_gsc, ntile)
            decode_into(wuT, d_upk, d_usc, ntile)

        for tc in range(16):
            ly = [p.load(ysb[:, ht * 512:(ht + 1) * 512],
                         d_y[ht * 128:(ht + 1) * 128, tc * 512:(tc + 1) * 512]) for ht in range(16)]
            for ntile in range(16):
                p.comp("tensor", [lambda t, ht=ht, ntile=ntile: t.matmul(
                    psG[:], wgT[:, ht * 2048 + ntile * 128: ht * 2048 + (ntile + 1) * 128],
                    ysb[:, ht * 512:(ht + 1) * 512], start=(ht == 0), stop=(ht == 15)) for ht in range(16)], loads=ly)
                p.comp("tensor", [lambda t, ht=ht, ntile=ntile: t.matmul(
                    psU[:], wuT[:, ht * 2048 + ntile * 128: ht * 2048 + (ntile + 1) * 128],
                    ysb[:, ht * 512:(ht + 1) * 512], start=(ht == 0), stop=(ht == 15)) for ht in range(16)])
                p.comp("vector", lambda v: v.tensor_copy(out=g32[:], in_=psG[:]))
                p.comp("scalar", lambda s: s.activation(out=t3[:], in_=g32[:], func=AF.Square))
                p.comp("vector", lambda v: v.tensor_scalar(out=t3[:], in0=t3[:], scalar1=GELU_C, scalar2=1.0, op0=ALU.mult, op1=ALU.add))
                p.comp("vector", lambda v: v.tensor_tensor(out=t4[:], in0=g32[:], in1=t3[:], op=ALU.mult))
                p.comp("scalar", lambda s: s.activation(out=sgm[:], in_=t4[:], func=AF.Sigmoid, scale=GELU_A))
                p.comp("vector", lambda v: v.tensor_tensor(out=w32[:], in0=g32[:], in1=sgm[:], op=ALU.mult))
                p.comp("vector", lambda v: v.scalar_tensor_tensor(out=inter[:], in0=w32[:], scalar=1.0, in1=psU[:], op0=ALU.mult, op1=ALU.mult))
                p.store(d_inter.ap()[ntile * 128:(ntile + 1) * 128, tc * 512:(tc + 1) * 512], inter[:])

        for ot in range(16):
            lpk = p.load(dec["pk16"][:], d_dpk[ot * 128:(ot + 1) * 128, :], queue="gpsimd")
            lsc = p.load(scsb[:], d_dsc[ot * 128:(ot + 1) * 128, :], queue="gpsimd")
            emit_decode(p, dec, [lpk, lalt], val[:])
            p.comp("vector", lambda v: v.tensor_copy(out=scb[:], in_=scsb[:]), loads=[lsc])
            vv = val[:].rearrange("p (c q) -> p c q", q=32)
            sbb = scb[:].unsqueeze(2).broadcast_to([128, 64, 32])
            p.comp("vector", lambda v: v.tensor_tensor(out=vv, in0=vv, in1=sbb, op=ALU.mult))
            for nt in range(16):
                p.comp("tensor", lambda t, nt=nt: t.transpose(
                    psT[:], val[:, nt * 128:(nt + 1) * 128], ident[:]), loads=[li])
                p.comp("vector", lambda v, nt=nt, ot=ot: v.tensor_copy(
                    out=wgT[:, nt * 2048 + ot * 128: nt * 2048 + (ot + 1) * 128], in_=psT[:]))

        for tc in range(16):
            lis = [p.load(ysb[:, nt * 512:(nt + 1) * 512],
                          d_inter.ap()[nt * 128:(nt + 1) * 128, tc * 512:(tc + 1) * 512]) for nt in range(16)]
            for tsub in range(4):
                for oc in range(4):
                    p.comp("tensor", [lambda t, nt=nt, tsub=tsub, oc=oc: t.matmul(
                        psG[:], ysb[:, nt * 512 + tsub * 128: nt * 512 + (tsub + 1) * 128],
                        wgT[:, nt * 2048 + oc * 512: nt * 2048 + (oc + 1) * 512],
                        start=(nt == 0), stop=(nt == 15)) for nt in range(16)], loads=lis)
                    p.comp("vector", lambda v: v.tensor_copy(out=osb[:], in_=psG[:]))
                    p.store(d_part[tc * 512 + tsub * 128: tc * 512 + (tsub + 1) * 128,
                                   oc * 512:(oc + 1) * 512], osb[:])

        p.build(block, ctx.sems)
    return nc


def build_reduce(nparts, parts_bf16, with_shared):
    nc = new_nc()
    pdt = BF16 if parts_bf16 else F32
    d_parts = nc.dram_tensor("parts", [nparts, TPC, D], pdt, kind="ExternalInput").ap()
    d_x = nc.dram_tensor("x", [TPC, D], F32, kind="ExternalInput").ap()
    if with_shared:
        d_sh = nc.dram_tensor("shared", [TPC, D], F32, kind="ExternalInput").ap()
    d_xn = nc.dram_tensor("o_xn", [TPC, D], F32, kind="ExternalOutput").ap()

    NT = TPC // 128
    with ExitStack() as stack:
        ctx = Ctx(nc, stack)
        p = Prog(nc)
        acc = ctx.sb("acc", [128, D], F32)
        pt = ctx.sb("pt", [128, D], F32)
        block = stack.enter_context(nc.Block())
        for tt in range(NT):
            lx = p.load(acc[:], d_x[tt * 128:(tt + 1) * 128, :])
            srcs = [d_parts[i][tt * 128:(tt + 1) * 128, :] for i in range(nparts)]
            if with_shared:
                srcs.append(d_sh[tt * 128:(tt + 1) * 128, :])
            first = True
            for sap in srcs:
                lp = p.load(pt[:], sap, queue="gpsimd")
                p.comp("vector", lambda v: v.tensor_tensor(out=acc[:], in0=acc[:], in1=pt[:], op=ALU.add),
                       loads=[lp] + ([lx] if first else []))
                first = False
            p.store(d_xn[tt * 128:(tt + 1) * 128, :], acc[:])
        p.build(block, ctx.sems)
    return nc


def build_moe():
    nc = new_nc()
    d_y = nc.dram_tensor("y3T32", [D, T], F32, kind="ExternalInput").ap()
    d_ym = nc.dram_tensor("y3Tmy32", [D, TPC], F32, kind="ExternalInput").ap()
    d_gw = nc.dram_tensor("gwT", [D, 16], F32, kind="ExternalInput").ap()
    d_w1 = nc.dram_tensor("w1", [2, D, MH], F32, kind="ExternalInput").ap()
    d_w2 = nc.dram_tensor("w2", [2, MH, D], F32, kind="ExternalInput").ap()
    d_sw1 = nc.dram_tensor("sw1", [D, MH], F32, kind="ExternalInput").ap()
    d_sw2 = nc.dram_tensor("sw2", [MH, D], F32, kind="ExternalInput").ap()
    d_esel = nc.dram_tensor("esel", [2, 1, 16], F32, kind="ExternalInput").ap()
    d_part = nc.dram_tensor("o_part", [2, T, D], BF16, kind="ExternalOutput").ap()
    d_shared = nc.dram_tensor("o_shared", [TPC, D], F32, kind="ExternalOutput").ap()

    NH = MH // 128
    with ExitStack() as stack:
        ctx = Ctx(nc, stack)
        p = Prog(nc)
        ysb = ctx.sb("ysb", [128, 16 * 512], BF16)
        yf = ctx.sb("yf", [128, 128], F32)
        gw = ctx.sb("gw", [128, 16 * 16], F32)
        comb = ctx.sb("comb", [128, 64 * 2], F32)
        combt = ctx.sb("combt", [128, 16], F32)
        eselt = ctx.sb("eselt", [128, 2 * 16], F32)
        ssb = ctx.sb("ssb", [128, 16], F32)
        r4 = ctx.sb("r4", [128, 4], F32)
        r4b = ctx.sb("r4b", [128, 4], F32)
        r16 = ctx.sb("r16", [128, 16], F32)
        r1 = ctx.sb("r1", [128, 8], F32)
        w1sb = ctx.sb("w1sb", [128, 16 * MH], BF16)
        w2sb = ctx.sb("w2sb", [128, NH * D], BF16)
        asb = ctx.sb("asb", [128, NH * 512], BF16)
        accb = ctx.sb("accb", [128, 512], BF16)
        accf = ctx.sb("accf", [128, 512], F32)
        psR = ctx.ps("psR", [128, 16], F32)
        psA = ctx.ps("psA", [128, 512], F32)
        psO = ctx.ps("psO", [128, 512], F32)
        block = stack.enter_context(nc.Block())

        lsel = [p.load(eselt[:, e * 16:(e + 1) * 16], d_esel[e].broadcast_to([128, 16])) for e in range(2)]
        lgw = [p.load(gw[:, kt * 16:(kt + 1) * 16], d_gw[kt * 128:(kt + 1) * 128, :]) for kt in range(16)]
        for tt in range(64):
            fns = []
            lyt = []
            for kt in range(16):
                lyt.append(p.load(yf[:], d_y[kt * 128:(kt + 1) * 128, tt * 128:(tt + 1) * 128]))
                fns = [lambda t, kt=kt, tt=tt: t.matmul(
                    psR[:], yf[:], gw[:, kt * 16:(kt + 1) * 16],
                    start=(kt == 0), stop=(kt == 15))]
                p.comp("tensor", fns, loads=[lyt[-1]] + (lgw if kt == 0 else []))
            p.comp("scalar", lambda s: s.activation(out=ssb[:], in_=psR[:], func=AF.Sigmoid))
            sg = ssb[:].rearrange("p (g q) -> p g q", q=4)
            p.comp("vector", lambda v: v.tensor_reduce(out=r4[:], in_=sg, axis=AX.X, op=ALU.max))
            bc4 = r4[:].unsqueeze(2).broadcast_to([128, 4, 4])
            p.comp("vector", lambda v: v.tensor_tensor(out=r16[:].rearrange("p (g q) -> p g q", q=4), in0=sg, in1=bc4, op=ALU.is_equal))
            p.comp("vector", lambda v: v.scalar_tensor_tensor(out=r16[:], in0=r16[:], scalar=-1e30, in1=ssb[:], op0=ALU.mult, op1=ALU.add))
            p.comp("vector", lambda v: v.tensor_reduce(out=r4b[:], in_=r16[:].rearrange("p (g q) -> p g q", q=4), axis=AX.X, op=ALU.max))
            p.comp("vector", lambda v: v.tensor_tensor(out=r4[:], in0=r4[:], in1=r4b[:], op=ALU.add))
            p.comp("vector", lambda v: v.tensor_reduce(out=r1[:, 0:1], in_=r4[:], axis=AX.X, op=ALU.max))
            p.comp("vector", lambda v: v.tensor_scalar(out=r4b[:], in0=r4[:], scalar1=r1[:, 0:1], scalar2=None, op0=ALU.is_equal))
            p.comp("vector", lambda v: v.scalar_tensor_tensor(out=r4b[:], in0=r4b[:], scalar=-1e30, in1=r4[:], op0=ALU.mult, op1=ALU.add))
            p.comp("vector", lambda v: v.tensor_reduce(out=r1[:, 1:2], in_=r4b[:], axis=AX.X, op=ALU.max))
            p.comp("vector", lambda v: v.tensor_scalar(out=r4b[:], in0=r4[:], scalar1=r1[:, 1:2], scalar2=None, op0=ALU.is_ge))
            bc4b = r4b[:].unsqueeze(2).broadcast_to([128, 4, 4])
            p.comp("vector", lambda v: v.tensor_tensor(out=r16[:].rearrange("p (g q) -> p g q", q=4), in0=sg, in1=bc4b, op=ALU.mult))
            p.comp("vector", lambda v: v.tensor_reduce(out=r1[:, 2:3], in_=r16[:], axis=AX.X, op=ALU.max))
            p.comp("vector", lambda v: v.tensor_scalar(out=ssb[:], in0=r16[:], scalar1=r1[:, 2:3], scalar2=None, op0=ALU.is_equal))
            p.comp("vector", lambda v: v.scalar_tensor_tensor(out=ssb[:], in0=ssb[:], scalar=-1e30, in1=r16[:], op0=ALU.mult, op1=ALU.add))
            p.comp("vector", lambda v: v.tensor_reduce(out=r1[:, 3:4], in_=ssb[:], axis=AX.X, op=ALU.max))
            p.comp("vector", lambda v: v.tensor_tensor(out=r1[:, 4:5], in0=r1[:, 2:3], in1=r1[:, 3:4], op=ALU.add))
            p.comp("vector", lambda v: v.tensor_scalar(out=r1[:, 4:5], in0=r1[:, 4:5], scalar1=1e-20, scalar2=None, op0=ALU.add))
            p.comp("vector", lambda v: v.reciprocal(out=r1[:, 5:6], in_=r1[:, 4:5]))
            p.comp("vector", lambda v: v.tensor_scalar(out=ssb[:], in0=r16[:], scalar1=r1[:, 3:4], scalar2=None, op0=ALU.is_ge))
            p.comp("vector", lambda v: v.tensor_tensor(out=ssb[:], in0=ssb[:], in1=r16[:], op=ALU.mult))
            p.comp("vector", lambda v: v.tensor_scalar(out=ssb[:], in0=ssb[:], scalar1=r1[:, 5:6], scalar2=None, op0=ALU.mult))
            for e in range(2):
                p.comp("vector", lambda v, e=e: v.tensor_tensor(out=combt[:], in0=ssb[:], in1=eselt[:, e * 16:(e + 1) * 16], op=ALU.mult), loads=[lsel[e]])
                p.comp("vector", lambda v, e=e, tt=tt: v.tensor_reduce(out=comb[:, tt * 2 + e: tt * 2 + e + 1], in_=combt[:], axis=AX.X, op=ALU.add))

        def expert_pass(d_w1src, d_w2src, n_tc, d_ysrc, use_comb, e, d_out):
            lw1 = [p.load(w1sb[:, kt * MH:(kt + 1) * MH], d_w1src[kt * 128:(kt + 1) * 128, :], queue="gpsimd")
                   for kt in range(16)]
            lw2 = [p.load(w2sb[:, ht * D:(ht + 1) * D], d_w2src[ht * 128:(ht + 1) * 128, :], queue="gpsimd")
                   for ht in range(NH)]
            for tc in range(n_tc):
                ly = [p.load(ysb[:, kt * 512:(kt + 1) * 512],
                             d_ysrc[kt * 128:(kt + 1) * 128, tc * 512:(tc + 1) * 512], queue="gpsimd")
                      for kt in range(16)]
                for ht in range(NH):
                    p.comp("tensor", [lambda t, kt=kt, ht=ht: t.matmul(
                        psA[:], w1sb[:, kt * MH + ht * 128: kt * MH + (ht + 1) * 128],
                        ysb[:, kt * 512:(kt + 1) * 512], start=(kt == 0), stop=(kt == 15))
                        for kt in range(16)], loads=ly + lw1)
                    p.comp("scalar", lambda s, ht=ht: s.activation(
                        out=asb[:, ht * 512:(ht + 1) * 512], in_=psA[:], func=AF.Silu))
                for tsub in range(4):
                    for oc in range(4):
                        p.comp("tensor", [lambda t, ht=ht, tsub=tsub, oc=oc: t.matmul(
                            psO[:], asb[:, ht * 512 + tsub * 128: ht * 512 + (tsub + 1) * 128],
                            w2sb[:, ht * D + oc * 512: ht * D + (oc + 1) * 512],
                            start=(ht == 0), stop=(ht == NH - 1)) for ht in range(NH)], loads=lw2)
                        if use_comb:
                            cap = comb[:, (tc * 4 + tsub) * 2 + e: (tc * 4 + tsub) * 2 + e + 1]
                            p.comp("vector", lambda v, cap=cap: v.tensor_scalar(
                                out=accb[:], in0=psO[:], scalar1=cap, scalar2=None, op0=ALU.mult))
                            p.store(d_out[tc * 512 + tsub * 128: tc * 512 + (tsub + 1) * 128,
                                          oc * 512:(oc + 1) * 512], accb[:])
                        else:
                            p.comp("vector", lambda v: v.tensor_copy(out=accf[:], in_=psO[:]))
                            p.store(d_out[tc * 512 + tsub * 128: tc * 512 + (tsub + 1) * 128,
                                          oc * 512:(oc + 1) * 512], accf[:])

        for e in range(2):
            expert_pass(d_w1[e], d_w2[e], 16, d_y, True, e, d_part[e])
        expert_pass(d_sw1, d_sw2, TPC // 512, d_ym, False, 0, d_shared)

        p.build(block, ctx.sems)
    return nc


def _t(a):
    return np.ascontiguousarray(np.asarray(a).T)


def attn_inputs(q, k, v, alibi):
    maps = []
    pos = np.arange(S, dtype=np.float32)
    for c in range(NCORES):
        units = [(b, h) for b in range(B) for h in (2 * c, 2 * c + 1)]
        qTu = np.stack([_t(q.reshape(B, S, H, DH)[b, :, h, :]) for (b, h) in units])
        kTu = np.stack([_t(k.reshape(B, S, H, DH)[b, :, h, :]) for (b, h) in units])
        vu = np.stack([np.ascontiguousarray(v.reshape(B, S, H, DH)[b, :, h, :]) for (b, h) in units])
        m = dict(qTu=qTu, kTu=kTu, vu=vu)
        if alibi:
            acolu = np.stack([
                (SLOPES[h] * pos - EXP_M).reshape(32, 128).T.copy() for (b, h) in units])
            shiftu = np.stack([
                np.stack([-SLOPES[h] * (r * 512 + np.arange(512, dtype=np.float32)) for r in range(8)])
                for (b, h) in units])
            m.update(acolu=np.ascontiguousarray(acolu, dtype=np.float32),
                     shiftu=np.ascontiguousarray(shiftu, dtype=np.float32),
                     ubu=np.ascontiguousarray(1024.0 + shiftu, dtype=np.float32),
                     lbu=np.ascontiguousarray(-1024.0 + shiftu, dtype=np.float32))
        maps.append(m)
    return maps


def attn_gather(res):
    z = np.zeros((B, S, H, DH), dtype=np.float32)
    for c in range(NCORES):
        units = [(b, h) for b in range(B) for h in (2 * c, 2 * c + 1)]
        for ui, (b, h) in enumerate(units):
            z[b, :, h, :] = res.results[c]["o_z"][ui]
    return z.reshape(T, D)


def kernel_multi(**inputs):
    xf = np.ascontiguousarray(np.asarray(inputs["L0_attn_resid_pre"], dtype=np.float32).reshape(T, D))
    my = lambda a, c: np.ascontiguousarray(a[c * TPC:(c + 1) * TPC])

    w0 = {n: _t(inputs["L0_attn_" + k]) for n, k in (("wqT", "Wq"), ("wkT", "Wk"), ("wvT", "Wv"))}
    nc = build_qkv(0)
    res = launch(nc, [dict(x=my(xf, c), **w0) for c in range(NCORES)])
    q0 = np.concatenate([r["o_q"] for r in res.results])
    k0 = np.concatenate([r["o_k"] for r in res.results])
    v0 = np.concatenate([r["o_v"] for r in res.results])

    nc = build_attn(0)
    res = launch(nc, attn_inputs(q0, k0, v0, True))
    z0 = attn_gather(res)

    wo0T = _t(inputs["L0_attn_Wo"])
    z0T = _t(z0)
    nc = build_oproj(0)
    res = launch(nc, [dict(zT=np.ascontiguousarray(z0T[:, c * TPC:(c + 1) * TPC]),
                           woT=wo0T, x=my(xf, c)) for c in range(NCORES)])
    x1 = np.concatenate([r["o_xn"] for r in res.results])
    y1 = np.concatenate([r["o_y"] for r in res.results])

    y1T = _t(y1)
    gpk = np.asarray(inputs["L0_ffn_gate_W_packed"], dtype=np.int32)
    upk = np.asarray(inputs["L0_ffn_up_W_packed"], dtype=np.int32)
    dpk = np.asarray(inputs["L0_ffn_down_W_packed"], dtype=np.int32)
    gsc = np.asarray(inputs["L0_ffn_gate_scales"], dtype=np.float32)
    usc = np.asarray(inputs["L0_ffn_up_scales"], dtype=np.float32)
    dsc = np.asarray(inputs["L0_ffn_down_scales"], dtype=np.float32)
    nc = build_ffn0()
    maps = []
    for c in range(NCORES):
        n0 = c * NPC
        maps.append(dict(
            y1T=y1T,
            gpk=np.ascontiguousarray(gpk[n0:n0 + NPC]).view(np.int16),
            upk=np.ascontiguousarray(upk[n0:n0 + NPC]).view(np.int16),
            gsc=np.ascontiguousarray(gsc[n0:n0 + NPC]),
            usc=np.ascontiguousarray(usc[n0:n0 + NPC]),
            dpk=np.ascontiguousarray(dpk[:, n0 // 2:(n0 + NPC) // 2]).view(np.int16),
            dsc=np.ascontiguousarray(dsc[:, n0 // 32:(n0 + NPC) // 32]),
        ))
    res = launch(nc, maps)
    parts0 = [r["o_part"] for r in res.results]

    nc = build_reduce(NCORES, False, False)
    res = launch(nc, [dict(parts=np.stack([pp[c * TPC:(c + 1) * TPC] for pp in parts0]),
                           x=my(x1, c)) for c in range(NCORES)])
    x2 = np.concatenate([r["o_xn"] for r in res.results])

    w1a = {n: _t(inputs["L1_attn_" + k]) for n, k in (("wqT", "Wq"), ("wkT", "Wk"), ("wvT", "Wv"))}
    b1 = {n: np.asarray(inputs["L1_attn_" + k], dtype=np.float32).reshape(1, D)
          for n, k in (("bq", "bq"), ("bk", "bk"), ("bv", "bv"))}
    half = DH // 2
    inv = 1.0 / (10000.0 ** (np.arange(half, dtype=np.float32) / half))
    tpos = np.arange(S, dtype=np.float32)
    fr = tpos[:, None] * inv[None, :]
    cosS = np.repeat(np.cos(fr), 2, axis=1).astype(np.float32)
    sinS = np.repeat(np.sin(fr), 2, axis=1).astype(np.float32)
    nc = build_qkv(1)
    maps = []
    for c in range(NCORES):
        rows = np.arange(c * TPC, (c + 1) * TPC) % S
        maps.append(dict(x=my(x2, c), **w1a, **b1,
                         cos4=np.ascontiguousarray(np.tile(cosS[rows], (1, 4))),
                         sin4=np.ascontiguousarray(np.tile(sinS[rows], (1, 4)))))
    res = launch(nc, maps)
    q1 = np.concatenate([r["o_q"] for r in res.results])
    k1 = np.concatenate([r["o_k"] for r in res.results])
    v1 = np.concatenate([r["o_v"] for r in res.results])

    nc = build_attn(1)
    res = launch(nc, attn_inputs(q1, k1, v1, False))
    z1 = attn_gather(res)

    wo1T = _t(inputs["L1_attn_Wo"])
    bo1 = np.asarray(inputs["L1_attn_bo"], dtype=np.float32).reshape(1, D)
    z1T = _t(z1)
    nc = build_oproj(1)
    res = launch(nc, [dict(zT=np.ascontiguousarray(z1T[:, c * TPC:(c + 1) * TPC]),
                           woT=wo1T, x=my(x2, c), bo=bo1) for c in range(NCORES)])
    x3 = np.concatenate([r["o_xn"] for r in res.results])
    y3_32 = np.concatenate([r["o_y32"] for r in res.results])

    y3T32 = _t(y3_32)
    gwT = _t(inputs["L1_ffn_gate_w"])
    w1m = np.asarray(inputs["L1_ffn_w1"], dtype=np.float32)
    w2m = np.asarray(inputs["L1_ffn_w2"], dtype=np.float32)
    sw1 = np.ascontiguousarray(np.asarray(inputs["L1_ffn_sw1"], dtype=np.float32)[0])
    sw2 = np.ascontiguousarray(np.asarray(inputs["L1_ffn_sw2"], dtype=np.float32)[0])
    nc = build_moe()
    maps = []
    for c in range(NCORES):
        esel = np.zeros((2, 1, 16), dtype=np.float32)
        esel[0, 0, 2 * c] = 1.0
        esel[1, 0, 2 * c + 1] = 1.0
        maps.append(dict(
            y3T32=y3T32,
            y3Tmy32=np.ascontiguousarray(y3T32[:, c * TPC:(c + 1) * TPC]),
            esel=esel,
            gwT=gwT,
            w1=np.ascontiguousarray(w1m[2 * c:2 * c + 2]),
            w2=np.ascontiguousarray(w2m[2 * c:2 * c + 2]),
            sw1=sw1, sw2=sw2))
    res = launch(nc, maps)
    partsM = [r["o_part"][e] for r in res.results for e in range(2)]
    sharedM = [r["o_shared"] for r in res.results]

    nc = build_reduce(2 * NCORES, True, True)
    res = launch(nc, [dict(parts=np.stack([pp[c * TPC:(c + 1) * TPC] for pp in partsM]),
                           x=my(x3, c), shared=sharedM[c]) for c in range(NCORES)])
    out = np.concatenate([r["o_xn"] for r in res.results])
    return out.reshape(B, S, D).astype(np.float32)


def _g3(src2d, p=128):
    return src2d.rearrange("(kt p) c -> p kt c", p=p)


def _s3(dst2d, k):
    return dst2d.rearrange("p (kt c) -> p kt c", kt=k)


def build_fused(upto=9):
    nc = new_nc()
    RG = [list(range(NCORES))]

    d_x0 = nc.dram_tensor("x0", [TPC, D], F32, kind="ExternalInput").ap()
    d_wq0 = nc.dram_tensor("wq0", [D, 256], F32, kind="ExternalInput").ap()
    d_wk0 = nc.dram_tensor("wk0", [D, 256], F32, kind="ExternalInput").ap()
    d_wv0 = nc.dram_tensor("wv0", [D, 256], F32, kind="ExternalInput").ap()
    d_wo0 = nc.dram_tensor("wo0", [256, D], F32, kind="ExternalInput").ap()
    d_acol = nc.dram_tensor("acolu", [4, 128, 32], F32, kind="ExternalInput").ap()
    d_shift = nc.dram_tensor("shiftu", [4, 8, 512], F32, kind="ExternalInput").ap()
    d_ub = nc.dram_tensor("ubu", [4, 8, 512], F32, kind="ExternalInput").ap()
    d_lb = nc.dram_tensor("lbu", [4, 8, 512], F32, kind="ExternalInput").ap()
    d_gpk = nc.dram_tensor("gpk", [NPC, 2048], I16, kind="ExternalInput").ap()
    d_upk = nc.dram_tensor("upk", [NPC, 2048], I16, kind="ExternalInput").ap()
    d_gsc = nc.dram_tensor("gsc", [NPC, 64], F32, kind="ExternalInput").ap()
    d_usc = nc.dram_tensor("usc", [NPC, 64], F32, kind="ExternalInput").ap()
    d_dpk = nc.dram_tensor("dpk", [D, NPC], I16, kind="ExternalInput").ap()
    d_dsc = nc.dram_tensor("dsc", [D, NPC // 32], F32, kind="ExternalInput").ap()
    d_wq1 = nc.dram_tensor("wq1", [D, 256], F32, kind="ExternalInput").ap()
    d_wk1 = nc.dram_tensor("wk1", [D, 256], F32, kind="ExternalInput").ap()
    d_wv1 = nc.dram_tensor("wv1", [D, 256], F32, kind="ExternalInput").ap()
    d_bq1 = nc.dram_tensor("bq1", [1, 256], F32, kind="ExternalInput").ap()
    d_bk1 = nc.dram_tensor("bk1", [1, 256], F32, kind="ExternalInput").ap()
    d_bv1 = nc.dram_tensor("bv1", [1, 256], F32, kind="ExternalInput").ap()
    d_wo1 = nc.dram_tensor("wo1", [256, D], F32, kind="ExternalInput").ap()
    d_bo1 = nc.dram_tensor("bo1", [1, D], F32, kind="ExternalInput").ap()
    d_cosS = nc.dram_tensor("cosS", [S, 128], F32, kind="ExternalInput").ap()
    d_sinS = nc.dram_tensor("sinS", [S, 128], F32, kind="ExternalInput").ap()
    d_gw = nc.dram_tensor("gwT", [D, 16], F32, kind="ExternalInput").ap()
    d_w1 = nc.dram_tensor("w1", [2, D, MH], F32, kind="ExternalInput").ap()
    d_w2 = nc.dram_tensor("w2", [2, MH, D], F32, kind="ExternalInput").ap()
    d_sw1 = nc.dram_tensor("sw1", [D, MH], F32, kind="ExternalInput").ap()
    d_sw2 = nc.dram_tensor("sw2", [MH, D], F32, kind="ExternalInput").ap()
    d_esel = nc.dram_tensor("esel", [2, 1, 16], F32, kind="ExternalInput").ap()
    d_out = nc.dram_tensor("o_x4", [TPC, D], F32, kind="ExternalOutput").ap()

    tri_np = np.where(np.arange(128)[:, None] <= np.arange(128)[None, :], 0.0, -1e30).astype(np.float32)
    d_tri = nc.inline_tensor(tri_np, "trimask")
    d_identb = nc.inline_tensor(np.eye(128, dtype=BF), "identb")
    d_identf = nc.inline_tensor(np.eye(128, dtype=np.float32), "identf")
    alt_np = np.tile(np.array([0, 4], dtype=np.int16), 1024).reshape(1, 2048)
    d_alt = nc.inline_tensor(alt_np, "altshift")

    it = lambda nm, sh, dt: nc.dram_tensor(nm, sh, dt, kind="Internal")
    yTl = it("yTl", [D, TPC], BF16)
    yTl32 = it("yTl32", [D, TPC], F32)
    G = [it(f"G{i}", [NCORES * D, TPC], BF16) for i in range(3)]
    G32 = it("G32", [NCORES * D, TPC], F32)
    qtu = it("qtu", [4, 128, S], BF16)
    ktu = it("ktu", [4, 128, S], BF16)
    vu = it("vu", [4, S, 128], BF16)
    zTl = [it(f"zTl{i}", [256, T], BF16) for i in range(2)]
    part = [it(f"part{i}", [T, D], BF16) for i in range(4)]
    rs = [it(f"rs{i}", [TPC, D], BF16) for i in range(4)]
    xsc = [it(f"xsc{i}", [TPC, D], F32) for i in range(3)]
    interT = it("interT", [NPC, T], BF16)
    d_interT_ap = None
    e0p = it("e0p", [T, D], BF16)
    shl = it("shl", [TPC, D], F32)

    NT = TPC // 128
    d_interT_ap = interT.ap()

    _uid = [0]

    def _u():
        _uid[0] += 1
        return f"u{_uid[0]}_"

    with ExitStack() as topstack:
        ctx = Ctx(nc, topstack)
        p = Prog(nc)
        block = topstack.enter_context(nc.Block())

        def ph_ln_ag(x_src, ytl_ap, g_ap, f32mode):
            dt = F32 if f32mode else BF16
            with ExitStack() as st_:
                sb = lambda nm, sh, dtt: st_.enter_context(nc.sbuf_tensor(_u() + nm, sh, dtt))
                xsb = sb("ln_x", [128, D], F32)
                xc = sb("ln_xc", [128, D], F32)
                sq = sb("ln_sq", [128, D], F32)
                stats = sb("ln_st", [128, 4], F32)
                epsc = sb("ln_eps", [128, 1], F32)
                ytile = sb("ln_y", [128, D], dt)
                ident = sb("ln_id", [128, 128], dt)
                stage = sb("ln_stg", [128, 16 * 128], dt)
                psT_ = st_.enter_context(nc.psum_tensor(_u() + "ln_psT", [128, 128], dt))
                li = p.load(ident[:], (d_identf if f32mode else d_identb).ap())
                p.comp("vector", lambda v: v.memset(epsc[:], 1e-5))
                for tt in range(NT):
                    lx = p.load(xsb[:], x_src[tt * 128:(tt + 1) * 128, :])
                    emit_ln(p, xsb[:], ytile[:], xc[:], sq[:], stats[:], epsc[:, 0:1], loads=[lx])
                    for kt in range(16):
                        p.comp("tensor", lambda t, kt=kt: t.transpose(
                            psT_[:], ytile[:, kt * 128:(kt + 1) * 128], ident[:]), loads=[li])
                        p.comp("vector", lambda v, kt=kt: v.tensor_copy(
                            out=stage[:, kt * 128:(kt + 1) * 128], in_=psT_[:]))
                    p.store(_g3(ytl_ap[:, tt * 128:(tt + 1) * 128]), _s3(stage[:], 16))
            p.comp("gpsimd", lambda g_: g_.collective_compute(
                "AllGather", ALU.bypass, replica_groups=RG,
                ins=[ytl_ap.opt()], outs=[g_ap.opt()]), collective=True)

        def ph_qkv_units0(g_ap, wq_ap, wk_ap, wv_ap, scaleq):
            with ExitStack() as st_:
                sb = lambda nm, sh, dtt: st_.enter_context(nc.sbuf_tensor(_u() + nm, sh, dtt))
                wsb = sb("qk_w", [128, 6 * 16 * 128], BF16)
                rhs = sb("qk_rhs", [128, 16 * 512], BF16)
                stg = sb("qk_stg", [128, 512], BF16)
                vstg = sb("qk_vstg", [128, 512], BF16)
                identb_ = sb("qk_id", [128, 128], BF16)
                pss = [st_.enter_context(nc.psum_tensor(_u() + f"qk_ps{i}", [128, 512], F32)) for i in range(6)]
                psT_ = st_.enter_context(nc.psum_tensor(_u() + "qk_psT", [128, 128], BF16))
                li = p.load(identb_[:], d_identb.ap())
                lw = []
                for wi, wap in enumerate((wq_ap, wk_ap, wv_ap)):
                    for hl in range(2):
                        lw.append(p.load(
                            _s3(wsb[:, (wi * 2 + hl) * 16 * 128:((wi * 2 + hl) * 16 + 16) * 128], 16),
                            _g3(wap[:, hl * 128:(hl + 1) * 128]),
                            queue="gpsimd"))
                for r in range(NCORES):
                    b = r // 4
                    for tc in range(2):
                        bcol = (r % 4) * TPC + tc * 512
                        lr = p.load(_s3(rhs[:], 16), _g3(g_ap[r * D:(r + 1) * D, tc * 512:(tc + 1) * 512]))
                        p.comp("tensor", [lambda t, slot=slot, kt=kt: t.matmul(
                            pss[slot][:], wsb[:, (slot * 16 + kt) * 128:(slot * 16 + kt + 1) * 128],
                            rhs[:, kt * 512:(kt + 1) * 512], start=(kt == 0), stop=(kt == 15))
                            for kt in range(16) for slot in range(6)],
                            loads=[lr] + lw)
                        for wi, dst in ((0, qtu), (1, ktu)):
                            for hl in range(2):
                                u = b * 2 + hl
                                slot = wi * 2 + hl
                                if wi == 0:
                                    p.comp("scalar", lambda s, slot=slot: s.activation(
                                        out=stg[:], in_=pss[slot][:], func=AF.Identity, scale=scaleq))
                                else:
                                    p.comp("vector", lambda v, slot=slot: v.tensor_copy(out=stg[:], in_=pss[slot][:]))
                                p.store(dst.ap()[u][:, bcol:bcol + 512], stg[:])
                        for hl in range(2):
                            u = b * 2 + hl
                            slot = 4 + hl
                            p.comp("vector", lambda v, slot=slot: v.tensor_copy(out=vstg[:], in_=pss[slot][:]))
                            for i in range(4):
                                p.comp("tensor", lambda t, i=i: t.transpose(
                                    psT_[:], vstg[:, i * 128:(i + 1) * 128], identb_[:]), loads=[li])
                                p.comp("vector", lambda v: v.tensor_copy(out=stg[:, 0:128], in_=psT_[:]))
                                p.store(vu.ap()[u][bcol + i * 128: bcol + (i + 1) * 128, :], stg[:, 0:128])

        def ph_qkv_units1(g_ap):
            with ExitStack() as st_:
                sb = lambda nm, sh, dtt: st_.enter_context(nc.sbuf_tensor(_u() + nm, sh, dtt))
                wsb = sb("q1_w", [128, 4 * 16 * 128], BF16)
                wvsb = sb("q1_wv", [128, 2 * 16 * 128], BF16)
                ltile = sb("q1_l", [128, 16 * 128], BF16)
                bbc = sb("q1_bbc", [128, 4 * 128], F32)
                bvcol = sb("q1_bvc", [128, 2], F32)
                costl = sb("q1_cos", [128, 128], F32)
                sintl = sb("q1_sin", [128, 128], F32)
                f32a = sb("q1_a", [128, 128], F32)
                ra = sb("q1_ra", [128, 128], F32)
                rb = sb("q1_rb", [128, 128], F32)
                rop = sb("q1_ro", [128, 128], F32)
                identf_ = sb("q1_idf", [128, 128], F32)
                identb_ = sb("q1_idb", [128, 128], BF16)
                stg = sb("q1_stg", [128, 512], BF16)
                vstg = sb("q1_vstg", [128, 512], BF16)
                rhs = sb("q1_rhs", [128, 16 * 512], BF16)
                psQK = st_.enter_context(nc.psum_tensor(_u() + "q1_psqk", [128, 512], F32))
                psTf = st_.enter_context(nc.psum_tensor(_u() + "q1_psTf", [128, 128], F32))
                psTb = st_.enter_context(nc.psum_tensor(_u() + "q1_psTb", [128, 128], BF16))
                psV = [st_.enter_context(nc.psum_tensor(_u() + f"q1_psv{i}", [128, 512], F32)) for i in range(2)]
                lif = p.load(identf_[:], d_identf.ap())
                lib = p.load(identb_[:], d_identb.ap())
                lw = []
                for wi, wap in enumerate((d_wq1, d_wk1)):
                    for hl in range(2):
                        lw.append(p.load(
                            _s3(wsb[:, (wi * 2 + hl) * 16 * 128:((wi * 2 + hl) * 16 + 16) * 128], 16),
                            _g3(wap[:, hl * 128:(hl + 1) * 128]),
                            queue="gpsimd"))
                for hl in range(2):
                    lw.append(p.load(
                        _s3(wvsb[:, hl * 16 * 128:(hl * 16 + 16) * 128], 16),
                        _g3(d_wv1[:, hl * 128:(hl + 1) * 128]),
                        queue="gpsimd"))
                lb = []
                for wi, bap in enumerate((d_bq1, d_bk1)):
                    for hl in range(2):
                        lb.append(p.load(bbc[:, (wi * 2 + hl) * 128:(wi * 2 + hl + 1) * 128],
                                         bap[:, hl * 128:(hl + 1) * 128].broadcast_to([128, 128])))
                lbv = [p.load(bvcol[:, hl:hl + 1],
                              d_bv1[:, hl * 128:(hl + 1) * 128].rearrange("o p -> p o"))
                       for hl in range(2)]
                for b in range(B):
                    for pt in range(32):
                        r = b * 4 + pt // 8
                        coff = (pt % 8) * 128
                        lts = p.load(_s3(ltile[:], 16), _g3(g_ap[r * D:(r + 1) * D, coff:coff + 128]))
                        p.comp("tensor", [lambda t, slot=slot, kt=kt: t.matmul(
                            psQK[:, slot * 128:(slot + 1) * 128], ltile[:, kt * 128:(kt + 1) * 128],
                            wsb[:, (slot * 16 + kt) * 128:(slot * 16 + kt + 1) * 128],
                            start=(kt == 0), stop=(kt == 15)) for kt in range(16) for slot in range(4)],
                            loads=[lts] + (lw + lb if (pt == 0 and b == 0) else []))
                        lc = p.load(costl[:], d_cosS[pt * 128:(pt + 1) * 128, :])
                        ls = p.load(sintl[:], d_sinS[pt * 128:(pt + 1) * 128, :])
                        for wi in range(2):
                            for hl in range(2):
                                slot = wi * 2 + hl
                                u = b * 2 + hl
                                p.comp("vector", lambda v, slot=slot: v.tensor_tensor(
                                    out=f32a[:], in0=psQK[:, slot * 128:(slot + 1) * 128],
                                    in1=bbc[:, slot * 128:(slot + 1) * 128], op=ALU.add))
                                p.comp("vector", lambda v: v.tensor_tensor(out=ra[:], in0=f32a[:], in1=costl[:], op=ALU.mult), loads=[lc])
                                p.comp("vector", lambda v: v.tensor_tensor(out=rb[:], in0=f32a[:], in1=sintl[:], op=ALU.mult), loads=[ls])
                                ev = lambda x: x.rearrange("p (f two) -> p f two", two=2)[:, :, 0:1]
                                od = lambda x: x.rearrange("p (f two) -> p f two", two=2)[:, :, 1:2]
                                p.comp("vector", lambda v: v.tensor_tensor(out=ev(rop[:]), in0=ev(ra[:]), in1=od(rb[:]), op=ALU.subtract))
                                p.comp("vector", lambda v: v.tensor_tensor(out=od(rop[:]), in0=od(ra[:]), in1=ev(rb[:]), op=ALU.add))
                                p.comp("tensor", lambda t: t.transpose(psTf[:], rop[:], identf_[:]), loads=[lif])
                                if wi == 0:
                                    p.comp("scalar", lambda s: s.activation(
                                        out=stg[:, 0:128], in_=psTf[:], func=AF.Identity, scale=1.0 / math.sqrt(DH)))
                                else:
                                    p.comp("vector", lambda v: v.tensor_copy(out=stg[:, 0:128], in_=psTf[:]))
                                dst = qtu if wi == 0 else ktu
                                p.store(dst.ap()[u][:, b * 0 + pt * 128: pt * 128 + 128], stg[:, 0:128])
                for r in range(NCORES):
                    b = r // 4
                    for tc in range(2):
                        bcol = (r % 4) * TPC + tc * 512
                        lr = p.load(_s3(rhs[:], 16), _g3(g_ap[r * D:(r + 1) * D, tc * 512:(tc + 1) * 512]))
                        p.comp("tensor", [lambda t, hl=hl, kt=kt: t.matmul(
                            psV[hl][:], wvsb[:, (hl * 16 + kt) * 128:(hl * 16 + kt + 1) * 128],
                            rhs[:, kt * 512:(kt + 1) * 512], start=(kt == 0), stop=(kt == 15))
                            for kt in range(16) for hl in range(2)], loads=[lr])
                        for hl in range(2):
                            u = b * 2 + hl
                            p.comp("vector", lambda v, hl=hl: v.tensor_scalar(
                                out=vstg[:], in0=psV[hl][:], scalar1=bvcol[:, hl:hl + 1],
                                scalar2=None, op0=ALU.add), loads=[lbv[hl]])
                            for i in range(4):
                                p.comp("tensor", lambda t, i=i: t.transpose(
                                    psTb[:], vstg[:, i * 128:(i + 1) * 128], identb_[:]), loads=[lib])
                                p.comp("vector", lambda v: v.tensor_copy(out=stg[:, 0:128], in_=psTb[:]))
                                p.store(vu.ap()[u][bcol + i * 128: bcol + (i + 1) * 128, :], stg[:, 0:128])

        def ph_attn(layer, ztl_ap):
            alibi = (layer == 0)
            NB = S // 128
            with ExitStack() as st_:
                sb = lambda nm, sh, dtt: st_.enter_context(nc.sbuf_tensor(_u() + nm, sh, dtt))
                qTb = sb("at_q", [128, S], BF16)
                kTb = sb("at_k", [128, S], BF16)
                vsb = sb("at_v", [128, NB * 132], BF16)
                acol = sb("at_ac", [128, 32], F32)
                sht = [sb(f"at_sht{i}", [128, 512], F32) for i in range(2)]
                tri = sb("at_tri", [128, 128], F32)
                Ssb = [sb(f"at_S{i}", [128, 512], F32) for i in range(2)]
                PT = [sb(f"at_PT{i}", [128, 512], BF16) for i in range(2)]
                recip = sb("at_rc", [128, 1], F32)
                mcol = sb("at_mc", [128, 1], F32)
                zsb = sb("at_z", [128, 128], F32)
                zbt = sb("at_zb", [128, 512], BF16)
                identf_ = sb("at_idf", [128, 128], F32)
                psS = [st_.enter_context(nc.psum_tensor(_u() + f"at_psS{i}", [128, 512], F32)) for i in range(2)]
                psTf = st_.enter_context(nc.psum_tensor(_u() + "at_psTf", [128, 128], F32))
                zps = [st_.enter_context(nc.psum_tensor(_u() + f"at_zps{i}", [128, 132], F32)) for i in range(4)]
                lt = p.load(tri[:], d_tri.ap())
                lif = p.load(identf_[:], d_identf.ap())
                p.comp("vector", lambda v: v.memset(mcol[:], -EXP_M))
                vset = p.comp("vector", lambda v: v.memset(
                    vsb[:].rearrange("p (b w) -> p b w", w=132)[:, :, 128:129], 1.0))

                prev_unit_smm = None
                prev_unit_av = vset
                for u in range(4):
                    b, hl = u // 2, u % 2
                    lq = p.load(qTb[:], qtu.ap()[u], queue="sync", deps=[prev_unit_smm])
                    lk = p.load(kTb[:], ktu.ap()[u], queue="sync")
                    lvs = []
                    for g_ in range(4):
                        lvs.append(p.load(
                            vsb[:].rearrange("p (b w) -> p b w", w=132)[:, g_ * 8:(g_ + 1) * 8, 0:128],
                            vu.ap()[u][g_ * 1024:(g_ + 1) * 1024, :].rearrange("(b p) w -> p b w", p=128),
                            queue="gpsimd", deps=[prev_unit_av] if g_ == 0 else None))
                    la = []
                    if alibi:
                        la = [p.load(acol[:], d_acol[u], queue="sync")]
                    sread = [None, None]
                    ptread = [None, None]
                    shread = [None, None]
                    norm_last = None
                    stz_prev = None
                    for r in range(8):
                        rp = r % 2
                        lrun = None
                        if alibi:
                            lrun = p.load(sht[rp][:], d_shift[u][r:r + 1, :].broadcast_to([128, 512]),
                                          deps=[shread[rp]])
                        avtok = [None] * 4
                        for kb in range(4 * r + 4):
                            sp = kb % 2
                            pp = kb % 2
                            tS = p.comp("tensor", lambda t, kb=kb, r=r, sp=sp: t.matmul(
                                psS[sp][:], kTb[:, kb * 128:(kb + 1) * 128],
                                qTb[:, r * 512:(r + 1) * 512], start=True, stop=True),
                                loads=[lq, lk], deps=[sread[sp]])
                            j = kb - 4 * r
                            if alibi:
                                clp = p.comp("vector", lambda v, sp=sp, rp=rp: v.scalar_tensor_tensor(
                                    out=Ssb[sp][:], in0=psS[sp][:], scalar=1024.0, in1=sht[rp][:],
                                    op0=ALU.min, op1=ALU.add), deps=[tS, lrun])
                                shread[rp] = clp
                                sread[sp] = clp
                                if j >= 0:
                                    if j > 0:
                                        clp = p.comp("vector", lambda v, j=j, sp=sp: v.memset(
                                            Ssb[sp][:, 0:j * 128], -1e30), deps=[clp])
                                    clp = p.comp("vector", lambda v, j=j, sp=sp: v.tensor_tensor(
                                        out=Ssb[sp][:, j * 128:(j + 1) * 128], in0=Ssb[sp][:, j * 128:(j + 1) * 128],
                                        in1=tri[:], op=ALU.add), loads=[lt], deps=[clp])
                                ex = p.comp("scalar", lambda s, kb=kb, sp=sp, pp=pp: s.activation(
                                    out=PT[pp][:], in_=Ssb[sp][:], func=AF.Exp, bias=acol[:, kb:kb + 1]),
                                    deps=[clp, ptread[pp]], loads=la)
                            else:
                                dpre = tS
                                if j >= 0:
                                    dpre = p.comp("vector", lambda v, j=j, sp=sp: v.tensor_tensor(
                                        out=psS[sp][:, j * 128:(j + 1) * 128], in0=psS[sp][:, j * 128:(j + 1) * 128],
                                        in1=tri[:], op=ALU.add), loads=[lt], deps=[tS])
                                ex = p.comp("scalar", lambda s, sp=sp, pp=pp: s.activation(
                                    out=PT[pp][:], in_=psS[sp][:], func=AF.Exp, bias=mcol[:, 0:1]),
                                    deps=[dpre, ptread[pp]])
                                sread[sp] = ex
                            fns = []
                            for i in range(4):
                                if kb <= 4 * r + i:
                                    fns.append(lambda t, i=i, kb=kb, r=r, pp=pp: t.matmul(
                                        zps[i][:, 0:129], PT[pp][:, i * 128:(i + 1) * 128],
                                        vsb[:].rearrange("p (b w) -> p b w", w=132)[:, kb, 0:129],
                                        start=(kb == 0), stop=(kb == 4 * r + i)))
                            av = p.comp("tensor", fns, loads=lvs,
                                        deps=[ex, norm_last if kb == 0 else None])
                            ptread[pp] = av
                            for i in range(4):
                                if kb == 4 * r + i:
                                    avtok[i] = av
                        tp_prev = None
                        cp_prev = None
                        for i in range(4):
                            rc = p.comp("vector", lambda v, i=i: v.reciprocal(out=recip[:], in_=zps[i][:, 128:129]),
                                        deps=[avtok[i]])
                            ts = p.comp("vector", lambda v, i=i: v.tensor_scalar(
                                out=zsb[:], in0=zps[i][:, 0:128], scalar1=recip[:, 0:1], scalar2=None, op0=ALU.mult),
                                deps=[rc, tp_prev])
                            tp = p.comp("tensor", lambda t: t.transpose(psTf[:], zsb[:], identf_[:]),
                                        loads=[lif], deps=[ts, cp_prev])
                            tp_prev = tp
                            cp = p.comp("vector", lambda v, i=i: v.tensor_copy(
                                out=zbt[:, i * 128:(i + 1) * 128], in_=psTf[:]), deps=[tp, stz_prev])
                            cp_prev = cp
                            norm_last = cp
                        stz_prev = p.store(ztl_ap[hl * 128:(hl + 1) * 128,
                                                  b * S + 4 * r * 128: b * S + (4 * r + 4) * 128], zbt[:],
                                           deps=[norm_last])
                    prev_unit_smm = tS
                    prev_unit_av = av

        def ph_oproj_partial(ztl_ap, wo_ap, part_ap):
            with ExitStack() as st_:
                sb = lambda nm, sh, dtt: st_.enter_context(nc.sbuf_tensor(_u() + nm, sh, dtt))
                wsb = sb("op_w", [128, 2 * D], BF16)
                zt = sb("op_z", [128, 2 * 128], BF16)
                osb = sb("op_o", [128, 2048], BF16)
                psA = st_.enter_context(nc.psum_tensor(_u() + "op_ps", [128, 512], F32))
                lw = [p.load(_s3(wsb[:], 2), _g3(wo_ap), queue="gpsimd")]
                for tt in range(T // 128):
                    lz = [p.load(_s3(zt[:], 2), _g3(ztl_ap[:, tt * 128:(tt + 1) * 128]))]
                    for oc in range(4):
                        p.comp("tensor", [lambda t, kt=kt, oc=oc: t.matmul(
                            psA[:], zt[:, kt * 128:(kt + 1) * 128],
                            wsb[:, kt * D + oc * 512: kt * D + (oc + 1) * 512],
                            start=(kt == 0), stop=(kt == 1)) for kt in range(2)], loads=lz + lw)
                        p.comp("vector", lambda v, oc=oc: v.tensor_copy(
                            out=osb[:, oc * 512:(oc + 1) * 512], in_=psA[:]))
                    p.store(part_ap[tt * 128:(tt + 1) * 128, :], osb[:])

        def ph_rs(part_ap, rs_ap):
            p.comp("gpsimd", lambda g_: g_.collective_compute(
                "ReduceScatter", ALU.add, replica_groups=RG,
                ins=[part_ap.opt()], outs=[rs_ap.opt()]), collective=True)

        def ph_resid(x_src, add_aps, out_ap, bo_bcast=False):
            with ExitStack() as st_:
                sb = lambda nm, sh, dtt: st_.enter_context(nc.sbuf_tensor(_u() + nm, sh, dtt))
                acc = sb("rd_acc", [128, D], F32)
                pt_ = sb("rd_pt", [128, D], F32)
                bob = sb("rd_bo", [128, D], F32)
                lbo = []
                if bo_bcast:
                    lbo = [p.load(bob[:], d_bo1.broadcast_to([128, D]), queue="sync")]
                for tt in range(NT):
                    lx = p.load(acc[:], x_src[tt * 128:(tt + 1) * 128, :])
                    first = [lx]
                    for (aap, isbf) in add_aps:
                        lp = p.load(pt_[:], aap[tt * 128:(tt + 1) * 128, :], queue="gpsimd")
                        p.comp("vector", lambda v: v.tensor_tensor(out=acc[:], in0=acc[:], in1=pt_[:], op=ALU.add),
                               loads=[lp] + first)
                        first = []
                    if bo_bcast:
                        p.comp("vector", lambda v: v.tensor_tensor(out=acc[:], in0=acc[:], in1=bob[:], op=ALU.add),
                               loads=lbo)
                    p.store(out_ap[tt * 128:(tt + 1) * 128, :], acc[:])

        def ph_ffn0(g_ap):
            with ExitStack() as st_:
                sb = lambda nm, sh, dtt: st_.enter_context(nc.sbuf_tensor(_u() + nm, sh, dtt))
                dec = {nm: sb("f_" + nm, [128, 2048], I16) for nm in ("pk16", "T16", "n16", "a1", "a2", "z16", "alt")}
                dec["w16"] = dec["pk16"]
                val = sb("f_val", [128, 2048], BF16)
                scsb = sb("f_sc", [128, 64], F32)
                scb = sb("f_scb", [128, 64], BF16)
                ident = sb("f_id", [128, 128], BF16)
                wgT = sb("f_wg", [128, 16 * 2048], BF16)
                wuT = sb("f_wu", [128, 16 * 2048], BF16)
                ysb = [sb(f"f_y{i}", [128, 16 * 512], BF16) for i in range(2)]
                g32 = sb("f_g32", [128, 512], F32)
                t3 = sb("f_t3", [128, 512], F32)
                t4 = sb("f_t4", [128, 512], F32)
                sgm = sb("f_sgm", [128, 512], F32)
                w32 = sb("f_w32", [128, 512], F32)
                inter = sb("f_int", [128, 512], BF16)
                osb = sb("f_o", [128, 2048], BF16)
                psG = [st_.enter_context(nc.psum_tensor(_u() + f"f_psG{i}", [128, 512], F32)) for i in range(2)]
                psU = [st_.enter_context(nc.psum_tensor(_u() + f"f_psU{i}", [128, 512], F32)) for i in range(2)]
                psT_ = [st_.enter_context(nc.psum_tensor(_u() + f"f_psT{i}", [128, 128], BF16)) for i in range(2)]
                li = p.load(ident[:], d_identb.ap())
                lalt = p.load(dec["alt"][:], d_alt.ap().broadcast_to([128, 2048]))

                def decode_into(dst, d_pk, d_sc, ntile, tp_dep):
                    lpk = p.load(dec["pk16"][:], d_pk[ntile * 128:(ntile + 1) * 128, :], queue="gpsimd")
                    lsc = p.load(scsb[:], d_sc[ntile * 128:(ntile + 1) * 128, :], queue="gpsimd")
                    emit_decode(p, dec, [lpk, lalt], val[:])
                    p.comp("vector", lambda v: v.tensor_copy(out=scb[:], in_=scsb[:]), loads=[lsc])
                    vv = val[:].rearrange("p (c q) -> p c q", q=32)
                    sbb = scb[:].unsqueeze(2).broadcast_to([128, 64, 32])
                    smul = p.comp("vector", lambda v: v.tensor_tensor(out=vv, in0=vv, in1=sbb, op=ALU.mult))
                    cp = None
                    for ht in range(16):
                        par = ht % 2
                        tp = p.comp("tensor", lambda t, ht=ht, par=par: t.transpose(
                            psT_[par][:], val[:, ht * 128:(ht + 1) * 128], ident[:]),
                            loads=[li], deps=[smul, tp_dep[par]])
                        cp = p.comp("vector", lambda v, ht=ht, ntile=ntile, par=par: v.tensor_copy(
                            out=dst[:, ht * 2048 + ntile * 128: ht * 2048 + (ntile + 1) * 128],
                            in_=psT_[par][:]), deps=[tp])
                        tp_dep[par] = cp
                    return cp, tp_dep

                tp_dep = [None, None]
                for ntile in range(16):
                    cg, tp_dep = decode_into(wgT, d_gpk, d_gsc, ntile, tp_dep)
                    cu, tp_dep = decode_into(wuT, d_upk, d_usc, ntile, tp_dep)

                lys = [None] * 16
                last_y_reader = [None, None]
                gread = [None, None]
                uread = [None, None]
                st_tok = None
                lys[0] = p.load(_s3(ysb[0][:], 16), _g3(g_ap[0:D, 0:512]), deps=[cu])
                for tc in range(16):
                    yp = tc % 2
                    if tc + 1 < 16:
                        r2, sub2 = (tc + 1) // 2, (tc + 1) % 2
                        lys[tc + 1] = p.load(_s3(ysb[1 - yp][:], 16),
                                             _g3(g_ap[r2 * D:(r2 + 1) * D, sub2 * 512:(sub2 + 1) * 512]),
                                             deps=[last_y_reader[1 - yp]])
                    for ntile in range(16):
                        par = ntile % 2
                        tG = p.comp("tensor", [lambda t, ht=ht, ntile=ntile, par=par, yp=yp: t.matmul(
                            psG[par][:], wgT[:, ht * 2048 + ntile * 128: ht * 2048 + (ntile + 1) * 128],
                            ysb[yp][:, ht * 512:(ht + 1) * 512], start=(ht == 0), stop=(ht == 15))
                            for ht in range(16)], loads=[lys[tc]], deps=[gread[par], cu if tc == 0 else None])
                        tU = p.comp("tensor", [lambda t, ht=ht, ntile=ntile, par=par, yp=yp: t.matmul(
                            psU[par][:], wuT[:, ht * 2048 + ntile * 128: ht * 2048 + (ntile + 1) * 128],
                            ysb[yp][:, ht * 512:(ht + 1) * 512], start=(ht == 0), stop=(ht == 15))
                            for ht in range(16)], deps=[uread[par]])
                        last_y_reader[yp] = tU
                        c1 = p.comp("vector", lambda v, par=par: v.tensor_copy(out=g32[:], in_=psG[par][:]), deps=[tG])
                        gread[par] = c1
                        c2 = p.comp("scalar", lambda s: s.activation(out=t3[:], in_=g32[:], func=AF.Square), deps=[c1])
                        c3 = p.comp("vector", lambda v: v.tensor_scalar(out=t3[:], in0=t3[:], scalar1=GELU_C, scalar2=1.0, op0=ALU.mult, op1=ALU.add), deps=[c2])
                        c4 = p.comp("vector", lambda v: v.tensor_tensor(out=t4[:], in0=g32[:], in1=t3[:], op=ALU.mult), deps=[c3])
                        c5 = p.comp("scalar", lambda s: s.activation(out=sgm[:], in_=t4[:], func=AF.Sigmoid, scale=GELU_A), deps=[c4])
                        c6 = p.comp("vector", lambda v: v.tensor_tensor(out=w32[:], in0=g32[:], in1=sgm[:], op=ALU.mult), deps=[c5])
                        c7 = p.comp("vector", lambda v, par=par: v.scalar_tensor_tensor(
                            out=inter[:], in0=w32[:], scalar=1.0, in1=psU[par][:], op0=ALU.mult, op1=ALU.mult),
                            deps=[c6, tU, st_tok])
                        uread[par] = c7
                        st_tok = p.store(d_interT_ap[ntile * 128:(ntile + 1) * 128, tc * 512:(tc + 1) * 512],
                                         inter[:], deps=[c7])

                p.comp("vector", lambda v: v.memset(scb[:, 0:1], 0.0), deps=[last_y_reader[0], last_y_reader[1], st_tok])
                tp_dep = [tp_dep[0], tp_dep[1]]
                lastcp = None
                for ot in range(16):
                    lpk = p.load(dec["pk16"][:], d_dpk[ot * 128:(ot + 1) * 128, :], queue="gpsimd")
                    lsc = p.load(scsb[:], d_dsc[ot * 128:(ot + 1) * 128, :], queue="gpsimd")
                    emit_decode(p, dec, [lpk, lalt], val[:])
                    p.comp("vector", lambda v: v.tensor_copy(out=scb[:], in_=scsb[:]), loads=[lsc])
                    vv = val[:].rearrange("p (c q) -> p c q", q=32)
                    sbb = scb[:].unsqueeze(2).broadcast_to([128, 64, 32])
                    smul = p.comp("vector", lambda v: v.tensor_tensor(out=vv, in0=vv, in1=sbb, op=ALU.mult))
                    for nt in range(16):
                        par = nt % 2
                        tp = p.comp("tensor", lambda t, nt=nt, par=par: t.transpose(
                            psT_[par][:], val[:, nt * 128:(nt + 1) * 128], ident[:]),
                            loads=[li], deps=[smul, tp_dep[par]])
                        lastcp = p.comp("vector", lambda v, nt=nt, ot=ot, par=par: v.tensor_copy(
                            out=wgT[:, nt * 2048 + ot * 128: nt * 2048 + (ot + 1) * 128],
                            in_=psT_[par][:]), deps=[tp])
                        tp_dep[par] = lastcp

                lis = [None] * 16
                last_i_reader = [None, None]
                dread = [None, None]
                stB = None
                lis[0] = p.load(_s3(ysb[0][:], 16), _g3(d_interT_ap[:, 0:512]), deps=[lastcp])
                for tc in range(16):
                    yp = tc % 2
                    if tc + 1 < 16:
                        lis[tc + 1] = p.load(_s3(ysb[1 - yp][:], 16),
                                             _g3(d_interT_ap[:, (tc + 1) * 512:(tc + 2) * 512]),
                                             deps=[last_i_reader[1 - yp]])
                    for tsub in range(4):
                        evs = []
                        for oc in range(4):
                            par = oc % 2
                            tD = p.comp("tensor", [lambda t, nt=nt, tsub=tsub, oc=oc, par=par, yp=yp: t.matmul(
                                psG[par][:], ysb[yp][:, nt * 512 + tsub * 128: nt * 512 + (tsub + 1) * 128],
                                wgT[:, nt * 2048 + oc * 512: nt * 2048 + (oc + 1) * 512],
                                start=(nt == 0), stop=(nt == 15)) for nt in range(16)],
                                loads=[lis[tc]], deps=[dread[par], lastcp if tc == 0 else None])
                            last_i_reader[yp] = tD
                            ev = p.comp("vector", lambda v, oc=oc, par=par: v.tensor_copy(
                                out=osb[:, oc * 512:(oc + 1) * 512], in_=psG[par][:]),
                                deps=[tD, stB])
                            dread[par] = ev
                            evs.append(ev)
                        stB = p.store(part[1].ap()[tc * 512 + tsub * 128: tc * 512 + (tsub + 1) * 128, :],
                                      osb[:], deps=evs)

        def ph_moe():
            NH = MH // 128
            with ExitStack() as st_:
                sb = lambda nm, sh, dtt: st_.enter_context(nc.sbuf_tensor(_u() + nm, sh, dtt))
                ysb = sb("m_y", [128, 16 * 512], BF16)
                yf = sb("m_yf", [128, 16 * 128], F32)
                gw = sb("m_gw", [128, 16 * 16], F32)
                comb = sb("m_cb", [128, 64 * 2], F32)
                combt = sb("m_cbt", [128, 16], F32)
                eselt = sb("m_es", [128, 2 * 16], F32)
                ssb = sb("m_s", [128, 16], F32)
                r4 = sb("m_r4", [128, 4], F32)
                r4b = sb("m_r4b", [128, 4], F32)
                r16 = sb("m_r16", [128, 16], F32)
                r1 = sb("m_r1", [128, 8], F32)
                w1sb = sb("m_w1", [128, 16 * MH], BF16)
                w2sb = sb("m_w2", [128, NH * D], BF16)
                asb = sb("m_a", [128, NH * 512], BF16)
                accb = sb("m_accb", [128, 2048], BF16)
                accf = sb("m_accf", [128, 2048], BF16)
                accs = sb("m_accs", [128, 2048], F32)
                e0t = sb("m_e0t", [128, 512], F32)
                psR = st_.enter_context(nc.psum_tensor(_u() + "m_psR", [128, 16], F32))
                psA = [st_.enter_context(nc.psum_tensor(_u() + f"m_psA{i}", [128, 512], F32)) for i in range(2)]
                psO = [st_.enter_context(nc.psum_tensor(_u() + f"m_psO{i}", [128, 512], F32)) for i in range(2)]
                lsel = [p.load(eselt[:, e * 16:(e + 1) * 16], d_esel[e].broadcast_to([128, 16])) for e in range(2)]
                lgw = [p.load(gw[:, kt * 16:(kt + 1) * 16], d_gw[kt * 128:(kt + 1) * 128, :]) for kt in range(16)]
                for tt in range(64):
                    r, coff = tt // 8, (tt % 8) * 128
                    lyt = p.load(_s3(yf[:], 16), _g3(G32.ap()[r * D:(r + 1) * D, coff:coff + 128]))
                    p.comp("tensor", [lambda t, kt=kt: t.matmul(
                        psR[:], yf[:, kt * 128:(kt + 1) * 128], gw[:, kt * 16:(kt + 1) * 16],
                        start=(kt == 0), stop=(kt == 15)) for kt in range(16)],
                        loads=[lyt] + (lgw if tt == 0 else []))
                    p.comp("scalar", lambda s: s.activation(out=ssb[:], in_=psR[:], func=AF.Sigmoid))
                    sg = ssb[:].rearrange("p (g q) -> p g q", q=4)
                    p.comp("vector", lambda v: v.tensor_reduce(out=r4[:], in_=sg, axis=AX.X, op=ALU.max))
                    bc4 = r4[:].unsqueeze(2).broadcast_to([128, 4, 4])
                    p.comp("vector", lambda v: v.tensor_tensor(out=r16[:].rearrange("p (g q) -> p g q", q=4), in0=sg, in1=bc4, op=ALU.is_equal))
                    p.comp("vector", lambda v: v.scalar_tensor_tensor(out=r16[:], in0=r16[:], scalar=-1e30, in1=ssb[:], op0=ALU.mult, op1=ALU.add))
                    p.comp("vector", lambda v: v.tensor_reduce(out=r4b[:], in_=r16[:].rearrange("p (g q) -> p g q", q=4), axis=AX.X, op=ALU.max))
                    p.comp("vector", lambda v: v.tensor_tensor(out=r4[:], in0=r4[:], in1=r4b[:], op=ALU.add))
                    p.comp("vector", lambda v: v.tensor_reduce(out=r1[:, 0:1], in_=r4[:], axis=AX.X, op=ALU.max))
                    p.comp("vector", lambda v: v.tensor_scalar(out=r4b[:], in0=r4[:], scalar1=r1[:, 0:1], scalar2=None, op0=ALU.is_equal))
                    p.comp("vector", lambda v: v.scalar_tensor_tensor(out=r4b[:], in0=r4b[:], scalar=-1e30, in1=r4[:], op0=ALU.mult, op1=ALU.add))
                    p.comp("vector", lambda v: v.tensor_reduce(out=r1[:, 1:2], in_=r4b[:], axis=AX.X, op=ALU.max))
                    p.comp("vector", lambda v: v.tensor_scalar(out=r4b[:], in0=r4[:], scalar1=r1[:, 1:2], scalar2=None, op0=ALU.is_ge))
                    bc4b = r4b[:].unsqueeze(2).broadcast_to([128, 4, 4])
                    p.comp("vector", lambda v: v.tensor_tensor(out=r16[:].rearrange("p (g q) -> p g q", q=4), in0=sg, in1=bc4b, op=ALU.mult))
                    p.comp("vector", lambda v: v.tensor_reduce(out=r1[:, 2:3], in_=r16[:], axis=AX.X, op=ALU.max))
                    p.comp("vector", lambda v: v.tensor_scalar(out=ssb[:], in0=r16[:], scalar1=r1[:, 2:3], scalar2=None, op0=ALU.is_equal))
                    p.comp("vector", lambda v: v.scalar_tensor_tensor(out=ssb[:], in0=ssb[:], scalar=-1e30, in1=r16[:], op0=ALU.mult, op1=ALU.add))
                    p.comp("vector", lambda v: v.tensor_reduce(out=r1[:, 3:4], in_=ssb[:], axis=AX.X, op=ALU.max))
                    p.comp("vector", lambda v: v.tensor_tensor(out=r1[:, 4:5], in0=r1[:, 2:3], in1=r1[:, 3:4], op=ALU.add))
                    p.comp("vector", lambda v: v.tensor_scalar(out=r1[:, 4:5], in0=r1[:, 4:5], scalar1=1e-20, scalar2=None, op0=ALU.add))
                    p.comp("vector", lambda v: v.reciprocal(out=r1[:, 5:6], in_=r1[:, 4:5]))
                    p.comp("vector", lambda v: v.tensor_scalar(out=ssb[:], in0=r16[:], scalar1=r1[:, 3:4], scalar2=None, op0=ALU.is_ge))
                    p.comp("vector", lambda v: v.tensor_tensor(out=ssb[:], in0=ssb[:], in1=r16[:], op=ALU.mult))
                    p.comp("vector", lambda v: v.tensor_scalar(out=ssb[:], in0=ssb[:], scalar1=r1[:, 5:6], scalar2=None, op0=ALU.mult))
                    for e in range(2):
                        p.comp("vector", lambda v, e=e: v.tensor_tensor(out=combt[:], in0=ssb[:], in1=eselt[:, e * 16:(e + 1) * 16], op=ALU.mult), loads=[lsel[e]])
                        rt_last = p.comp("vector", lambda v, e=e, tt=tt: v.tensor_reduce(out=comb[:, tt * 2 + e: tt * 2 + e + 1], in_=combt[:], axis=AX.X, op=ALU.add))

                def expert_pass(d_w1src, d_w2src, n_tc, ysrc, e, mode, w_dep, y_dep0, e0st=None):
                    lw1 = p.load(_s3(w1sb[:], 16), _g3(d_w1src), queue="gpsimd", deps=[w_dep])
                    lw2 = p.load(_s3(w2sb[:], NH), _g3(d_w2src), queue="gpsimd")

                    def ysrc_ap(tc):
                        if ysrc == "G32":
                            r, sub = tc // 2, tc % 2
                            return _g3(G32.ap()[r * D:(r + 1) * D, sub * 512:(sub + 1) * 512])
                        return _g3(yTl32.ap()[:, tc * 512:(tc + 1) * 512])

                    lys = [None] * n_tc
                    aread = [None, None]
                    oread = [None, None]
                    silu_last_prev = None
                    out_last = None
                    stT = None
                    ev_last = None
                    lys[0] = p.load(_s3(ysb[:], 16), ysrc_ap(0), queue="gpsimd", deps=[y_dep0])
                    for tc in range(n_tc):
                        if tc > 0:
                            lys[tc] = p.load(_s3(ysb[:], 16), ysrc_ap(tc), queue="gpsimd",
                                             deps=[lys[tc - 1], last_a_mm])
                        silu_tok = None
                        for ht in range(NH):
                            par = ht % 2
                            tA = p.comp("tensor", [lambda t, kt=kt, ht=ht, par=par: t.matmul(
                                psA[par][:], w1sb[:, kt * MH + ht * 128: kt * MH + (ht + 1) * 128],
                                ysb[:, kt * 512:(kt + 1) * 512], start=(kt == 0), stop=(kt == 15))
                                for kt in range(16)], loads=[lys[tc], lw1], deps=[aread[par]])
                            last_a_mm = tA
                            silu_tok = p.comp("scalar", lambda s, ht=ht, par=par: s.activation(
                                out=asb[:, ht * 512:(ht + 1) * 512], in_=psA[par][:], func=AF.Silu),
                                deps=[tA, silu_last_prev if ht < 2 else None])
                            aread[par] = silu_tok
                        for tsub in range(4):
                            evs = []
                            for oc in range(4):
                                par = oc % 2
                                tO = p.comp("tensor", [lambda t, ht=ht, tsub=tsub, oc=oc, par=par: t.matmul(
                                    psO[par][:], asb[:, ht * 512 + tsub * 128: ht * 512 + (tsub + 1) * 128],
                                    w2sb[:, ht * D + oc * 512: ht * D + (oc + 1) * 512],
                                    start=(ht == 0), stop=(ht == NH - 1)) for ht in range(NH)],
                                    loads=[lw2], deps=[silu_tok, oread[par]])
                                out_last = tO
                                row0 = tc * 512 + tsub * 128
                                if mode == 0:
                                    cap = comb[:, (tc * 4 + tsub) * 2 + 0: (tc * 4 + tsub) * 2 + 1]
                                    ev = p.comp("vector", lambda v, cap=cap, oc=oc, par=par: v.tensor_scalar(
                                        out=accb[:, oc * 512:(oc + 1) * 512], in0=psO[par][:],
                                        scalar1=cap, scalar2=None, op0=ALU.mult), deps=[tO, stT])
                                elif mode == 1:
                                    cap = comb[:, (tc * 4 + tsub) * 2 + 1: (tc * 4 + tsub) * 2 + 2]
                                    le0 = p.load(e0t[:], e0p.ap()[row0:row0 + 128, oc * 512:(oc + 1) * 512],
                                                 queue="gpsimd", deps=[ev_last, e0st])
                                    ev = p.comp("vector", lambda v, cap=cap, oc=oc, par=par: v.scalar_tensor_tensor(
                                        out=accf[:, oc * 512:(oc + 1) * 512], in0=psO[par][:], scalar=cap,
                                        in1=e0t[:], op0=ALU.mult, op1=ALU.add), deps=[tO, stT], loads=[le0])
                                else:
                                    ev = p.comp("vector", lambda v, oc=oc, par=par: v.tensor_copy(
                                        out=accs[:, oc * 512:(oc + 1) * 512], in_=psO[par][:]), deps=[tO, stT])
                                oread[par] = ev
                                ev_last = ev
                                evs.append(ev)
                            dstt = e0p.ap() if mode == 0 else (part[3].ap() if mode == 1 else shl.ap())
                            srct = accb if mode == 0 else (accf if mode == 1 else accs)
                            stT = p.store(dstt[row0:row0 + 128, :], srct[:], deps=evs)
                        silu_last_prev = out_last
                    return out_last, stT

                wtok, sttok = expert_pass(d_w1[0], d_w2[0], 16, "G32", 0, 0, rt_last, rt_last)
                wtok, sttok = expert_pass(d_w1[1], d_w2[1], 16, "G32", 1, 1, wtok, wtok, e0st=sttok)
                expert_pass(d_sw1, d_sw2, TPC // 512, "local", 0, 2, wtok, wtok)

        cur_x = d_x0
        if upto >= 0:
            ph_ln_ag(d_x0, yTl.ap(), G[0].ap(), False)
        if upto >= 1:
            ph_qkv_units0(G[0].ap(), d_wq0, d_wk0, d_wv0, 1.0 / math.sqrt(DH))
            ph_attn(0, zTl[0].ap())
        if upto >= 2:
            ph_oproj_partial(zTl[0].ap(), d_wo0, part[0].ap())
            ph_rs(part[0].ap(), rs[0].ap())
            ph_resid(d_x0, [(rs[0].ap(), True)], xsc[0].ap())
            cur_x = xsc[0].ap()
        if upto >= 3:
            ph_ln_ag(xsc[0].ap(), yTl.ap(), G[1].ap(), False)
        if upto >= 4:
            ph_ffn0(G[1].ap())
            ph_rs(part[1].ap(), rs[1].ap())
            ph_resid(xsc[0].ap(), [(rs[1].ap(), True)], xsc[1].ap())
            cur_x = xsc[1].ap()
        if upto >= 5:
            ph_ln_ag(xsc[1].ap(), yTl.ap(), G[2].ap(), False)
        if upto >= 6:
            ph_qkv_units1(G[2].ap())
            ph_attn(1, zTl[1].ap())
        if upto >= 7:
            ph_oproj_partial(zTl[1].ap(), d_wo1, part[2].ap())
            ph_rs(part[2].ap(), rs[2].ap())
            ph_resid(xsc[1].ap(), [(rs[2].ap(), True)], xsc[2].ap(), bo_bcast=True)
            cur_x = xsc[2].ap()
        if upto >= 8:
            ph_ln_ag(xsc[2].ap(), yTl32.ap(), G32.ap(), True)
        if upto >= 9:
            ph_moe()
            ph_rs(part[3].ap(), rs[3].ap())
            ph_resid(xsc[2].ap(), [(rs[3].ap(), True), (shl.ap(), False)], d_out)
        if upto < 9:
            with ExitStack() as st_:
                cpy = st_.enter_context(nc.sbuf_tensor(_u() + "dbgcpy", [128, D], F32))
                cpy2 = st_.enter_context(nc.sbuf_tensor(_u() + "dbgcpy2", [128, D], F32))
                for tt in range(NT):
                    lx = p.load(cpy[:], cur_x[tt * 128:(tt + 1) * 128, :])
                    p.comp("vector", lambda v: v.tensor_copy(out=cpy2[:], in_=cpy[:]), loads=[lx])
                    p.store(d_out[tt * 128:(tt + 1) * 128, :], cpy2[:])

        p.build(block, ctx.sems)
    return nc


def fused_in_maps(inputs):
    xf = np.ascontiguousarray(np.asarray(inputs["L0_attn_resid_pre"], dtype=np.float32).reshape(T, D))
    Wq0T, Wk0T, Wv0T = (_t(inputs["L0_attn_Wq"]), _t(inputs["L0_attn_Wk"]), _t(inputs["L0_attn_Wv"]))
    Wo0T = _t(inputs["L0_attn_Wo"])
    Wq1T, Wk1T, Wv1T = (_t(inputs["L1_attn_Wq"]), _t(inputs["L1_attn_Wk"]), _t(inputs["L1_attn_Wv"]))
    Wo1T = _t(inputs["L1_attn_Wo"])
    bq1 = np.asarray(inputs["L1_attn_bq"], dtype=np.float32)
    bk1 = np.asarray(inputs["L1_attn_bk"], dtype=np.float32)
    bv1 = np.asarray(inputs["L1_attn_bv"], dtype=np.float32)
    bo1 = np.asarray(inputs["L1_attn_bo"], dtype=np.float32).reshape(1, D)
    gpk = np.asarray(inputs["L0_ffn_gate_W_packed"], dtype=np.int32)
    upk = np.asarray(inputs["L0_ffn_up_W_packed"], dtype=np.int32)
    dpk = np.asarray(inputs["L0_ffn_down_W_packed"], dtype=np.int32)
    gsc = np.asarray(inputs["L0_ffn_gate_scales"], dtype=np.float32)
    usc = np.asarray(inputs["L0_ffn_up_scales"], dtype=np.float32)
    dsc = np.asarray(inputs["L0_ffn_down_scales"], dtype=np.float32)
    gwT = _t(inputs["L1_ffn_gate_w"])
    w1m = np.asarray(inputs["L1_ffn_w1"], dtype=np.float32)
    w2m = np.asarray(inputs["L1_ffn_w2"], dtype=np.float32)
    sw1 = np.ascontiguousarray(np.asarray(inputs["L1_ffn_sw1"], dtype=np.float32)[0])
    sw2 = np.ascontiguousarray(np.asarray(inputs["L1_ffn_sw2"], dtype=np.float32)[0])
    half = DH // 2
    inv = 1.0 / (10000.0 ** (np.arange(half, dtype=np.float32) / half))
    fr = np.arange(S, dtype=np.float32)[:, None] * inv[None, :]
    cosS = np.ascontiguousarray(np.repeat(np.cos(fr), 2, axis=1).astype(np.float32))
    sinS = np.ascontiguousarray(np.repeat(np.sin(fr), 2, axis=1).astype(np.float32))
    pos = np.arange(S, dtype=np.float32)
    maps = []
    for c in range(NCORES):
        n0 = c * NPC
        hc = slice(2 * c * 128, (2 * c + 2) * 128)
        units = [(b, h) for b in range(B) for h in (2 * c, 2 * c + 1)]
        acolu = np.stack([(SLOPES[h] * pos - EXP_M).reshape(32, 128).T.copy() for (b, h) in units])
        shiftu = np.stack([
            np.stack([-SLOPES[h] * (r * 512 + np.arange(512, dtype=np.float32)) for r in range(8)])
            for (b, h) in units])
        esel = np.zeros((2, 1, 16), dtype=np.float32)
        esel[0, 0, 2 * c] = 1.0
        esel[1, 0, 2 * c + 1] = 1.0
        maps.append(dict(
            x0=np.ascontiguousarray(xf[c * TPC:(c + 1) * TPC]),
            wq0=np.ascontiguousarray(Wq0T[:, hc]), wk0=np.ascontiguousarray(Wk0T[:, hc]),
            wv0=np.ascontiguousarray(Wv0T[:, hc]), wo0=np.ascontiguousarray(Wo0T[hc, :]),
            acolu=np.ascontiguousarray(acolu, dtype=np.float32),
            shiftu=np.ascontiguousarray(shiftu, dtype=np.float32),
            ubu=np.ascontiguousarray(1024.0 + shiftu, dtype=np.float32),
            lbu=np.ascontiguousarray(-1024.0 + shiftu, dtype=np.float32),
            gpk=np.ascontiguousarray(gpk[n0:n0 + NPC]).view(np.int16),
            upk=np.ascontiguousarray(upk[n0:n0 + NPC]).view(np.int16),
            gsc=np.ascontiguousarray(gsc[n0:n0 + NPC]),
            usc=np.ascontiguousarray(usc[n0:n0 + NPC]),
            dpk=np.ascontiguousarray(dpk[:, n0 // 2:(n0 + NPC) // 2]).view(np.int16),
            dsc=np.ascontiguousarray(dsc[:, n0 // 32:(n0 + NPC) // 32]),
            wq1=np.ascontiguousarray(Wq1T[:, hc]), wk1=np.ascontiguousarray(Wk1T[:, hc]),
            wv1=np.ascontiguousarray(Wv1T[:, hc]),
            bq1=np.ascontiguousarray(bq1[hc]).reshape(1, 256),
            bk1=np.ascontiguousarray(bk1[hc]).reshape(1, 256),
            bv1=np.ascontiguousarray(bv1[hc]).reshape(1, 256),
            wo1=np.ascontiguousarray(Wo1T[hc, :]), bo1=bo1,
            cosS=cosS, sinS=sinS,
            gwT=gwT,
            w1=np.ascontiguousarray(w1m[2 * c:2 * c + 2]),
            w2=np.ascontiguousarray(w2m[2 * c:2 * c + 2]),
            sw1=sw1, sw2=sw2, esel=esel,
        ))
    return maps


def kernel_fused(upto=9, **inputs):
    nc = build_fused(upto)
    res = launch(nc, fused_in_maps(inputs))
    out = np.concatenate([r["o_x4"] for r in res.results])
    return out.reshape(B, S, D).astype(np.float32)


def kernel(**inputs):
    return kernel_fused(9, **inputs)
